# version 08
# baseline (speedup 1.0000x reference)
import math
import numpy as np
import ml_dtypes
from contextlib import ExitStack

import concourse.bass as bass
import concourse.mybir as mybir
from concourse.bass_utils import run_bass_kernel_spmd

F32 = mybir.dt.float32
BF16 = mybir.dt.bfloat16
I32 = mybir.dt.int32
AF = mybir.ActivationFunctionType
ALU = mybir.AluOpType
AX = mybir.AxisListType

T = 2048
D = 1024
NCH = D // 128
S = 4096
BATCH = 4
MAGIC0 = 0x5A000000
LN_EPS = 1e-5

bf16 = ml_dtypes.bfloat16
ENGINES = ("sync", "tensor", "scalar", "vector", "gpsimd")


class Sem:
    def __init__(self, ctx, name):
        self.h = ctx.e(ctx.nc.semaphore(name))
        self.n = 0

    def inc(self, ins, k=1):
        ins.then_inc(self.h, k)
        self.n += k
        return self.n


class Ctx:
    def __init__(self, nc):
        self.nc = nc
        self.stack = ExitStack()
        self.e = self.stack.enter_context
        self.block = None
        self.dram = {}
        self.consts = {}
        self.n_events = 0
        self._uid = 0
        self._bank_rr = 0
        self.psum_owner = {}
        self.parity_regs = {}
        self.poll_regs = None
        self.sub = None

    def setup_sems(self):
        self.E = {e_: Sem(self, "s_" + e_) for e_ in ENGINES}
        self.dma = {e_: Sem(self, "d_" + e_) for e_ in ("sync", "scalar", "gpsimd")}
        self.psem = Sem(self, "s_poll")
        self.xrdy = Sem(self, "s_xrdy")
        self.dwb = [Sem(self, "d_wb0"), Sem(self, "d_wb1")]
        self.dk = Sem(self, "d_k")
        self.dv = Sem(self, "d_v")
        self.rsem = [Sem(self, f"d_ring{i}") for i in range(8)]
        self.ldpool = [Sem(self, f"d_ld{i}") for i in range(16)]
        self.dconst = Sem(self, "d_const")
        self.dE = Sem(self, "d_E")
        self._ldrr = 0
        self._ld_used = 0

    def all_dma_sems(self):
        return ([*self.dma.values(), self.psem, *self.dwb, self.dk, self.dv,
                 *self.rsem, *self.ldpool, self.dconst, self.dE])

    def new_ldsem(self):
        assert self._ld_used < 16, "more than 16 load groups in one sublayer"
        s_ = self.ldpool[self._ldrr % 16]
        self._ldrr += 1
        self._ld_used += 1
        return s_

    def sublayer_scope(self):
        if self.sub is not None:
            self.sub.close()
            self.barrier()
        self._ld_used = 0
        self.sub = ExitStack()

    def tmp(self, name, shape, dtype):
        return self.sub.enter_context(self.nc.sbuf_tensor(name + str(self.uid()), shape, dtype))

    def pers(self, name, shape, dtype):
        return self.e(self.nc.sbuf_tensor(name + str(self.uid()), shape, dtype,
                                          side="right"))

    def close_scopes(self):
        if getattr(self, "sub", None) is not None:
            self.sub.close()
            self.sub = None

    def uid(self):
        self._uid += 1
        return self._uid

    def new_event(self):
        i = self.n_events
        self.n_events += 1
        assert i < 64
        return i

    def on(self, engine_name, fn):
        getattr(self.block, engine_name)(fn)

    def fconst(self, value, p=128):
        key = (float(value), p)
        if not hasattr(self, "_fconsts"):
            self._fconsts = {}
        if key not in self._fconsts:
            t = self.e(self.nc.sbuf_tensor(f"fc{self.uid()}", [p, 1], F32, side="right"))
            Ev = self.E["vector"]

            def _(vector):
                ins = vector.memset(t[:], float(value))
                Ev.inc(ins, 1)

            self.on("vector", _)
            self._fconsts[key] = (t, (Ev, Ev.n))
        t, dep = self._fconsts[key]
        return t[:], dep

    def next_bank(self):
        b = self._bank_rr % 4
        self._bank_rr += 1
        return b

    def psum_wait(self, tensor_eng, bank_key):
        own = self.psum_owner.get(bank_key)
        if own is not None:
            tensor_eng.wait_ge(own[0].h, own[1])

    def psum_set_owner(self, bank_key, sem, tok):
        self.psum_owner[bank_key] = (sem, tok)

    def barrier(self):
        toks = {}
        dtoks = [(s_, s_.n) for s_ in self.all_dma_sems()]
        for e_ in ENGINES:
            def _(eng, e_=e_):
                toks[e_] = self.E[e_].inc(eng.drain(), 1)
            self.on(e_, _)
        for e_ in ENGINES:
            def _(eng):
                for o in ENGINES:
                    eng.wait_ge(self.E[o].h, toks[o])
                for s_, t_ in dtoks:
                    if t_:
                        eng.wait_ge(s_.h, t_)
            self.on(e_, _)

    def _sync_regs(self):
        if self.poll_regs is None:
            regs = {}

            def _(sync):
                regs["parity"] = sync.alloc_register("r_parity")
                sync.wait_ge(self.dconst.h, self.ld_const_tok)
                sync.reg_load(regs["parity"], self.slot_sb[:])
                regs["flag"] = sync.alloc_register("r_flag")
                regs["diff"] = sync.alloc_register("r_diff")
                regs["pc"] = sync.alloc_register("r_pc")
                sync.reg_mov(regs["pc"], 0)

            self.on("sync", _)
            self.poll_regs = regs
        return self.poll_regs

    def parity_reg(self, eng_name):
        if eng_name == "sync":
            return self._sync_regs()["parity"]
        if eng_name not in self.parity_regs:
            holder = {}

            def _(eng):
                holder["r"] = eng.alloc_register("r_par_" + eng_name)
                eng.wait_ge(self.dconst.h, self.ld_const_tok)
                eng.reg_load(holder["r"], self.slot_sb[:])

            self.on(eng_name, _)
            self.parity_regs[eng_name] = holder["r"]
        return self.parity_regs[eng_name]

    def by_parity(self, eng_name, emit_fn):
        preg = self.parity_reg(eng_name)

        def _(eng):
            with eng.If_eq(preg, 0):
                emit_fn(eng, 0)
            with eng.Else():
                emit_fn(eng, 1)
            eng.end_ifs()

        self.on(eng_name, _)

    def mbox_flag(self, ev, dep_waits=()):
        self._sync_regs()
        dsem = self.dma["sync"]

        def _(sync):
            for s_, t_ in dep_waits:
                sync.wait_ge(s_.h, t_)

        self.on("sync", _)

        def emit_flag(sync, slot):
            sync.dma_start(out=self.shflag[slot, :, ev:ev + 1],
                           in_=self.magic_sb[:, ev:ev + 1]).then_inc(dsem.h, 16)

        self.by_parity("sync", emit_flag)
        dsem.n += 16
        return dsem.n

    def mbox_send(self, writes, dep_waits=()):
        ev = self.new_event()
        self._sync_regs()
        dsem = self.dma["sync"]

        def _(sync):
            for s_, t_ in dep_waits:
                sync.wait_ge(s_.h, t_)

        self.on("sync", _)

        def emit_writes(sync, slot):
            for dst_by_slot, src in writes:
                sync.dma_start(out=dst_by_slot(slot), in_=src,
                               allow_slow_non_contiguous=True).then_inc(dsem.h, 16)

        self.by_parity("sync", emit_writes)
        dsem.n += 16 * len(writes)
        data_tok = dsem.n

        def _(sync):
            sync.wait_ge(dsem.h, data_tok)

        self.on("sync", _)

        def emit_flag(sync, slot):
            sync.dma_start(out=self.shflag[slot, :, ev:ev + 1],
                           in_=self.magic_sb[:, ev:ev + 1]).then_inc(dsem.h, 16)

        self.by_parity("sync", emit_flag)
        dsem.n += 16
        return ev

    def mbox_wait(self, ev):
        regs = self._sync_regs()

        def _(sync):
            sync.store(self.flag_sb[:], 0)
            sync.reg_mov(regs["diff"], 1)
            with sync.While(regs["diff"]):
                with sync.If_eq(regs["parity"], 0):
                    sync.dma_start(out=self.flag_sb[:],
                                   in_=self.shflag[1, :, ev:ev + 1]).then_inc(self.psem.h, 16)
                with sync.Else():
                    sync.dma_start(out=self.flag_sb[:],
                                   in_=self.shflag[0, :, ev:ev + 1]).then_inc(self.psem.h, 16)
                sync.end_ifs()
                sync.reg_add(regs["pc"], regs["pc"], 16)
                sync.wait_ge(self.psem.h, regs["pc"])
                sync.reg_load(regs["flag"], self.flag_sb[:])
                sync.reg_sub(regs["diff"], regs["flag"], MAGIC0 + ev)
            ins = sync.drain()
            self.xrdy.inc(ins, 1)

        self.on("sync", _)
        return self.xrdy.n


def setup_shared_flags(ctx):
    ctx.shflag = ctx.nc.dram_tensor("shflag", [2, 1, 64], I32, kind="Internal",
                                    addr_space="Shared").ap()


def ceil_div(a, b):
    return (a + b - 1) // b



def alloc_core_buffers(ctx):
    nc, e = ctx.nc, ctx.e
    ctx.x_sb = e(nc.sbuf_tensor("x_sb", [128, NCH * T], F32))
    ctx.xn_sb = e(nc.sbuf_tensor("xn_sb", [128, NCH * T], BF16))
    ctx.ones_row = e(nc.sbuf_tensor("ones_row", [1, 128], BF16))
    ctx.ones_row_t = e(nc.sbuf_tensor("ones_row_t", [1, 512], BF16))
    ctx.ones_rowf = e(nc.sbuf_tensor("ones_rowf", [1, 128], F32))
    ctx.ones_colf = e(nc.sbuf_tensor("ones_colf", [128, 1], F32))
    ctx.ones_colb = e(nc.sbuf_tensor("ones_colb", [128, 1], BF16))
    ctx.ident = e(nc.sbuf_tensor("ident", [128, 128], BF16))
    ck = getattr(ctx, "const_kinds", {"l2f", "l2a", "l3f"})
    if "l2f" in ck:
        ctx.identf = e(nc.sbuf_tensor("identf", [128, 128], F32))
    if "l2a" in ck:
        ctx.l2rot1 = e(nc.sbuf_tensor("l2rot1", [48, 48], BF16))
        ctx.l2rot2 = e(nc.sbuf_tensor("l2rot2", [48, 48], BF16))
        ctx.l2tri = e(nc.sbuf_tensor("l2tri", [48, 16], BF16))
    if "l3f" in ck:
        ctx.l3f_ind2 = e(nc.sbuf_tensor("l3f_ind2", [128, 2], F32))
        ctx.l3f_ind2T = e(nc.sbuf_tensor("l3f_ind2T", [2, 128], F32))
    ctx.magic_sb = e(nc.sbuf_tensor("magic_sb", [1, 64], I32))
    ctx.ind16_sb = e(nc.sbuf_tensor("ind16_sb", [128, 128], F32))
    ctx.permswap_sb = e(nc.sbuf_tensor("permswap_sb", [128, 128], BF16))
    ctx.inddup_sb = e(nc.sbuf_tensor("inddup_sb", [32, 128], BF16))
    ctx.inddupsgn_sb = e(nc.sbuf_tensor("inddupsgn_sb", [32, 128], BF16))
    ctx.slot_sb = e(nc.sbuf_tensor("slot_sb", [1, 1], I32))
    ctx.flag_sb = e(nc.sbuf_tensor("flag_sb", [1, 1], I32))
    ctx.ps = [e(nc.psum_tensor(f"ps{i}", [128, 512], F32)) for i in range(6)]
    ctx.psb = e(nc.psum_tensor("psb", [128, 1024], BF16))
    ctx.wbuf = [e(nc.sbuf_tensor(f"wbuf{i}", [128, 4096], BF16)) for i in range(2)]
    ctx.wbuf_state = [None, None]


def setup_const_inputs(ctx):
    ctx.consts["c_ones_row"] = np.ones((1, 128), dtype=bf16)
    ctx.consts["c_ones_row_t"] = np.ones((1, 512), dtype=bf16)
    ctx.consts["c_ones_rowf"] = np.ones((1, 128), dtype=np.float32)
    ctx.consts["c_ones_colf"] = np.ones((128, 1), dtype=np.float32)
    ctx.consts["c_ones_colb"] = np.ones((128, 1), dtype=bf16)
    ctx.consts["c_ident"] = np.eye(128, dtype=bf16)
    ck = getattr(ctx, "const_kinds", {"l2f", "l2a", "l3f"})
    if "l2f" in ck:
        ctx.consts["c_identf"] = np.eye(128, dtype=np.float32)
    if "l2a" in ck:
        rot1 = np.zeros((48, 48), dtype=bf16)
        rot2 = np.zeros((48, 48), dtype=bf16)
        tri = np.zeros((48, 16), dtype=bf16)
        for h_ in range(16):
            for c_ in range(3):
                rot1[3 * h_ + (c_ + 1) % 3, 3 * h_ + c_] = 1
                rot2[3 * h_ + (c_ + 2) % 3, 3 * h_ + c_] = 1
                tri[3 * h_ + c_, h_] = 1
        ctx.consts["c_l2rot1"] = rot1
        ctx.consts["c_l2rot2"] = rot2
        ctx.consts["c_l2tri"] = tri
    if "l3f" in ck:
        ind2 = np.zeros((128, 2), dtype=np.float32)
        ind2[0:64, 0] = 1.0
        ind2[64:128, 1] = 1.0
        ctx.consts["c_l3f_ind2"] = ind2
        ctx.consts["c_l3f_ind2T"] = np.ascontiguousarray(ind2.T)
    ctx.consts["c_magic"] = (MAGIC0 + np.arange(64, dtype=np.int32)).reshape(1, 64)
    ctx.consts["c_slot"] = np.zeros((1, 1), dtype=np.int32)
    ind16 = np.zeros((128, 128), dtype=np.float32)
    for c in range(8):
        for p in range(128):
            ind16[p, c * 16 + 2 * c + (p // 64)] = 1
    ctx.consts["c_ind16"] = ind16
    perm = np.zeros((128, 128), dtype=bf16)
    for i in range(128):
        j = i + 32 if (i % 64) < 32 else i - 32
        perm[j, i] = 1
    ctx.consts["c_permswap"] = perm
    dup = np.zeros((32, 128), dtype=bf16)
    sgn = np.zeros((32, 128), dtype=bf16)
    for r in range(32):
        for rep in range(4):
            dup[r, r + rep * 32] = 1
            sgn[r, r + rep * 32] = -1 if rep % 2 == 0 else 1
    ctx.consts["c_inddup"] = dup
    ctx.consts["c_inddupsgn"] = sgn
    for name, arr in ctx.consts.items():
        dt = {np.dtype(np.float32): F32, np.dtype(bf16): BF16,
              np.dtype(np.int32): I32}[arr.dtype]
        ctx.dram[name] = ctx.nc.dram_tensor(name, list(arr.shape), dt,
                                            kind="ExternalInput").ap()


def emit_setup(ctx, extra_loads=()):
    dsem = ctx.dconst

    def _(sync):
        for nm, dst in (("c_ones_row", ctx.ones_row), ("c_ones_row_t", ctx.ones_row_t),
                        ("c_ones_rowf", ctx.ones_rowf),
                        ("c_ones_colf", ctx.ones_colf), ("c_ones_colb", ctx.ones_colb),
                        ("c_ident", ctx.ident),
                        *([("c_identf", ctx.identf)] if hasattr(ctx, "identf") else []),
                        *([("c_l2rot1", ctx.l2rot1), ("c_l2rot2", ctx.l2rot2),
                           ("c_l2tri", ctx.l2tri)] if hasattr(ctx, "l2rot1") else []),
                        *([("c_l3f_ind2", ctx.l3f_ind2),
                           ("c_l3f_ind2T", ctx.l3f_ind2T)] if hasattr(ctx, "l3f_ind2") else []),
                        ("c_magic", ctx.magic_sb), ("c_slot", ctx.slot_sb),
                        ("c_ind16", ctx.ind16_sb), ("c_permswap", ctx.permswap_sb),
                        ("c_inddup", ctx.inddup_sb), ("c_inddupsgn", ctx.inddupsgn_sb)):
            ins = sync.dma_start(out=dst[:], in_=ctx.dram[nm])
            dsem.inc(ins, 16)
        for dst_ap, src_ap in extra_loads:
            ins = sync.dma_start(out=dst_ap, in_=src_ap)
            dsem.inc(ins, 16)

    ctx.on("sync", _)
    ctx.ld_const_tok = dsem.n
    return ctx.ld_const_tok


def emit_load_vec(ctx, dram_ap, n, dst_ap, scale=1.0):
    dsem = ctx.new_ldsem()

    def _(g):
        src = dram_ap.rearrange("(c p) -> p c", p=128)
        ins = g.dma_start(out=dst_ap, in_=src, allow_slow_non_contiguous=True)
        dsem.inc(ins, 16)

    ctx.on("gpsimd", _)
    tok = dsem.n
    if scale == 1.0:
        return (dsem, tok)
    Ev = ctx.E["vector"]

    def _(vector):
        vector.wait_ge(dsem.h, tok)
        ins = vector.tensor_scalar(out=dst_ap, in0=dst_ap, scalar1=float(scale),
                                   scalar2=None, op0=ALU.mult)
        Ev.inc(ins, 1)

    ctx.on("vector", _)
    return (Ev, Ev.n)



def emit_layernorm(ctx, out_sb=None, src_sb=None, gain_row=None, beta_row=None,
                   src_tile=None, src_bf16=False):
    if out_sb is None:
        out_sb = ctx.xn_sb
    if src_sb is None:
        src_sb = ctx.x_sb
    if src_tile is None:
        src_tile = lambda d, c0, w: src_sb[:, d * T + c0: d * T + c0 + w]
    Et, Es, Ev = ctx.E["tensor"], ctx.E["scalar"], ctx.E["vector"]
    ps_sum, ps_sq = ctx.ps[4], ctx.ps[5]
    _ln_stack = ExitStack()
    _lne = _ln_stack.enter_context
    mu = _lne(ctx.nc.sbuf_tensor(f"ln_mu{ctx.uid()}", [1, T], F32))
    rstd = _lne(ctx.nc.sbuf_tensor(f"ln_rstd{ctx.uid()}", [1, T], F32))
    bc = _lne(ctx.nc.sbuf_tensor(f"ln_bc{ctx.uid()}", [128, 2 * T], BF16))
    sq = _lne(ctx.nc.sbuf_tensor(f"ln_sq{ctx.uid()}", [128, 1024], F32))
    eps_ap, eps_dep = ctx.fconst(LN_EPS, 1)

    for tt in range(4):
        tsl = slice(tt * 512, (tt + 1) * 512)
        sc_base = Es.n
        mm_base = Et.n

        def _(scalar):
            for d in range(NCH):
                half = (d % 2) * 512
                if d >= 2:
                    scalar.wait_ge(Et.h, mm_base + (d - 2) + 1)
                ins = scalar.activation(out=sq[:, half:half + 512],
                                        in_=src_tile(d, tt * 512, 512),
                                        func=AF.Square)
                Es.inc(ins, 1)

        ctx.on("scalar", _)

        def _(tensor):
            ctx.psum_wait(tensor, 4)
            ctx.psum_wait(tensor, 5)
            for d in range(NCH):
                half = (d % 2) * 512
                tensor.matmul(ps_sum[0:1, :],
                              ctx.ones_colb[:] if src_bf16 else ctx.ones_colf[:],
                              src_tile(d, tt * 512, 512),
                              start=(d == 0), stop=(d == NCH - 1))
                tensor.wait_ge(Es.h, sc_base + d + 1)
                ins = tensor.matmul(ps_sq[0:1, :], ctx.ones_colf[:],
                                    sq[:, half:half + 512],
                                    start=(d == 0), stop=(d == NCH - 1))
                Et.inc(ins, 1)

        ctx.on("tensor", _)
        mm_sums = Et.n

        def _(vector):
            vector.wait_ge(Et.h, mm_sums)
            ins = vector.tensor_scalar(out=mu[0:1, tsl], in0=ps_sum[0:1, :],
                                       scalar1=1.0 / D, scalar2=None, op0=ALU.mult)
            mu_tok = Ev.inc(ins, 1)
            vector.wait_ge(Ev.h, mu_tok)
            ins = vector.tensor_tensor(out=rstd[0:1, tsl], in0=mu[0:1, tsl],
                                       in1=mu[0:1, tsl], op=ALU.mult)
            t2 = Ev.inc(ins, 1)
            vector.wait_ge(Ev.h, t2)
            ins = vector.scalar_tensor_tensor(out=rstd[0:1, tsl], in0=ps_sq[0:1, :],
                                              scalar=1.0 / D, in1=rstd[0:1, tsl],
                                              op0=ALU.mult, op1=ALU.subtract)
            Ev.inc(ins, 1)

        ctx.on("vector", _)
        var_tok = Ev.n

        def _(scalar):
            scalar.wait_ge(Ev.h, var_tok)
            scalar.wait_ge(eps_dep[0].h, eps_dep[1])
            ins = scalar.activation(out=rstd[0:1, tsl], in_=rstd[0:1, tsl],
                                    func=AF.Ln, bias=eps_ap)
            t1 = Es.inc(ins, 1)
            scalar.wait_ge(Es.h, t1)
            ins = scalar.activation(out=rstd[0:1, tsl], in_=rstd[0:1, tsl],
                                    func=AF.Exp, scale=-0.5)
            Es.inc(ins, 1)

        ctx.on("scalar", _)
        rstd_tok = Es.n

        def _(tensor):
            tensor.wait_ge(Es.h, rstd_tok)
            tensor.matmul(ps_sum[:, :], ctx.ones_rowf[:], mu[0:1, tsl],
                          start=True, stop=True)
            ins = tensor.matmul(ps_sq[:, :], ctx.ones_rowf[:], rstd[0:1, tsl],
                                start=True, stop=True)
            Et.inc(ins, 1)

        ctx.on("tensor", _)
        bc_mm = Et.n

        def _(scalar):
            scalar.wait_ge(Et.h, bc_mm)
            scalar.activation(out=bc[:, tt * 512:(tt + 1) * 512], in_=ps_sum[:, :],
                              func=AF.Copy)
            ins = scalar.activation(out=bc[:, T + tt * 512: T + (tt + 1) * 512],
                                    in_=ps_sq[:, :], func=AF.Copy)
            Es.inc(ins, 1)

        ctx.on("scalar", _)
        ctx.psum_set_owner(4, Es, Es.n)
        ctx.psum_set_owner(5, Es, Es.n)

    bc_all = Es.n

    def _(vector):
        vector.wait_ge(Es.h, bc_all)
        if gain_row is not None:
            vector.wait_ge(gain_row[1][0].h, gain_row[1][1])
            vector.wait_ge(beta_row[1][0].h, beta_row[1][1])
        for d in range(NCH):
            for tt in range(4):
                osl = slice(d * T + tt * 512, d * T + (tt + 1) * 512)
                xs = src_tile(d, tt * 512, 512)
                half = (tt % 2) * 512
                ins = vector.tensor_tensor(out=sq[:, half:half + 512], in0=xs,
                                           in1=bc[:, tt * 512:(tt + 1) * 512],
                                           op=ALU.subtract)
                t1 = Ev.inc(ins, 1)
                vector.wait_ge(Ev.h, t1)
                ins = vector.tensor_tensor(out=out_sb[:, osl],
                                           in0=sq[:, half:half + 512],
                                           in1=bc[:, T + tt * 512: T + (tt + 1) * 512],
                                           op=ALU.mult)
                t2 = Ev.inc(ins, 1)
                vector.wait_ge(Ev.h, t2)
                if gain_row is not None:
                    ins = vector.tensor_scalar(out=out_sb[:, osl], in0=out_sb[:, osl],
                                               scalar1=gain_row[0][:, d:d + 1],
                                               scalar2=beta_row[0][:, d:d + 1],
                                               op0=ALU.mult, op1=ALU.add)
                    t3 = Ev.inc(ins, 1)
                    vector.wait_ge(Ev.h, t3)

    ctx.on("vector", _)
    _ln_stack.close()
    if not getattr(ctx, "no_ln_barrier", False):
        ctx.barrier()
    return Ev.n



class WDirect:

    def __init__(self, ap, K, N, cast=True, off=0):
        self.ap, self.K, self.N, self.cast, self.off = ap, K, N, cast, off

    def load(self, ctx, ob0, ncols, bi):
        kc_n = self.K // 128
        eng = "gpsimd" if self.cast else "scalar"
        dsem = ctx.dwb[bi]
        prev = ctx.wbuf_state[bi]
        ob0 = ob0 + self.off

        def _(g):
            if prev is not None:
                g.wait_ge(prev[0].h, prev[1])
            src = self.ap[:, ob0:ob0 + ncols].rearrange("(c p) n -> p c n", p=128)
            dst = ctx.wbuf[bi][:, 0:kc_n * ncols].rearrange("p (c n) -> p c n", n=ncols)
            ins = g.dma_start(out=dst, in_=src)
            dsem.inc(ins, 16)

        ctx.on(eng, _)
        return (dsem, dsem.n)


class WTransposedScratch:

    def __init__(self, ap, K, N):
        self.ap, self.K, self.N = ap, K, N

    def load(self, ctx, ob0, ncols, bi):
        kc_n = self.K // 128
        dsem = ctx.dwb[bi]
        prev = ctx.wbuf_state[bi]

        def _(sc):
            if prev is not None:
                sc.wait_ge(prev[0].h, prev[1])
            for kc_i in range(kc_n):
                src = self.ap[ob0:ob0 + ncols, kc_i * 128:(kc_i + 1) * 128]
                dst = ctx.wbuf[bi][:, kc_i * ncols:(kc_i + 1) * ncols]
                ins = sc.dma_start_transpose(dst, src)
                dsem.inc(ins, 16)

        ctx.on("scalar", _)
        return (dsem, dsem.n)


def emit_proj(ctx, wsrc, sink, src=None, obk=512, tensor_dep=None, tiles_t=4,
              tile_w=512, tail_mm=None, tt_major=False, src_tile=None):
    if src is None:
        src = lambda kc: ctx.xn_sb[:, kc * T:(kc + 1) * T]
    Et = ctx.E["tensor"]
    K, N = wsrc.K, wsrc.N
    kc_n = K // 128
    nblocks = ceil_div(N, obk)
    loads = [None] * nblocks

    def emit_block(i):
        bi_, (wsem, wtok), ob0, ncols = loads[i]
        last_tok = None
        nsub = ceil_div(ncols, 128)
        order = ([(osub, tt) for osub in range(nsub) for tt in range(tiles_t)]
                 if not tt_major else
                 [(osub, tt) for tt in range(tiles_t) for osub in range(nsub)])
        for osub, tt in order:
            o128 = min(128, ncols - osub * 128)
            if True:
                bank = ctx.next_bank()
                mm_tok = [None]

                def _(tensor):
                    tensor.wait_ge(wsem.h, wtok)
                    if tensor_dep:
                        for s_, t_ in tensor_dep:
                            tensor.wait_ge(s_.h, t_)
                    ctx.psum_wait(tensor, bank)
                    for kc_i in range(kc_n):
                        lhsT = ctx.wbuf[bi_][:, kc_i * ncols + osub * 128:
                                             kc_i * ncols + osub * 128 + o128]
                        if src_tile is not None:
                            rhs = src_tile(kc_i, tt * tile_w, tile_w)
                        else:
                            rhs = src(kc_i)[:, tt * tile_w:(tt + 1) * tile_w]
                        ins = tensor.matmul(ctx.ps[bank][0:o128, :], lhsT, rhs,
                                            start=(kc_i == 0),
                                            stop=(kc_i == kc_n - 1 and tail_mm is None))
                    if tail_mm is not None:
                        ins = tail_mm(tensor, (ob0 + osub * 128) // 128, tt,
                                      ctx.ps[bank][0:o128, :])
                    mm_tok[0] = Et.inc(ins, 1)

                ctx.on("tensor", _)
                consumer = sink((ob0 + osub * 128) // 128, tt,
                                ctx.ps[bank][0:o128, 0:tile_w], mm_tok[0])
                ctx.psum_set_owner(bank, *consumer)
                last_tok = mm_tok[0]
        ctx.wbuf_state[bi_] = (Et, last_tok)

    for i in range(nblocks):
        ob0 = i * obk
        ncols = min(obk, N - ob0)
        bi_ = i % 2
        loads[i] = (bi_, wsrc.load(ctx, ob0, ncols, bi_), ob0, ncols)
        if i >= 1:
            emit_block(i - 1)
    emit_block(nblocks - 1)


def sink_to_sbuf(ctx, target, func=AF.Copy, scale=1.0, bias=None, out_dtype=None,
                 accum_out=None):
    Es = ctx.E["scalar"]

    def sink(oc, tt, psum_ap, mm_tok):
        def _(scalar):
            scalar.wait_ge(ctx.E["tensor"].h, mm_tok)
            kw = {}
            if bias is not None:
                kw["bias"] = bias(oc)
            if accum_out is not None:
                kw["accum_out"] = accum_out(oc, tt)
            ins = scalar.activation(out=target(oc, tt), in_=psum_ap, func=func,
                                    scale=scale, **kw)
            Es.inc(ins, 1)

        ctx.on("scalar", _)
        return (Es, Es.n)

    return sink


def sink_resid(ctx, scale=1.0, oc_base=0):
    Ev = ctx.E["vector"]

    def sink(oc, tt, psum_ap, mm_tok):
        oc2 = oc + oc_base

        def _(vector):
            vector.wait_ge(ctx.E["tensor"].h, mm_tok)
            xs = ctx.x_sb[:, oc2 * T + tt * 512: oc2 * T + (tt + 1) * 512]
            ins = vector.scalar_tensor_tensor(out=xs, in0=psum_ap, scalar=float(scale),
                                              in1=xs, op0=ALU.mult, op1=ALU.add)
            Ev.inc(ins, 1)

        ctx.on("vector", _)
        return (Ev, Ev.n)

    return sink


def emit_bias_row_load(ctx, dram_vec_ap, n, name=None):
    row = ctx.tmp("brow", [1, n], BF16)
    dg = ctx.new_ldsem()

    def _(g):
        ins = g.dma_start(out=row[:], in_=dram_vec_ap.rearrange("(o n) -> o n", o=1))
        dg.inc(ins, 16)

    ctx.on("gpsimd", _)
    return row, (dg, dg.n)


def bias_tail_mm(ctx, bias_row, bias_dep, ones_len=512):

    def tail(tensor, oc, tt, bank_ap):
        tensor.wait_ge(bias_dep[0].h, bias_dep[1])
        o128 = bank_ap.shape[0]
        return tensor.matmul(bank_ap, bias_row[:, oc * 128: oc * 128 + o128],
                             ctx.ones_row_t[0:1, 0:bank_ap.shape[1]],
                             start=False, stop=True)

    return tail


def emit_residual_add(ctx, src_sb=None, dep=None):
    if src_sb is None:
        src_sb = ctx.out_sb
    Ev = ctx.E["vector"]

    def _(vector):
        if dep:
            for s_, t_ in dep:
                vector.wait_ge(s_.h, t_)
        for d in range(NCH):
            xs = ctx.x_sb[:, d * T:(d + 1) * T]
            ins = vector.tensor_tensor(out=xs, in0=xs,
                                       in1=src_sb[:, d * T:(d + 1) * T], op=ALU.add)
            tok = Ev.inc(ins, 1)
            vector.wait_ge(Ev.h, tok)

    ctx.on("vector", _)
    return Ev.n



def to_feature_major(x_shard):
    t, d = x_shard.shape
    return np.ascontiguousarray(
        x_shard.reshape(t, d // 128, 128).transpose(2, 1, 0).reshape(128, -1))


def from_feature_major(fm, t=T):
    nch = fm.shape[1] // t
    return np.ascontiguousarray(
        fm.reshape(128, nch, t).transpose(2, 1, 0).reshape(t, nch * 128))



L0_SCALING = 32.0 / 16.0


def emit_l0_ffn(ctx, xn_tok):
    nc = ctx.nc
    Et, Es, Ev = ctx.E["tensor"], ctx.E["scalar"], ctx.E["vector"]
    xn_dep = [(Ev, xn_tok)]
    A_sb = ctx.tmp("l0A", [128, 1024], BF16)
    B_sb = ctx.tmp("l0B", [128, 1024], BF16)
    rwT_sb = ctx.tmp("l0rw", [128, 64], BF16)
    ind_sb = ctx.tmp("l0ind", [8, 128], BF16)
    lg_sb = ctx.tmp("l0lg", [128, 128], F32)
    m1_sb = ctx.tmp("l0m1", [128, 32], F32)
    wtok_sb = ctx.tmp("l0wt", [128, 128], BF16)
    wfeat_sb = ctx.tmp("l0wf", [8, T], BF16)
    ax_sb = ctx.tmp("l0ax", [128, T], BF16)
    sxa_sb = ctx.tmp("l0sxa", [128, T], BF16)
    scr_sb = ctx.tmp("l0scr", [128, 256], F32)
    rbrow_sb = ctx.tmp("l0rbr", [1, 8], BF16)

    dg = ctx.new_ldsem()

    def _(g):
        for dst, src_ in (
            (A_sb[:].rearrange("p (c n) -> p c n", n=128),
             ctx.dram["L0f_A_all"].rearrange("(c p) n -> p c n", p=128)),
            (B_sb[:], ctx.dram["L0f_B_all"]),
            (ind_sb[:], ctx.dram["c_ind_er"]),
            (rwT_sb[:].rearrange("p (c n) -> p c n", n=8),
             ctx.dram["L0f_rwT"].rearrange("(c p) n -> p c n", p=128)),
            (rbrow_sb[:], ctx.dram["L0f_rb"].rearrange("(o n) -> o n", o=1)),
        ):
            ins = g.dma_start(out=dst, in_=src_)
            dg.inc(ins, 16)

    ctx.on("gpsimd", _)
    misc_tok = dg.n
    bb_row, bb_dep = emit_bias_row_load(ctx, ctx.dram["L0f_base_b"], D)

    lg_bank = 4
    def _(tensor):
        ctx.psum_wait(tensor, lg_bank)
        tensor.wait_ge(dg.h, misc_tok)
        for s_, t_ in xn_dep:
            tensor.wait_ge(s_.h, t_)
        for i in range(16):
            for kc in range(NCH):
                ins = tensor.matmul(ctx.ps[lg_bank][:, i * 8:(i + 1) * 8],
                                    ctx.xn_sb[:, kc * T + i * 128: kc * T + (i + 1) * 128],
                                    rwT_sb[:, kc * 8:(kc + 1) * 8],
                                    start=(i == 0 and kc == 0), stop=False)
        for i in range(16):
            ins = tensor.matmul(ctx.ps[lg_bank][:, i * 8:(i + 1) * 8],
                                ctx.ones_row[:], rbrow_sb[:],
                                start=False, stop=(i == 15))
        Et.inc(ins, 1)

    ctx.on("tensor", _)
    lg_mm = Et.n

    def _(vector):
        vector.wait_ge(Et.h, lg_mm)
        ins = vector.tensor_copy(out=lg_sb[:], in_=ctx.ps[lg_bank][:, 0:128])
        t = Ev.inc(ins, 1)
        vector.wait_ge(Ev.h, t)
        ins = vector.tensor_reduce(out=m1_sb[:, 0:16].rearrange("p (a o) -> p a o", o=1),
                                   in_=lg_sb[:].rearrange("p (a e) -> p a e", e=8),
                                   axis=AX.X, op=ALU.max)
        t = Ev.inc(ins, 1)
        vector.wait_ge(Ev.h, t)
        for i in range(16):
            ins = vector.tensor_scalar(out=lg_sb[:, i * 8:(i + 1) * 8],
                                       in0=lg_sb[:, i * 8:(i + 1) * 8],
                                       scalar1=m1_sb[:, i:i + 1], scalar2=None,
                                       op0=ALU.subtract)
        t = Ev.inc(ins, 1)
        vector.wait_ge(Ev.h, t)
        ins = vector.tensor_scalar(out=scr_sb[:, 0:128], in0=lg_sb[:],
                                   scalar1=0.0, scalar2=-1e9, op0=ALU.is_ge,
                                   op1=ALU.mult)
        t = Ev.inc(ins, 1)
        vector.wait_ge(Ev.h, t)
        ins = vector.tensor_tensor(out=scr_sb[:, 0:128], in0=scr_sb[:, 0:128],
                                   in1=lg_sb[:], op=ALU.add)
        t = Ev.inc(ins, 1)
        vector.wait_ge(Ev.h, t)
        ins = vector.tensor_reduce(out=m1_sb[:, 16:32].rearrange("p (a o) -> p a o", o=1),
                                   in_=scr_sb[:, 0:128].rearrange("p (a e) -> p a e", e=8),
                                   axis=AX.X, op=ALU.max)
        t = Ev.inc(ins, 1)
        vector.wait_ge(Ev.h, t)

    ctx.on("vector", _)
    lm_tok = Ev.n

    def _(scalar):
        scalar.wait_ge(Ev.h, lm_tok)
        ins = scalar.activation(out=scr_sb[:, 128:256], in_=lg_sb[:], func=AF.Exp)
        Es.inc(ins, 1)

    ctx.on("scalar", _)
    exp_tok = Es.n

    def _(vector):
        vector.wait_ge(Es.h, exp_tok)
        for i in range(16):
            ins = vector.tensor_scalar(out=scr_sb[:, i * 8:(i + 1) * 8],
                                       in0=lg_sb[:, i * 8:(i + 1) * 8],
                                       scalar1=m1_sb[:, 16 + i:17 + i], scalar2=None,
                                       op0=ALU.is_ge)
        t = Ev.inc(ins, 1)
        vector.wait_ge(Ev.h, t)
        ins = vector.tensor_tensor(out=scr_sb[:, 0:128], in0=scr_sb[:, 0:128],
                                   in1=scr_sb[:, 128:256], op=ALU.mult)
        t = Ev.inc(ins, 1)
        vector.wait_ge(Ev.h, t)
        ins = vector.tensor_reduce(out=m1_sb[:, 0:16].rearrange("p (a o) -> p a o", o=1),
                                   in_=scr_sb[:, 0:128].rearrange("p (a e) -> p a e", e=8),
                                   axis=AX.X, op=ALU.add)
        t = Ev.inc(ins, 1)
        vector.wait_ge(Ev.h, t)
        ins = vector.reciprocal(out=m1_sb[:, 16:32], in_=m1_sb[:, 0:16])
        t = Ev.inc(ins, 1)
        vector.wait_ge(Ev.h, t)
        for i in range(16):
            ins = vector.tensor_scalar(out=wtok_sb[:, i * 8:(i + 1) * 8],
                                       in0=scr_sb[:, i * 8:(i + 1) * 8],
                                       scalar1=m1_sb[:, 16 + i:17 + i],
                                       scalar2=L0_SCALING,
                                       op0=ALU.mult, op1=ALU.mult)
        Ev.inc(ins, 1)
        vector.wait_ge(Ev.h, Ev.n)

    ctx.on("vector", _)
    w_tok = Ev.n

    for rnd in range(2):
        def _(tensor, rnd=rnd):
            tensor.wait_ge(Ev.h, w_tok)
            ctx.psum_wait(tensor, "b0")
            for j in range(8):
                i = rnd * 8 + j
                ins = tensor.transpose(ctx.psb[0:8, j * 128:(j + 1) * 128],
                                       wtok_sb[:, i * 8:(i + 1) * 8], ctx.ident[:])
            Et.inc(ins, 1)

        ctx.on("tensor", _)
        wT_mm = Et.n

        def _(scalar, rnd=rnd, wT_mm=wT_mm):
            scalar.wait_ge(Et.h, wT_mm)
            ins = scalar.activation(out=wfeat_sb[:, rnd * 1024:(rnd + 1) * 1024],
                                    in_=ctx.psb[0:8, 0:1024], func=AF.Copy)
            Es.inc(ins, 1)

        ctx.on("scalar", _)
        ctx.psum_set_owner("b0", Es, Es.n)
    wf_tok = Es.n

    for tt in range(4):
        bank = ctx.next_bank()

        def _(tensor, tt=tt, bank=bank):
            ctx.psum_wait(tensor, bank)
            tensor.wait_ge(dg.h, misc_tok)
            for s_, t_ in xn_dep:
                tensor.wait_ge(s_.h, t_)
            for kc in range(NCH):
                ins = tensor.matmul(ctx.ps[bank][:, :], A_sb[:, kc * 128:(kc + 1) * 128],
                                    ctx.xn_sb[:, kc * T + tt * 512: kc * T + (tt + 1) * 512],
                                    start=(kc == 0), stop=(kc == NCH - 1))
            Et.inc(ins, 1)

        ctx.on("tensor", _)
        mm_tok = Et.n

        def _(scalar, tt=tt, bank=bank, mm_tok=mm_tok):
            scalar.wait_ge(Et.h, mm_tok)
            ins = scalar.activation(out=ax_sb[:, tt * 512:(tt + 1) * 512],
                                    in_=ctx.ps[bank][:, :], func=AF.Copy)
            Es.inc(ins, 1)

        ctx.on("scalar", _)
        ctx.psum_set_owner(bank, Es, Es.n)
    ax_tok = Es.n

    for tt in range(4):
        def _(tensor, tt=tt):
            ctx.psum_wait(tensor, 5)
            tensor.wait_ge(Es.h, wf_tok)
            ins = tensor.matmul(ctx.ps[5][:, :], ind_sb[:],
                                wfeat_sb[:, tt * 512:(tt + 1) * 512],
                                start=True, stop=True)
            Et.inc(ins, 1)

        ctx.on("tensor", _)
        wbc_mm = Et.n

        def _(vector, tt=tt, wbc_mm=wbc_mm):
            vector.wait_ge(Et.h, wbc_mm)
            vector.wait_ge(Es.h, ax_tok)
            ins = vector.tensor_tensor(out=sxa_sb[:, tt * 512:(tt + 1) * 512],
                                       in0=ax_sb[:, tt * 512:(tt + 1) * 512],
                                       in1=ctx.ps[5][:, :], op=ALU.mult)
            Ev.inc(ins, 1)
            vector.wait_ge(Ev.h, Ev.n)

        ctx.on("vector", _)
        ctx.psum_set_owner(5, Ev, Ev.n)
    sxa_tok = Ev.n

    def tail_mm(tensor, oc, tt, bank_ap):
        tensor.wait_ge(Ev.h, sxa_tok)
        tensor.wait_ge(dg.h, misc_tok)
        tensor.matmul(bank_ap, B_sb[:, oc * 128:(oc + 1) * 128],
                      sxa_sb[:, tt * 512:(tt + 1) * 512],
                      start=False, stop=False)
        tensor.wait_ge(bb_dep[0].h, bb_dep[1])
        return tensor.matmul(bank_ap, bb_row[:, oc * 128:(oc + 1) * 128],
                             ctx.ones_row_t[:], start=False, stop=True)

    emit_proj(ctx, WDirect(ctx.dram["L0f_base_wT"], D, D), sink_resid(ctx, 0.5),
              tensor_dep=xn_dep, tail_mm=tail_mm)
    return ctx.E["vector"].n


def l0_ffn_host_inputs(inp, pref="L0_ffn"):
    base_w = np.asarray(inp[pref + "_base_w"], dtype=np.float32)
    A = np.asarray(inp[pref + "_lora_A"], dtype=np.float32)
    Bm = np.asarray(inp[pref + "_lora_B"], dtype=np.float32)
    rw = np.asarray(inp[pref + "_router_w"], dtype=np.float32)
    return {
        "L0f_base_wT": np.ascontiguousarray(base_w.T),
        "L0f_base_b": np.asarray(inp[pref + "_base_b"], dtype=np.float32),
        "L0f_A_all": np.ascontiguousarray(A.transpose(1, 0, 2).reshape(D, 128)),
        "L0f_B_all": np.ascontiguousarray(Bm.reshape(128, D)),
        "L0f_rwT": np.ascontiguousarray(rw.T),
        "L0f_rb": np.asarray(inp[pref + "_router_b"], dtype=np.float32),
    }


L0F_INPUT_SPECS = [
    ("L0f_base_wT", [D, D], F32), ("L0f_base_b", [D], F32),
    ("L0f_A_all", [D, 128], F32), ("L0f_B_all", [128, D], F32),
    ("L0f_rwT", [D, 8], F32), ("L0f_rb", [8], F32),
]


def ind_er_const():
    ind = np.zeros((8, 128), dtype=bf16)
    for e_ in range(8):
        ind[e_, e_ * 16:(e_ + 1) * 16] = 1
    return ind



def emit_proj_tokmaj(ctx, wsrc, sink, src=None, obk=512, tensor_dep=None,
                     tail_mm=None, src_tile=None):
    if src is None:
        src = lambda kc: ctx.xn_sb[:, kc * T:(kc + 1) * T]
    Et = ctx.E["tensor"]
    K, N = wsrc.K, wsrc.N
    kc_n = K // 128
    nblocks = ceil_div(N, obk)
    loads = [None] * nblocks

    def emit_block(i):
        bi_, (wsem, wtok), ob0, ncols = loads[i]
        last_tok = None
        for tb in range(T // 128):
            bank = ctx.next_bank()

            def _(tensor):
                tensor.wait_ge(wsem.h, wtok)
                if tensor_dep:
                    for s_, t_ in tensor_dep:
                        tensor.wait_ge(s_.h, t_)
                ctx.psum_wait(tensor, bank)
                for kc_i in range(kc_n):
                    if src_tile is not None:
                        lhsT = src_tile(kc_i, tb * 128, 128)
                    else:
                        lhsT = src(kc_i)[:, tb * 128:(tb + 1) * 128]
                    rhs = ctx.wbuf[bi_][:, kc_i * ncols: (kc_i + 1) * ncols]
                    ins = tensor.matmul(ctx.ps[bank][:, 0:ncols], lhsT, rhs,
                                        start=(kc_i == 0),
                                        stop=(kc_i == kc_n - 1 and tail_mm is None))
                if tail_mm is not None:
                    ins2 = tail_mm(tensor, tb, i, ctx.ps[bank][:, 0:ncols])
                    if ins2 is not None:
                        ins = ins2
                nonlocal_tok[0] = Et.inc(ins, 1)

            nonlocal_tok = [None]
            ctx.on("tensor", _)
            consumer = sink(tb, i, ctx.ps[bank][:, 0:ncols], nonlocal_tok[0])
            ctx.psum_set_owner(bank, *consumer)
            last_tok = nonlocal_tok[0]
        ctx.wbuf_state[bi_] = (Et, last_tok)

    for i in range(nblocks):
        ob0 = i * obk
        ncols = min(obk, N - ob0)
        bi_ = i % 2
        loads[i] = (bi_, wsrc.load(ctx, ob0, ncols, bi_), ob0, ncols)
        if i >= 1:
            emit_block(i - 1)
    emit_block(nblocks - 1)


def emit_attn_dense(ctx, tag, xn_tok, wq, wk, wv, wo, exp_scale,
                    bq_row=None, bk_row=None, bv_row=None, bo_row=None,
                    kv_width=D, kv_group=1, rope_tabs=None,
                    prob_mult=None, skip_tile=None, out_scale=1.0,
                    kv_mod=False):
    nc = ctx.nc
    Et, Es, Ev = ctx.E["tensor"], ctx.E["scalar"], ctx.E["vector"]
    ds = ctx.dma["sync"]
    xn_dep = [(Ev, xn_tok)]
    kv_nch = kv_width // 128
    n_kvh = kv_width // 64
    load_all_v = kv_width <= 256

    sh_k = nc.dram_tensor(f"sh_k_{tag}", [2, kv_width, T], BF16, kind="Internal",
                          addr_space="Shared").ap()
    sh_v = nc.dram_tensor(f"sh_v_{tag}", [2, T, kv_width], BF16, kind="Internal",
                          addr_space="Shared").ap()
    sh_kn = nc.dram_tensor(f"sh_kn_{tag}", [2, 1, 16], F32, kind="Internal",
                           addr_space="Shared").ap()

    q_sb = ctx.tmp("at_q", [128, NCH * T], BF16)
    kst = ctx.tmp("at_kst", [128, 2 * 512], BF16)
    vst = ctx.tmp("at_vst", [128, 2 * 512], BF16)
    nsq = ctx.tmp("at_nsq", [128, 512], F32)
    nrm = ctx.tmp("at_nrm", [16, 16], F32)
    nrm2 = ctx.tmp("at_nrm2", [16, 4 * 512], BF16)
    nrow = ctx.tmp("at_nrow", [1, 48], F32)
    negM = ctx.tmp("at_negM", [128, 16], F32)
    k_sb = ctx.tmp("at_k", [128, S], BF16)
    if load_all_v:
        v_all = ctx.tmp("at_vall", [128, 32 * kv_width], BF16)
    else:
        v_raw = ctx.tmp("at_vraw", [128, 32 * 128], BF16)
    v_aug = ctx.tmp("at_vaug", [128, 32 * 65], BF16)
    pr = ctx.tmp("at_pr", [128, 2 * 512], BF16)
    av_tmp = nsq
    recip = nrm2[0:1, 0:1024].bitcast(F32)
    knrow = ctx.tmp("at_knrow", [1, 16], F32)
    if rope_tabs is not None:
        kraw = ctx.tmp("at_kraw", [128, 512], BF16)
        ksw = ctx.tmp("at_ksw", [128, 512], BF16)
        kc1 = ctx.tmp("at_kc1", [128, 512], BF16)
        ropeC, ropeS, rope_dep = rope_tabs

    ev_kv = ctx.new_event()

    def norm_sink_factory(store_stage, do_rope):
        state = {"stores": [], "count": 0, "rope_ev": 0}

        def sink(oc, tt, psum_ap, mm_tok):
            ring = (oc * 4 + tt) % 2
            rs = ctx.rsem[ring]

            def _(scalar):
                scalar.wait_ge(Et.h, mm_tok)
                if store_stage is None:
                    out = q_sb[:, oc * T + tt * 512: oc * T + (tt + 1) * 512]
                else:
                    if do_rope:
                        out = kraw[:]
                        if state["rope_ev"]:
                            scalar.wait_ge(Ev.h, state["rope_ev"])
                    else:
                        out = store_stage[:, ring * 512:(ring + 1) * 512]
                        if rs.n:
                            scalar.wait_ge(rs.h, rs.n)
                ins = scalar.activation(out=out, in_=psum_ap, func=AF.Copy)
                Es.inc(ins, 1)
                ins = scalar.activation(out=nsq[:], in_=psum_ap, func=AF.Square)
                Es.inc(ins, 1)

            ctx.on("scalar", _)
            sq_tok = Es.n
            cp_tok = sq_tok - 1

            def _(tensor):
                tensor.wait_ge(Es.h, sq_tok)
                ctx.psum_wait(tensor, 4)
                ins = tensor.matmul(ctx.ps[4][0:16, :],
                                    ctx.ind16_sb[:, oc * 16:(oc + 1) * 16],
                                    nsq[:], start=True, stop=True)
                Et.inc(ins, 1)

            ctx.on("tensor", _)
            nm_tok = Et.n

            def _(vector):
                vector.wait_ge(Et.h, nm_tok)
                dst = nrm2[0:16, tt * 512:(tt + 1) * 512]
                if state["count"] < 4:
                    ins = vector.tensor_copy(out=dst, in_=ctx.ps[4][0:16, :])
                else:
                    ins = vector.tensor_tensor(out=dst, in0=dst,
                                               in1=ctx.ps[4][0:16, :], op=ALU.add)
                t = Ev.inc(ins, 1)
                vector.wait_ge(Ev.h, t)

            ctx.on("vector", _)
            state["count"] += 1
            ctx.psum_set_owner(4, Ev, Ev.n)
            consumer_tok = sq_tok
            consumer_sem = Es

            if store_stage is not None and do_rope:
                def _(tensor):
                    tensor.wait_ge(Es.h, cp_tok)
                    ctx.psum_wait(tensor, 3)
                    ins = tensor.matmul(ctx.ps[3][:, :], ctx.permswap_sb[:], kraw[:],
                                        start=True, stop=True)
                    Et.inc(ins, 1)

                ctx.on("tensor", _)
                swmm = Et.n

                def _(scalar):
                    scalar.wait_ge(Et.h, swmm)
                    if state["rope_ev"]:
                        scalar.wait_ge(Ev.h, state["rope_ev"])
                    ins = scalar.activation(out=ksw[:], in_=ctx.ps[3][:, :], func=AF.Copy)
                    Es.inc(ins, 1)

                ctx.on("scalar", _)
                swcp = Es.n
                ctx.psum_set_owner(3, Es, swcp)

                def _(vector):
                    for s_, t_ in rope_dep:
                        vector.wait_ge(s_.h, t_)
                    vector.wait_ge(Es.h, swcp)
                    if rs.n:
                        vector.wait_ge(rs.h, rs.n)
                    tsl = slice(tt * 512, (tt + 1) * 512)
                    ins = vector.tensor_tensor(out=kc1[:], in0=kraw[:],
                                               in1=ropeC[:, tsl], op=ALU.mult)
                    t = Ev.inc(ins, 1)
                    vector.wait_ge(Ev.h, t)
                    ins = vector.tensor_tensor(out=ksw[:], in0=ksw[:],
                                               in1=ropeS[:, tsl], op=ALU.mult)
                    t = Ev.inc(ins, 1)
                    vector.wait_ge(Ev.h, t)
                    ins = vector.tensor_tensor(
                        out=store_stage[:, ring * 512:(ring + 1) * 512],
                        in0=kc1[:], in1=ksw[:], op=ALU.add)
                    t = Ev.inc(ins, 1)
                    vector.wait_ge(Ev.h, t)

                ctx.on("vector", _)
                consumer_tok = Ev.n
                consumer_sem = Ev
                state["rope_ev"] = Ev.n

            if store_stage is not None:
                def emit_store(sync, slot):
                    tgt = sh_k[slot, oc * 128:(oc + 1) * 128, tt * 512:(tt + 1) * 512]
                    sync.dma_start(out=tgt,
                                   in_=store_stage[:, ring * 512:(ring + 1) * 512]).then_inc(rs.h, 16)

                def _(sync):
                    sync.wait_ge(consumer_sem.h, consumer_tok)

                ctx.on("sync", _)
                ctx.by_parity("sync", emit_store)
                rs.n += 16
                state["stores"].append((rs, rs.n))
            return (consumer_sem, consumer_tok)

        return sink, state

    def reduce_norms(dst_col):
        acc_done = Ev.n

        def _(vector):
            vector.wait_ge(Ev.h, acc_done)
            ins = vector.reduce_max(out=nrm[0:16, dst_col:dst_col + 1],
                                    in_=nrm2[0:16, :], axis=AX.X)
            t = Ev.inc(ins, 1)
            vector.wait_ge(Ev.h, t)

        ctx.on("vector", _)
        return Ev.n

    q_sink, _qs = norm_sink_factory(None, False)
    emit_proj(ctx, wq, q_sink, tensor_dep=xn_dep,
              tail_mm=(bias_tail_mm(ctx, bq_row[0], bq_row[1]) if bq_row else None))
    qn_tok = reduce_norms(0)
    q_done = Es.n
    if rope_tabs is not None:
        rope_q_dep = emit_rope_apply_q(ctx, q_sb, NCH, ropeC, ropeS, rope_dep,
                                       [(Es, q_done)],
                                       scratch=(pr[:, 0:512], pr[:, 512:1024]))
        q_done_sem, q_done_tok = rope_q_dep
    else:
        q_done_sem, q_done_tok = Es, q_done

    k_sink, kstate = norm_sink_factory(kst, rope_tabs is not None)
    emit_proj(ctx, wk, k_sink, tensor_dep=xn_dep,
              tail_mm=(bias_tail_mm(ctx, bk_row[0], bk_row[1]) if bk_row else None))
    kn_tok = reduce_norms(1)
    k_store_deps = [(rs, rs.n) for rs in ctx.rsem[0:2]]

    vstate = {"stores": []}

    def v_sink(tb, ob_i, psum_ap, mm_tok):
        ring = (tb + ob_i) % 2
        rs = ctx.rsem[4 + ring]
        ncols = psum_ap.shape[1]

        def _(scalar):
            scalar.wait_ge(Et.h, mm_tok)
            if rs.n:
                scalar.wait_ge(rs.h, rs.n)
            ins = scalar.activation(out=v_aug[:, 1024 + ring * 512: 1024 + ring * 512 + ncols],
                                    in_=psum_ap, func=AF.Copy)
            Es.inc(ins, 1)

        ctx.on("scalar", _)
        cp_tok = Es.n

        def emit_store(sync, slot):
            tgt = sh_v[slot, tb * 128:(tb + 1) * 128,
                       ob_i * 512: ob_i * 512 + ncols]
            sync.dma_start(out=tgt,
                           in_=v_aug[:, 1024 + ring * 512: 1024 + ring * 512 + ncols]).then_inc(rs.h, 16)

        def _(sync):
            sync.wait_ge(Es.h, cp_tok)

        ctx.on("sync", _)
        ctx.by_parity("sync", emit_store)
        rs.n += 16
        vstate["stores"].append((rs, rs.n))
        return (Es, cp_tok)

    def v_tail(tensor, tb, ob_i, bank_ap):
        tensor.wait_ge(bv_row[1][0].h, bv_row[1][1])
        ncols = bank_ap.shape[1]
        return tensor.matmul(bank_ap, ctx.ones_row[:],
                             bv_row[0][0:1, ob_i * 512: ob_i * 512 + ncols],
                             start=False, stop=True)

    emit_proj_tokmaj(ctx, wv, v_sink, tensor_dep=xn_dep,
                     tail_mm=(v_tail if bv_row else None))
    v_store_deps = [(rs, rs.n) for rs in ctx.rsem[4:6]]

    nrm_bf = ctx.tmp("at_nrmbf", [16, 2], BF16)

    def _(scalar):
        scalar.wait_ge(Ev.h, kn_tok)
        ins = scalar.activation(out=nrm_bf[:], in_=nrm[0:16, 0:2], func=AF.Copy)
        Es.inc(ins, 1)

    ctx.on("scalar", _)
    nbf_tok = Es.n

    def _(tensor):
        tensor.wait_ge(Es.h, nbf_tok)
        ctx.psum_wait(tensor, "b0")
        tensor.transpose(ctx.psb[0:1, 0:16], nrm_bf[:, 0:1], ctx.ident[0:16, 0:16])
        ins = tensor.transpose(ctx.psb[0:1, 16:32], nrm_bf[:, 1:2], ctx.ident[0:16, 0:16])
        Et.inc(ins, 1)

    ctx.on("tensor", _)
    nT_mm = Et.n

    def _(scalar):
        scalar.wait_ge(Et.h, nT_mm)
        ins = scalar.activation(out=nrow[0:1, 0:16], in_=ctx.psb[0:1, 0:16], func=AF.Sqrt)
        Es.inc(ins, 1)
        ins = scalar.activation(out=knrow[0:1, 0:16], in_=ctx.psb[0:1, 16:32], func=AF.Copy)
        Es.inc(ins, 1)

    ctx.on("scalar", _)
    ctx.psum_set_owner("b0", Es, Es.n)
    kn_row_tok = Es.n

    def emit_knstore(sync, slot):
        sync.dma_start(out=sh_kn[slot], in_=knrow[0:1, 0:16]).then_inc(ds.h, 16)

    def _(sync):
        sync.wait_ge(Es.h, kn_row_tok)

    ctx.on("sync", _)
    ctx.by_parity("sync", emit_knstore)
    ds.n += 16
    ctx.mbox_flag(ev_kv, dep_waits=[(ds, ds.n)] + k_store_deps + v_store_deps)
    xrdy_tok = ctx.mbox_wait(ev_kv)

    preg = ctx.parity_reg("sync")
    dpk = ctx.new_ldsem()

    def _(sync):
        with sync.If_eq(preg, 0):
            sync.dma_start(out=nrow[0:1, 16:32], in_=sh_kn[1]).then_inc(dpk.h, 16)
        with sync.Else():
            sync.dma_start(out=nrow[0:1, 16:32], in_=sh_kn[0]).then_inc(dpk.h, 16)
        sync.end_ifs()

    ctx.on("sync", _)
    dpk.n += 16
    peer_kn = dpk.n

    def _(vector):
        vector.wait_ge(dpk.h, peer_kn)
        vector.wait_ge(Es.h, kn_row_tok)
        ins = vector.tensor_tensor(out=nrow[0:1, 16:32], in0=knrow[0:1, 0:16],
                                   in1=nrow[0:1, 16:32], op=ALU.max)
        t = Ev.inc(ins, 1)
        vector.wait_ge(Ev.h, t)

    ctx.on("vector", _)
    kmax2_tok = Ev.n

    def _(scalar):
        scalar.wait_ge(Ev.h, kmax2_tok)
        ins = scalar.activation(out=nrow[0:1, 16:32], in_=nrow[0:1, 16:32], func=AF.Sqrt)
        Es.inc(ins, 1)

    ctx.on("scalar", _)
    kmax_tok = Es.n

    def _(vector):
        vector.wait_ge(Es.h, kmax_tok)
        if kv_group == 1:
            ins = vector.tensor_tensor(out=nrow[0:1, 32:48], in0=nrow[0:1, 0:16],
                                       in1=nrow[0:1, 16:32], op=ALU.mult)
            t = Ev.inc(ins, 1)
            vector.wait_ge(Ev.h, t)
        elif kv_mod:
            for h_ in range(16):
                ins = vector.tensor_tensor(
                    out=nrow[0:1, 32 + h_:33 + h_],
                    in0=nrow[0:1, h_:h_ + 1],
                    in1=nrow[0:1, 16 + (h_ % n_kvh):17 + (h_ % n_kvh)],
                    op=ALU.mult)
                t = Ev.inc(ins, 1)
                vector.wait_ge(Ev.h, t)
        else:
            for g in range(n_kvh):
                ins = vector.tensor_scalar(
                    out=nrow[0:1, 32 + g * kv_group: 32 + (g + 1) * kv_group],
                    in0=nrow[0:1, g * kv_group: (g + 1) * kv_group],
                    scalar1=nrow[0:1, 16 + g:17 + g], scalar2=None, op0=ALU.mult)
                t = Ev.inc(ins, 1)
                vector.wait_ge(Ev.h, t)
        ins = vector.tensor_scalar(out=nrow[0:1, 32:48], in0=nrow[0:1, 32:48],
                                   scalar1=-float(exp_scale), scalar2=None,
                                   op0=ALU.mult)
        t = Ev.inc(ins, 1)
        vector.wait_ge(Ev.h, t)

    ctx.on("vector", _)
    M_tok = Ev.n

    def _(tensor):
        tensor.wait_ge(Ev.h, M_tok)
        ctx.psum_wait(tensor, 3)
        ins = tensor.matmul(ctx.ps[3][:, 0:16], ctx.ones_rowf[:], nrow[0:1, 32:48],
                            start=True, stop=True)
        Et.inc(ins, 1)

    ctx.on("tensor", _)
    negM_mm = Et.n

    def _(scalar):
        scalar.wait_ge(Et.h, negM_mm)
        ins = scalar.activation(out=negM[:], in_=ctx.ps[3][:, 0:16], func=AF.Copy)
        Es.inc(ins, 1)

    ctx.on("scalar", _)
    negM_tok = Es.n
    ctx.psum_set_owner(3, Es, negM_tok)

    def _(vector):
        for s_, t_ in k_store_deps + v_store_deps:
            vector.wait_ge(s_.h, t_)
        ins = vector.memset(v_aug[:], 1.0)
        Ev.inc(ins, 1)

    ctx.on("vector", _)
    vaug_ones = Ev.n

    def _(tensor):
        for b_ in (0, 1, 2):
            ctx.psum_wait(tensor, b_)

    ctx.on("tensor", _)
    stage = ctx.xn_sb
    kload_tok = [0]
    vload_tok = [0]
    score_mm_last = [0]
    score_owner = [0, 0]
    score_owner_sem = [Es, Es]
    pr_owner = [0, 0]
    bc_owner = [negM_tok]
    bc_owner_sem = [Es]
    dc = ctx.dma["scalar"]

    if load_all_v:
        def _(sc):
            sc.wait_ge(ctx.xrdy.h, xrdy_tok)
            for sl in range(2):
                for j in range(4):
                    dst = v_all[:, (sl * 16 + j * 4) * kv_width:
                                (sl * 16 + (j + 1) * 4) * kv_width]
                    dst = dst.rearrange("p (b c) -> p b c", c=kv_width)
                    srcv = sh_v[sl, j * 512:(j + 1) * 512, :]
                    srcv = srcv.rearrange("(b p) c -> p b c", p=128)
                    ins = sc.dma_start(out=dst, in_=srcv)
                    ctx.dv.inc(ins, 16)

        ctx.on("scalar", _)
        vload_tok[0] = ctx.dv.n

    for hp in range(8):
        g_kv = (hp * 2) // kv_group
        if not load_all_v:
            def _(sc):
                sc.wait_ge(ctx.xrdy.h, xrdy_tok)
                if score_mm_last[0]:
                    sc.wait_ge(Et.h, score_mm_last[0])
                for sl in range(2):
                    for j in range(4):
                        dst = v_raw[:, (sl * 16 + j * 4) * 128:(sl * 16 + (j + 1) * 4) * 128]
                        dst = dst.rearrange("p (b c) -> p b c", c=128)
                        srcv = sh_v[sl, j * 512:(j + 1) * 512, hp * 128:(hp + 1) * 128]
                        srcv = srcv.rearrange("(b p) c -> p b c", p=128)
                        ins = sc.dma_start(out=dst, in_=srcv)
                        ctx.dv.inc(ins, 16)

            ctx.on("scalar", _)
            vload_tok[0] = ctx.dv.n

        for hh in range(2):
            h = hp * 2 + hh
            kv_h = (h % n_kvh) if kv_mod else (h // kv_group)

            def _(sc):
                sc.wait_ge(ctx.xrdy.h, xrdy_tok)
                if score_mm_last[0]:
                    sc.wait_ge(Et.h, score_mm_last[0])
                for sl in range(2):
                    for half in range(2):
                        ins = sc.dma_start(out=k_sb[half * 64:(half + 1) * 64,
                                                    sl * T:(sl + 1) * T],
                                           in_=sh_k[sl, kv_h * 64:(kv_h + 1) * 64, :])
                        ctx.dk.inc(ins, 16)

            ctx.on("scalar", _)
            kload_tok[0] = ctx.dk.n

            def _(vector):
                vector.wait_ge(ctx.dv.h, vload_tok[0])
                vector.wait_ge(Ev.h, vaug_ones)
                if score_mm_last[0]:
                    vector.wait_ge(Et.h, score_mm_last[0])
                if load_all_v:
                    src_v = v_all[:].rearrange("p (a c) -> p a c", c=kv_width)[
                        :, :, (kv_h % (kv_width // 64)) * 64:(kv_h % (kv_width // 64)) * 64 + 64]
                else:
                    src_v = v_raw[:].rearrange("p (a c) -> p a c", c=128)[
                        :, :, hh * 64:(hh + 1) * 64]
                ins = vector.tensor_copy(
                    out=v_aug[:].rearrange("p (a c) -> p a c", c=65)[:, :, 0:64],
                    in_=src_v)
                Ev.inc(ins, 1)
                vector.wait_ge(Ev.h, Ev.n)

            ctx.on("vector", _)
            vaug_tok = Ev.n

            for qt in range(4):
                qsl = slice((h // 2) * T + qt * 512, (h // 2) * T + (qt + 1) * 512)
                qrows = slice((h % 2) * 64, (h % 2) * 64 + 64)
                kts = [kt for kt in range(32)
                       if skip_tile is None or not skip_tile(h, qt, kt)]
                pend = [None]

                def emit_av(stop_flag):
                    sb_, kt_, i_, osem, otok = pend[0]
                    first_av = (i_ == 0)

                    def _(tensor):
                        tensor.wait_ge(osem.h, otok)
                        tensor.wait_ge(Ev.h, vaug_tok)
                        if first_av:
                            ctx.psum_wait(tensor, 2)
                        ins = tensor.matmul(ctx.ps[2][0:65, :],
                                            v_aug[:, kt_ * 65:(kt_ + 1) * 65],
                                            pr[:, sb_ * 512:(sb_ + 1) * 512],
                                            start=first_av, stop=stop_flag)
                        Et.inc(ins, 1)

                    ctx.on("tensor", _)
                    pr_owner[sb_] = Et.n
                    score_mm_last[0] = Et.n

                for i_kt, kt in enumerate(kts):
                    sb = i_kt % 2

                    def _(tensor, sb=sb, kt=kt, i_kt=i_kt):
                        tensor.wait_ge(ctx.dk.h, kload_tok[0])
                        if score_owner[sb]:
                            tensor.wait_ge(score_owner_sem[sb].h, score_owner[sb])
                        if i_kt == 0:
                            tensor.wait_ge(q_done_sem.h, q_done_tok)
                        b0 = (h % 2) * 64
                        ins = tensor.matmul(ctx.ps[sb][:, :],
                                            k_sb[b0:b0 + 64, kt * 128:(kt + 1) * 128],
                                            q_sb[qrows, qsl],
                                            start=True, stop=True)
                        Et.inc(ins, 1)

                    ctx.on("tensor", _)
                    mm_tok = Et.n
                    score_mm_last[0] = mm_tok

                    def _(scalar, sb=sb, mm_tok=mm_tok):
                        scalar.wait_ge(Et.h, mm_tok)
                        scalar.wait_ge(Es.h, negM_tok)
                        if pr_owner[sb]:
                            scalar.wait_ge(Et.h, pr_owner[sb])
                        ins = scalar.activation(out=pr[:, sb * 512:(sb + 1) * 512],
                                                in_=ctx.ps[sb][:, :], func=AF.Exp,
                                                scale=float(exp_scale),
                                                bias=negM[:, h:h + 1])
                        Es.inc(ins, 1)

                    ctx.on("scalar", _)
                    score_owner[sb] = Es.n
                    score_owner_sem[sb] = Es

                    if prob_mult is not None:
                        pm = prob_mult(h, qt, kt)
                        if pm is not None:
                            pm_ap, pm_dep = pm
                            exp_tok = Es.n

                            def _(vector, sb=sb, pm_ap=pm_ap, pm_dep=pm_dep,
                                  exp_tok=exp_tok):
                                for s_, t_ in pm_dep:
                                    vector.wait_ge(s_.h, t_)
                                vector.wait_ge(Es.h, exp_tok)
                                if pr_owner[sb]:
                                    vector.wait_ge(Et.h, pr_owner[sb])
                                ins = vector.tensor_tensor(
                                    out=pr[:, sb * 512:(sb + 1) * 512],
                                    in0=pr[:, sb * 512:(sb + 1) * 512],
                                    in1=pm_ap, op=ALU.mult)
                                t = Ev.inc(ins, 1)
                                vector.wait_ge(Ev.h, t)

                            ctx.on("vector", _)
                            score_owner[sb] = Ev.n
                            score_owner_sem[sb] = Ev

                    prev = pend[0]
                    pend[0] = (sb, kt, i_kt, score_owner_sem[sb], score_owner[sb])
                    if prev is not None:
                        pend2 = pend[0]
                        pend[0] = prev
                        emit_av(False)
                        pend[0] = pend2
                emit_av(True)
                av_done = Et.n

                def _(scalar, av_done=av_done):
                    scalar.wait_ge(Et.h, av_done)
                    scalar.wait_ge(bc_owner_sem[0].h, bc_owner[0])
                    ins = scalar.activation(out=av_tmp[0:65, 0:512], in_=ctx.ps[2][0:65, :],
                                            func=AF.Copy)
                    Es.inc(ins, 1)

                ctx.on("scalar", _)
                av_cp = Es.n
                ctx.psum_set_owner(2, Es, av_cp)

                def _(vector, av_cp=av_cp):
                    vector.wait_ge(Es.h, av_cp)
                    ins = vector.reciprocal(out=recip, in_=av_tmp[64:65, 0:512])
                    Ev.inc(ins, 1)
                    vector.wait_ge(Ev.h, Ev.n)

                ctx.on("vector", _)
                recip_tok = Ev.n

                def _(tensor, recip_tok=recip_tok):
                    tensor.wait_ge(Ev.h, recip_tok)
                    tensor.wait_ge(bc_owner_sem[0].h, bc_owner[0])
                    ins = tensor.matmul(ctx.ps[3][0:64, :], ctx.ones_rowf[0:1, 0:64],
                                        recip, start=True, stop=True)
                    Et.inc(ins, 1)

                ctx.on("tensor", _)
                bc_mm = Et.n

                def _(vector, bc_mm=bc_mm):
                    vector.wait_ge(Et.h, bc_mm)
                    ins = vector.tensor_tensor(out=stage[qrows, qsl],
                                               in0=av_tmp[0:64, 0:512],
                                               in1=ctx.ps[3][0:64, :], op=ALU.mult)
                    Ev.inc(ins, 1)
                    vector.wait_ge(Ev.h, Ev.n)

                ctx.on("vector", _)
                bc_owner[0] = Ev.n
                bc_owner_sem[0] = Ev

    stage_tok = Ev.n
    ctx.psum_set_owner(3, Ev, Ev.n)

    tail = None
    if bo_row is not None:
        tail = bias_tail_mm(ctx, bo_row[0], bo_row[1])
    emit_proj(ctx, wo, sink_resid(ctx, out_scale),
              src=lambda kc: stage[:, kc * T:(kc + 1) * T],
              tensor_dep=[(Ev, stage_tok)], tail_mm=tail)
    return ctx.E["vector"].n



def emit_rope_tables(ctx, c32, s32, src_dep, C128=None, S128=None):
    Et, Es = ctx.E["tensor"], ctx.E["scalar"]
    if C128 is None:
        C128 = ctx.tmp("rope_C", [128, T], BF16)
        S128 = ctx.tmp("rope_S", [128, T], BF16)
    for tt in range(4):
        def _(tensor, tt=tt):
            for s_, t_ in src_dep:
                tensor.wait_ge(s_.h, t_)
            ctx.psum_wait(tensor, 4)
            ctx.psum_wait(tensor, 5)
            tensor.matmul(ctx.ps[4][:, :], ctx.inddup_sb[:],
                          c32[0:32, tt * 512:(tt + 1) * 512], start=True, stop=True)
            ins = tensor.matmul(ctx.ps[5][:, :], ctx.inddupsgn_sb[:],
                                s32[0:32, tt * 512:(tt + 1) * 512], start=True, stop=True)
            Et.inc(ins, 1)

        ctx.on("tensor", _)
        mm = Et.n

        def _(scalar, tt=tt, mm=mm):
            scalar.wait_ge(Et.h, mm)
            scalar.activation(out=C128[:, tt * 512:(tt + 1) * 512], in_=ctx.ps[4][:, :],
                              func=AF.Copy)
            ins = scalar.activation(out=S128[:, tt * 512:(tt + 1) * 512], in_=ctx.ps[5][:, :],
                                    func=AF.Copy)
            Es.inc(ins, 1)

        ctx.on("scalar", _)
        ctx.psum_set_owner(4, Es, Es.n)
        ctx.psum_set_owner(5, Es, Es.n)
    return C128, S128, [(Es, Es.n)]


def emit_rope_apply_q(ctx, q_sb, nch, C128, S128, tab_dep, q_dep, scratch=None):
    Et, Es, Ev = ctx.E["tensor"], ctx.E["scalar"], ctx.E["vector"]
    if scratch is not None:
        tmp, sw = scratch
    else:
        tmp = ctx.tmp("rope_tmp", [128, 512], BF16)
        sw = ctx.tmp("rope_sw", [128, 512], BF16)
    prev_ev = [0]
    for c in range(nch):
        for tt in range(4):
            sl = slice(c * T + tt * 512, c * T + (tt + 1) * 512)

            def _(tensor, sl=sl):
                for s_, t_ in q_dep:
                    tensor.wait_ge(s_.h, t_)
                if prev_ev[0]:
                    tensor.wait_ge(Ev.h, prev_ev[0])
                ctx.psum_wait(tensor, 3)
                ins = tensor.matmul(ctx.ps[3][:, :], ctx.permswap_sb[:], q_sb[:, sl],
                                    start=True, stop=True)
                Et.inc(ins, 1)

            ctx.on("tensor", _)
            mm = Et.n

            def _(scalar, mm=mm):
                scalar.wait_ge(Et.h, mm)
                if prev_ev[0]:
                    scalar.wait_ge(Ev.h, prev_ev[0])
                ins = scalar.activation(out=sw[:], in_=ctx.ps[3][:, :], func=AF.Copy)
                Es.inc(ins, 1)

            ctx.on("scalar", _)
            sw_tok = Es.n
            ctx.psum_set_owner(3, Es, sw_tok)

            def _(vector, sl=sl, tt=tt, sw_tok=sw_tok):
                for s_, t_ in tab_dep:
                    vector.wait_ge(s_.h, t_)
                vector.wait_ge(Es.h, sw_tok)
                tsl = slice(tt * 512, (tt + 1) * 512)
                ins = vector.tensor_tensor(out=tmp[:], in0=q_sb[:, sl],
                                           in1=C128[:, tsl], op=ALU.mult)
                t1 = Ev.inc(ins, 1)
                vector.wait_ge(Ev.h, t1)
                ins = vector.tensor_tensor(out=sw[:], in0=sw[:], in1=S128[:, tsl],
                                           op=ALU.mult)
                t2 = Ev.inc(ins, 1)
                vector.wait_ge(Ev.h, t2)
                ins = vector.tensor_tensor(out=q_sb[:, sl], in0=tmp[:], in1=sw[:],
                                           op=ALU.add)
                t3 = Ev.inc(ins, 1)
                vector.wait_ge(Ev.h, t3)

            ctx.on("vector", _)
            prev_ev[0] = Ev.n
    return (Ev, Ev.n)


def emit_l1_rope_sources(ctx, scope, c32, s32):
    Et, Es, Ev = ctx.E["tensor"], ctx.E["scalar"], ctx.E["vector"]
    dl = ctx.new_ldsem()
    le = scope.enter_context
    fr = le(ctx.nc.sbuf_tensor(f"l1_fr{ctx.uid()}", [1, 32], F32))
    pos = le(ctx.nc.sbuf_tensor(f"l1_pos{ctx.uid()}", [1, T], F32))
    ang = le(ctx.nc.sbuf_tensor(f"l1_ang{ctx.uid()}", [32, 2 * T], F32))

    def _(g):
        ins = g.dma_start(out=fr[:], in_=ctx.dram["L1a_freqs"].rearrange("(o n) -> o n", o=1))
        dl.inc(ins, 16)
        ins = g.dma_start(out=pos[:], in_=ctx.dram["c_mypos"])
        dl.inc(ins, 16)

    ctx.on("gpsimd", _)
    ld = dl.n
    TWO_PI = 2 * math.pi

    for tt in range(4):
        def _(tensor, tt=tt):
            tensor.wait_ge(dl.h, ld)
            ctx.psum_wait(tensor, 4)
            ins = tensor.matmul(ctx.ps[4][0:32, :], fr[:],
                                pos[0:1, tt * 512:(tt + 1) * 512], start=True, stop=True)
            Et.inc(ins, 1)

        ctx.on("tensor", _)
        mm = Et.n

        def _(vector, tt=tt, mm=mm):
            vector.wait_ge(Et.h, mm)
            tsl = slice(tt * 512, (tt + 1) * 512)
            ins = vector.tensor_copy(out=ang[0:32, tsl], in_=ctx.ps[4][0:32, :])
            t = Ev.inc(ins, 1)
            vector.wait_ge(Ev.h, t)
            ins = vector.tensor_scalar(out=ang[0:32, T + tt * 512: T + (tt + 1) * 512],
                                       in0=ctx.ps[4][0:32, :], scalar1=math.pi / 2,
                                       scalar2=None, op0=ALU.add)
            t = Ev.inc(ins, 1)
            vector.wait_ge(Ev.h, t)

        ctx.on("vector", _)
        ctx.psum_set_owner(4, Ev, Ev.n)
    modtok = Ev.n

    ladscr = scope.enter_context(ctx.nc.sbuf_tensor(f"l1_lad{ctx.uid()}", [32, 512], F32))

    def _(vector):
        vector.wait_ge(Ev.h, modtok)
        for half in range(2):
            for tt in range(4):
                sl = slice(half * T + tt * 512, half * T + (tt + 1) * 512)
                for k in range(9, -1, -1):
                    Ck = TWO_PI * (2 ** k)
                    ins = vector.tensor_scalar(out=ladscr[:], in0=ang[0:32, sl],
                                               scalar1=Ck, scalar2=Ck,
                                               op0=ALU.is_ge, op1=ALU.mult)
                    t = Ev.inc(ins, 1)
                    vector.wait_ge(Ev.h, t)
                    ins = vector.tensor_tensor(out=ang[0:32, sl], in0=ang[0:32, sl],
                                               in1=ladscr[:], op=ALU.subtract)
                    t = Ev.inc(ins, 1)
                    vector.wait_ge(Ev.h, t)

    ctx.on("vector", _)
    modtok = Ev.n

    wrapscr = scope.enter_context(ctx.nc.sbuf_tensor(f"l1_wrap{ctx.uid()}", [32, 512], F32))

    def _(vector):
        vector.wait_ge(Ev.h, modtok)
        for half in range(2):
            for tt in range(4):
                sl = slice(half * T + tt * 512, half * T + (tt + 1) * 512)
                ins = vector.tensor_scalar(out=wrapscr[:], in0=ang[0:32, sl],
                                           scalar1=math.pi, scalar2=-TWO_PI,
                                           op0=ALU.is_gt, op1=ALU.mult)
                t = Ev.inc(ins, 1)
                vector.wait_ge(Ev.h, t)
                ins = vector.tensor_tensor(out=ang[0:32, sl], in0=ang[0:32, sl],
                                           in1=wrapscr[:], op=ALU.add)
                t = Ev.inc(ins, 1)
                vector.wait_ge(Ev.h, t)

    ctx.on("vector", _)
    wrap_tok = Ev.n

    def _(scalar):
        scalar.wait_ge(Ev.h, wrap_tok)
        scalar.activation(out=s32[:], in_=ang[0:32, 0:T], func=AF.Sin)
        ins = scalar.activation(out=c32[:], in_=ang[0:32, T:2 * T], func=AF.Sin)
        Es = ctx.E["scalar"]
        Es.inc(ins, 1)

    ctx.on("scalar", _)
    return [(ctx.E["scalar"], ctx.E["scalar"].n)]



def emit_l1_kb_prep(ctx):
    nc = ctx.nc
    Es, Ev = ctx.E["scalar"], ctx.E["vector"]
    kbext = nc.dram_tensor(f"kbext_{ctx.uid()}", [16, 8448], BF16, kind="Internal").ap()
    scope = ExitStack()
    le = scope.enter_context
    pa = le(nc.sbuf_tensor(f"kb_pa{ctx.uid()}", [16, 4], F32))
    dch = le(nc.sbuf_tensor(f"kb_dch{ctx.uid()}", [16, 2112], F32))
    ech = le(nc.sbuf_tensor(f"kb_ech{ctx.uid()}", [16, 2112], BF16))
    dl = ctx.new_ldsem()
    one_ap, one_dep = ctx.fconst(1.0, 16)

    def _(g):
        ins = g.dma_start(out=pa[:, 0:1], in_=ctx.dram["L1a_bias_p"].rearrange("(p o) -> p o", o=1))
        dl.inc(ins, 16)
        ins = g.dma_start(out=pa[:, 1:2], in_=ctx.dram["L1a_bias_a"].rearrange("(p o) -> p o", o=1))
        dl.inc(ins, 16)

    ctx.on("gpsimd", _)
    pa_ld = dl.n

    def _(vector):
        vector.wait_ge(dl.h, pa_ld)
        ins = vector.tensor_scalar(out=pa[:, 0:2], in0=pa[:, 0:2], scalar1=0.01,
                                   scalar2=None, op0=ALU.max)
        t = Ev.inc(ins, 1)
        vector.wait_ge(Ev.h, t)
        ins = vector.tensor_scalar(out=pa[:, 2:3], in0=pa[:, 0:1], scalar1=-1.0,
                                   scalar2=None, op0=ALU.mult)
        t = Ev.inc(ins, 1)
        vector.wait_ge(Ev.h, t)

    ctx.on("vector", _)
    pa_tok = Ev.n
    st_tok = 0

    for ch in range(4):
        csl = slice(ch * 2112, (ch + 1) * 2112)
        dl2 = dl

        def _(g, ch=ch, st_tok=st_tok):
            if st_tok:
                g.wait_ge(ctx.dma["gpsimd"].h, st_tok)
            src = ctx.dram["c_distR"][0:1, ch * 2112:(ch + 1) * 2112].to_broadcast([16, 2112])
            ins = g.dma_start(out=dch[:], in_=src)
            dl2.inc(ins, 16)

        ctx.on("gpsimd", _)
        d_tok = dl2.n

        def _(vector, d_tok=d_tok):
            vector.wait_ge(dl.h, d_tok)
            vector.wait_ge(Ev.h, pa_tok)
            ins = vector.tensor_scalar(out=dch[:], in0=dch[:], scalar1=pa[:, 1:2],
                                       scalar2=None, op0=ALU.mult)
            t = Ev.inc(ins, 1)
            vector.wait_ge(Ev.h, t)

        ctx.on("vector", _)
        a_tok = Ev.n

        def _(scalar, a_tok=a_tok):
            scalar.wait_ge(Ev.h, a_tok)
            scalar.wait_ge(one_dep[0].h, one_dep[1])
            ins = scalar.activation(out=dch[:], in_=dch[:], func=AF.Ln, bias=one_ap)
            Es.inc(ins, 1)

        ctx.on("scalar", _)
        ln_tok = Es.n

        def _(vector, ln_tok=ln_tok):
            vector.wait_ge(Es.h, ln_tok)
            ins = vector.tensor_scalar(out=dch[:], in0=dch[:], scalar1=pa[:, 2:3],
                                       scalar2=None, op0=ALU.mult)
            t = Ev.inc(ins, 1)
            vector.wait_ge(Ev.h, t)

        ctx.on("vector", _)
        m_tok = Ev.n

        def _(scalar, m_tok=m_tok):
            scalar.wait_ge(Ev.h, m_tok)
            ins = scalar.activation(out=ech[:], in_=dch[:], func=AF.Exp)
            Es.inc(ins, 1)

        ctx.on("scalar", _)
        e_tok = Es.n

        def _(g, ch=ch, e_tok=e_tok):
            g.wait_ge(Es.h, e_tok)
            ins = g.dma_start(out=kbext[:, ch * 2112:(ch + 1) * 2112], in_=ech[:])
            ctx.dma["gpsimd"].inc(ins, 16)

        ctx.on("gpsimd", _)
        st_tok = ctx.dma["gpsimd"].n
    scope.close()
    ctx.barrier()
    return kbext, (ctx.dma["gpsimd"], st_tok)


def make_l1_prob_mult(ctx, kbext, kb_dep):
    from concourse.bass_types import AP as BassAP
    E_sb = ctx.tmp("l1_E", [128, 6144], BF16)
    state = {"h": -1, "tok": 0, "last_read": 0}

    def prob_mult(h, qt, kt):
        Ev = ctx.E["vector"]
        if state["h"] != h:
            def _(sc):
                sc.wait_ge(kb_dep[0].h, kb_dep[1])
                if state["last_read"]:
                    sc.wait_ge(Ev.h, state["last_read"])
                src = BassAP(tensor=kbext.tensor, offset=h * 8448,
                             ap=[[1, 128], [1, 6144]])
                ins = sc.dma_start(out=E_sb[:], in_=src)
                ctx.dE.inc(ins, 16)

            ctx.on("scalar", _)
            state["h"] = h
            state["tok"] = ctx.dE.n
        m0 = 4095 - kt * 128 + qt * 512
        state["last_read"] = Ev.n + 1
        view = BassAP(tensor=E_sb.ap().tensor, offset=6143 - m0,
                      ap=[[6144, 128], [-1, 512]])
        return view, [(ctx.dE, state["tok"])]

    return prob_mult


def emit_l1_attn(ctx, xn_tok):
    C128 = ctx.tmp("rope_C", [128, T], BF16)
    S128 = ctx.tmp("rope_S", [128, T], BF16)
    rs_scope = ExitStack()
    c32 = rs_scope.enter_context(ctx.nc.sbuf_tensor(f"l1_c32{ctx.uid()}", [32, T], BF16))
    s32 = rs_scope.enter_context(ctx.nc.sbuf_tensor(f"l1_s32{ctx.uid()}", [32, T], BF16))
    src_dep = emit_l1_rope_sources(ctx, rs_scope, c32, s32)
    C128, S128, tab_dep = emit_rope_tables(ctx, c32, s32, src_dep, C128, S128)
    rs_scope.close()
    ctx.barrier()
    kbext, kb_dep = emit_l1_kb_prep(ctx)
    pm = make_l1_prob_mult(ctx, kbext, kb_dep)
    qkvT = ctx.dram["L1a_qkvT"]
    return emit_attn_dense(
        ctx, "l1", xn_tok,
        WDirect(qkvT[:, 0:D], D, D),
        WDirect(qkvT[:, D:2 * D], D, D),
        WDirect(qkvT[:, 2 * D:3 * D], D, D),
        WDirect(ctx.dram["L1a_outT"], D, D),
        exp_scale=0.125, rope_tabs=(C128, S128, tab_dep), prob_mult=pm)


L1A_INPUT_SPECS = [
    ("L1a_qkvT", [D, 3 * D], F32), ("L1a_outT", [D, D], F32),
    ("L1a_freqs", [32], F32), ("L1a_bias_p", [16], F32), ("L1a_bias_a", [16], F32),
]


def l1_attn_host_inputs(inp):
    qkv_w = np.asarray(inp["L1_attn_qkv_w"], dtype=np.float32)
    out_w = np.asarray(inp["L1_attn_out_w"], dtype=np.float32)
    return {
        "L1a_qkvT": np.ascontiguousarray(qkv_w.T),
        "L1a_outT": np.ascontiguousarray(out_w.T),
        "L1a_freqs": np.asarray(inp["L1_attn_rope_freqs"], dtype=np.float32),
        "L1a_bias_p": np.ascontiguousarray(np.asarray(inp["L1_attn_bias_p"], dtype=np.float32).reshape(16)),
        "L1a_bias_a": np.ascontiguousarray(np.asarray(inp["L1_attn_bias_a"], dtype=np.float32).reshape(16)),
    }


def percore_consts(core):
    my0 = (core % 2) * T
    out = {
        "c_slot": np.array([[core % 2]], dtype=np.int32),
        "c_mypos": np.arange(my0, my0 + T, dtype=np.float32).reshape(1, T),
        "c_distR": np.abs(np.arange(8448, dtype=np.float32) - 2048.0 - my0).reshape(1, 8448),
    }
    return out



def emit_l1_ffn(ctx, xn_tok, tag):
    nc = ctx.nc
    Et, Es, Ev = ctx.E["tensor"], ctx.E["scalar"], ctx.E["vector"]
    ds = ctx.dma["sync"]
    xn_dep = [(Ev, xn_tok)]
    FH = 4096

    sh_s = nc.dram_tensor(f"sh_s_{tag}", [2, 1, T], F32, kind="Internal",
                          addr_space="Shared").ap()
    wr_sb = ctx.tmp("mf_wr", [128, 8], F32)
    wr_bf = ctx.tmp("mf_wrb", [128, 8], BF16)
    s_row = ctx.tmp("mf_s", [1, T], F32)
    cnt_sb = ctx.tmp("mf_cnt", [128, 32], F32)
    cm_sb = ctx.tmp("mf_cm", [128, 33], F32)
    tb_sb = ctx.tmp("mf_tb", [128, 1], F32)
    r_sb = ctx.tmp("mf_r", [1, 16], F32)
    wm_row = ctx.tmp("mf_wm", [1, T], F32)
    wm_bf = ctx.tmp("mf_wmb", [1, T], BF16)
    wmh_bc = ctx.tmp("mf_wmbc", [128, T], BF16)
    h_sb = ctx.tmp("mf_h", [128, 4 * T], BF16)
    u_sb = ctx.tmp("mf_u", [128, 512], BF16)
    sq_s = ctx.tmp("mf_sq", [128, 512], BF16)
    z_s = ctx.tmp("mf_z", [128, 512], BF16)
    th_s = ctx.tmp("mf_th", [128, 512], BF16)

    wr_dep = emit_load_vec(ctx, ctx.dram["L1f_wr"], D, wr_sb[:])

    def _(scalar):
        scalar.wait_ge(wr_dep[0].h, wr_dep[1])
        ins = scalar.activation(out=wr_bf[:], in_=wr_sb[:], func=AF.Copy)
        Es.inc(ins, 1)

    ctx.on("scalar", _)
    wrb_dep = (Es, Es.n)
    b1_row, b1_dep = emit_bias_row_load(ctx, ctx.dram["L1f_b1"], FH)
    b2_row, b2_dep = emit_bias_row_load(ctx, ctx.dram["L1f_b2"], D)
    ev_s = ctx.new_event()

    for tt in range(4):
        def _(tensor, tt=tt):
            wait_dep_pair(tensor, wrb_dep)
            for s_, t_ in xn_dep:
                tensor.wait_ge(s_.h, t_)
            ctx.psum_wait(tensor, 4)
            for kc in range(NCH):
                ins = tensor.matmul(ctx.ps[4][0:1, :], wr_bf[:, kc:kc + 1],
                                    ctx.xn_sb[:, kc * T + tt * 512: kc * T + (tt + 1) * 512],
                                    start=(kc == 0), stop=(kc == NCH - 1))
            Et.inc(ins, 1)

        ctx.on("tensor", _)
        mm = Et.n

        def _(scalar, tt=tt, mm=mm):
            scalar.wait_ge(Et.h, mm)
            ins = scalar.activation(out=s_row[0:1, tt * 512:(tt + 1) * 512],
                                    in_=ctx.ps[4][0:1, :], func=AF.Copy)
            Es.inc(ins, 1)

        ctx.on("scalar", _)
        ctx.psum_set_owner(4, Es, Es.n)
    s_tok = Es.n

    def emit_sstore(sync, slot):
        sync.dma_start(out=sh_s[slot], in_=s_row[:]).then_inc(ds.h, 16)

    def _(sync):
        sync.wait_ge(Es.h, s_tok)

    ctx.on("sync", _)
    ctx.by_parity("sync", emit_sstore)
    ds.n += 16
    ctx.mbox_flag(ev_s, dep_waits=[(ds, ds.n)])
    xrdy = ctx.mbox_wait(ev_s)
    dcnt = ctx.new_ldsem()

    def _(sync):
        sync.wait_ge(ctx.xrdy.h, xrdy)
        for sl in range(2):
            ins = sync.dma_start(out=cnt_sb[:, sl * 16:(sl + 1) * 16],
                                 in_=sh_s[sl, 0, :].rearrange("(p c) -> p c", p=128))
            dcnt.inc(ins, 16)

    ctx.on("sync", _)
    cnt_tok = dcnt.n

    def _(vector):
        vector.memset(r_sb[0:1, 0:1], -16384.0)
        ins = vector.memset(r_sb[0:1, 1:2], 16384.0)
        Ev.inc(ins, 1)
        vector.wait_ge(Ev.h, Ev.n)

    ctx.on("vector", _)

    for it in range(48):
        def _(vector):
            ins = vector.tensor_tensor(out=r_sb[0:1, 2:3], in0=r_sb[0:1, 0:1],
                                       in1=r_sb[0:1, 1:2], op=ALU.add)
            t = Ev.inc(ins, 1)
            vector.wait_ge(Ev.h, t)
            ins = vector.tensor_scalar(out=r_sb[0:1, 2:3], in0=r_sb[0:1, 2:3],
                                       scalar1=0.5, scalar2=None, op0=ALU.mult)
            t = Ev.inc(ins, 1)
            vector.wait_ge(Ev.h, t)

        ctx.on("vector", _)
        t_tok = Ev.n

        def _(tensor, t_tok=t_tok):
            tensor.wait_ge(Ev.h, t_tok)
            ctx.psum_wait(tensor, 5)
            ins = tensor.matmul(ctx.ps[5][:, 0:1], ctx.ones_rowf[:], r_sb[0:1, 2:3],
                                start=True, stop=True)
            Et.inc(ins, 1)

        ctx.on("tensor", _)
        bc_mm = Et.n

        def _(vector, bc_mm=bc_mm):
            vector.wait_ge(Et.h, bc_mm)
            vector.wait_ge(dcnt.h, cnt_tok)
            ins = vector.tensor_copy(out=tb_sb[:], in_=ctx.ps[5][:, 0:1])
            t = Ev.inc(ins, 1)
            vector.wait_ge(Ev.h, t)
            ins = vector.tensor_scalar(out=cm_sb[:, 0:32], in0=cnt_sb[:],
                                       scalar1=tb_sb[:], scalar2=None, op0=ALU.is_ge,
                                       op1=ALU.add, accum_out=cm_sb[:, 32:33])
            t = Ev.inc(ins, 1)
            vector.wait_ge(Ev.h, t)

        ctx.on("vector", _)
        ctx.psum_set_owner(5, Ev, Ev.n)
        c_tok = Ev.n

        def _(tensor, c_tok=c_tok):
            tensor.wait_ge(Ev.h, c_tok)
            ctx.psum_wait(tensor, 5)
            ins = tensor.matmul(ctx.ps[5][0:1, 0:1], ctx.ones_colf[:], cm_sb[:, 32:33],
                                start=True, stop=True)
            Et.inc(ins, 1)

        ctx.on("tensor", _)
        cnt_mm = Et.n

        def _(vector, cnt_mm=cnt_mm):
            vector.wait_ge(Et.h, cnt_mm)
            ins = vector.tensor_scalar(out=r_sb[0:1, 3:4], in0=ctx.ps[5][0:1, 0:1],
                                       scalar1=2048.0, scalar2=None, op0=ALU.is_ge)
            t = Ev.inc(ins, 1)
            vector.wait_ge(Ev.h, t)
            ins = vector.tensor_scalar(out=r_sb[0:1, 4:5], in0=r_sb[0:1, 3:4],
                                       scalar1=-1.0, scalar2=1.0, op0=ALU.mult,
                                       op1=ALU.add)
            t = Ev.inc(ins, 1)
            vector.wait_ge(Ev.h, t)
            ins = vector.tensor_tensor(out=r_sb[0:1, 5:6], in0=r_sb[0:1, 3:4],
                                       in1=r_sb[0:1, 2:3], op=ALU.mult)
            t = Ev.inc(ins, 1)
            vector.wait_ge(Ev.h, t)
            ins = vector.tensor_tensor(out=r_sb[0:1, 0:1], in0=r_sb[0:1, 0:1],
                                       in1=r_sb[0:1, 4:5], op=ALU.mult)
            t = Ev.inc(ins, 1)
            vector.wait_ge(Ev.h, t)
            ins = vector.tensor_tensor(out=r_sb[0:1, 0:1], in0=r_sb[0:1, 0:1],
                                       in1=r_sb[0:1, 5:6], op=ALU.add)
            t = Ev.inc(ins, 1)
            vector.wait_ge(Ev.h, t)
            ins = vector.tensor_tensor(out=r_sb[0:1, 5:6], in0=r_sb[0:1, 4:5],
                                       in1=r_sb[0:1, 2:3], op=ALU.mult)
            t = Ev.inc(ins, 1)
            vector.wait_ge(Ev.h, t)
            ins = vector.tensor_tensor(out=r_sb[0:1, 1:2], in0=r_sb[0:1, 1:2],
                                       in1=r_sb[0:1, 3:4], op=ALU.mult)
            t = Ev.inc(ins, 1)
            vector.wait_ge(Ev.h, t)
            ins = vector.tensor_tensor(out=r_sb[0:1, 1:2], in0=r_sb[0:1, 1:2],
                                       in1=r_sb[0:1, 5:6], op=ALU.add)
            t = Ev.inc(ins, 1)
            vector.wait_ge(Ev.h, t)

        ctx.on("vector", _)
    thr_tok = Ev.n

    def _(scalar):
        scalar.wait_ge(Es.h, s_tok)
        ins = scalar.activation(out=wm_row[:], in_=s_row[:], func=AF.Sigmoid)
        Es.inc(ins, 1)

    ctx.on("scalar", _)
    sig_tok = Es.n

    def _(vector):
        vector.wait_ge(Es.h, sig_tok)
        vector.wait_ge(Ev.h, thr_tok)
        ins = vector.tensor_scalar(out=s_row[:], in0=s_row[:], scalar1=r_sb[0:1, 0:1],
                                   scalar2=None, op0=ALU.is_ge)
        t = Ev.inc(ins, 1)
        vector.wait_ge(Ev.h, t)
        ins = vector.tensor_tensor(out=wm_row[:], in0=wm_row[:], in1=s_row[:],
                                   op=ALU.mult)
        t = Ev.inc(ins, 1)
        vector.wait_ge(Ev.h, t)

    ctx.on("vector", _)
    wm_tok = Ev.n

    def _(scalar):
        scalar.wait_ge(Ev.h, wm_tok)
        ins = scalar.activation(out=wm_bf[:], in_=wm_row[:], func=AF.Copy)
        Es.inc(ins, 1)

    ctx.on("scalar", _)
    wmbf_tok = Es.n

    for tt in range(4):
        def _(tensor, tt=tt):
            tensor.wait_ge(Ev.h, wm_tok)
            ctx.psum_wait(tensor, 5)
            ins = tensor.matmul(ctx.ps[5][:, :], ctx.ones_rowf[:],
                                wm_row[0:1, tt * 512:(tt + 1) * 512],
                                start=True, stop=True)
            Et.inc(ins, 1)

        ctx.on("tensor", _)
        mm = Et.n

        def _(scalar, tt=tt, mm=mm):
            scalar.wait_ge(Et.h, mm)
            ins = scalar.activation(out=wmh_bc[:, tt * 512:(tt + 1) * 512],
                                    in_=ctx.ps[5][:, :], func=AF.Copy)
            Es.inc(ins, 1)

        ctx.on("scalar", _)
        ctx.psum_set_owner(5, Es, Es.n)
    wmbc_tok = Es.n

    GC = 0.7978845608028654
    GA = 0.044715
    prev_w2_mm = [0]
    gel_prev = [0]

    for ob in range(8):
        def gelu_sink(oc, tt, psum_ap, mm_tok, ob=ob):
            lc = oc % 4

            def _(scalar):
                scalar.wait_ge(Et.h, mm_tok)
                if gel_prev[0]:
                    scalar.wait_ge(Ev.h, gel_prev[0])
                if prev_w2_mm[0]:
                    scalar.wait_ge(Et.h, prev_w2_mm[0])
                ins = scalar.activation(out=u_sb[:], in_=psum_ap, func=AF.Copy)
                Es.inc(ins, 1)

            ctx.on("scalar", _)
            u_tok = Es.n

            def _(vector, u_tok=u_tok):
                vector.wait_ge(Es.h, u_tok)
                ins = vector.tensor_tensor(out=sq_s[:], in0=u_sb[:], in1=u_sb[:],
                                           op=ALU.mult)
                t = Ev.inc(ins, 1)
                vector.wait_ge(Ev.h, t)
                ins = vector.tensor_tensor(out=sq_s[:], in0=sq_s[:], in1=u_sb[:],
                                           op=ALU.mult)
                t = Ev.inc(ins, 1)
                vector.wait_ge(Ev.h, t)
                ins = vector.scalar_tensor_tensor(out=z_s[:], in0=sq_s[:], scalar=GA,
                                                  in1=u_sb[:], op0=ALU.mult,
                                                  op1=ALU.add)
                t = Ev.inc(ins, 1)
                vector.wait_ge(Ev.h, t)

            ctx.on("vector", _)
            z_tok = Ev.n

            def _(scalar, z_tok=z_tok):
                scalar.wait_ge(Ev.h, z_tok)
                ins = scalar.activation(out=th_s[:], in_=z_s[:], func=AF.Tanh,
                                        scale=GC)
                Es.inc(ins, 1)

            ctx.on("scalar", _)
            th_tok = Es.n

            def _(vector, th_tok=th_tok, lc=lc, tt=tt):
                vector.wait_ge(Es.h, th_tok)
                ins = vector.scalar_tensor_tensor(out=th_s[:], in0=th_s[:], scalar=1.0,
                                                  in1=u_sb[:], op0=ALU.add,
                                                  op1=ALU.mult)
                t = Ev.inc(ins, 1)
                vector.wait_ge(Ev.h, t)
                ins = vector.scalar_tensor_tensor(
                    out=h_sb[:, lc * T + tt * 512: lc * T + (tt + 1) * 512],
                    in0=th_s[:], scalar=0.5,
                    in1=wmh_bc[:, tt * 512:(tt + 1) * 512],
                    op0=ALU.mult, op1=ALU.mult)
                t = Ev.inc(ins, 1)
                vector.wait_ge(Ev.h, t)

            ctx.on("vector", _)
            gel_prev[0] = Ev.n
            return (Ev, Ev.n)

        def gelu_sink_wrap(oc, tt, psum_ap, mm_tok):
            def _(scalar):
                scalar.wait_ge(Es.h, wmbc_tok)
            ctx.on("scalar", _)
            return gelu_sink(oc, tt, psum_ap, mm_tok)

        emit_proj(ctx, WDirect(ctx.dram["L1f_W1"][:, ob * 512:(ob + 1) * 512], D, 512),
                  gelu_sink_wrap, tensor_dep=xn_dep,
                  tail_mm=bias_tail_mm_off(ctx, b1_row, b1_dep, ob * 512))
        h_tok = Ev.n

        def w2_tail(tensor, oc, tt, bank_ap, ob=ob):
            if ob != 0:
                return None
            tensor.wait_ge(b2_dep[0].h, b2_dep[1])
            tensor.wait_ge(Es.h, wmbf_tok)
            return tensor.matmul(bank_ap, b2_row[:, oc * 128:(oc + 1) * 128],
                                 wm_bf[0:1, tt * 512:(tt + 1) * 512],
                                 start=False, stop=True)

        emit_proj(ctx, WDirect(ctx.dram["L1f_W2"][ob * 512:(ob + 1) * 512, :], 512, D),
                  sink_resid(ctx, 0.5),
                  src=lambda kc: h_sb[:, kc * T:(kc + 1) * T],
                  tensor_dep=[(Ev, h_tok)],
                  tail_mm=(w2_tail if ob == 0 else None))
        prev_w2_mm[0] = Et.n

    def _(vector):
        for d in range(NCH):
            xs = ctx.x_sb[:, d * T:(d + 1) * T]
            ins = vector.scalar_tensor_tensor(out=xs, in0=ctx.xn_sb[:, d * T:(d + 1) * T],
                                              scalar=0.5, in1=xs, op0=ALU.mult,
                                              op1=ALU.add)
            t = Ev.inc(ins, 1)
            vector.wait_ge(Ev.h, t)

    ctx.on("vector", _)
    return Ev.n


def bias_tail_mm_off(ctx, bias_row, bias_dep, col_off):

    def tail(tensor, oc, tt, bank_ap):
        tensor.wait_ge(bias_dep[0].h, bias_dep[1])
        o128 = bank_ap.shape[0]
        c0 = col_off + oc * 128
        return tensor.matmul(bank_ap, bias_row[:, c0:c0 + o128],
                             ctx.ones_row_t[0:1, 0:bank_ap.shape[1]],
                             start=False, stop=True)

    return tail


def wait_dep_pair(eng, dep):
    eng.wait_ge(dep[0].h, dep[1])



def emit_l3_attn(ctx, xn_tok):
    from concourse.bass_types import AP as BassAP
    Ev = ctx.E["vector"]
    C128 = ctx.tmp("rope_C", [128, T], BF16)
    S128 = ctx.tmp("rope_S", [128, T], BF16)
    cs_scope = ExitStack()
    c32 = cs_scope.enter_context(ctx.nc.sbuf_tensor(f"l3a_c32{ctx.uid()}", [32, T], BF16))
    s32 = cs_scope.enter_context(ctx.nc.sbuf_tensor(f"l3a_s32{ctx.uid()}", [32, T], BF16))
    dl = ctx.new_ldsem()

    def _(g):
        dl.inc(g.dma_start(out=c32[:], in_=ctx.dram["c_l3a_c32"]), 16)
        dl.inc(g.dma_start(out=s32[:], in_=ctx.dram["c_l3a_s32"]), 16)

    ctx.on("gpsimd", _)
    src_dep = [(dl, dl.n)]
    C128, S128, tab_dep = emit_rope_tables(ctx, c32, s32, src_dep, C128, S128)
    cs_scope.close()
    ctx.barrier()

    E_sb = ctx.tmp("l3a_E", [128, 2944], BF16)
    st = {"loaded": False}
    dEl = ctx.new_ldsem()

    def prob_mult(h, qt, kt):
        if kt < 4 * qt:
            return None
        if not st["loaded"]:
            def _(sc):
                src = BassAP(tensor=ctx.dram["c_stepE"].tensor, offset=1537,
                             ap=[[1, 128], [1, 2944]])
                ins = sc.dma_start(out=E_sb[:], in_=src)
                dEl.inc(ins, 16)

            ctx.on("scalar", _)
            st["loaded"] = True
            st["tok"] = dEl.n
        m0 = 4095 - kt * 128 + qt * 512
        view = BassAP(tensor=E_sb.ap().tensor, offset=4606 - m0,
                      ap=[[2944, 128], [-1, 512]])
        return view, [(dEl, st["tok"])]

    def skip_tile(h, qt, kt):
        return kt >= 4 * qt + 20

    return emit_attn_dense(
        ctx, "l3a", xn_tok,
        WDirect(ctx.dram["L3a_wq"], D, D),
        WDirect(ctx.dram["L3a_wk"], D, 256),
        WDirect(ctx.dram["L3a_wv"], D, 256),
        WDirect(ctx.dram["L3a_wo"], D, D),
        exp_scale=0.125, kv_width=256, kv_group=4, kv_mod=True,
        rope_tabs=(C128, S128, tab_dep),
        prob_mult=prob_mult, skip_tile=skip_tile)


L3A_INPUT_SPECS = [
    ("L3a_wq", [D, D], F32), ("L3a_wk", [D, 256], F32),
    ("L3a_wv", [D, 256], F32), ("L3a_wo", [D, D], F32),
]


def l3a_percore_consts(core):
    my0 = (core % 2) * T
    pos = np.arange(my0, my0 + T, dtype=np.float32)
    invf = 1.0 / (10000.0 ** (np.arange(32, dtype=np.float32) / 32.0))
    ang = invf[:, None] * pos[None, :]
    return {
        "c_l3a_c32": np.cos(ang).astype(bf16),
        "c_l3a_s32": np.sin(ang).astype(bf16),
        "c_stepE": (np.arange(8448) <= 2048 + my0).astype(bf16).reshape(1, 8448),
    }



FP4_MIDS = (0.25, 0.75, 1.25, 1.75)
FP4_MIDS2 = (2.5, 3.5)
FP4_MID3 = 5.0


def emit_fp4_quant(ctx, src_ap, out_ap, ncols, scr, dep=None):
    Ev, Es = ctx.E["vector"], ctx.E["scalar"]
    nb = ncols // 16
    vq, acc, acc2, acc3, sgn, scale = (scr["vq"], scr["acc"], scr["acc2"],
                                       scr["acc3"], scr["sgn"], scr["scale"])

    def _(vector):
        if dep:
            for s_, t_ in dep:
                vector.wait_ge(s_.h, t_)
        ins = vector.tensor_reduce(out=scale[0:128, 0:nb].rearrange("p (a o) -> p a o", o=1),
                                   in_=src_ap.rearrange("p (a w) -> p a w", w=16),
                                   axis=AX.X, op=ALU.max, apply_absolute_value=True)
        t = Ev.inc(ins, 1)
        vector.wait_ge(Ev.h, t)
        ins = vector.tensor_scalar(out=scale[0:128, 0:nb], in0=scale[0:128, 0:nb],
                                   scalar1=1.0 / 6.0, scalar2=1e-12,
                                   op0=ALU.mult, op1=ALU.max)
        Ev.inc(ins, 1)

    ctx.on("vector", _)
    v_tok = Ev.n

    def _(scalar):
        scalar.wait_ge(Ev.h, v_tok)
        scalar.activation(out=sgn[0:128, 0:ncols], in_=src_ap, func=AF.Sign)
        ins = scalar.activation(out=vq[0:128, 0:ncols], in_=src_ap, func=AF.Abs)
        Es.inc(ins, 1)

    ctx.on("scalar", _)
    sa_tok = Es.n

    def _(vector):
        vector.wait_ge(Es.h, sa_tok)
        av = vq[0:128, 0:ncols].rearrange("p (a w) -> p a w", w=16)
        thr = scale[0:128, nb:2 * nb]
        thr_bc = thr.rearrange("p (a o) -> p a o", o=1).to_broadcast([128, nb, 16])

        def sw(ins):
            t = Ev.inc(ins, 1)
            vector.wait_ge(Ev.h, t)

        def cmp_to(dst, m):
            sw(vector.tensor_scalar(out=thr, in0=scale[0:128, 0:nb], scalar1=m,
                                    scalar2=None, op0=ALU.mult))
            sw(vector.tensor_tensor(out=dst.rearrange("p (a w) -> p a w", w=16),
                                    in0=av, in1=thr_bc, op=ALU.is_gt))

        a_, a2, a3 = acc[0:128, 0:ncols], acc2[0:128, 0:ncols], acc3[0:128, 0:ncols]
        cmp_to(a_, FP4_MIDS[0])
        for m in FP4_MIDS[1:]:
            cmp_to(a3, m)
            sw(vector.tensor_tensor(out=a_, in0=a_, in1=a3, op=ALU.add))
        cmp_to(a2, FP4_MIDS2[0])
        cmp_to(a3, FP4_MIDS2[1])
        sw(vector.tensor_tensor(out=a2, in0=a2, in1=a3, op=ALU.add))
        cmp_to(a3, FP4_MID3)
        sw(vector.scalar_tensor_tensor(out=a2, in0=a3, scalar=2.0, in1=a2,
                                       op0=ALU.mult, op1=ALU.add))
        sw(vector.scalar_tensor_tensor(out=a_, in0=a_, scalar=0.5, in1=a2,
                                       op0=ALU.mult, op1=ALU.add))
        sw(vector.tensor_tensor(out=a_, in0=a_, in1=sgn[0:128, 0:ncols], op=ALU.mult))
        sbc = scale[0:128, 0:nb].rearrange("p (a o) -> p a o", o=1).to_broadcast(
            [128, nb, 16])
        sw(vector.tensor_tensor(out=out_ap.rearrange("p (a w) -> p a w", w=16),
                                in0=a_.rearrange("p (a w) -> p a w", w=16),
                                in1=sbc, op=ALU.mult))

    ctx.on("vector", _)
    return (Ev, Ev.n)


def l2f_quant_scratch(ctx, ncols):
    nb = ncols // 16
    return {
        "vq": ctx.tmp("q_vq", [128, ncols], F32),
        "acc": ctx.tmp("q_acc", [128, ncols], BF16),
        "acc2": ctx.tmp("q_acc2", [128, ncols], BF16),
        "acc3": ctx.tmp("q_acc3", [128, ncols], BF16),
        "sgn": ctx.tmp("q_sgn", [128, ncols], BF16),
        "scale": ctx.tmp("q_scale", [128, 2 * nb], F32),
    }


def emit_l2f_weight_prep(ctx):
    nc = ctx.nc
    scr = {}
    for nm, shape in (("wg", (2816, D)), ("wu", (2816, D)), ("wd", (D, 2816))):
        scr[nm] = nc.dram_tensor(f"scr_{nm}_{ctx.uid()}", list(shape), BF16,
                                 kind="Internal").ap()
    scope = ExitStack()
    le = scope.enter_context
    wt = le(nc.sbuf_tensor(f"l2p_wt{ctx.uid()}", [128, 2816], F32))
    wq = le(nc.sbuf_tensor(f"l2p_wq{ctx.uid()}", [128, 2816], BF16))
    qscr = {
        "vq": le(nc.sbuf_tensor(f"l2p_vq{ctx.uid()}", [128, 2816], F32)),
        "acc": le(nc.sbuf_tensor(f"l2p_a{ctx.uid()}", [128, 2816], BF16)),
        "acc2": le(nc.sbuf_tensor(f"l2p_a2{ctx.uid()}", [128, 2816], BF16)),
        "acc3": le(nc.sbuf_tensor(f"l2p_a3{ctx.uid()}", [128, 2816], BF16)),
        "sgn": le(nc.sbuf_tensor(f"l2p_sg{ctx.uid()}", [128, 2816], BF16)),
        "scale": le(nc.sbuf_tensor(f"l2p_sc{ctx.uid()}", [128, 2 * 176], F32)),
    }
    dl = ctx.new_ldsem()
    dst_sem = ctx.new_ldsem()
    jobs = []
    for r0 in range(0, 2816, 128):
        jobs.append(("L2f_w_gate", scr["wg"], r0, D))
        jobs.append(("L2f_w_up", scr["wu"], r0, D))
    for r0 in range(0, D, 128):
        jobs.append(("L2f_w_down", scr["wd"], r0, 2816))
    prev_q = [0]
    for src_name, dst, r0, ncols in jobs:
        def _(sync, src_name=src_name, r0=r0, ncols=ncols):
            if prev_q[0]:
                sync.wait_ge(ctx.E["vector"].h, prev_q[0])
            ins = sync.dma_start(out=wt[:, 0:ncols],
                                 in_=ctx.dram[src_name][r0:r0 + 128, :])
            dl.inc(ins, 16)

        ctx.on("sync", _)
        ld_tok = dl.n
        qdep = emit_fp4_quant(ctx, wt[:, 0:ncols], wq[:, 0:ncols], ncols, qscr,
                              dep=[(dl, ld_tok), (dst_sem, dst_sem.n)])
        prev_q[0] = qdep[1]

        def _(sync, dst=dst, r0=r0, ncols=ncols, qdep=qdep):
            sync.wait_ge(qdep[0].h, qdep[1])
            ins = sync.dma_start(out=dst[r0:r0 + 128, :], in_=wq[:, 0:ncols])
            dst_sem.inc(ins, 16)

        ctx.on("sync", _)
    scope.close()
    ctx.barrier()
    return scr["wg"], scr["wu"], scr["wd"], (dst_sem, dst_sem.n)


def emit_l2_ffn(ctx, x_dep, scr_wg, scr_wu, scr_wd, wprep_dep, tag,
                tbs=None, dbg=None):
    nc = ctx.nc
    Et, Es, Ev = ctx.E["tensor"], ctx.E["scalar"], ctx.E["vector"]
    I2 = 2816
    if tbs is None:
        tbs = range(4)
    dbg_sem = [None]

    def dump(tb, name, buf_ap, wait):
        if not dbg or name not in dbg or tb != 0:
            return None
        if dbg_sem[0] is None:
            dbg_sem[0] = ctx.new_ldsem()
        ds_ = dbg_sem[0]

        def _(sync, buf_ap=buf_ap, wait=wait):
            sync.wait_ge(wait[0].h, wait[1])
            ins = sync.dma_start(out=dbg[name], in_=buf_ap)
            ds_.inc(ins, 16)

        ctx.on("sync", _)
        return (ds_, ds_.n)
    xn_t = ctx.tmp("l2_xnt", [128, 4 * 1024], F32)
    xq_t = ctx.tmp("l2_xqt", [128, 4 * 1024], BF16)
    xq_f = ctx.tmp("l2_xqf", [128, 8 * 512], BF16)
    h_t = ctx.tmp("l2_ht", [128, 4 * I2], BF16)
    g_blk = ctx.tmp("l2_g", [128, 4 * 512], F32)
    hqT = ctx.tmp("l2_hqT", [128, 512], BF16)
    lns = ctx.tmp("l2_lns", [128, 8], F32)
    qscr = l2f_quant_scratch(ctx, 1024)
    nb_sizes = [512, 512, 512, 512, 512, 256]
    wsg = WTransposedScratch(scr_wg, D, I2)
    wsu = WTransposedScratch(scr_wu, D, I2)

    prev_phase = list(x_dep)

    for tb in tbs:
        for j in range(4):
            for c in range(NCH):
                bank = 4 + ((j * NCH + c) % 2)

                def _(tensor, j=j, c=c, bank=bank):
                    for s_, t_ in prev_phase:
                        tensor.wait_ge(s_.h, t_)
                    ctx.psum_wait(tensor, bank)
                    ins = tensor.transpose(
                        ctx.ps[bank][:, 0:128],
                        ctx.x_sb[:, c * T + (tb * 4 + j) * 128: c * T + (tb * 4 + j) * 128 + 128],
                        ctx.identf[:])
                    Et.inc(ins, 1)

                ctx.on("tensor", _)
                mm = Et.n

                def _(scalar, j=j, c=c, mm=mm, bank=bank):
                    scalar.wait_ge(Et.h, mm)
                    ins = scalar.activation(out=xn_t[:, j * 1024 + c * 128: j * 1024 + (c + 1) * 128],
                                            in_=ctx.ps[bank][:, 0:128], func=AF.Copy)
                    Es.inc(ins, 1)

                ctx.on("scalar", _)
                ctx.psum_set_owner(bank, Es, Es.n)
        xt_tok = Es.n

        for j in range(4):
            xs = xn_t[:, j * 1024:(j + 1) * 1024]

            def _(vector, j=j, xs=xs):
                if j == 0:
                    vector.wait_ge(Es.h, xt_tok)

                def sw(ins):
                    t = Ev.inc(ins, 1)
                    vector.wait_ge(Ev.h, t)

                sw(vector.tensor_reduce(out=lns[:, 0:1].rearrange("p (a o) -> p a o", o=1),
                                        in_=xs.rearrange("p (a w) -> p a w", a=1),
                                        axis=AX.X, op=ALU.add))
                sw(vector.tensor_scalar(out=lns[:, 2:3], in0=lns[:, 0:1],
                                        scalar1=1.0 / D, scalar2=None, op0=ALU.mult))
                sw(vector.tensor_tensor(out=qscr["vq"][:, 0:1024], in0=xs, in1=xs,
                                        op=ALU.mult))
                sw(vector.tensor_reduce(out=lns[:, 1:2].rearrange("p (a o) -> p a o", o=1),
                                        in_=qscr["vq"][:, 0:1024].rearrange("p (a w) -> p a w", a=1),
                                        axis=AX.X, op=ALU.add))
                sw(vector.tensor_tensor(out=lns[:, 4:5], in0=lns[:, 2:3],
                                        in1=lns[:, 2:3], op=ALU.mult))
                sw(vector.scalar_tensor_tensor(out=lns[:, 3:4], in0=lns[:, 1:2],
                                               scalar=1.0 / D, in1=lns[:, 4:5],
                                               op0=ALU.mult, op1=ALU.subtract))
                sw(vector.tensor_scalar(out=lns[:, 3:4], in0=lns[:, 3:4],
                                        scalar1=LN_EPS, scalar2=None, op0=ALU.add))

            ctx.on("vector", _)
            var_tok = Ev.n

            def _(scalar, var_tok=var_tok):
                scalar.wait_ge(Ev.h, var_tok)
                ins = scalar.activation(out=lns[:, 3:4], in_=lns[:, 3:4], func=AF.Ln)
                t = Es.inc(ins, 1)
                scalar.wait_ge(Es.h, t)
                ins = scalar.activation(out=lns[:, 3:4], in_=lns[:, 3:4],
                                        func=AF.Exp, scale=-0.5)
                Es.inc(ins, 1)

            ctx.on("scalar", _)
            rstd_tok = Es.n

            def _(vector, xs=xs, rstd_tok=rstd_tok):
                vector.wait_ge(Es.h, rstd_tok)
                mu_bc = lns[:, 2:3].rearrange("p (a o) -> p a o", o=1).to_broadcast(
                    [128, 1, 1024])
                rs_bc = lns[:, 3:4].rearrange("p (a o) -> p a o", o=1).to_broadcast(
                    [128, 1, 1024])
                ins = vector.tensor_tensor(out=xs.rearrange("p (a w) -> p a w", a=1),
                                           in0=xs.rearrange("p (a w) -> p a w", a=1),
                                           in1=mu_bc, op=ALU.subtract)
                t = Ev.inc(ins, 1)
                vector.wait_ge(Ev.h, t)
                ins = vector.tensor_tensor(out=xs.rearrange("p (a w) -> p a w", a=1),
                                           in0=xs.rearrange("p (a w) -> p a w", a=1),
                                           in1=rs_bc, op=ALU.mult)
                t = Ev.inc(ins, 1)
                vector.wait_ge(Ev.h, t)

            ctx.on("vector", _)
        ln_tok = Ev.n
        dump(tb, "xnt", xn_t[:, 0:4096], (Ev, ln_tok))

        qdep = (Ev, ln_tok)
        for qq in range(4):
            qdep = emit_fp4_quant(ctx, xn_t[:, qq * 1024:(qq + 1) * 1024],
                                  xq_t[:, qq * 1024:(qq + 1) * 1024], 1024, qscr,
                                  dep=[qdep])
        xq_tok = qdep[1]
        dump(tb, "xq", xq_t[:, 0:4096], (Ev, xq_tok))

        for c in range(NCH):
            for j in range(4):
                def _(tensor, j=j, c=c):
                    tensor.wait_ge(Ev.h, xq_tok)
                    ctx.psum_wait(tensor, "b0")
                    ins = tensor.transpose(
                        ctx.psb[:, 0:128],
                        xq_t[:, j * 1024 + c * 128: j * 1024 + (c + 1) * 128],
                        ctx.ident[:])
                    Et.inc(ins, 1)

                ctx.on("tensor", _)
                mm = Et.n

                def _(scalar, j=j, c=c, mm=mm):
                    scalar.wait_ge(Et.h, mm)
                    ins = scalar.activation(out=xq_f[:, c * 512 + j * 128: c * 512 + (j + 1) * 128],
                                            in_=ctx.psb[:, 0:128], func=AF.Copy)
                    Es.inc(ins, 1)

                ctx.on("scalar", _)
                ctx.psum_set_owner("b0", Es, Es.n)
        xqf_tok = Es.n
        dump(tb, "xqf", xq_f[:, :], (Es, xqf_tok))

        nb0 = 0
        hdep = None
        for nbi, ncols in enumerate(nb_sizes):
            for role in (0, 1):
                wsrc = wsg if role == 0 else wsu
                bi = (nbi * 2 + role) % 2
                ld = wsrc.load(ctx, nb0, ncols, bi)
                for j in range(4):
                    bank = ctx.next_bank()

                    def _(tensor, bi=bi, j=j, ncols=ncols, ld=ld):
                        tensor.wait_ge(ld[0].h, ld[1])
                        tensor.wait_ge(Es.h, xqf_tok)
                        ctx.psum_wait(tensor, bank)
                        for kc in range(NCH):
                            ins = tensor.matmul(
                                ctx.ps[bank][:, 0:ncols],
                                xq_f[:, kc * 512 + j * 128: kc * 512 + (j + 1) * 128],
                                ctx.wbuf[bi][:, kc * ncols:(kc + 1) * ncols],
                                start=(kc == 0), stop=(kc == NCH - 1))
                        Et.inc(ins, 1)

                    ctx.on("tensor", _)
                    mm = Et.n

                    if role == 0:
                        g_guard = hdep

                        def _(scalar, j=j, mm=mm, ncols=ncols, g_guard=g_guard):
                            scalar.wait_ge(Et.h, mm)
                            if g_guard is not None:
                                scalar.wait_ge(g_guard[0].h, g_guard[1])
                            ins = scalar.activation(out=g_blk[:, j * 512: j * 512 + ncols],
                                                    in_=ctx.ps[bank][:, 0:ncols],
                                                    func=AF.Silu)
                            Es.inc(ins, 1)

                        ctx.on("scalar", _)
                        ctx.psum_set_owner(bank, Es, Es.n)
                    else:
                        sg_tok = Es.n

                        def _(vector, j=j, mm=mm, ncols=ncols, sg_tok=sg_tok):
                            vector.wait_ge(Et.h, mm)
                            vector.wait_ge(Es.h, sg_tok)
                            ins = vector.tensor_tensor(
                                out=g_blk[:, j * 512: j * 512 + ncols],
                                in0=g_blk[:, j * 512: j * 512 + ncols],
                                in1=ctx.ps[bank][:, 0:ncols], op=ALU.mult)
                            t = Ev.inc(ins, 1)
                            vector.wait_ge(Ev.h, t)

                        ctx.on("vector", _)
                        ctx.psum_set_owner(bank, Ev, Ev.n)
                ctx.wbuf_state[bi] = (Et, Et.n)
            hdep = (Ev, Ev.n)
            for j in range(4):
                hdep = emit_fp4_quant(ctx, g_blk[:, j * 512: j * 512 + ncols],
                                      h_t[:, j * I2 + nb0: j * I2 + nb0 + ncols],
                                      ncols, qscr, dep=[hdep])
            nb0 += ncols
        hq_tok = hdep[1]
        dump(tb, "hq", h_t[:, :], (Ev, hq_tok))

        wsd_ap = scr_wd
        hq_prev = [0]
        for grp in range(2):
            def _(tensor):
                for b_ in range(4):
                    ctx.psum_wait(tensor, b_)

            ctx.on("tensor", _)
            for kc in range(22):
                bi = kc % 2
                dload = ctx.dwb[bi]
                prevst = ctx.wbuf_state[bi]

                def _(sc, kc=kc, grp=grp, bi=bi, dload=dload, prevst=prevst):
                    if prevst is not None:
                        sc.wait_ge(prevst[0].h, prevst[1])
                    ins = sc.dma_start_transpose(
                        ctx.wbuf[bi][:, 0:512],
                        wsd_ap[grp * 512:(grp + 1) * 512, kc * 128:(kc + 1) * 128])
                    dload.inc(ins, 16)

                ctx.on("scalar", _)
                wtok = dload.n

                for j in range(4):
                    def _(tensor, kc=kc, j=j):
                        tensor.wait_ge(Ev.h, hq_tok)
                        if hq_prev[0]:
                            tensor.wait_ge(Es.h, hq_prev[0])
                        ctx.psum_wait(tensor, "b0")
                        ins = tensor.transpose(
                            ctx.psb[:, 0:128],
                            h_t[:, j * I2 + kc * 128: j * I2 + (kc + 1) * 128],
                            ctx.ident[:])
                        Et.inc(ins, 1)

                    ctx.on("tensor", _)
                    mmT = Et.n

                    def _(scalar, j=j, mmT=mmT):
                        scalar.wait_ge(Et.h, mmT)
                        ins = scalar.activation(out=hqT[:, j * 128:(j + 1) * 128],
                                                in_=ctx.psb[:, 0:128], func=AF.Copy)
                        Es.inc(ins, 1)

                    ctx.on("scalar", _)
                    ctx.psum_set_owner("b0", Es, Es.n)
                hq_prev[0] = Es.n
                hqT_tok = Es.n

                def _(tensor, kc=kc, bi=bi, dload=dload, wtok=wtok, hqT_tok=hqT_tok):
                    tensor.wait_ge(dload.h, wtok)
                    tensor.wait_ge(Es.h, hqT_tok)
                    for ocl in range(4):
                        ins = tensor.matmul(ctx.ps[ocl][:, :],
                                            ctx.wbuf[bi][:, ocl * 128:(ocl + 1) * 128],
                                            hqT[:, 0:512],
                                            start=(kc == 0), stop=(kc == 21))
                        Et.inc(ins, 1)

                ctx.on("tensor", _)
                ctx.wbuf_state[bi] = (Et, Et.n)
            down_mm = Et.n

            for ocl in range(4):
                oc = grp * 4 + ocl

                def _(vector, oc=oc, ocl=ocl, down_mm=down_mm):
                    vector.wait_ge(Et.h, down_mm)
                    xs = ctx.x_sb[:, oc * T + tb * 512: oc * T + (tb + 1) * 512]
                    ins = vector.scalar_tensor_tensor(out=xs, in0=ctx.ps[ocl][:, :],
                                                      scalar=0.5, in1=xs, op0=ALU.mult,
                                                      op1=ALU.add)
                    t = Ev.inc(ins, 1)
                    vector.wait_ge(Ev.h, t)

                ctx.on("vector", _)
                ctx.psum_set_owner(ocl, Ev, Ev.n)
        prev_phase = [(Ev, Ev.n)]
    return Ev.n



L2A_INPUT_SPECS = [
    ("L2a_WqT", [D, D], F32), ("L2a_WkT", [D, D], F32),
    ("L2a_WvT", [D, D], F32), ("L2a_WoT", [D, D], F32),
    ("L2a_bq", [D], F32), ("L2a_bk", [D], F32),
    ("L2a_bv", [D], F32), ("L2a_bo", [D], F32),
    ("L2a_wqs", [128, 3], F32), ("L2a_wvs", [128, 3], F32),
    ("L2a_wk3", [128, 8 * 48], F32),
    ("L2a_bqs", [3], F32), ("L2a_bvs", [3], F32), ("L2a_bks48", [1, 48], F32),
    ("L2a_temp", [1, 1], F32),
]


def l2a_host_inputs(Wq, bq, Wk, bk, Wv, bv, Wo, bo, Wqs, bqs, Wks, bks,
                    Wvs, bvs, temp):
    wk3 = np.zeros((128, 8 * 48), dtype=np.float32)
    for kc in range(8):
        for hh in range(2):
            h = 2 * kc + hh
            for c in range(3):
                wk3[hh * 64:(hh + 1) * 64, kc * 48 + 3 * h + c] = Wks[c]
    dup = lambda w: np.concatenate([w.T, w.T], axis=0).astype(np.float32)
    return {
        "L2a_WqT": np.ascontiguousarray(Wq.T), "L2a_WkT": np.ascontiguousarray(Wk.T),
        "L2a_WvT": np.ascontiguousarray(Wv.T), "L2a_WoT": np.ascontiguousarray(Wo.T),
        "L2a_bq": bq, "L2a_bk": bk, "L2a_bv": bv, "L2a_bo": bo,
        "L2a_wqs": dup(Wqs), "L2a_wvs": dup(Wvs), "L2a_wk3": wk3,
        "L2a_bqs": bqs, "L2a_bvs": bvs,
        "L2a_bks48": np.tile(bks, 16).reshape(1, 48).astype(np.float32),
        "L2a_temp": np.array([[np.float32(temp)]], dtype=np.float32),
    }


def emit_l2_attn(ctx, tag, xn_tok):
    nc = ctx.nc
    Et, Es, Ev = ctx.E["tensor"], ctx.E["scalar"], ctx.E["vector"]
    ds = ctx.dma["sync"]
    xn_dep = [(Ev, xn_tok)]
    dr = ctx.dram

    sh_k = nc.dram_tensor(f"sh_k_{tag}", [2, D, T], BF16, kind="Internal",
                          addr_space="Shared").ap()
    sh_v = nc.dram_tensor(f"sh_v_{tag}", [2, T, D], BF16, kind="Internal",
                          addr_space="Shared").ap()
    sh_v3 = nc.dram_tensor(f"sh_v3_{tag}", [2, 48, T], BF16, kind="Internal",
                           addr_space="Shared").ap()
    sh_kn = nc.dram_tensor(f"sh_kn_{tag}", [2, 1, 16], F32, kind="Internal",
                           addr_space="Shared").ap()

    q_sb = ctx.tmp("l2a_q", [128, NCH * T], BF16)
    v3st = ctx.tmp("l2a_v3st", [3, 512], BF16)
    nsq = ctx.tmp("l2a_nsq", [128, 512], F32)
    nrmS = ctx.tmp("l2a_nrmS", [16, 64], F32)
    nrm = ctx.tmp("l2a_nrm", [16, 2], F32)
    nrm_bf = ctx.tmp("l2a_nrmbf", [16, 2], BF16)
    nrow = ctx.tmp("l2a_nrow", [1, 48], F32)
    knrow = ctx.tmp("l2a_knrow", [1, 16], F32)
    crow = ctx.tmp("l2a_crow", [1, 16], F32)
    crm = ctx.tmp("l2a_crm", [16, 9], F32)
    negM = ctx.tmp("l2a_negM", [128, 16], F32)
    k_sb = ctx.tmp("l2a_k", [128, S], BF16)
    v_aug = ctx.tmp("l2a_vaug", [128, 32 * 65], BF16)
    pr = ctx.tmp("l2a_pr", [128, 2 * 512], BF16)
    aug_pool = ctx.tmp("l2a_augp", [128, 4096], BF16)
    kchunk = aug_pool
    k3h = ctx.tmp("l2a_k3h", [48, 1024], BF16)
    v3h = ctx.tmp("l2a_v3h", [48, 1024], BF16)
    Ra = ctx.tmp("l2a_Ra", [48, 512], BF16)
    Rb = ctx.tmp("l2a_Rb", [48, 512], BF16)
    Rc = ctx.tmp("l2a_Rc", [48, 512], BF16)
    Rd = ctx.tmp("l2a_Rd", [48, 512], BF16)
    cross_all = ctx.tmp("l2a_cra", [48, S], BF16)
    cross_h = aug_pool
    qstp_h = ctx.tmp("l2a_qst", [3, T], BF16)
    wqs_sb = ctx.tmp("l2a_wqs", [128, 3], BF16)
    wvs_sb = ctx.tmp("l2a_wvs", [128, 3], BF16)
    wk3_sb = ctx.tmp("l2a_wk3", [128, 8 * 48], BF16)
    bqs_row = ctx.tmp("l2a_bqs", [1, 3], BF16)
    bvs_row = ctx.tmp("l2a_bvs", [1, 3], BF16)
    bks48_row = ctx.tmp("l2a_bks", [1, 48], BF16)
    temp_sb = ctx.tmp("l2a_tmp", [1, 1], F32)
    tneg = ctx.tmp("l2a_tneg", [1, 1], F32)
    temp_col = ctx.tmp("l2a_tcol", [128, 1], F32)
    mrow = ctx.tmp("l2a_mrow", [1, 8], F32)
    av_tmp = nsq
    rec_sb = ctx.tmp("l2a_rec", [1, 512], F32)[0:1, 0:512]

    ev_kv = ctx.new_event()

    sm = ctx.new_ldsem()

    def _(g):
        for dst, src in ((wqs_sb, dr["L2a_wqs"]), (wvs_sb, dr["L2a_wvs"]),
                         (wk3_sb, dr["L2a_wk3"]), (temp_sb, dr["L2a_temp"])):
            ins = g.dma_start(out=dst[:], in_=src)
            sm.inc(ins, 16)
        for dst, src in ((bqs_row, dr["L2a_bqs"]), (bvs_row, dr["L2a_bvs"])):
            ins = g.dma_start(out=dst[:], in_=src.rearrange("(o n) -> o n", o=1))
            sm.inc(ins, 16)
        ins = g.dma_start(out=bks48_row[:], in_=dr["L2a_bks48"])
        sm.inc(ins, 16)

    ctx.on("gpsimd", _)
    sm_tok = sm.n

    def _(scalar):
        scalar.wait_ge(sm.h, sm_tok)
        ins = scalar.activation(out=tneg[:], in_=temp_sb[:], func=AF.Abs)
        t = Es.inc(ins, 1)
        scalar.wait_ge(Es.h, t)
        ins = scalar.activation(out=tneg[:], in_=tneg[:], func=AF.Copy, scale=-1.0)
        Es.inc(ins, 1)

    ctx.on("scalar", _)
    tneg_tok = Es.n

    def _(tensor):
        tensor.wait_ge(sm.h, sm_tok)
        ctx.psum_wait(tensor, 5)
        ins = tensor.matmul(ctx.ps[5][:, 0:1], ctx.ones_rowf[:], temp_sb[:],
                            start=True, stop=True)
        Et.inc(ins, 1)

    ctx.on("tensor", _)
    tc_mm = Et.n

    def _(scalar):
        scalar.wait_ge(Et.h, tc_mm)
        ins = scalar.activation(out=temp_col[:], in_=ctx.ps[5][:, 0:1], func=AF.Copy)
        Es.inc(ins, 1)

    ctx.on("scalar", _)
    tcol_tok = Es.n
    ctx.psum_set_owner(5, Es, tcol_tok)

    def qk_sink_factory(store, col_base):
        def sink(oc, tt, psum_ap, mm_tok):
            ring = (oc * 4 + tt) % 2
            rs = ctx.rsem[ring]

            def _(scalar):
                scalar.wait_ge(Et.h, mm_tok)
                if store is None:
                    out = q_sb[:, oc * T + tt * 512: oc * T + (tt + 1) * 512]
                else:
                    out = v_aug[:, ring * 512:(ring + 1) * 512]
                    if rs.n:
                        scalar.wait_ge(rs.h, rs.n)
                ins = scalar.activation(out=out, in_=psum_ap, func=AF.Copy)
                Es.inc(ins, 1)
                ins = scalar.activation(out=nsq[:], in_=psum_ap, func=AF.Square)
                Es.inc(ins, 1)

            ctx.on("scalar", _)
            sq_tok = Es.n
            cp_tok = sq_tok - 1

            def _(tensor):
                tensor.wait_ge(Es.h, sq_tok)
                ctx.psum_wait(tensor, 4)
                ins = tensor.matmul(ctx.ps[4][0:16, :],
                                    ctx.ind16_sb[:, oc * 16:(oc + 1) * 16],
                                    nsq[:], start=True, stop=True)
                Et.inc(ins, 1)

            ctx.on("tensor", _)
            nm_tok = Et.n

            def _(vector):
                vector.wait_ge(Et.h, nm_tok)
                col = col_base + oc * 4 + tt
                ins = vector.reduce_max(out=nrmS[0:16, col:col + 1],
                                        in_=ctx.ps[4][0:16, :], axis=AX.X)
                t = Ev.inc(ins, 1)
                vector.wait_ge(Ev.h, t)

            ctx.on("vector", _)
            ctx.psum_set_owner(4, Ev, Ev.n)

            if store is not None:
                def emit_store(sync, slot):
                    tgt = sh_k[slot, oc * 128:(oc + 1) * 128, tt * 512:(tt + 1) * 512]
                    sync.dma_start(out=tgt,
                                   in_=v_aug[:, ring * 512:(ring + 1) * 512]).then_inc(rs.h, 16)

                def _(sync):
                    sync.wait_ge(Es.h, cp_tok)

                ctx.on("sync", _)
                ctx.by_parity("sync", emit_store)
                rs.n += 16
            return (Es, sq_tok)

        return sink

    bq_row = emit_bias_row_load(ctx, dr["L2a_bq"], D)
    bk_row = emit_bias_row_load(ctx, dr["L2a_bk"], D)
    bv_row = emit_bias_row_load(ctx, dr["L2a_bv"], D)
    bo_row = emit_bias_row_load(ctx, dr["L2a_bo"], D)

    emit_proj(ctx, WDirect(dr["L2a_WqT"], D, D), qk_sink_factory(None, 0),
              tensor_dep=xn_dep, tail_mm=bias_tail_mm(ctx, bq_row[0], bq_row[1]))
    q_done = Es.n
    emit_proj(ctx, WDirect(dr["L2a_WkT"], D, D), qk_sink_factory(True, 32),
              tensor_dep=xn_dep, tail_mm=bias_tail_mm(ctx, bk_row[0], bk_row[1]))
    qk_norm_acc = Ev.n
    k_store_deps = [(rs, rs.n) for rs in ctx.rsem[0:2]]

    def v_sink(tb, ob_i, psum_ap, mm_tok):
        ring = (tb + ob_i) % 2
        rs = ctx.rsem[4 + ring]
        ncols = psum_ap.shape[1]

        def _(scalar):
            scalar.wait_ge(Et.h, mm_tok)
            if rs.n:
                scalar.wait_ge(rs.h, rs.n)
            ins = scalar.activation(out=v_aug[:, 1024 + ring * 512: 1024 + ring * 512 + ncols],
                                    in_=psum_ap, func=AF.Copy)
            Es.inc(ins, 1)

        ctx.on("scalar", _)
        cp_tok = Es.n

        def emit_store(sync, slot):
            tgt = sh_v[slot, tb * 128:(tb + 1) * 128, ob_i * 512: ob_i * 512 + ncols]
            sync.dma_start(out=tgt,
                           in_=v_aug[:, 1024 + ring * 512: 1024 + ring * 512 + ncols]).then_inc(rs.h, 16)

        def _(sync):
            sync.wait_ge(Es.h, cp_tok)

        ctx.on("sync", _)
        ctx.by_parity("sync", emit_store)
        rs.n += 16
        return (Es, cp_tok)

    def v_tail(tensor, tb, ob_i, bank_ap):
        tensor.wait_ge(bv_row[1][0].h, bv_row[1][1])
        ncols = bank_ap.shape[1]
        return tensor.matmul(bank_ap, ctx.ones_row[:],
                             bv_row[0][0:1, ob_i * 512: ob_i * 512 + ncols],
                             start=False, stop=True)

    emit_proj_tokmaj(ctx, WDirect(dr["L2a_WvT"], D, D), v_sink,
                     tensor_dep=xn_dep, tail_mm=v_tail)
    v_store_deps = [(rs, rs.n) for rs in ctx.rsem[4:6]]

    v3_rs = ctx.rsem[2]
    v3prev_mm = [0]

    def v3_sink(oc, tt, psum_ap, mm_tok):
        def _(scalar):
            scalar.wait_ge(Et.h, mm_tok)
            if v3prev_mm[0]:
                scalar.wait_ge(Et.h, v3prev_mm[0])
            ins = scalar.activation(out=pr[:, 0:512], in_=psum_ap, func=AF.Copy)
            Es.inc(ins, 1)

        ctx.on("scalar", _)
        cp = Es.n
        for hh in range(2):
            h = oc * 2 + hh
            b0 = hh * 64

            def _(tensor, h=h, b0=b0, cp=cp):
                tensor.wait_ge(Es.h, cp)
                tensor.wait_ge(sm.h, sm_tok)
                ctx.psum_wait(tensor, 5)
                tensor.matmul(ctx.ps[5][0:3, :], wvs_sb[b0:b0 + 64, 0:3],
                              pr[b0:b0 + 64, 0:512], start=True, stop=False)
                ins = tensor.matmul(ctx.ps[5][0:3, :], bvs_row[0:1, 0:3],
                                    ctx.ones_row_t[0:1, 0:512], start=False, stop=True)
                Et.inc(ins, 1)

            ctx.on("tensor", _)
            v3mm = Et.n
            v3prev_mm[0] = v3mm

            def _(scalar, v3mm=v3mm):
                scalar.wait_ge(Et.h, v3mm)
                if v3_rs.n:
                    scalar.wait_ge(v3_rs.h, v3_rs.n)
                ins = scalar.activation(out=v3st[:], in_=ctx.ps[5][0:3, :], func=AF.Copy)
                Es.inc(ins, 1)

            ctx.on("scalar", _)
            stcp = Es.n
            ctx.psum_set_owner(5, Es, stcp)

            def emit_store(sync, slot, h=h, tt=tt):
                sync.dma_start(out=sh_v3[slot, 3 * h:3 * h + 3, tt * 512:(tt + 1) * 512],
                               in_=v3st[:]).then_inc(v3_rs.h, 16)

            def _(sync, stcp=stcp):
                sync.wait_ge(Es.h, stcp)

            ctx.on("sync", _)
            ctx.by_parity("sync", emit_store)
            v3_rs.n += 16
        return (Es, cp)

    emit_proj(ctx, WDirect(dr["L2a_WvT"], D, D), v3_sink, tensor_dep=xn_dep,
              tail_mm=bias_tail_mm(ctx, bv_row[0], bv_row[1]))
    v3_store_deps = [(v3_rs, v3_rs.n)]

    def _(vector):
        vector.wait_ge(Ev.h, qk_norm_acc)
        ins = vector.reduce_max(out=nrm[0:16, 0:1], in_=nrmS[0:16, 0:32], axis=AX.X)
        t = Ev.inc(ins, 1)
        vector.wait_ge(Ev.h, t)
        ins = vector.reduce_max(out=nrm[0:16, 1:2], in_=nrmS[0:16, 32:64], axis=AX.X)
        t = Ev.inc(ins, 1)
        vector.wait_ge(Ev.h, t)

    ctx.on("vector", _)
    nred_tok = Ev.n

    def _(scalar):
        scalar.wait_ge(Ev.h, nred_tok)
        ins = scalar.activation(out=nrm_bf[:], in_=nrm[0:16, 0:2], func=AF.Copy)
        Es.inc(ins, 1)

    ctx.on("scalar", _)
    nbf_tok = Es.n

    def _(tensor):
        tensor.wait_ge(Es.h, nbf_tok)
        ctx.psum_wait(tensor, "b0")
        tensor.transpose(ctx.psb[0:1, 0:16], nrm_bf[:, 0:1], ctx.ident[0:16, 0:16])
        ins = tensor.transpose(ctx.psb[0:1, 16:32], nrm_bf[:, 1:2], ctx.ident[0:16, 0:16])
        Et.inc(ins, 1)

    ctx.on("tensor", _)
    nT_mm = Et.n

    def _(scalar):
        scalar.wait_ge(Et.h, nT_mm)
        ins = scalar.activation(out=nrow[0:1, 0:16], in_=ctx.psb[0:1, 0:16], func=AF.Sqrt)
        Es.inc(ins, 1)
        ins = scalar.activation(out=knrow[0:1, 0:16], in_=ctx.psb[0:1, 16:32], func=AF.Copy)
        Es.inc(ins, 1)

    ctx.on("scalar", _)
    ctx.psum_set_owner("b0", Es, Es.n)
    kn_row_tok = Es.n

    def emit_knstore(sync, slot):
        sync.dma_start(out=sh_kn[slot], in_=knrow[0:1, 0:16]).then_inc(ds.h, 16)

    def _(sync):
        sync.wait_ge(Es.h, kn_row_tok)

    ctx.on("sync", _)
    ctx.by_parity("sync", emit_knstore)
    ds.n += 16
    ctx.mbox_flag(ev_kv, dep_waits=[(ds, ds.n)] + k_store_deps + v_store_deps
                  + v3_store_deps)
    xrdy_tok = ctx.mbox_wait(ev_kv)

    preg = ctx.parity_reg("sync")
    dpk = ctx.new_ldsem()

    def _(sync):
        sync.wait_ge(ctx.xrdy.h, xrdy_tok)
        with sync.If_eq(preg, 0):
            sync.dma_start(out=nrow[0:1, 16:32], in_=sh_kn[1]).then_inc(dpk.h, 16)
        with sync.Else():
            sync.dma_start(out=nrow[0:1, 16:32], in_=sh_kn[0]).then_inc(dpk.h, 16)
        sync.end_ifs()

    ctx.on("sync", _)
    dpk.n += 16
    peer_kn = dpk.n

    def _(vector):
        vector.wait_ge(dpk.h, peer_kn)
        vector.wait_ge(Es.h, kn_row_tok)
        ins = vector.tensor_tensor(out=nrow[0:1, 16:32], in0=knrow[0:1, 0:16],
                                   in1=nrow[0:1, 16:32], op=ALU.max)
        t = Ev.inc(ins, 1)
        vector.wait_ge(Ev.h, t)

    ctx.on("vector", _)
    kmax2_tok = Ev.n

    def _(scalar):
        scalar.wait_ge(Ev.h, kmax2_tok)
        ins = scalar.activation(out=nrow[0:1, 16:32], in_=nrow[0:1, 16:32], func=AF.Sqrt)
        Es.inc(ins, 1)

    ctx.on("scalar", _)
    kmax_tok = Es.n

    def _(vector):
        vector.wait_ge(Es.h, kmax_tok)
        ins = vector.tensor_tensor(out=nrow[0:1, 32:48], in0=nrow[0:1, 0:16],
                                   in1=nrow[0:1, 16:32], op=ALU.mult)
        t = Ev.inc(ins, 1)
        vector.wait_ge(Ev.h, t)

    ctx.on("vector", _)
    arow_tok = Ev.n

    dk, dv = ctx.dk, ctx.dv
    dv3 = ctx.new_ldsem()
    kch_mm = [0, 0]
    cross_parts = []
    for sl in range(2):
        for hf in range(2):
            base = sl * T + hf * 1024
            def _(tensor):
                ctx.psum_wait(tensor, 0)
                ctx.psum_wait(tensor, 1)

            ctx.on("tensor", _)
            for kc in range(NCH):
                kb = kc % 2
                prev_mm = kch_mm[kb]

                def _(sc, sl=sl, hf=hf, kc=kc, kb=kb, prev_mm=prev_mm):
                    sc.wait_ge(ctx.xrdy.h, xrdy_tok)
                    for s_, t_ in k_store_deps:
                        sc.wait_ge(s_.h, t_)
                    if prev_mm:
                        sc.wait_ge(Et.h, prev_mm)
                    ins = sc.dma_start(
                        out=kchunk[:, kb * 1024:(kb + 1) * 1024],
                        in_=sh_k[sl, kc * 128:(kc + 1) * 128, hf * 1024:(hf + 1) * 1024])
                    dk.inc(ins, 16)

                ctx.on("scalar", _)
                ld_tok = dk.n

                def _(tensor, kc=kc, kb=kb, ld_tok=ld_tok):
                    tensor.wait_ge(dk.h, ld_tok)
                    tensor.wait_ge(sm.h, sm_tok)
                    for ti in range(2):
                        ins = tensor.matmul(
                            ctx.ps[ti][0:48, :],
                            wk3_sb[:, kc * 48:(kc + 1) * 48],
                            kchunk[:, kb * 1024 + ti * 512: kb * 1024 + (ti + 1) * 512],
                            start=(kc == 0), stop=False)
                        Et.inc(ins, 1)

                ctx.on("tensor", _)
                kch_mm[kb] = Et.n

            def _(tensor):
                for ti in range(2):
                    ins = tensor.matmul(ctx.ps[ti][0:48, :], bks48_row[0:1, 0:48],
                                        ctx.ones_row_t[0:1, 0:512],
                                        start=False, stop=True)
                    Et.inc(ins, 1)

            ctx.on("tensor", _)
            k3_mm = Et.n

            def _(scalar, k3_mm=k3_mm):
                scalar.wait_ge(Et.h, k3_mm)
                for ti in range(2):
                    ins = scalar.activation(out=k3h[0:48, ti * 512:(ti + 1) * 512],
                                            in_=ctx.ps[ti][0:48, :], func=AF.Copy)
                    Es.inc(ins, 1)

            ctx.on("scalar", _)
            k3cp = Es.n
            ctx.psum_set_owner(0, Es, k3cp)
            ctx.psum_set_owner(1, Es, k3cp)

            def _(sync, sl=sl, hf=hf):
                sync.wait_ge(ctx.xrdy.h, xrdy_tok)
                for s_, t_ in v3_store_deps:
                    sync.wait_ge(s_.h, t_)
                if cross_parts:
                    sync.wait_ge(Ev.h, cross_parts[-1])
                ins = sync.dma_start(out=v3h[:],
                                     in_=sh_v3[sl, :, hf * 1024:(hf + 1) * 1024])
                dv3.inc(ins, 16)

            ctx.on("sync", _)
            v3_tok = dv3.n

            for ti in range(2):
                def _(tensor, ti=ti, k3cp=k3cp, v3_tok=v3_tok):
                    tensor.wait_ge(Es.h, k3cp)
                    tensor.wait_ge(dv3.h, v3_tok)
                    ctx.psum_wait(tensor, 2)
                    ctx.psum_wait(tensor, 3)
                    tensor.matmul(ctx.ps[2][0:48, :], ctx.l2rot1[:],
                                  k3h[0:48, ti * 512:(ti + 1) * 512], start=True, stop=True)
                    ins = tensor.matmul(ctx.ps[3][0:48, :], ctx.l2rot2[:],
                                        k3h[0:48, ti * 512:(ti + 1) * 512], start=True,
                                        stop=True)
                    Et.inc(ins, 1)

                ctx.on("tensor", _)
                mmk = Et.n

                def _(scalar, mmk=mmk):
                    scalar.wait_ge(Et.h, mmk)
                    scalar.activation(out=Ra[:], in_=ctx.ps[2][0:48, :], func=AF.Copy)
                    ins = scalar.activation(out=Rb[:], in_=ctx.ps[3][0:48, :], func=AF.Copy)
                    Es.inc(ins, 1)

                ctx.on("scalar", _)
                kcp = Es.n
                ctx.psum_set_owner(2, Es, kcp)
                ctx.psum_set_owner(3, Es, kcp)

                def _(tensor, ti=ti, v3_tok=v3_tok, kcp=kcp):
                    tensor.wait_ge(dv3.h, v3_tok)
                    tensor.wait_ge(Es.h, kcp)
                    tensor.matmul(ctx.ps[2][0:48, :], ctx.l2rot2[:],
                                  v3h[0:48, ti * 512:(ti + 1) * 512], start=True, stop=True)
                    ins = tensor.matmul(ctx.ps[3][0:48, :], ctx.l2rot1[:],
                                        v3h[0:48, ti * 512:(ti + 1) * 512], start=True,
                                        stop=True)
                    Et.inc(ins, 1)

                ctx.on("tensor", _)
                mmv = Et.n

                def _(scalar, mmv=mmv):
                    scalar.wait_ge(Et.h, mmv)
                    scalar.activation(out=Rc[:], in_=ctx.ps[2][0:48, :], func=AF.Copy)
                    ins = scalar.activation(out=Rd[:], in_=ctx.ps[3][0:48, :], func=AF.Copy)
                    Es.inc(ins, 1)

                ctx.on("scalar", _)
                vcp = Es.n
                ctx.psum_set_owner(2, Es, vcp)
                ctx.psum_set_owner(3, Es, vcp)

                def _(vector, base=base, ti=ti, vcp=vcp):
                    vector.wait_ge(Es.h, vcp)
                    ins = vector.tensor_tensor(out=Ra[:], in0=Ra[:], in1=Rc[:],
                                               op=ALU.mult)
                    t = Ev.inc(ins, 1)
                    vector.wait_ge(Ev.h, t)
                    ins = vector.tensor_tensor(out=Rb[:], in0=Rb[:], in1=Rd[:],
                                               op=ALU.mult)
                    t = Ev.inc(ins, 1)
                    vector.wait_ge(Ev.h, t)
                    ins = vector.tensor_tensor(
                        out=cross_all[0:48, base + ti * 512: base + (ti + 1) * 512],
                        in0=Ra[:], in1=Rb[:], op=ALU.subtract)
                    t = Ev.inc(ins, 1)
                    vector.wait_ge(Ev.h, t)

                ctx.on("vector", _)
                cross_parts.append(Ev.n)
    cross_tok = Ev.n

    for st in range(8):
        def _(scalar, st=st):
            scalar.wait_ge(Ev.h, cross_tok)
            if st:
                scalar.wait_ge(Et.h, Et.n)
            ins = scalar.activation(out=Ra[:], in_=cross_all[0:48, st * 512:(st + 1) * 512],
                                    func=AF.Square)
            Es.inc(ins, 1)

        ctx.on("scalar", _)
        sq_tok = Es.n

        def _(tensor, sq_tok=sq_tok):
            tensor.wait_ge(Es.h, sq_tok)
            ctx.psum_wait(tensor, 4)
            ins = tensor.matmul(ctx.ps[4][0:16, :], ctx.l2tri[:], Ra[:],
                                start=True, stop=True)
            Et.inc(ins, 1)

        ctx.on("tensor", _)
        tm = Et.n

        def _(vector, st=st, tm=tm):
            vector.wait_ge(Et.h, tm)
            ins = vector.reduce_max(out=crm[0:16, st:st + 1], in_=ctx.ps[4][0:16, :],
                                    axis=AX.X)
            t = Ev.inc(ins, 1)
            vector.wait_ge(Ev.h, t)

        ctx.on("vector", _)
        ctx.psum_set_owner(4, Ev, Ev.n)

    def _(vector):
        ins = vector.reduce_max(out=crm[0:16, 8:9], in_=crm[0:16, 0:8], axis=AX.X)
        t = Ev.inc(ins, 1)
        vector.wait_ge(Ev.h, t)

    ctx.on("vector", _)
    crm_red = Ev.n

    def _(scalar):
        scalar.wait_ge(Ev.h, crm_red)
        scalar.wait_ge(Et.h, nT_mm)
        ins = scalar.activation(out=crm[0:16, 8:9], in_=crm[0:16, 8:9], func=AF.Sqrt)
        t = Es.inc(ins, 1)
        scalar.wait_ge(Es.h, t)
        ins = scalar.activation(out=nrm_bf[0:16, 0:1], in_=crm[0:16, 8:9], func=AF.Copy)
        Es.inc(ins, 1)

    ctx.on("scalar", _)
    crbf_tok = Es.n

    def _(tensor):
        tensor.wait_ge(Es.h, crbf_tok)
        ctx.psum_wait(tensor, "b0")
        ins = tensor.transpose(ctx.psb[0:1, 0:16], nrm_bf[:, 0:1], ctx.ident[0:16, 0:16])
        Et.inc(ins, 1)

    ctx.on("tensor", _)
    crT_mm = Et.n

    def _(scalar):
        scalar.wait_ge(Et.h, crT_mm)
        ins = scalar.activation(out=crow[0:1, 0:16], in_=ctx.psb[0:1, 0:16], func=AF.Copy)
        Es.inc(ins, 1)

    ctx.on("scalar", _)
    ctx.psum_set_owner("b0", Es, Es.n)
    crow_tok = Es.n

    def _(vector):
        for s_, t_ in k_store_deps + v_store_deps:
            vector.wait_ge(s_.h, t_)
        ins = vector.memset(v_aug[:], 1.0)
        Ev.inc(ins, 1)

    ctx.on("vector", _)
    vaug_ones = Ev.n

    def _(tensor):
        for b_ in (0, 1, 2, 3):
            ctx.psum_wait(tensor, b_)

    ctx.on("tensor", _)
    stage = ctx.xn_sb
    kload_tok = [0]
    vload_tok = [0]
    score_mm_last = [0]
    score_owner = [0, 0]
    score_owner_sem = [Es, Es]
    pr_owner = [0, 0]
    bc_owner = [0]
    bc_owner_sem = [Es]
    qstp_guard = [(Es, 0)]
    crh_guard = [(Es, 0)]
    mrow_guard = [0]

    for h in range(16):
        b0 = (h % 2) * 64
        qrows = slice(b0, b0 + 64)

        def _(sc, h=h):
            sc.wait_ge(ctx.xrdy.h, xrdy_tok)
            if score_mm_last[0]:
                sc.wait_ge(Et.h, score_mm_last[0])
            for sl in range(2):
                for half in range(2):
                    ins = sc.dma_start(out=k_sb[half * 64:(half + 1) * 64,
                                                sl * T:(sl + 1) * T],
                                       in_=sh_k[sl, h * 64:(h + 1) * 64, :])
                    dk.inc(ins, 16)

        ctx.on("scalar", _)
        kload_tok[0] = dk.n

        def _(sc, h=h):
            sc.wait_ge(ctx.xrdy.h, xrdy_tok)
            if score_mm_last[0]:
                sc.wait_ge(Et.h, score_mm_last[0])
            for sl in range(2):
                for j in range(4):
                    dst = v_aug[:, (sl * 16 + j * 4) * 65:(sl * 16 + (j + 1) * 4) * 65]
                    dst = dst.rearrange("p (b c) -> p b c", c=65)[:, :, 0:64]
                    srcv = sh_v[sl, j * 512:(j + 1) * 512, h * 64:(h + 1) * 64]
                    srcv = srcv.rearrange("(b p) c -> p b c", p=128)
                    ins = sc.dma_start(out=dst, in_=srcv)
                    dv.inc(ins, 16)

        ctx.on("scalar", _)
        vload_tok[0] = dv.n

        for tt in range(4):
            def _(tensor, h=h, tt=tt, b0=b0):
                tensor.wait_ge(sm.h, sm_tok)
                tensor.wait_ge(Es.h, q_done)
                tensor.wait_ge(qstp_guard[0][0].h, qstp_guard[0][1])
                ctx.psum_wait(tensor, 5)
                tensor.matmul(ctx.ps[5][0:3, :], wqs_sb[b0:b0 + 64, 0:3],
                              q_sb[b0:b0 + 64, (h // 2) * T + tt * 512:
                                   (h // 2) * T + (tt + 1) * 512],
                              start=True, stop=False)
                ins = tensor.matmul(ctx.ps[5][0:3, :], bqs_row[0:1, 0:3],
                                    ctx.ones_row_t[0:1, 0:512], start=False, stop=True)
                Et.inc(ins, 1)

            ctx.on("tensor", _)
            qs_mm = Et.n

            def _(scalar, tt=tt, qs_mm=qs_mm):
                scalar.wait_ge(Et.h, qs_mm)
                ins = scalar.activation(out=qstp_h[0:3, tt * 512:(tt + 1) * 512],
                                        in_=ctx.ps[5][0:3, :], func=AF.Copy)
                Es.inc(ins, 1)

            ctx.on("scalar", _)
            ctx.psum_set_owner(5, Es, Es.n)
        qstp_tok = Es.n

        for tt in range(4):
            def _(scalar, tt=tt, h=h):
                if tt == 0:
                    scalar.wait_ge(Es.h, qstp_tok)
                    if h == 0:
                        scalar.wait_ge(Ev.h, cross_tok)
                else:
                    scalar.wait_ge(Et.h, Et.n)
                ins = scalar.activation(out=Rc[0:3, :],
                                        in_=qstp_h[0:3, tt * 512:(tt + 1) * 512],
                                        func=AF.Square)
                Es.inc(ins, 1)

            ctx.on("scalar", _)
            sqt = Es.n

            def _(tensor, sqt=sqt):
                tensor.wait_ge(Es.h, sqt)
                ctx.psum_wait(tensor, 4)
                ins = tensor.matmul(ctx.ps[4][0:1, :], ctx.l2tri[0:3, 0:1],
                                    Rc[0:3, :], start=True, stop=True)
                Et.inc(ins, 1)

            ctx.on("tensor", _)
            qm = Et.n

            def _(vector, tt=tt, qm=qm):
                vector.wait_ge(Et.h, qm)
                ins = vector.reduce_max(out=mrow[0:1, tt:tt + 1], in_=ctx.ps[4][0:1, :],
                                        axis=AX.X)
                t = Ev.inc(ins, 1)
                vector.wait_ge(Ev.h, t)

            ctx.on("vector", _)
            ctx.psum_set_owner(4, Ev, Ev.n)

        def _(vector):
            ins = vector.reduce_max(out=mrow[0:1, 4:5], in_=mrow[0:1, 0:4], axis=AX.X)
            t = Ev.inc(ins, 1)
            vector.wait_ge(Ev.h, t)

        ctx.on("vector", _)
        qsm_red = Ev.n

        def _(scalar, qsm_red=qsm_red):
            scalar.wait_ge(Ev.h, qsm_red)
            ins = scalar.activation(out=mrow[0:1, 4:5], in_=mrow[0:1, 4:5], func=AF.Sqrt)
            Es.inc(ins, 1)

        ctx.on("scalar", _)
        qsm_tok = Es.n

        def _(vector, h=h, qsm_tok=qsm_tok, mg=mrow_guard[0]):
            vector.wait_ge(Es.h, qsm_tok)
            vector.wait_ge(Es.h, crow_tok)
            vector.wait_ge(Ev.h, arow_tok)
            vector.wait_ge(Es.h, tneg_tok)
            if mg:
                vector.wait_ge(Et.h, mg)
            ins = vector.tensor_tensor(out=mrow[0:1, 5:6], in0=mrow[0:1, 4:5],
                                       in1=crow[0:1, h:h + 1], op=ALU.mult)
            t = Ev.inc(ins, 1)
            vector.wait_ge(Ev.h, t)
            ins = vector.tensor_tensor(out=mrow[0:1, 5:6], in0=mrow[0:1, 5:6],
                                       in1=nrow[0:1, 32 + h:33 + h], op=ALU.add)
            t = Ev.inc(ins, 1)
            vector.wait_ge(Ev.h, t)
            ins = vector.tensor_tensor(out=mrow[0:1, 5:6], in0=mrow[0:1, 5:6],
                                       in1=tneg[:], op=ALU.mult)
            t = Ev.inc(ins, 1)
            vector.wait_ge(Ev.h, t)

        ctx.on("vector", _)
        mv_tok = Ev.n

        def _(tensor, mv_tok=mv_tok):
            tensor.wait_ge(Ev.h, mv_tok)
            ctx.psum_wait(tensor, 4)
            ins = tensor.matmul(ctx.ps[4][:, 0:1], ctx.ones_rowf[:], mrow[0:1, 5:6],
                                start=True, stop=True)
            Et.inc(ins, 1)

        ctx.on("tensor", _)
        nm_mm = Et.n
        mrow_guard[0] = nm_mm

        def _(scalar, h=h, nm_mm=nm_mm):
            scalar.wait_ge(Et.h, nm_mm)
            ins = scalar.activation(out=negM[:, h:h + 1], in_=ctx.ps[4][:, 0:1],
                                    func=AF.Copy)
            Es.inc(ins, 1)

        ctx.on("scalar", _)
        negM_tok = Es.n
        ctx.psum_set_owner(4, Es, negM_tok)

        for st in range(8):
            def _(tensor, h=h, st=st):
                tensor.wait_ge(Ev.h, cross_tok)
                tensor.wait_ge(crh_guard[0][0].h, crh_guard[0][1])
                ctx.psum_wait(tensor, 5)
                ins = tensor.matmul(ctx.ps[5][0:3, :], ctx.ident[0:48, 3 * h:3 * h + 3],
                                    cross_all[0:48, st * 512:(st + 1) * 512],
                                    start=True, stop=True)
                Et.inc(ins, 1)

            ctx.on("tensor", _)
            ex_mm = Et.n

            def _(scalar, st=st, ex_mm=ex_mm):
                scalar.wait_ge(Et.h, ex_mm)
                ins = scalar.activation(out=cross_h[0:3, st * 512:(st + 1) * 512],
                                        in_=ctx.ps[5][0:3, :], func=AF.Copy)
                Es.inc(ins, 1)

            ctx.on("scalar", _)
            ctx.psum_set_owner(5, Es, Es.n)
        crh_tok = Es.n

        for qt in range(4):
            qsl = slice((h // 2) * T + qt * 512, (h // 2) * T + (qt + 1) * 512)
            pend = [None]

            def emit_av2(stop_flag):
                sb_, kt_, osem, otok = pend[0]
                first_av = (kt_ == 0)

                def _(tensor):
                    tensor.wait_ge(osem.h, otok)
                    tensor.wait_ge(dv.h, vload_tok[0])
                    tensor.wait_ge(Ev.h, vaug_ones)
                    if first_av:
                        ctx.psum_wait(tensor, 2)
                    ins = tensor.matmul(ctx.ps[2][0:65, :],
                                        v_aug[:, kt_ * 65:(kt_ + 1) * 65],
                                        pr[:, sb_ * 512:(sb_ + 1) * 512],
                                        start=first_av, stop=stop_flag)
                    Et.inc(ins, 1)

                ctx.on("tensor", _)
                pr_owner[sb_] = Et.n
                score_mm_last[0] = Et.n

            for kt in range(32):
                sb = kt % 2
                first_av = (kt == 0)

                def _(tensor, sb=sb, kt=kt, h=h, b0=b0, qsl=qsl, first_av=first_av,
                      crh_tok=crh_tok, qstp_tok=qstp_tok):
                    tensor.wait_ge(dk.h, kload_tok[0])
                    tensor.wait_ge(Es.h, crh_tok)
                    tensor.wait_ge(Es.h, qstp_tok)
                    if score_owner[sb]:
                        tensor.wait_ge(score_owner_sem[sb].h, score_owner[sb])
                    if first_av:
                        tensor.wait_ge(Es.h, q_done)
                    tensor.matmul(ctx.ps[sb][:, :],
                                  k_sb[b0:b0 + 64, kt * 128:(kt + 1) * 128],
                                  q_sb[b0:b0 + 64, qsl], start=True, stop=False)
                    ins = tensor.matmul(ctx.ps[sb][:, :],
                                        cross_h[0:3, kt * 128:(kt + 1) * 128],
                                        qstp_h[0:3, (qsl.start - (h // 2) * T):
                                               (qsl.start - (h // 2) * T) + 512],
                                        start=False, stop=True)
                    Et.inc(ins, 1)

                ctx.on("tensor", _)
                mm_tok = Et.n
                score_mm_last[0] = mm_tok

                def _(scalar, sb=sb, h=h, mm_tok=mm_tok, negM_tok=negM_tok):
                    scalar.wait_ge(Et.h, mm_tok)
                    scalar.wait_ge(Es.h, negM_tok)
                    scalar.wait_ge(Es.h, tcol_tok)
                    if pr_owner[sb]:
                        scalar.wait_ge(Et.h, pr_owner[sb])
                    ins = scalar.activation(out=pr[:, sb * 512:(sb + 1) * 512],
                                            in_=ctx.ps[sb][:, :], func=AF.Exp,
                                            scale=temp_col[:],
                                            bias=negM[:, h:h + 1])
                    Es.inc(ins, 1)

                ctx.on("scalar", _)
                score_owner[sb] = Es.n
                score_owner_sem[sb] = Es

                prev = pend[0]
                pend[0] = (sb, kt, Es, Es.n)
                if prev is not None:
                    cur = pend[0]
                    pend[0] = prev
                    emit_av2(False)
                    pend[0] = cur
            emit_av2(True)
            av_done = Et.n

            def _(scalar, av_done=av_done):
                scalar.wait_ge(Et.h, av_done)
                scalar.wait_ge(bc_owner_sem[0].h, bc_owner[0])
                ins = scalar.activation(out=av_tmp[0:65, 0:512], in_=ctx.ps[2][0:65, :],
                                        func=AF.Copy)
                Es.inc(ins, 1)

            ctx.on("scalar", _)
            av_cp = Es.n
            ctx.psum_set_owner(2, Es, av_cp)

            def _(vector, av_cp=av_cp):
                vector.wait_ge(Es.h, av_cp)
                ins = vector.reciprocal(out=rec_sb, in_=av_tmp[64:65, 0:512])
                Ev.inc(ins, 1)
                vector.wait_ge(Ev.h, Ev.n)

            ctx.on("vector", _)
            recip_tok = Ev.n

            def _(tensor, recip_tok=recip_tok):
                tensor.wait_ge(Ev.h, recip_tok)
                tensor.wait_ge(bc_owner_sem[0].h, bc_owner[0])
                ctx.psum_wait(tensor, 3)
                ins = tensor.matmul(ctx.ps[3][0:64, :], ctx.ones_rowf[0:1, 0:64],
                                    rec_sb, start=True, stop=True)
                Et.inc(ins, 1)

            ctx.on("tensor", _)
            bc_mm = Et.n

            def _(vector, bc_mm=bc_mm, qrows=qrows, qsl=qsl):
                vector.wait_ge(Et.h, bc_mm)
                ins = vector.tensor_tensor(out=stage[qrows, qsl],
                                           in0=av_tmp[0:64, 0:512],
                                           in1=ctx.ps[3][0:64, :], op=ALU.mult)
                Ev.inc(ins, 1)
                vector.wait_ge(Ev.h, Ev.n)

            ctx.on("vector", _)
            bc_owner[0] = Ev.n
            bc_owner_sem[0] = Ev
            ctx.psum_set_owner(3, Ev, Ev.n)
        qstp_guard[0] = (Et, score_mm_last[0])
        crh_guard[0] = (Et, score_mm_last[0])

    stage_tok = Ev.n

    tail = bias_tail_mm(ctx, bo_row[0], bo_row[1])
    emit_proj(ctx, WDirect(dr["L2a_WoT"], D, D), sink_resid(ctx, 1.0),
              src=lambda kc: stage[:, kc * T:(kc + 1) * T],
              tensor_dep=[(Ev, stage_tok)], tail_mm=tail)
    return ctx.E["vector"].n



L3F_HID = 2730
L3F_BLOCKS = [(i * 128, min(128, L3F_HID - i * 128)) for i in range(22)]

L3F_INPUT_SPECS = (
    [(f"L3f_wqkv{l}", [1024, 1536], F32) for l in range(2)]
    + [(f"L3f_wo{l}", [512, 1024], F32) for l in range(2)]
    + [(f"L3f_w1{l}", [1024, 2 * L3F_HID], F32) for l in range(2)]
    + [(f"L3f_b1{l}", [1, 2 * L3F_HID], F32) for l in range(2)]
    + [(f"L3f_w2{l}", [L3F_HID, 1024], F32) for l in range(2)]
    + [(f"L3f_b2{l}", [1, 1024], F32) for l in range(2)]
    + [(f"L3f_qsc{l}", [128, 1], F32) for l in range(2)]
    + [(f"L3f_ksc{l}", [128, 1], F32) for l in range(2)]
    + [(f"L3f_qsr{l}", [1, 64], F32) for l in range(2)]
    + [(f"L3f_ksr{l}", [1, 64], F32) for l in range(2)]
    + [(f"L3f_lg{l}", [128, 8], F32) for l in range(2)]
    + [(f"L3f_lb{l}", [128, 8], F32) for l in range(2)]
    + [("L3f_cq", [128, T], BF16), ("L3f_sq", [128, T], BF16),
       ("L3f_ck", [128, T], BF16), ("L3f_sk", [128, T], BF16),
       ("L3f_band", [128, 8 * 512], BF16), ("L3f_m0sel", [128, 1], F32)]
)


def l3f_host_inputs(wqkv, q_scale, k_scale, wo, ln_g, ln_b, w1, b1, w2, b2):
    d = {}
    for l in range(2):
        d[f"L3f_wqkv{l}"] = np.ascontiguousarray(wqkv[l], dtype=np.float32)
        d[f"L3f_wo{l}"] = np.ascontiguousarray(wo[l], dtype=np.float32)
        d[f"L3f_w1{l}"] = np.ascontiguousarray(w1[l], dtype=np.float32)
        d[f"L3f_b1{l}"] = np.asarray(b1[l], np.float32).reshape(1, 2 * L3F_HID)
        d[f"L3f_w2{l}"] = np.ascontiguousarray(w2[l], dtype=np.float32)
        d[f"L3f_b2{l}"] = np.asarray(b2[l], np.float32).reshape(1, 1024)
        d[f"L3f_qsc{l}"] = np.tile(np.asarray(q_scale[l], np.float32), 2).reshape(128, 1)
        d[f"L3f_ksc{l}"] = np.tile(np.asarray(k_scale[l], np.float32), 2).reshape(128, 1)
        d[f"L3f_qsr{l}"] = np.asarray(q_scale[l], np.float32).reshape(1, 64)
        d[f"L3f_ksr{l}"] = np.asarray(k_scale[l], np.float32).reshape(1, 64)
        d[f"L3f_lg{l}"] = np.ascontiguousarray(
            np.asarray(ln_g[l], np.float32).reshape(8, 128).T)
        d[f"L3f_lb{l}"] = np.ascontiguousarray(
            np.asarray(ln_b[l], np.float32).reshape(8, 128).T)
    return d


def l3f_percore_inputs(parity):
    t = np.arange(T, dtype=np.float64) + parity * T
    half_idx = np.arange(0, 64, 2, dtype=np.float64)
    invf = 1.0 / 10000.0 ** (half_idx / 64.0)
    fr = invf[:, None] * t[None, :]
    base = (half_idx + 0.4 * 64.0) / (1.4 * 64.0)
    power = (t - S // 2) / 256.0
    scale = base[:, None] ** power[None, :]
    cos, sin = np.cos(fr), np.sin(fr)

    def dup4(a32):
        a64 = np.concatenate([a32, a32], axis=0)
        return np.concatenate([a64, a64], axis=0).astype(bf16)

    def dup4s(a32):
        a64 = np.concatenate([-a32, a32], axis=0)
        return np.concatenate([a64, a64], axis=0).astype(bf16)

    band = np.zeros((128, 8 * 512), dtype=np.float32)
    row = np.arange(128)[:, None]
    col = np.arange(512)[None, :]
    for r in range(8):
        delta = 512 - r * 128 + col - row
        band[:, r * 512:(r + 1) * 512] = np.where((delta < 0) | (delta > 512),
                                                  -1e6, 0.0)
    return {
        "L3f_cq": dup4(cos * scale), "L3f_sq": dup4s(sin * scale),
        "L3f_ck": dup4(cos / scale), "L3f_sk": dup4s(sin / scale),
        "L3f_band": band.astype(bf16),
        "L3f_m0sel": np.full((128, 1), -1e13 if parity == 0 else 0.0, np.float32),
    }


def emit_l3_ffn(ctx, tag, xn_tok, dbg=None):
    nc = ctx.nc
    Et, Es, Ev = ctx.E["tensor"], ctx.E["scalar"], ctx.E["vector"]
    ds = ctx.dma["sync"]
    dr = ctx.dram

    x3 = ctx.tmp("l3f_x3", [128, NCH * T], BF16)
    bnd = ctx.tmp("l3f_bnd", [128, 4], BF16)
    zero4 = ctx.tmp("l3f_z4", [128, 4], BF16)

    def _(vector):
        vector.wait_ge(Ev.h, xn_tok)
        vector.memset(zero4[:], 0.0)
        for c_ in range(NCH):
            ins = vector.tensor_copy(out=x3[:, c_ * T:(c_ + 1) * T],
                                     in_=ctx.xn_sb[:, c_ * T:(c_ + 1) * T])
            t_ = Ev.inc(ins, 1)
            vector.wait_ge(Ev.h, t_)

    ctx.on("vector", _)
    x3_tok = (Ev, Ev.n)

    def boundary_exchange(src_buf, src_dep, hsh0_ap, copies_tok):
        sh_b = nc.dram_tensor(f"sh_b_{tag}_{ctx.uid()}", [2, 128, 4], BF16,
                              kind="Internal", addr_space="Shared").ap()
        src_view = src_buf[:].rearrange("p (c t) -> p c t", t=T)[:, 4:8, 2047:2048]
        ev = ctx.mbox_send(
            [(lambda slot: sh_b[slot].rearrange("p (c o) -> p c o", o=1), src_view)],
            dep_waits=[src_dep])
        xt = ctx.mbox_wait(ev)
        dE = ctx.dE

        def _(sync):
            sync.wait_ge(ctx.xrdy.h, xt)
            with sync.If_eq(ctx.parity_reg("sync"), 0):
                sync.dma_start(out=bnd[:], in_=zero4[:]).then_inc(dE.h, 16)
            with sync.Else():
                sync.dma_start(out=bnd[:], in_=sh_b[0]).then_inc(dE.h, 16)
            sync.end_ifs()

        ctx.on("sync", _)
        dE.n += 16
        bt = dE.n

        def _(sync):
            sync.wait_ge(dE.h, bt)
            for s_, t_ in [copies_tok]:
                sync.wait_ge(s_.h, t_)
            dst = hsh0_ap.rearrange("p (c t) -> p c t", t=512)[:, :, 0:1]
            ins = sync.dma_start(out=dst, in_=bnd[:].rearrange("p (c o) -> p c o", o=1),
                                 allow_slow_non_contiguous=True)
            dE.inc(ins, 16)

        ctx.on("sync", _)
        return (dE, dE.n)

    def build_hsh0(src_buf, src_dep, hsh0_ap):
        def _(vector):
            vector.wait_ge(src_dep[0].h, src_dep[1])
            for g in range(4):
                ins = vector.tensor_copy(
                    out=hsh0_ap[:, g * 512 + 1:(g + 1) * 512],
                    in_=src_buf[:, (4 + g) * T:(4 + g) * T + 511])
                t_ = Ev.inc(ins, 1)
                vector.wait_ge(Ev.h, t_)

        ctx.on("vector", _)
        cp_tok = (Ev, Ev.n)
        bdep = boundary_exchange(src_buf, src_dep, hsh0_ap, cp_tok)
        return [cp_tok, bdep]

    def mk_shift_src(src_buf, hsh0_ap):
        def st(kc, col0, w):
            if kc < 4:
                return src_buf[:, kc * T + col0: kc * T + col0 + w]
            if col0 >= 512:
                return src_buf[:, kc * T + col0 - 1: kc * T + col0 - 1 + w]
            return hsh0_ap[:, (kc - 4) * 512 + col0:(kc - 4) * 512 + col0 + w]
        return st

    prev_tok = x3_tok
    for l in range(2):
        ph = ExitStack()
        tp = lambda n_, s_, d_: ph.enter_context(
            nc.sbuf_tensor(n_ + str(ctx.uid()), s_, d_))
        q3 = ctx.xn_sb
        v3 = tp("l3f_v3", [128, 16 * 512], BF16)
        tbuf = tp("l3f_tb", [128, 4 * 512], BF16)
        band = tp("l3f_bd", [128, 8 * 512], BF16)
        A = tp("l3f_A", [128, 3712], BF16)
        kprevh = tbuf[:, 0:512]
        vprevh = tbuf[:, 512:768]
        negMt = tp("l3f_nM", [128, 2], F32)
        nsmall = tp("l3f_ns", [1, 8], F32)
        qsc_sb = tp("l3f_qsc", [128, 2], F32)
        hsh0 = A[:, 0:2048]
        nsq = A[:, 0:1024].bitcast(F32)
        nr2 = A[0:2, 1024:2048].bitcast(F32)
        swp = A[:, 2048:2560]
        pr = A[:, 0:1024]
        v_aug = A[:, 1024:1544]
        av_tmp = A[:, 1544:2568].bitcast(F32)
        rec = A[0:1, 2568:3592].bitcast(F32)

        sm = ctx.new_ldsem()

        def _(g, l=l):
            ins = g.dma_start(out=band[:], in_=dr["L3f_band"])
            sm.inc(ins, 16)
            ins = g.dma_start(out=qsc_sb[:, 0:1], in_=dr[f"L3f_qsc{l}"])
            sm.inc(ins, 16)
            ins = g.dma_start(out=qsc_sb[:, 1:2], in_=dr[f"L3f_ksc{l}"])
            sm.inc(ins, 16)

        ctx.on("gpsimd", _)
        sm_tok = sm.n

        smr = ctx.new_ldsem()
        m0sel_sb = tp("l3f_m0", [128, 1], F32)
        qsr_sb = tp("l3f_qsr", [1, 128], F32)

        def _(g, l=l):
            ins = g.dma_start(out=qsr_sb[0:1, 0:64], in_=dr[f"L3f_qsr{l}"])
            smr.inc(ins, 16)
            ins = g.dma_start(out=qsr_sb[0:1, 64:128], in_=dr[f"L3f_ksr{l}"])
            smr.inc(ins, 16)
            ins = g.dma_start(out=m0sel_sb[:], in_=dr["L3f_m0sel"])
            smr.inc(ins, 16)

        ctx.on("gpsimd", _)
        smr_tok = smr.n

        def _(vector):
            vector.wait_ge(smr.h, smr_tok)
            ins = vector.tensor_reduce(out=nsmall[0:1, 0:1].rearrange("p (a o) -> p a o", o=1),
                                       in_=qsr_sb[0:1, 0:64].rearrange("p (a w) -> p a w", a=1),
                                       axis=AX.X, op=ALU.max, apply_absolute_value=True)
            t_ = Ev.inc(ins, 1)
            vector.wait_ge(Ev.h, t_)
            ins = vector.tensor_reduce(out=nsmall[0:1, 1:2].rearrange("p (a o) -> p a o", o=1),
                                       in_=qsr_sb[0:1, 64:128].rearrange("p (a w) -> p a w", a=1),
                                       axis=AX.X, op=ALU.max, apply_absolute_value=True)
            t_ = Ev.inc(ins, 1)
            vector.wait_ge(Ev.h, t_)
            ins = vector.tensor_tensor(out=nsmall[0:1, 2:3], in0=nsmall[0:1, 0:1],
                                       in1=nsmall[0:1, 1:2], op=ALU.mult)
            t_ = Ev.inc(ins, 1)
            vector.wait_ge(Ev.h, t_)
            ins = vector.tensor_scalar(out=nsmall[0:1, 2:3], in0=nsmall[0:1, 2:3],
                                       scalar1=-8.0, scalar2=None, op0=ALU.mult)
            t_ = Ev.inc(ins, 1)
            vector.wait_ge(Ev.h, t_)

        ctx.on("vector", _)
        mval_tok = Ev.n

        def _(tensor, mval_tok=mval_tok):
            tensor.wait_ge(Ev.h, mval_tok)
            ctx.psum_wait(tensor, 4)
            ins = tensor.matmul(ctx.ps[4][:, 0:1], ctx.ones_rowf[:],
                                nsmall[0:1, 2:3], start=True, stop=True)
            Et.inc(ins, 1)

        ctx.on("tensor", _)
        nmm = Et.n

        def _(scalar, nmm=nmm):
            scalar.wait_ge(Et.h, nmm)
            ins = scalar.activation(out=negMt[:, 0:1], in_=ctx.ps[4][:, 0:1],
                                    func=AF.Copy)
            Es.inc(ins, 1)

        ctx.on("scalar", _)
        negM_cp = Es.n
        ctx.psum_set_owner(4, Es, negM_cp)

        def _(vector, negM_cp=negM_cp):
            vector.wait_ge(Es.h, negM_cp)
            vector.wait_ge(smr.h, smr_tok)
            ins = vector.tensor_tensor(out=negMt[:, 1:2], in0=negMt[:, 0:1],
                                       in1=m0sel_sb[:], op=ALU.add)
            t_ = Ev.inc(ins, 1)
            vector.wait_ge(Ev.h, t_)

        ctx.on("vector", _)
        negM_tok = Ev.n

        sdep = build_hsh0(x3, prev_tok, hsh0)
        shsrc = mk_shift_src(x3, hsh0)

        def qk_sink(oc, tt, psum_ap, mm_tok):
            def _(scalar):
                scalar.wait_ge(Et.h, mm_tok)
                ins = scalar.activation(out=ctx.xn_sb[:, oc * T + tt * 512: oc * T + (tt + 1) * 512],
                                        in_=psum_ap, func=AF.Copy)
                Es.inc(ins, 1)

            ctx.on("scalar", _)
            return (Es, Es.n)

        emit_proj(ctx, WDirect(dr[f"L3f_wqkv{l}"], 1024, 1024), qk_sink,
                  tensor_dep=sdep, src_tile=shsrc)
        qk_done = Es.n

        def v_sink(tb, ob_i, psum_ap, mm_tok):
            def _(scalar):
                scalar.wait_ge(Et.h, mm_tok)
                ins = scalar.activation(out=v3[:, tb * 512:(tb + 1) * 512],
                                        in_=psum_ap, func=AF.Copy)
                Es.inc(ins, 1)

            ctx.on("scalar", _)
            return (Es, Es.n)

        emit_proj_tokmaj(ctx, WDirect(dr[f"L3f_wqkv{l}"], 1024, 512, off=1024),
                         v_sink, tensor_dep=sdep, src_tile=shsrc)
        v_done = Es.n
        v_mm_guard = Et.n

        dtb = ctx.new_ldsem()
        rope_done = [0]
        for tt in range(4):
            def _(sync, tt=tt, l=l):
                sync.wait_ge(Es.h, qk_done)
                if rope_done[0]:
                    sync.wait_ge(Ev.h, rope_done[0])
                for i, nm in enumerate(("L3f_cq", "L3f_sq", "L3f_ck", "L3f_sk")):
                    ins = sync.dma_start(out=tbuf[:, i * 512:(i + 1) * 512],
                                         in_=dr[nm][:, tt * 512:(tt + 1) * 512])
                    dtb.inc(ins, 16)

            ctx.on("sync", _)
            tb_tok = dtb.n

            for qk in range(2):
                ctab = tbuf[:, (qk * 2) * 512:(qk * 2 + 1) * 512]
                stab = tbuf[:, (qk * 2 + 1) * 512:(qk * 2 + 2) * 512]
                for c_ in range(4):
                    cc = c_ + 4 * qk
                    tile = ctx.xn_sb[:, cc * T + tt * 512: cc * T + (tt + 1) * 512]

                    def _(scalar, tile=tile):
                        scalar.wait_ge(Es.h, qk_done)
                        scalar.wait_ge(Et.h, v_mm_guard)
                        ins = scalar.activation(out=nsq[:, 0:512], in_=tile,
                                                func=AF.Square)
                        Es.inc(ins, 1)

                    ctx.on("scalar", _)
                    sq_tok = Es.n

                    def _(tensor, sq_tok=sq_tok):
                        tensor.wait_ge(Es.h, sq_tok)
                        ctx.psum_wait(tensor, 4)
                        ins = tensor.matmul(ctx.ps[4][0:2, :], ctx.l3f_ind2[:],
                                            nsq[:, 0:512], start=True, stop=True)
                        Et.inc(ins, 1)

                    ctx.on("tensor", _)
                    nm_mm = Et.n

                    def _(scalar, nm_mm=nm_mm):
                        scalar.wait_ge(Et.h, nm_mm)
                        ins = scalar.activation(out=nr2[0:2, 0:512],
                                                in_=ctx.ps[4][0:2, :], func=AF.Sqrt)
                        Es.inc(ins, 1)

                    ctx.on("scalar", _)
                    nrm_cp = Es.n
                    ctx.psum_set_owner(4, Es, nrm_cp)

                    def _(vector, nrm_cp=nrm_cp):
                        vector.wait_ge(Es.h, nrm_cp)
                        ins = vector.tensor_scalar(out=nr2[0:2, 0:512], in0=nr2[0:2, 0:512],
                                                   scalar1=1e-12, scalar2=None,
                                                   op0=ALU.max)
                        t_ = Ev.inc(ins, 1)
                        vector.wait_ge(Ev.h, t_)
                        ins = vector.reciprocal(out=nr2[0:2, 0:512], in_=nr2[0:2, 0:512])
                        t_ = Ev.inc(ins, 1)
                        vector.wait_ge(Ev.h, t_)

                    ctx.on("vector", _)
                    rcp_tok = Ev.n

                    def _(tensor, rcp_tok=rcp_tok):
                        tensor.wait_ge(Ev.h, rcp_tok)
                        ctx.psum_wait(tensor, 5)
                        ins = tensor.matmul(ctx.ps[5][:, :], ctx.l3f_ind2T[:],
                                            nr2[0:2, 0:512], start=True, stop=True)
                        Et.inc(ins, 1)

                    ctx.on("tensor", _)
                    bc_mm = Et.n

                    def _(vector, tile=tile, qk=qk, bc_mm=bc_mm):
                        vector.wait_ge(Et.h, bc_mm)
                        vector.wait_ge(sm.h, sm_tok)
                        ins = vector.tensor_tensor(out=tile, in0=tile,
                                                   in1=ctx.ps[5][:, :], op=ALU.mult)
                        t_ = Ev.inc(ins, 1)
                        vector.wait_ge(Ev.h, t_)
                        ins = vector.tensor_scalar(out=tile, in0=tile,
                                                   scalar1=qsc_sb[:, qk:qk + 1],
                                                   scalar2=None, op0=ALU.mult)
                        t_ = Ev.inc(ins, 1)
                        vector.wait_ge(Ev.h, t_)

                    ctx.on("vector", _)
                    scl_tok = Ev.n
                    ctx.psum_set_owner(5, Ev, scl_tok)

                    def _(tensor, tile=tile, scl_tok=scl_tok):
                        tensor.wait_ge(Ev.h, scl_tok)
                        ctx.psum_wait(tensor, 4)
                        ins = tensor.matmul(ctx.ps[4][:, :], ctx.permswap_sb[:],
                                            tile, start=True, stop=True)
                        Et.inc(ins, 1)

                    ctx.on("tensor", _)
                    sw_mm = Et.n

                    def _(vector, tile=tile, sw_mm=sw_mm, tb_tok=tb_tok,
                          ctab=ctab, stab=stab):
                        vector.wait_ge(Et.h, sw_mm)
                        vector.wait_ge(dtb.h, tb_tok)
                        ins = vector.tensor_tensor(out=swp[:, 0:512], in0=tile,
                                                   in1=ctab, op=ALU.mult)
                        t_ = Ev.inc(ins, 1)
                        vector.wait_ge(Ev.h, t_)
                        ins = vector.tensor_tensor(out=tile, in0=ctx.ps[4][:, :],
                                                   in1=stab, op=ALU.mult)
                        t_ = Ev.inc(ins, 1)
                        vector.wait_ge(Ev.h, t_)
                        ins = vector.tensor_tensor(out=tile, in0=tile,
                                                   in1=swp[:, 0:512], op=ALU.add)
                        t_ = Ev.inc(ins, 1)
                        vector.wait_ge(Ev.h, t_)

                    ctx.on("vector", _)
                    ctx.psum_set_owner(4, Ev, Ev.n)
            rope_done[0] = Ev.n
        rope_tok = Ev.n
        if dbg is not None and l == 0:
            dsm = ctx.new_ldsem()

            def _(sync):
                sync.wait_ge(Ev.h, rope_tok)
                ins = sync.dma_start(out=dbg["qk"], in_=ctx.xn_sb[:])
                dsm.inc(ins, 16)
                sync.wait_ge(Es.h, v_done)
                ins = sync.dma_start(out=dbg["v3"], in_=v3[:])
                dsm.inc(ins, 16)
                sync.wait_ge(dsm.h, dsm.n)

            ctx.on("sync", _)

        sh_k3 = nc.dram_tensor(f"sh_l3k_{tag}_{l}", [2, 128, 2048], BF16,
                               kind="Internal", addr_space="Shared").ap()
        sh_v3 = nc.dram_tensor(f"sh_l3v_{tag}_{l}", [2, 128, 2048], BF16,
                               kind="Internal", addr_space="Shared").ap()
        k_w3 = ctx.xn_sb[:].rearrange("p (c t) -> p c t", t=T)[:, 4:8, 1536:2048]
        ev_kv = ctx.mbox_send(
            [(lambda slot: sh_k3[slot].rearrange("p (c t) -> p c t", t=512), k_w3),
             (lambda slot: sh_v3[slot], v3[:, 12 * 512:16 * 512])],
            dep_waits=[(Ev, rope_tok), (Es, v_done)])
        kv_rdy = ctx.mbox_wait(ev_kv)

        def _(tensor):
            for b_ in (0, 1, 2, 3):
                ctx.psum_wait(tensor, b_)

        ctx.on("tensor", _)

        def _(vector):
            vector.wait_ge(Ev.h, rope_tok)
            ins = vector.memset(v_aug, 1.0)
            Ev.inc(ins, 1)

        ctx.on("vector", _)
        vaug_ones = Ev.n

        dkv = ctx.new_ldsem()
        score_owner = [0, 0]
        score_owner_sem = [Es, Es]
        pr_owner = [0, 0]
        bc_owner = [0]
        bc_owner_sem = [Es]
        last_av = [0]
        preg = ctx.parity_reg("sync")

        for h in range(8):
            b0 = (h % 2) * 64
            hc = h // 2

            def _(sync, h=h, b0=b0, hc=hc):
                sync.wait_ge(ctx.xrdy.h, kv_rdy)
                if last_av[0]:
                    sync.wait_ge(Et.h, last_av[0])
                with sync.If_eq(preg, 0):
                    sync.dma_start(out=kprevh[b0:b0 + 64, 0:512],
                                   in_=sh_k3[1][b0:b0 + 64, hc * 512:(hc + 1) * 512]).then_inc(dkv.h, 16)
                    sync.dma_start(
                        out=vprevh.rearrange("p (j f) -> p j f", f=64),
                        in_=sh_v3[1].rearrange("p (j f) -> p j f", f=512)[:, :, h * 64:(h + 1) * 64]).then_inc(dkv.h, 16)
                with sync.Else():
                    sync.dma_start(out=kprevh[b0:b0 + 64, 0:512],
                                   in_=sh_k3[0][b0:b0 + 64, hc * 512:(hc + 1) * 512]).then_inc(dkv.h, 16)
                    sync.dma_start(
                        out=vprevh.rearrange("p (j f) -> p j f", f=64),
                        in_=sh_v3[0].rearrange("p (j f) -> p j f", f=512)[:, :, h * 64:(h + 1) * 64]).then_inc(dkv.h, 16)
                sync.end_ifs()

            ctx.on("sync", _)
            dkv.n += 32
            kv_tok = dkv.n

            for w in range(4):
                def _(vector, h=h, w=w, kv_tok=kv_tok):
                    vector.wait_ge(Ev.h, vaug_ones)
                    vector.wait_ge(Es.h, v_done)
                    if w == 0:
                        vector.wait_ge(dkv.h, kv_tok)
                    if last_av[0]:
                        vector.wait_ge(Et.h, last_av[0])
                    for r in range(8):
                        if w == 0 and r < 4:
                            srcv = vprevh[:, r * 64:(r + 1) * 64]
                        else:
                            g = (w - 1) * 4 + r
                            srcv = v3[:, g * 512 + h * 64: g * 512 + h * 64 + 64]
                        ins = vector.tensor_copy(
                            out=v_aug[:, r * 65: r * 65 + 64], in_=srcv)
                        t_ = Ev.inc(ins, 1)
                        vector.wait_ge(Ev.h, t_)

                ctx.on("vector", _)
                vaug_tok = Ev.n

                pend = [None]

                def emit_av3(stop_flag, vaug_tok=vaug_tok):
                    sb_, r_, osem, otok = pend[0]
                    first = (r_ == 0)

                    def _(tensor):
                        tensor.wait_ge(osem.h, otok)
                        tensor.wait_ge(Ev.h, vaug_tok)
                        if first:
                            ctx.psum_wait(tensor, 2)
                        ins = tensor.matmul(ctx.ps[2][0:65, :],
                                            v_aug[:, r_ * 65:(r_ + 1) * 65],
                                            pr[:, sb_ * 512:(sb_ + 1) * 512],
                                            start=first, stop=stop_flag)
                        Et.inc(ins, 1)

                    ctx.on("tensor", _)
                    pr_owner[sb_] = Et.n
                    last_av[0] = Et.n

                for r in range(8):
                    sb = r % 2

                    def _(tensor, h=h, w=w, r=r, sb=sb, b0=b0, hc=hc):
                        tensor.wait_ge(Ev.h, rope_tok)
                        if w == 0 and r < 4:
                            tensor.wait_ge(dkv.h, kv_tok)
                        if score_owner[sb]:
                            tensor.wait_ge(score_owner_sem[sb].h, score_owner[sb])
                        if w == 0 and r == 0:
                            tensor.wait_ge(sm.h, sm_tok)
                        if w == 0 and r < 4:
                            lhs = kprevh[b0:b0 + 64, r * 128:(r + 1) * 128]
                        else:
                            koff = (w - 1) * 512 + r * 128
                            lhs = ctx.xn_sb[b0:b0 + 64, (4 + hc) * T + koff:
                                            (4 + hc) * T + koff + 128]
                        tensor.matmul(ctx.ps[sb][:, :], lhs,
                                      q3[b0:b0 + 64, hc * T + w * 512: hc * T + (w + 1) * 512],
                                      start=True, stop=False)
                        ins = tensor.matmul(ctx.ps[sb][:, :], ctx.ident[:],
                                            band[:, r * 512:(r + 1) * 512],
                                            start=False, stop=True)
                        Et.inc(ins, 1)

                    ctx.on("tensor", _)
                    mm_tok = Et.n

                    def _(scalar, sb=sb, w=w, r=r, mm_tok=mm_tok):
                        scalar.wait_ge(Et.h, mm_tok)
                        scalar.wait_ge(Ev.h, negM_tok)
                        if pr_owner[sb]:
                            scalar.wait_ge(Et.h, pr_owner[sb])
                        bias_col = negMt[:, 1:2] if (w == 0 and r < 4) else negMt[:, 0:1]
                        ins = scalar.activation(out=pr[:, sb * 512:(sb + 1) * 512],
                                                in_=ctx.ps[sb][:, :], func=AF.Exp,
                                                scale=8.0, bias=bias_col)
                        Es.inc(ins, 1)

                    ctx.on("scalar", _)
                    score_owner[sb] = Es.n
                    score_owner_sem[sb] = Es

                    prev = pend[0]
                    pend[0] = (sb, r, Es, Es.n)
                    if prev is not None:
                        cur = pend[0]
                        pend[0] = prev
                        emit_av3(False)
                        pend[0] = cur
                emit_av3(True)
                av_done = Et.n

                def _(scalar, av_done=av_done):
                    scalar.wait_ge(Et.h, av_done)
                    scalar.wait_ge(bc_owner_sem[0].h, bc_owner[0])
                    ins = scalar.activation(out=av_tmp[0:65, 0:512],
                                            in_=ctx.ps[2][0:65, :], func=AF.Copy)
                    Es.inc(ins, 1)

                ctx.on("scalar", _)
                av_cp = Es.n
                ctx.psum_set_owner(2, Es, av_cp)

                def _(vector, av_cp=av_cp):
                    vector.wait_ge(Es.h, av_cp)
                    ins = vector.reciprocal(out=rec[0:1, 0:512],
                                            in_=av_tmp[64:65, 0:512])
                    Ev.inc(ins, 1)
                    vector.wait_ge(Ev.h, Ev.n)

                ctx.on("vector", _)
                recip_tok = Ev.n

                def _(tensor, recip_tok=recip_tok):
                    tensor.wait_ge(Ev.h, recip_tok)
                    tensor.wait_ge(bc_owner_sem[0].h, bc_owner[0])
                    ctx.psum_wait(tensor, 3)
                    ins = tensor.matmul(ctx.ps[3][0:64, :], ctx.ones_rowf[0:1, 0:64],
                                        rec[0:1, 0:512], start=True, stop=True)
                    Et.inc(ins, 1)

                ctx.on("tensor", _)
                bc_mm = Et.n

                def _(vector, h=h, w=w, b0=b0, hc=hc, bc_mm=bc_mm):
                    vector.wait_ge(Et.h, bc_mm)
                    ins = vector.tensor_tensor(
                        out=q3[b0:b0 + 64, hc * T + w * 512: hc * T + (w + 1) * 512],
                        in0=av_tmp[0:64, 0:512], in1=ctx.ps[3][0:64, :], op=ALU.mult)
                    Ev.inc(ins, 1)
                    vector.wait_ge(Ev.h, Ev.n)

                ctx.on("vector", _)
                bc_owner[0] = Ev.n
                bc_owner_sem[0] = Ev
                ctx.psum_set_owner(3, Ev, Ev.n)
        o_tok = Ev.n

        def x3_sink(oc, tt, psum_ap, mm_tok):
            def _(vector):
                vector.wait_ge(Et.h, mm_tok)
                xs = x3[:, oc * T + tt * 512: oc * T + (tt + 1) * 512]
                ins = vector.tensor_tensor(out=xs, in0=psum_ap, in1=xs, op=ALU.add)
                Ev.inc(ins, 1)
                vector.wait_ge(Ev.h, Ev.n)

            ctx.on("vector", _)
            return (Ev, Ev.n)

        emit_proj(ctx, WDirect(dr[f"L3f_wo{l}"], 512, 1024), x3_sink,
                  src=lambda kc: q3[:, kc * T:(kc + 1) * T],
                  tensor_dep=[(Ev, o_tok)])
        x3a_tok = (Ev, Ev.n)
        if dbg is not None and l == 0:
            dsm2 = ctx.new_ldsem()

            def _(sync):
                sync.wait_ge(Ev.h, o_tok)
                ins = sync.dma_start(out=dbg["o"], in_=ctx.xn_sb[:, 0:4 * T])
                dsm2.inc(ins, 16)
                sync.wait_ge(x3a_tok[0].h, x3a_tok[1])
                ins = sync.dma_start(out=dbg["x3a"], in_=x3[:])
                dsm2.inc(ins, 16)
                sync.wait_ge(dsm2.h, dsm2.n)

            ctx.on("sync", _)
        if dbg is not None and dbg.get("stop_l0a"):
            ph.close()
            ctx.barrier()
            return Ev.n

        ph.close()
        ctx.barrier()

        pf = ExitStack()
        tpf = lambda n_, s_, d_: pf.enter_context(
            nc.sbuf_tensor(n_ + str(ctx.uid()), s_, d_))
        pf1 = ExitStack()
        tpf1 = lambda n_, s_, d_: pf1.enter_context(
            nc.sbuf_tensor(n_ + str(ctx.uid()), s_, d_))
        hsh0f = tpf1("l3f_h0f", [128, 4 * 512], BF16)
        lgb = tpf1("l3f_lgb", [128, 16], F32)
        slg = ctx.new_ldsem()

        def _(g, l=l):
            ins = g.dma_start(out=lgb[:, 0:8], in_=dr[f"L3f_lg{l}"])
            slg.inc(ins, 16)
            ins = g.dma_start(out=lgb[:, 8:16], in_=dr[f"L3f_lb{l}"])
            slg.inc(ins, 16)

        ctx.on("gpsimd", _)
        slg_tok = slg.n

        sdep2 = build_hsh0(x3, x3a_tok, hsh0f)
        sh2 = mk_shift_src(x3, hsh0f)

        def _(eng):
            for s_, t_ in sdep2:
                eng.wait_ge(s_.h, t_)

        for e_ in ("tensor", "scalar", "vector"):
            ctx.on(e_, _)
        ln_tok = emit_layernorm(ctx, src_tile=sh2, src_bf16=True,
                                gain_row=(lgb[:, 0:8], (slg, slg_tok)),
                                beta_row=(lgb[:, 8:16], (slg, slg_tok)))
        pf1.close()

        h3 = tpf("l3f_h3", [128, 22 * 1024], BF16)
        b1row = tpf("l3f_b1", [1, 2 * L3F_HID], BF16)
        b2row = tpf("l3f_b2", [1, 1024], BF16)
        gsb = tpf("l3f_g", [128, 512], BF16)
        sb1 = ctx.new_ldsem()

        def _(g, l=l):
            ins = g.dma_start(out=b1row[:], in_=dr[f"L3f_b1{l}"])
            sb1.inc(ins, 16)
            ins = g.dma_start(out=b2row[:], in_=dr[f"L3f_b2{l}"])
            sb1.inc(ins, 16)

        ctx.on("gpsimd", _)
        sb1_tok = sb1.n

        w1_ap = dr[f"L3f_w1{l}"]
        w2_ap = dr[f"L3f_w2{l}"]
        x3f_tok = None
        for half in range(2):
            tthalf = [half * 2, half * 2 + 1]
            gelu_guard = [0]
            for bi_i, (nb0, wk) in enumerate(L3F_BLOCKS):
                for role in range(2):
                    col0 = nb0 + role * L3F_HID
                    bi = (bi_i * 2 + role) % 2
                    dsem = ctx.dwb[bi]
                    prev = ctx.wbuf_state[bi]

                    def _(g, col0=col0, wk=wk, bi=bi, prev=prev, dsem=dsem):
                        if prev is not None:
                            g.wait_ge(prev[0].h, prev[1])
                        src = w1_ap[:, col0:col0 + wk].rearrange("(c p) n -> p c n", p=128)
                        dst = ctx.wbuf[bi][:, 0:8 * wk].rearrange("p (c n) -> p c n", n=wk)
                        ins = g.dma_start(out=dst, in_=src)
                        dsem.inc(ins, 16)

                    ctx.on("gpsimd", _)
                    ld_tok = dsem.n

                    for ti, tt in enumerate(tthalf):
                        bank = ctx.next_bank()

                        def _(tensor, bi=bi, wk=wk, tt=tt, col0=col0, ld_tok=ld_tok,
                              dsem=dsem):
                            tensor.wait_ge(dsem.h, ld_tok)
                            tensor.wait_ge(Ev.h, ln_tok)
                            tensor.wait_ge(sb1.h, sb1_tok)
                            ctx.psum_wait(tensor, bank)
                            for kc in range(8):
                                tensor.matmul(
                                    ctx.ps[bank][0:wk, :],
                                    ctx.wbuf[bi][:, kc * wk:(kc + 1) * wk],
                                    ctx.xn_sb[:, kc * T + tt * 512: kc * T + (tt + 1) * 512],
                                    start=(kc == 0), stop=False)
                            ins = tensor.matmul(ctx.ps[bank][0:wk, :],
                                                b1row[0:1, col0:col0 + wk],
                                                ctx.ones_row_t[0:1, 0:512],
                                                start=False, stop=True)
                            Et.inc(ins, 1)

                        ctx.on("tensor", _)
                        mm_tok = Et.n
                        ctx.wbuf_state[bi] = (Et, mm_tok)

                        if role == 0:
                            def _(scalar, bi_i=bi_i, ti=ti, wk=wk, mm_tok=mm_tok):
                                scalar.wait_ge(Et.h, mm_tok)
                                ins = scalar.activation(
                                    out=h3[0:wk, bi_i * 1024 + ti * 512: bi_i * 1024 + (ti + 1) * 512],
                                    in_=ctx.ps[bank][0:wk, :], func=AF.Copy)
                                Es.inc(ins, 1)

                            ctx.on("scalar", _)
                            ctx.psum_set_owner(bank, Es, Es.n)
                        else:
                            def _(scalar, wk=wk, mm_tok=mm_tok, gg=gelu_guard[0]):
                                scalar.wait_ge(Et.h, mm_tok)
                                if gg:
                                    scalar.wait_ge(Ev.h, gg)
                                ins = scalar.activation(out=gsb[0:wk, :],
                                                        in_=ctx.ps[bank][0:wk, :],
                                                        func=AF.Gelu)
                                Es.inc(ins, 1)

                            ctx.on("scalar", _)
                            g_tok = Es.n
                            ctx.psum_set_owner(bank, Es, g_tok)

                            def _(vector, bi_i=bi_i, ti=ti, wk=wk, g_tok=g_tok):
                                vector.wait_ge(Es.h, g_tok)
                                hsl = h3[0:wk, bi_i * 1024 + ti * 512: bi_i * 1024 + (ti + 1) * 512]
                                ins = vector.tensor_tensor(out=hsl, in0=hsl,
                                                           in1=gsb[0:wk, :], op=ALU.mult)
                                t_ = Ev.inc(ins, 1)
                                vector.wait_ge(Ev.h, t_)

                            ctx.on("vector", _)
                            gelu_guard[0] = Ev.n
            h3_tok = Ev.n

            for dg in range(4):
                def _(tensor):
                    for b_ in (0, 1, 2, 3):
                        ctx.psum_wait(tensor, b_)

                ctx.on("tensor", _)
                for kci, (nb0, wk) in enumerate(L3F_BLOCKS):
                    bi = kci % 2
                    dsem = ctx.dwb[bi]
                    prev = ctx.wbuf_state[bi]

                    def _(g, nb0=nb0, wk=wk, dg=dg, bi=bi, prev=prev, dsem=dsem):
                        if prev is not None:
                            g.wait_ge(prev[0].h, prev[1])
                        ins = g.dma_start(out=ctx.wbuf[bi][0:wk, 0:256],
                                          in_=w2_ap[nb0:nb0 + wk, dg * 256:(dg + 1) * 256])
                        dsem.inc(ins, 16)

                    ctx.on("gpsimd", _)
                    ld_tok = dsem.n

                    def _(tensor, kci=kci, wk=wk, bi=bi, dg=dg, ld_tok=ld_tok,
                          dsem=dsem, h3_tok=h3_tok):
                        tensor.wait_ge(dsem.h, ld_tok)
                        tensor.wait_ge(Ev.h, h3_tok)
                        if kci == 21:
                            tensor.wait_ge(sb1.h, sb1_tok)
                        ins = None
                        for dd in range(2):
                            for ti in range(2):
                                bank = dd * 2 + ti
                                ins = tensor.matmul(
                                    ctx.ps[bank][:, :],
                                    ctx.wbuf[bi][0:wk, dd * 128:(dd + 1) * 128],
                                    h3[0:wk, kci * 1024 + ti * 512: kci * 1024 + (ti + 1) * 512],
                                    start=(kci == 0), stop=False)
                                if kci == 21:
                                    dcol = dg * 256 + dd * 128
                                    ins = tensor.matmul(ctx.ps[bank][:, :],
                                                        b2row[0:1, dcol:dcol + 128],
                                                        ctx.ones_row_t[0:1, 0:512],
                                                        start=False, stop=True)
                        Et.inc(ins, 1)

                    ctx.on("tensor", _)
                    ctx.wbuf_state[bi] = (Et, Et.n)
                w2_mm = Et.n

                for dd in range(2):
                    for ti in range(2):
                        bank = dd * 2 + ti
                        oc = (dg * 256 + dd * 128) // 128
                        tt = half * 2 + ti

                        def _(vector, bank=bank, oc=oc, tt=tt, w2_mm=w2_mm):
                            vector.wait_ge(Et.h, w2_mm)
                            xs = x3[:, oc * T + tt * 512: oc * T + (tt + 1) * 512]
                            ins = vector.tensor_tensor(out=xs, in0=ctx.ps[bank][:, :],
                                                       in1=xs, op=ALU.add)
                            Ev.inc(ins, 1)
                            vector.wait_ge(Ev.h, Ev.n)

                        ctx.on("vector", _)
                        ctx.psum_set_owner(bank, Ev, Ev.n)
            x3f_tok = (Ev, Ev.n)

        prev_tok = x3f_tok
        pf.close()
        ctx.barrier()

    def _(vector):
        vector.wait_ge(prev_tok[0].h, prev_tok[1])
        for c_ in range(NCH):
            for tt in range(4):
                xs = ctx.x_sb[:, c_ * T + tt * 512: c_ * T + (tt + 1) * 512]
                ins = vector.scalar_tensor_tensor(
                    out=xs, in0=x3[:, c_ * T + tt * 512: c_ * T + (tt + 1) * 512],
                    scalar=0.5, in1=xs, op0=ALU.mult, op1=ALU.add)
                t_ = Ev.inc(ins, 1)
                vector.wait_ge(Ev.h, t_)

    ctx.on("vector", _)
    return Ev.n



L0A_INPUT_SPECS = [
    ("L0a_Wq", [D, D], F32), ("L0a_Wk", [D, D], F32),
    ("L0a_Wv", [D, D], F32), ("L0a_Wo", [D, D], F32), ("L0a_bo", [D], F32),
]
L1F_INPUT_SPECS = [
    ("L1f_wr", [D], F32), ("L1f_W1", [D, 4096], F32), ("L1f_b1", [4096], F32),
    ("L1f_W2", [4096, D], F32), ("L1f_b2", [D], F32),
]
L2F_INPUT_SPECS = [
    ("L2f_w_gate", [2816, D], F32), ("L2f_w_up", [2816, D], F32),
    ("L2f_w_down", [D, 2816], F32),
]
PERCORE_L1A_SPECS = [("c_mypos", [1, T], F32), ("c_distR", [1, 8448], F32)]
PERCORE_L3A_SPECS = [("c_l3a_c32", [32, T], BF16), ("c_l3a_s32", [32, T], BF16),
                     ("c_stepE", [1, 8448], BF16)]

STEP_DRAM = {
    "l0f": L0F_INPUT_SPECS + [("c_ind_er", [8, 128], BF16)],
    "l0a": L0A_INPUT_SPECS,
    "l1f": L1F_INPUT_SPECS,
    "l1a": L1A_INPUT_SPECS + PERCORE_L1A_SPECS,
    "l2f": L2F_INPUT_SPECS,
    "l2a": L2A_INPUT_SPECS,
    "l3f": L3F_INPUT_SPECS,
    "l3a": L3A_INPUT_SPECS + PERCORE_L3A_SPECS,
}

MODULE_STEPS = ["l0f", "l0a", "l0f", "l1f", "l1a", "l1f",
                "l2f", "l2a", "l2f", "l3f", "l3a", "l3f"]
FUSE = True


def emit_module_steps(ctx, kinds):
    l2f_scr = [None]
    last = None
    counts = {}
    for kind in kinds:
        i = counts.get(kind, 0)
        counts[kind] = i + 1
        ctx.sublayer_scope()
        if kind == "l0f":
            last = emit_l0_ffn(ctx, emit_layernorm(ctx))
        elif kind == "l0a":
            bo_row = emit_bias_row_load(ctx, ctx.dram["L0a_bo"], D)
            last = emit_attn_dense(ctx, f"l0a{i}", emit_layernorm(ctx),
                                   WDirect(ctx.dram["L0a_Wq"], D, D),
                                   WDirect(ctx.dram["L0a_Wk"], D, D),
                                   WDirect(ctx.dram["L0a_Wv"], D, D),
                                   WDirect(ctx.dram["L0a_Wo"], D, D),
                                   exp_scale=0.125, bo_row=bo_row)
        elif kind == "l1f":
            last = emit_l1_ffn(ctx, emit_layernorm(ctx), f"l1f{i}")
        elif kind == "l1a":
            last = emit_l1_attn(ctx, emit_layernorm(ctx))
        elif kind == "l2f":
            if l2f_scr[0] is None:
                l2f_scr[0] = emit_l2f_weight_prep(ctx)
            wg, wu, wd, wdep = l2f_scr[0]
            last = emit_l2_ffn(ctx, [], wg, wu, wd, wdep, f"l2f{i}")
        elif kind == "l2a":
            last = emit_l2_attn(ctx, f"l2a{i}", emit_layernorm(ctx))
        elif kind == "l3f":
            last = emit_l3_ffn(ctx, f"l3f{i}", emit_layernorm(ctx))
        elif kind == "l3a":
            last = emit_l3_attn(ctx, emit_layernorm(ctx))
        else:
            raise ValueError(kind)
    return last


def build_program(kinds):
    nc = bass.Bass("TRN2", target_bir_lowering=False, debug=False)
    ctx = Ctx(nc)
    ctx.const_kinds = set(kinds)
    ctx.dram["x_in"] = nc.dram_tensor("x_in", [128, NCH * T], F32,
                                      kind="ExternalInput").ap()
    ctx.dram["x_out"] = nc.dram_tensor("x_out", [128, NCH * T], F32,
                                       kind="ExternalOutput").ap()
    names = set()
    for kind in dict.fromkeys(kinds):
        for name, shape, dt in STEP_DRAM[kind]:
            if name in names:
                continue
            names.add(name)
            ctx.dram[name] = nc.dram_tensor(name, list(shape), dt,
                                            kind="ExternalInput").ap()
    setup_const_inputs(ctx)
    setup_shared_flags(ctx)
    with ctx.stack:
        alloc_core_buffers(ctx)
        ctx.setup_sems()
        ctx.block = ctx.e(nc.Block())
        xtok = emit_setup(ctx, extra_loads=[(ctx.x_sb[:], ctx.dram["x_in"])])
        dsem = ctx.dma["sync"]
        for e_ in ("tensor", "scalar", "vector"):
            def _(eng):
                eng.wait_ge(ctx.dconst.h, xtok)
            ctx.on(e_, _)
        res_tok = emit_module_steps(ctx, kinds)

        def _(sync):
            sync.wait_ge(ctx.E["vector"].h, res_tok)
            ins = sync.dma_start(out=ctx.dram["x_out"], in_=ctx.x_sb[:])
            dsem.inc(ins, 16)
            sync.wait_ge(dsem.h, dsem.n)

        ctx.on("sync", _)
        ctx.close_scopes()
    return nc, ctx, names


def module_host_inputs(inputs):
    f32 = lambda a: np.ascontiguousarray(np.asarray(a, dtype=np.float32))
    hin = dict(l0_ffn_host_inputs(inputs))
    for nm in ("Wq", "Wk", "Wv", "Wo", "bo"):
        hin["L0a_" + nm] = f32(inputs["L0_attn_" + nm])
    hin.update(l1_attn_host_inputs(inputs))
    hin["L1f_wr"] = f32(inputs["L1_ffn_w_router"])
    hin["L1f_W1"] = f32(inputs["L1_ffn_W1"])
    hin["L1f_b1"] = f32(inputs["L1_ffn_b1"])
    hin["L1f_W2"] = f32(inputs["L1_ffn_W2"])
    hin["L1f_b2"] = f32(inputs["L1_ffn_b2"])
    hin["L2f_w_gate"] = f32(inputs["L2_ffn_w_gate"])
    hin["L2f_w_up"] = f32(inputs["L2_ffn_w_up"])
    hin["L2f_w_down"] = f32(inputs["L2_ffn_w_down"])
    hin.update(l2a_host_inputs(
        f32(inputs["L2_attn_Wq"]), f32(inputs["L2_attn_bq"]),
        f32(inputs["L2_attn_Wk"]), f32(inputs["L2_attn_bk"]),
        f32(inputs["L2_attn_Wv"]), f32(inputs["L2_attn_bv"]),
        f32(inputs["L2_attn_Wo"]), f32(inputs["L2_attn_bo"]),
        f32(inputs["L2_attn_Wqs"]), f32(inputs["L2_attn_bqs"]),
        f32(inputs["L2_attn_Wks"]), f32(inputs["L2_attn_bks"]),
        f32(inputs["L2_attn_Wvs"]), f32(inputs["L2_attn_bvs"]),
        np.asarray(inputs["L2_attn_temp"], dtype=np.float32)))
    for nm in ("wq", "wk", "wv", "wo"):
        hin["L3a_" + nm] = f32(inputs["L3_attn_" + nm])
    hin.update(l3f_host_inputs(
        f32(inputs["L3_ffn_wqkv"]), f32(inputs["L3_ffn_q_scale"]),
        f32(inputs["L3_ffn_k_scale"]), f32(inputs["L3_ffn_wo"]),
        f32(inputs["L3_ffn_ln_g"]), f32(inputs["L3_ffn_ln_b"]),
        f32(inputs["L3_ffn_w1"]), f32(inputs["L3_ffn_b1"]),
        f32(inputs["L3_ffn_w2"]), f32(inputs["L3_ffn_b2"])))
    return hin


def module_percore_inputs(core):
    m = {}
    m.update(percore_consts(core))
    m.update(l3a_percore_consts(core))
    m.update(l3f_percore_inputs(core % 2))
    m["c_ind_er"] = ind_er_const()
    return m


def kernel(**inputs):
    x = np.asarray(inputs["L0_attn_x"], dtype=np.float32)
    hin = module_host_inputs(inputs)
    percore = [module_percore_inputs(c) for c in range(8)]
    xs = [to_feature_major(x[c >> 1, (c & 1) * T:(c & 1) * T + T])
          for c in range(8)]
    groups = [MODULE_STEPS] if FUSE else [[k] for k in MODULE_STEPS]
    progs = {}
    for g in groups:
        key = tuple(g)
        if key not in progs:
            progs[key] = build_program(list(g))
    for g in groups:
        nc, ctx, names = progs[tuple(g)]
        in_maps = []
        for c in range(8):
            m = {k: v for k, v in ctx.consts.items()}
            m["c_slot"] = percore[c]["c_slot"]
            for k in names:
                m[k] = hin[k] if k in hin else percore[c][k]
            m["x_in"] = xs[c]
            in_maps.append(m)
        r = run_bass_kernel_spmd(nc, in_maps, core_ids=list(range(8)))
        xs = [np.asarray(r.results[c]["x_out"]) for c in range(8)]
    out = np.zeros((4, S, D), dtype=np.float32)
    for c in range(8):
        h = (c & 1) * T
        out[c >> 1, h:h + T] = from_feature_major(xs[c])
    return out


# version 09 (final)
# speedup vs baseline: 1.6047x; 1.6047x over previous
import math
import numpy as np
import ml_dtypes
from contextlib import ExitStack

import concourse.bass as bass
import concourse.mybir as mybir
from concourse.bass_utils import run_bass_kernel_spmd

F32 = mybir.dt.float32
BF16 = mybir.dt.bfloat16
I32 = mybir.dt.int32
AF = mybir.ActivationFunctionType
ALU = mybir.AluOpType
AX = mybir.AxisListType

T = 2048
D = 1024
NCH = D // 128
S = 4096
BATCH = 4
MAGIC0 = 0x5A000000
LN_EPS = 1e-5

bf16 = ml_dtypes.bfloat16
ENGINES = ("sync", "tensor", "scalar", "vector", "gpsimd")


class Sem:
    def __init__(self, ctx, name):
        self.h = ctx.e(ctx.nc.semaphore(name))
        self.n = 0

    def inc(self, ins, k=1):
        ins.then_inc(self.h, k)
        self.n += k
        return self.n


class Ctx:
    def __init__(self, nc):
        self.nc = nc
        self.stack = ExitStack()
        self.e = self.stack.enter_context
        self.block = None
        self.dram = {}
        self.consts = {}
        self.n_events = 0
        self._uid = 0
        self._bank_rr = 0
        self.psum_owner = {}
        self.parity_regs = {}
        self.poll_regs = None
        self.sub = None

    def setup_sems(self):
        self.E = {e_: Sem(self, "s_" + e_) for e_ in ENGINES}
        self.dma = {e_: Sem(self, "d_" + e_) for e_ in ("sync", "scalar", "gpsimd")}
        self.psem = Sem(self, "s_poll")
        self.xrdy = Sem(self, "s_xrdy")
        self.dwb = [Sem(self, "d_wb0"), Sem(self, "d_wb1")]
        self.dk = Sem(self, "d_k")
        self.dv = Sem(self, "d_v")
        self.rsem = [Sem(self, f"d_ring{i}") for i in range(8)]
        self.ldpool = [Sem(self, f"d_ld{i}") for i in range(16)]
        self.dconst = Sem(self, "d_const")
        self.dE = Sem(self, "d_E")
        self._ldrr = 0
        self._ld_used = 0

    def all_dma_sems(self):
        return ([*self.dma.values(), self.psem, *self.dwb, self.dk, self.dv,
                 *self.rsem, *self.ldpool, self.dconst, self.dE])

    def new_ldsem(self):
        assert self._ld_used < 16, "more than 16 load groups in one sublayer"
        s_ = self.ldpool[self._ldrr % 16]
        self._ldrr += 1
        self._ld_used += 1
        return s_

    def sublayer_scope(self):
        if self.sub is not None:
            self.sub.close()
            self.barrier()
        self._ld_used = 0
        self.sub = ExitStack()

    def tmp(self, name, shape, dtype):
        return self.sub.enter_context(self.nc.sbuf_tensor(name + str(self.uid()), shape, dtype))

    def pers(self, name, shape, dtype):
        return self.e(self.nc.sbuf_tensor(name + str(self.uid()), shape, dtype,
                                          side="right"))

    def close_scopes(self):
        if getattr(self, "sub", None) is not None:
            self.sub.close()
            self.sub = None

    def uid(self):
        self._uid += 1
        return self._uid

    def new_event(self):
        i = self.n_events
        self.n_events += 1
        assert i < 64
        return i

    def on(self, engine_name, fn):
        getattr(self.block, engine_name)(fn)

    def fconst(self, value, p=128):
        key = (float(value), p)
        if not hasattr(self, "_fconsts"):
            self._fconsts = {}
        if key not in self._fconsts:
            t = self.e(self.nc.sbuf_tensor(f"fc{self.uid()}", [p, 1], F32, side="right"))
            Ev = self.E["vector"]

            def _(vector):
                ins = vector.memset(t[:], float(value))
                Ev.inc(ins, 1)

            self.on("vector", _)
            self._fconsts[key] = (t, (Ev, Ev.n))
        t, dep = self._fconsts[key]
        return t[:], dep

    def next_bank(self):
        b = self._bank_rr % 4
        self._bank_rr += 1
        return b

    def psum_wait(self, tensor_eng, bank_key):
        own = self.psum_owner.get(bank_key)
        if own is not None:
            tensor_eng.wait_ge(own[0].h, own[1])

    def psum_set_owner(self, bank_key, sem, tok):
        self.psum_owner[bank_key] = (sem, tok)

    def barrier(self):
        toks = {}
        dtoks = [(s_, s_.n) for s_ in self.all_dma_sems()]
        for e_ in ENGINES:
            def _(eng, e_=e_):
                toks[e_] = self.E[e_].inc(eng.drain(), 1)
            self.on(e_, _)
        for e_ in ENGINES:
            def _(eng):
                for o in ENGINES:
                    eng.wait_ge(self.E[o].h, toks[o])
                for s_, t_ in dtoks:
                    if t_:
                        eng.wait_ge(s_.h, t_)
            self.on(e_, _)

    def _sync_regs(self):
        if self.poll_regs is None:
            regs = {}

            def _(sync):
                regs["parity"] = sync.alloc_register("r_parity")
                sync.wait_ge(self.dconst.h, self.ld_const_tok)
                sync.reg_load(regs["parity"], self.slot_sb[:])
                regs["flag"] = sync.alloc_register("r_flag")
                regs["diff"] = sync.alloc_register("r_diff")
                regs["pc"] = sync.alloc_register("r_pc")
                sync.reg_mov(regs["pc"], 0)

            self.on("sync", _)
            self.poll_regs = regs
        return self.poll_regs

    def parity_reg(self, eng_name):
        if eng_name == "sync":
            return self._sync_regs()["parity"]
        if eng_name not in self.parity_regs:
            holder = {}

            def _(eng):
                holder["r"] = eng.alloc_register("r_par_" + eng_name)
                eng.wait_ge(self.dconst.h, self.ld_const_tok)
                eng.reg_load(holder["r"], self.slot_sb[:])

            self.on(eng_name, _)
            self.parity_regs[eng_name] = holder["r"]
        return self.parity_regs[eng_name]

    def by_parity(self, eng_name, emit_fn):
        preg = self.parity_reg(eng_name)

        def _(eng):
            with eng.If_eq(preg, 0):
                emit_fn(eng, 0)
            with eng.Else():
                emit_fn(eng, 1)
            eng.end_ifs()

        self.on(eng_name, _)

    def mbox_flag(self, ev, dep_waits=()):
        self._sync_regs()
        dsem = self.dma["sync"]

        def _(sync):
            for s_, t_ in dep_waits:
                sync.wait_ge(s_.h, t_)

        self.on("sync", _)

        def emit_flag(sync, slot):
            sync.dma_start(out=self.shflag[slot, :, ev:ev + 1],
                           in_=self.magic_sb[:, ev:ev + 1]).then_inc(dsem.h, 16)

        self.by_parity("sync", emit_flag)
        dsem.n += 16
        return dsem.n

    def mbox_send(self, writes, dep_waits=()):
        ev = self.new_event()
        self._sync_regs()
        dsem = self.dma["sync"]

        def _(sync):
            for s_, t_ in dep_waits:
                sync.wait_ge(s_.h, t_)

        self.on("sync", _)

        def emit_writes(sync, slot):
            for dst_by_slot, src in writes:
                sync.dma_start(out=dst_by_slot(slot), in_=src,
                               allow_slow_non_contiguous=True).then_inc(dsem.h, 16)

        self.by_parity("sync", emit_writes)
        dsem.n += 16 * len(writes)
        data_tok = dsem.n

        def _(sync):
            sync.wait_ge(dsem.h, data_tok)

        self.on("sync", _)

        def emit_flag(sync, slot):
            sync.dma_start(out=self.shflag[slot, :, ev:ev + 1],
                           in_=self.magic_sb[:, ev:ev + 1]).then_inc(dsem.h, 16)

        self.by_parity("sync", emit_flag)
        dsem.n += 16
        return ev

    def mbox_wait(self, ev):
        regs = self._sync_regs()

        def _(sync):
            sync.store(self.flag_sb[:], 0)
            sync.reg_mov(regs["diff"], 1)
            with sync.While(regs["diff"]):
                with sync.If_eq(regs["parity"], 0):
                    sync.dma_start(out=self.flag_sb[:],
                                   in_=self.shflag[1, :, ev:ev + 1]).then_inc(self.psem.h, 16)
                with sync.Else():
                    sync.dma_start(out=self.flag_sb[:],
                                   in_=self.shflag[0, :, ev:ev + 1]).then_inc(self.psem.h, 16)
                sync.end_ifs()
                sync.reg_add(regs["pc"], regs["pc"], 16)
                sync.wait_ge(self.psem.h, regs["pc"])
                sync.reg_load(regs["flag"], self.flag_sb[:])
                sync.reg_sub(regs["diff"], regs["flag"], MAGIC0 + ev)
            ins = sync.drain()
            self.xrdy.inc(ins, 1)

        self.on("sync", _)
        return self.xrdy.n


def setup_shared_flags(ctx):
    ctx.shflag = ctx.nc.dram_tensor("shflag", [2, 1, 64], I32, kind="Internal",
                                    addr_space="Shared").ap()


def ceil_div(a, b):
    return (a + b - 1) // b



def alloc_core_buffers(ctx):
    nc, e = ctx.nc, ctx.e
    ctx.x_sb = e(nc.sbuf_tensor("x_sb", [128, NCH * T], F32))
    ctx.xn_sb = e(nc.sbuf_tensor("xn_sb", [128, NCH * T], BF16))
    ctx.ones_row = e(nc.sbuf_tensor("ones_row", [1, 128], BF16))
    ctx.ones_row_t = e(nc.sbuf_tensor("ones_row_t", [1, 512], BF16))
    ctx.ones_rowf = e(nc.sbuf_tensor("ones_rowf", [1, 128], F32))
    ctx.ones_colf = e(nc.sbuf_tensor("ones_colf", [128, 1], F32))
    ctx.ones_colb = e(nc.sbuf_tensor("ones_colb", [128, 1], BF16))
    ctx.ident = e(nc.sbuf_tensor("ident", [128, 128], BF16))
    ck = getattr(ctx, "const_kinds", {"l2f", "l2a", "l3f"})
    if "l2f" in ck:
        ctx.identf = e(nc.sbuf_tensor("identf", [128, 128], F32))
    if "l2a" in ck:
        ctx.l2rot1 = e(nc.sbuf_tensor("l2rot1", [48, 48], BF16))
        ctx.l2rot2 = e(nc.sbuf_tensor("l2rot2", [48, 48], BF16))
        ctx.l2tri = e(nc.sbuf_tensor("l2tri", [48, 16], BF16))
    if "l3f" in ck:
        ctx.l3f_ind2 = e(nc.sbuf_tensor("l3f_ind2", [128, 2], F32))
        ctx.l3f_ind2T = e(nc.sbuf_tensor("l3f_ind2T", [2, 128], F32))
    ctx.magic_sb = e(nc.sbuf_tensor("magic_sb", [1, 64], I32))
    ctx.ind16_sb = e(nc.sbuf_tensor("ind16_sb", [128, 128], F32))
    ctx.permswap_sb = e(nc.sbuf_tensor("permswap_sb", [128, 128], BF16))
    ctx.inddup_sb = e(nc.sbuf_tensor("inddup_sb", [32, 128], BF16))
    ctx.inddupsgn_sb = e(nc.sbuf_tensor("inddupsgn_sb", [32, 128], BF16))
    ctx.slot_sb = e(nc.sbuf_tensor("slot_sb", [1, 1], I32))
    ctx.flag_sb = e(nc.sbuf_tensor("flag_sb", [1, 1], I32))
    ctx.ps = [e(nc.psum_tensor(f"ps{i}", [128, 512], F32)) for i in range(6)]
    ctx.psb = e(nc.psum_tensor("psb", [128, 1024], BF16))
    ctx.wbuf = [e(nc.sbuf_tensor(f"wbuf{i}", [128, 4096], BF16)) for i in range(2)]
    ctx.wbuf_state = [None, None]


def setup_const_inputs(ctx):
    ctx.consts["c_ones_row"] = np.ones((1, 128), dtype=bf16)
    ctx.consts["c_ones_row_t"] = np.ones((1, 512), dtype=bf16)
    ctx.consts["c_ones_rowf"] = np.ones((1, 128), dtype=np.float32)
    ctx.consts["c_ones_colf"] = np.ones((128, 1), dtype=np.float32)
    ctx.consts["c_ones_colb"] = np.ones((128, 1), dtype=bf16)
    ctx.consts["c_ident"] = np.eye(128, dtype=bf16)
    ck = getattr(ctx, "const_kinds", {"l2f", "l2a", "l3f"})
    if "l2f" in ck:
        ctx.consts["c_identf"] = np.eye(128, dtype=np.float32)
    if "l2a" in ck:
        rot1 = np.zeros((48, 48), dtype=bf16)
        rot2 = np.zeros((48, 48), dtype=bf16)
        tri = np.zeros((48, 16), dtype=bf16)
        for h_ in range(16):
            for c_ in range(3):
                rot1[3 * h_ + (c_ + 1) % 3, 3 * h_ + c_] = 1
                rot2[3 * h_ + (c_ + 2) % 3, 3 * h_ + c_] = 1
                tri[3 * h_ + c_, h_] = 1
        ctx.consts["c_l2rot1"] = rot1
        ctx.consts["c_l2rot2"] = rot2
        ctx.consts["c_l2tri"] = tri
    if "l3f" in ck:
        ind2 = np.zeros((128, 2), dtype=np.float32)
        ind2[0:64, 0] = 1.0
        ind2[64:128, 1] = 1.0
        ctx.consts["c_l3f_ind2"] = ind2
        ctx.consts["c_l3f_ind2T"] = np.ascontiguousarray(ind2.T)
    ctx.consts["c_magic"] = (MAGIC0 + np.arange(64, dtype=np.int32)).reshape(1, 64)
    ctx.consts["c_slot"] = np.zeros((1, 1), dtype=np.int32)
    ind16 = np.zeros((128, 128), dtype=np.float32)
    for c in range(8):
        for p in range(128):
            ind16[p, c * 16 + 2 * c + (p // 64)] = 1
    ctx.consts["c_ind16"] = ind16
    perm = np.zeros((128, 128), dtype=bf16)
    for i in range(128):
        j = i + 32 if (i % 64) < 32 else i - 32
        perm[j, i] = 1
    ctx.consts["c_permswap"] = perm
    dup = np.zeros((32, 128), dtype=bf16)
    sgn = np.zeros((32, 128), dtype=bf16)
    for r in range(32):
        for rep in range(4):
            dup[r, r + rep * 32] = 1
            sgn[r, r + rep * 32] = -1 if rep % 2 == 0 else 1
    ctx.consts["c_inddup"] = dup
    ctx.consts["c_inddupsgn"] = sgn
    for name, arr in ctx.consts.items():
        dt = {np.dtype(np.float32): F32, np.dtype(bf16): BF16,
              np.dtype(np.int32): I32}[arr.dtype]
        ctx.dram[name] = ctx.nc.dram_tensor(name, list(arr.shape), dt,
                                            kind="ExternalInput").ap()


def emit_setup(ctx, extra_loads=()):
    dsem = ctx.dconst

    def _(sync):
        for nm, dst in (("c_ones_row", ctx.ones_row), ("c_ones_row_t", ctx.ones_row_t),
                        ("c_ones_rowf", ctx.ones_rowf),
                        ("c_ones_colf", ctx.ones_colf), ("c_ones_colb", ctx.ones_colb),
                        ("c_ident", ctx.ident),
                        *([("c_identf", ctx.identf)] if hasattr(ctx, "identf") else []),
                        *([("c_l2rot1", ctx.l2rot1), ("c_l2rot2", ctx.l2rot2),
                           ("c_l2tri", ctx.l2tri)] if hasattr(ctx, "l2rot1") else []),
                        *([("c_l3f_ind2", ctx.l3f_ind2),
                           ("c_l3f_ind2T", ctx.l3f_ind2T)] if hasattr(ctx, "l3f_ind2") else []),
                        ("c_magic", ctx.magic_sb), ("c_slot", ctx.slot_sb),
                        ("c_ind16", ctx.ind16_sb), ("c_permswap", ctx.permswap_sb),
                        ("c_inddup", ctx.inddup_sb), ("c_inddupsgn", ctx.inddupsgn_sb)):
            ins = sync.dma_start(out=dst[:], in_=ctx.dram[nm])
            dsem.inc(ins, 16)
        for dst_ap, src_ap in extra_loads:
            ins = sync.dma_start(out=dst_ap, in_=src_ap)
            dsem.inc(ins, 16)

    ctx.on("sync", _)
    ctx.ld_const_tok = dsem.n
    return ctx.ld_const_tok


def emit_load_vec(ctx, dram_ap, n, dst_ap, scale=1.0):
    dsem = ctx.new_ldsem()

    def _(g):
        src = dram_ap.rearrange("(c p) -> p c", p=128)
        ins = g.dma_start(out=dst_ap, in_=src, allow_slow_non_contiguous=True)
        dsem.inc(ins, 16)

    ctx.on("gpsimd", _)
    tok = dsem.n
    if scale == 1.0:
        return (dsem, tok)
    Ev = ctx.E["vector"]

    def _(vector):
        vector.wait_ge(dsem.h, tok)
        ins = vector.tensor_scalar(out=dst_ap, in0=dst_ap, scalar1=float(scale),
                                   scalar2=None, op0=ALU.mult)
        Ev.inc(ins, 1)

    ctx.on("vector", _)
    return (Ev, Ev.n)



def emit_layernorm(ctx, out_sb=None, src_sb=None, gain_row=None, beta_row=None,
                   src_tile=None, src_bf16=False):
    if out_sb is None:
        out_sb = ctx.xn_sb
    if src_sb is None:
        src_sb = ctx.x_sb
    if src_tile is None:
        src_tile = lambda d, c0, w: src_sb[:, d * T + c0: d * T + c0 + w]
    Et, Es, Ev = ctx.E["tensor"], ctx.E["scalar"], ctx.E["vector"]
    ps_sum, ps_sq = ctx.ps[4], ctx.ps[5]
    _ln_stack = ExitStack()
    _lne = _ln_stack.enter_context
    mu = _lne(ctx.nc.sbuf_tensor(f"ln_mu{ctx.uid()}", [1, T], F32))
    rstd = _lne(ctx.nc.sbuf_tensor(f"ln_rstd{ctx.uid()}", [1, T], F32))
    bc = _lne(ctx.nc.sbuf_tensor(f"ln_bc{ctx.uid()}", [128, 2 * T], BF16))
    sq = _lne(ctx.nc.sbuf_tensor(f"ln_sq{ctx.uid()}", [128, 1024], F32))
    eps_ap, eps_dep = ctx.fconst(LN_EPS, 1)

    for tt in range(4):
        tsl = slice(tt * 512, (tt + 1) * 512)
        sc_base = Es.n
        mm_base = Et.n

        def _(scalar):
            for d in range(NCH):
                half = (d % 2) * 512
                if d >= 2:
                    scalar.wait_ge(Et.h, mm_base + (d - 2) + 1)
                ins = scalar.activation(out=sq[:, half:half + 512],
                                        in_=src_tile(d, tt * 512, 512),
                                        func=AF.Square)
                Es.inc(ins, 1)

        ctx.on("scalar", _)

        def _(tensor):
            ctx.psum_wait(tensor, 4)
            ctx.psum_wait(tensor, 5)
            for d in range(NCH):
                half = (d % 2) * 512
                tensor.matmul(ps_sum[0:1, :],
                              ctx.ones_colb[:] if src_bf16 else ctx.ones_colf[:],
                              src_tile(d, tt * 512, 512),
                              start=(d == 0), stop=(d == NCH - 1))
                tensor.wait_ge(Es.h, sc_base + d + 1)
                ins = tensor.matmul(ps_sq[0:1, :], ctx.ones_colf[:],
                                    sq[:, half:half + 512],
                                    start=(d == 0), stop=(d == NCH - 1))
                Et.inc(ins, 1)

        ctx.on("tensor", _)
        mm_sums = Et.n

        def _(vector):
            vector.wait_ge(Et.h, mm_sums)
            ins = vector.tensor_scalar(out=mu[0:1, tsl], in0=ps_sum[0:1, :],
                                       scalar1=1.0 / D, scalar2=None, op0=ALU.mult)
            mu_tok = Ev.inc(ins, 1)
            vector.wait_ge(Ev.h, mu_tok)
            ins = vector.tensor_tensor(out=rstd[0:1, tsl], in0=mu[0:1, tsl],
                                       in1=mu[0:1, tsl], op=ALU.mult)
            t2 = Ev.inc(ins, 1)
            vector.wait_ge(Ev.h, t2)
            ins = vector.scalar_tensor_tensor(out=rstd[0:1, tsl], in0=ps_sq[0:1, :],
                                              scalar=1.0 / D, in1=rstd[0:1, tsl],
                                              op0=ALU.mult, op1=ALU.subtract)
            Ev.inc(ins, 1)

        ctx.on("vector", _)
        var_tok = Ev.n

        def _(scalar):
            scalar.wait_ge(Ev.h, var_tok)
            scalar.wait_ge(eps_dep[0].h, eps_dep[1])
            ins = scalar.activation(out=rstd[0:1, tsl], in_=rstd[0:1, tsl],
                                    func=AF.Ln, bias=eps_ap)
            t1 = Es.inc(ins, 1)
            scalar.wait_ge(Es.h, t1)
            ins = scalar.activation(out=rstd[0:1, tsl], in_=rstd[0:1, tsl],
                                    func=AF.Exp, scale=-0.5)
            Es.inc(ins, 1)

        ctx.on("scalar", _)
        rstd_tok = Es.n

        def _(tensor):
            tensor.wait_ge(Es.h, rstd_tok)
            tensor.matmul(ps_sum[:, :], ctx.ones_rowf[:], mu[0:1, tsl],
                          start=True, stop=True)
            ins = tensor.matmul(ps_sq[:, :], ctx.ones_rowf[:], rstd[0:1, tsl],
                                start=True, stop=True)
            Et.inc(ins, 1)

        ctx.on("tensor", _)
        bc_mm = Et.n

        def _(scalar):
            scalar.wait_ge(Et.h, bc_mm)
            scalar.activation(out=bc[:, tt * 512:(tt + 1) * 512], in_=ps_sum[:, :],
                              func=AF.Copy)
            ins = scalar.activation(out=bc[:, T + tt * 512: T + (tt + 1) * 512],
                                    in_=ps_sq[:, :], func=AF.Copy)
            Es.inc(ins, 1)

        ctx.on("scalar", _)
        ctx.psum_set_owner(4, Es, Es.n)
        ctx.psum_set_owner(5, Es, Es.n)

    bc_all = Es.n

    def _(vector):
        vector.wait_ge(Es.h, bc_all)
        if gain_row is not None:
            vector.wait_ge(gain_row[1][0].h, gain_row[1][1])
            vector.wait_ge(beta_row[1][0].h, beta_row[1][1])
        for d in range(NCH):
            for tt in range(4):
                osl = slice(d * T + tt * 512, d * T + (tt + 1) * 512)
                xs = src_tile(d, tt * 512, 512)
                half = (tt % 2) * 512
                ins = vector.tensor_tensor(out=sq[:, half:half + 512], in0=xs,
                                           in1=bc[:, tt * 512:(tt + 1) * 512],
                                           op=ALU.subtract)
                t1 = Ev.inc(ins, 1)
                vector.wait_ge(Ev.h, t1)
                ins = vector.tensor_tensor(out=out_sb[:, osl],
                                           in0=sq[:, half:half + 512],
                                           in1=bc[:, T + tt * 512: T + (tt + 1) * 512],
                                           op=ALU.mult)
                t2 = Ev.inc(ins, 1)
                vector.wait_ge(Ev.h, t2)
                if gain_row is not None:
                    ins = vector.tensor_scalar(out=out_sb[:, osl], in0=out_sb[:, osl],
                                               scalar1=gain_row[0][:, d:d + 1],
                                               scalar2=beta_row[0][:, d:d + 1],
                                               op0=ALU.mult, op1=ALU.add)
                    t3 = Ev.inc(ins, 1)
                    vector.wait_ge(Ev.h, t3)

    ctx.on("vector", _)
    _ln_stack.close()
    if not getattr(ctx, "no_ln_barrier", False):
        ctx.barrier()
    return Ev.n



class WDirect:

    def __init__(self, ap, K, N, cast=True, off=0):
        self.ap, self.K, self.N, self.cast, self.off = ap, K, N, cast, off

    def load(self, ctx, ob0, ncols, bi):
        kc_n = self.K // 128
        eng = "gpsimd" if self.cast else "scalar"
        dsem = ctx.dwb[bi]
        prev = ctx.wbuf_state[bi]
        ob0 = ob0 + self.off

        def _(g):
            if prev is not None:
                g.wait_ge(prev[0].h, prev[1])
            src = self.ap[:, ob0:ob0 + ncols].rearrange("(c p) n -> p c n", p=128)
            dst = ctx.wbuf[bi][:, 0:kc_n * ncols].rearrange("p (c n) -> p c n", n=ncols)
            ins = g.dma_start(out=dst, in_=src)
            dsem.inc(ins, 16)

        ctx.on(eng, _)
        return (dsem, dsem.n)


class WTransposedScratch:

    def __init__(self, ap, K, N):
        self.ap, self.K, self.N = ap, K, N

    def load(self, ctx, ob0, ncols, bi):
        kc_n = self.K // 128
        dsem = ctx.dwb[bi]
        prev = ctx.wbuf_state[bi]

        def _(sc):
            if prev is not None:
                sc.wait_ge(prev[0].h, prev[1])
            for kc_i in range(kc_n):
                src = self.ap[ob0:ob0 + ncols, kc_i * 128:(kc_i + 1) * 128]
                dst = ctx.wbuf[bi][:, kc_i * ncols:(kc_i + 1) * ncols]
                ins = sc.dma_start_transpose(dst, src)
                dsem.inc(ins, 16)

        ctx.on("scalar", _)
        return (dsem, dsem.n)


def emit_proj(ctx, wsrc, sink, src=None, obk=512, tensor_dep=None, tiles_t=4,
              tile_w=512, tail_mm=None, tt_major=False, src_tile=None):
    if src is None:
        src = lambda kc: ctx.xn_sb[:, kc * T:(kc + 1) * T]
    Et = ctx.E["tensor"]
    K, N = wsrc.K, wsrc.N
    kc_n = K // 128
    nblocks = ceil_div(N, obk)
    loads = [None] * nblocks

    def emit_block(i):
        bi_, (wsem, wtok), ob0, ncols = loads[i]
        last_tok = None
        nsub = ceil_div(ncols, 128)
        order = ([(osub, tt) for osub in range(nsub) for tt in range(tiles_t)]
                 if not tt_major else
                 [(osub, tt) for tt in range(tiles_t) for osub in range(nsub)])
        for osub, tt in order:
            o128 = min(128, ncols - osub * 128)
            if True:
                bank = ctx.next_bank()
                mm_tok = [None]

                def _(tensor):
                    tensor.wait_ge(wsem.h, wtok)
                    if tensor_dep:
                        for s_, t_ in tensor_dep:
                            tensor.wait_ge(s_.h, t_)
                    ctx.psum_wait(tensor, bank)
                    for kc_i in range(kc_n):
                        lhsT = ctx.wbuf[bi_][:, kc_i * ncols + osub * 128:
                                             kc_i * ncols + osub * 128 + o128]
                        if src_tile is not None:
                            rhs = src_tile(kc_i, tt * tile_w, tile_w)
                        else:
                            rhs = src(kc_i)[:, tt * tile_w:(tt + 1) * tile_w]
                        ins = tensor.matmul(ctx.ps[bank][0:o128, :], lhsT, rhs,
                                            start=(kc_i == 0),
                                            stop=(kc_i == kc_n - 1 and tail_mm is None))
                    if tail_mm is not None:
                        ins = tail_mm(tensor, (ob0 + osub * 128) // 128, tt,
                                      ctx.ps[bank][0:o128, :])
                    mm_tok[0] = Et.inc(ins, 1)

                ctx.on("tensor", _)
                consumer = sink((ob0 + osub * 128) // 128, tt,
                                ctx.ps[bank][0:o128, 0:tile_w], mm_tok[0])
                ctx.psum_set_owner(bank, *consumer)
                last_tok = mm_tok[0]
        ctx.wbuf_state[bi_] = (Et, last_tok)

    for i in range(nblocks):
        ob0 = i * obk
        ncols = min(obk, N - ob0)
        bi_ = i % 2
        loads[i] = (bi_, wsrc.load(ctx, ob0, ncols, bi_), ob0, ncols)
        if i >= 1:
            emit_block(i - 1)
    emit_block(nblocks - 1)


def sink_to_sbuf(ctx, target, func=AF.Copy, scale=1.0, bias=None, out_dtype=None,
                 accum_out=None):
    Es = ctx.E["scalar"]

    def sink(oc, tt, psum_ap, mm_tok):
        def _(scalar):
            scalar.wait_ge(ctx.E["tensor"].h, mm_tok)
            kw = {}
            if bias is not None:
                kw["bias"] = bias(oc)
            if accum_out is not None:
                kw["accum_out"] = accum_out(oc, tt)
            ins = scalar.activation(out=target(oc, tt), in_=psum_ap, func=func,
                                    scale=scale, **kw)
            Es.inc(ins, 1)

        ctx.on("scalar", _)
        return (Es, Es.n)

    return sink


def sink_resid(ctx, scale=1.0, oc_base=0):
    Ev = ctx.E["vector"]

    def sink(oc, tt, psum_ap, mm_tok):
        oc2 = oc + oc_base

        def _(vector):
            vector.wait_ge(ctx.E["tensor"].h, mm_tok)
            xs = ctx.x_sb[:, oc2 * T + tt * 512: oc2 * T + (tt + 1) * 512]
            ins = vector.scalar_tensor_tensor(out=xs, in0=psum_ap, scalar=float(scale),
                                              in1=xs, op0=ALU.mult, op1=ALU.add)
            Ev.inc(ins, 1)

        ctx.on("vector", _)
        return (Ev, Ev.n)

    return sink


def emit_bias_row_load(ctx, dram_vec_ap, n, name=None):
    row = ctx.tmp("brow", [1, n], BF16)
    dg = ctx.new_ldsem()

    def _(g):
        ins = g.dma_start(out=row[:], in_=dram_vec_ap.rearrange("(o n) -> o n", o=1))
        dg.inc(ins, 16)

    ctx.on("gpsimd", _)
    return row, (dg, dg.n)


def bias_tail_mm(ctx, bias_row, bias_dep, ones_len=512):

    def tail(tensor, oc, tt, bank_ap):
        tensor.wait_ge(bias_dep[0].h, bias_dep[1])
        o128 = bank_ap.shape[0]
        return tensor.matmul(bank_ap, bias_row[:, oc * 128: oc * 128 + o128],
                             ctx.ones_row_t[0:1, 0:bank_ap.shape[1]],
                             start=False, stop=True)

    return tail


def emit_residual_add(ctx, src_sb=None, dep=None):
    if src_sb is None:
        src_sb = ctx.out_sb
    Ev = ctx.E["vector"]

    def _(vector):
        if dep:
            for s_, t_ in dep:
                vector.wait_ge(s_.h, t_)
        for d in range(NCH):
            xs = ctx.x_sb[:, d * T:(d + 1) * T]
            ins = vector.tensor_tensor(out=xs, in0=xs,
                                       in1=src_sb[:, d * T:(d + 1) * T], op=ALU.add)
            tok = Ev.inc(ins, 1)
            vector.wait_ge(Ev.h, tok)

    ctx.on("vector", _)
    return Ev.n



def to_feature_major(x_shard):
    t, d = x_shard.shape
    return np.ascontiguousarray(
        x_shard.reshape(t, d // 128, 128).transpose(2, 1, 0).reshape(128, -1))


def from_feature_major(fm, t=T):
    nch = fm.shape[1] // t
    return np.ascontiguousarray(
        fm.reshape(128, nch, t).transpose(2, 1, 0).reshape(t, nch * 128))



L0_SCALING = 32.0 / 16.0


def emit_l0_ffn(ctx, xn_tok):
    nc = ctx.nc
    Et, Es, Ev = ctx.E["tensor"], ctx.E["scalar"], ctx.E["vector"]
    xn_dep = [(Ev, xn_tok)]
    A_sb = ctx.tmp("l0A", [128, 1024], BF16)
    B_sb = ctx.tmp("l0B", [128, 1024], BF16)
    rwT_sb = ctx.tmp("l0rw", [128, 64], BF16)
    ind_sb = ctx.tmp("l0ind", [8, 128], BF16)
    lg_sb = ctx.tmp("l0lg", [128, 128], F32)
    m1_sb = ctx.tmp("l0m1", [128, 32], F32)
    wtok_sb = ctx.tmp("l0wt", [128, 128], BF16)
    wfeat_sb = ctx.tmp("l0wf", [8, T], BF16)
    ax_sb = ctx.tmp("l0ax", [128, T], BF16)
    sxa_sb = ctx.tmp("l0sxa", [128, T], BF16)
    scr_sb = ctx.tmp("l0scr", [128, 256], F32)
    rbrow_sb = ctx.tmp("l0rbr", [1, 8], BF16)

    dg = ctx.new_ldsem()

    def _(g):
        for dst, src_ in (
            (A_sb[:].rearrange("p (c n) -> p c n", n=128),
             ctx.dram["L0f_A_all"].rearrange("(c p) n -> p c n", p=128)),
            (B_sb[:], ctx.dram["L0f_B_all"]),
            (ind_sb[:], ctx.dram["c_ind_er"]),
            (rwT_sb[:].rearrange("p (c n) -> p c n", n=8),
             ctx.dram["L0f_rwT"].rearrange("(c p) n -> p c n", p=128)),
            (rbrow_sb[:], ctx.dram["L0f_rb"].rearrange("(o n) -> o n", o=1)),
        ):
            ins = g.dma_start(out=dst, in_=src_)
            dg.inc(ins, 16)

    ctx.on("gpsimd", _)
    misc_tok = dg.n
    bb_row, bb_dep = emit_bias_row_load(ctx, ctx.dram["L0f_base_b"], D)

    lg_bank = 4
    def _(tensor):
        ctx.psum_wait(tensor, lg_bank)
        tensor.wait_ge(dg.h, misc_tok)
        for s_, t_ in xn_dep:
            tensor.wait_ge(s_.h, t_)
        for i in range(16):
            for kc in range(NCH):
                ins = tensor.matmul(ctx.ps[lg_bank][:, i * 8:(i + 1) * 8],
                                    ctx.xn_sb[:, kc * T + i * 128: kc * T + (i + 1) * 128],
                                    rwT_sb[:, kc * 8:(kc + 1) * 8],
                                    start=(i == 0 and kc == 0), stop=False)
        for i in range(16):
            ins = tensor.matmul(ctx.ps[lg_bank][:, i * 8:(i + 1) * 8],
                                ctx.ones_row[:], rbrow_sb[:],
                                start=False, stop=(i == 15))
        Et.inc(ins, 1)

    ctx.on("tensor", _)
    lg_mm = Et.n

    def _(vector):
        vector.wait_ge(Et.h, lg_mm)
        ins = vector.tensor_copy(out=lg_sb[:], in_=ctx.ps[lg_bank][:, 0:128])
        t = Ev.inc(ins, 1)
        vector.wait_ge(Ev.h, t)
        ins = vector.tensor_reduce(out=m1_sb[:, 0:16].rearrange("p (a o) -> p a o", o=1),
                                   in_=lg_sb[:].rearrange("p (a e) -> p a e", e=8),
                                   axis=AX.X, op=ALU.max)
        t = Ev.inc(ins, 1)
        vector.wait_ge(Ev.h, t)
        for i in range(16):
            ins = vector.tensor_scalar(out=lg_sb[:, i * 8:(i + 1) * 8],
                                       in0=lg_sb[:, i * 8:(i + 1) * 8],
                                       scalar1=m1_sb[:, i:i + 1], scalar2=None,
                                       op0=ALU.subtract)
        t = Ev.inc(ins, 1)
        vector.wait_ge(Ev.h, t)
        ins = vector.tensor_scalar(out=scr_sb[:, 0:128], in0=lg_sb[:],
                                   scalar1=0.0, scalar2=-1e9, op0=ALU.is_ge,
                                   op1=ALU.mult)
        t = Ev.inc(ins, 1)
        vector.wait_ge(Ev.h, t)
        ins = vector.tensor_tensor(out=scr_sb[:, 0:128], in0=scr_sb[:, 0:128],
                                   in1=lg_sb[:], op=ALU.add)
        t = Ev.inc(ins, 1)
        vector.wait_ge(Ev.h, t)
        ins = vector.tensor_reduce(out=m1_sb[:, 16:32].rearrange("p (a o) -> p a o", o=1),
                                   in_=scr_sb[:, 0:128].rearrange("p (a e) -> p a e", e=8),
                                   axis=AX.X, op=ALU.max)
        t = Ev.inc(ins, 1)
        vector.wait_ge(Ev.h, t)

    ctx.on("vector", _)
    lm_tok = Ev.n

    def _(scalar):
        scalar.wait_ge(Ev.h, lm_tok)
        ins = scalar.activation(out=scr_sb[:, 128:256], in_=lg_sb[:], func=AF.Exp)
        Es.inc(ins, 1)

    ctx.on("scalar", _)
    exp_tok = Es.n

    def _(vector):
        vector.wait_ge(Es.h, exp_tok)
        for i in range(16):
            ins = vector.tensor_scalar(out=scr_sb[:, i * 8:(i + 1) * 8],
                                       in0=lg_sb[:, i * 8:(i + 1) * 8],
                                       scalar1=m1_sb[:, 16 + i:17 + i], scalar2=None,
                                       op0=ALU.is_ge)
        t = Ev.inc(ins, 1)
        vector.wait_ge(Ev.h, t)
        ins = vector.tensor_tensor(out=scr_sb[:, 0:128], in0=scr_sb[:, 0:128],
                                   in1=scr_sb[:, 128:256], op=ALU.mult)
        t = Ev.inc(ins, 1)
        vector.wait_ge(Ev.h, t)
        ins = vector.tensor_reduce(out=m1_sb[:, 0:16].rearrange("p (a o) -> p a o", o=1),
                                   in_=scr_sb[:, 0:128].rearrange("p (a e) -> p a e", e=8),
                                   axis=AX.X, op=ALU.add)
        t = Ev.inc(ins, 1)
        vector.wait_ge(Ev.h, t)
        ins = vector.reciprocal(out=m1_sb[:, 16:32], in_=m1_sb[:, 0:16])
        t = Ev.inc(ins, 1)
        vector.wait_ge(Ev.h, t)
        for i in range(16):
            ins = vector.tensor_scalar(out=wtok_sb[:, i * 8:(i + 1) * 8],
                                       in0=scr_sb[:, i * 8:(i + 1) * 8],
                                       scalar1=m1_sb[:, 16 + i:17 + i],
                                       scalar2=L0_SCALING,
                                       op0=ALU.mult, op1=ALU.mult)
        Ev.inc(ins, 1)
        vector.wait_ge(Ev.h, Ev.n)

    ctx.on("vector", _)
    w_tok = Ev.n

    for rnd in range(2):
        def _(tensor, rnd=rnd):
            tensor.wait_ge(Ev.h, w_tok)
            ctx.psum_wait(tensor, "b0")
            for j in range(8):
                i = rnd * 8 + j
                ins = tensor.transpose(ctx.psb[0:8, j * 128:(j + 1) * 128],
                                       wtok_sb[:, i * 8:(i + 1) * 8], ctx.ident[:])
            Et.inc(ins, 1)

        ctx.on("tensor", _)
        wT_mm = Et.n

        def _(scalar, rnd=rnd, wT_mm=wT_mm):
            scalar.wait_ge(Et.h, wT_mm)
            ins = scalar.activation(out=wfeat_sb[:, rnd * 1024:(rnd + 1) * 1024],
                                    in_=ctx.psb[0:8, 0:1024], func=AF.Copy)
            Es.inc(ins, 1)

        ctx.on("scalar", _)
        ctx.psum_set_owner("b0", Es, Es.n)
    wf_tok = Es.n

    for tt in range(4):
        bank = ctx.next_bank()

        def _(tensor, tt=tt, bank=bank):
            ctx.psum_wait(tensor, bank)
            tensor.wait_ge(dg.h, misc_tok)
            for s_, t_ in xn_dep:
                tensor.wait_ge(s_.h, t_)
            for kc in range(NCH):
                ins = tensor.matmul(ctx.ps[bank][:, :], A_sb[:, kc * 128:(kc + 1) * 128],
                                    ctx.xn_sb[:, kc * T + tt * 512: kc * T + (tt + 1) * 512],
                                    start=(kc == 0), stop=(kc == NCH - 1))
            Et.inc(ins, 1)

        ctx.on("tensor", _)
        mm_tok = Et.n

        def _(scalar, tt=tt, bank=bank, mm_tok=mm_tok):
            scalar.wait_ge(Et.h, mm_tok)
            ins = scalar.activation(out=ax_sb[:, tt * 512:(tt + 1) * 512],
                                    in_=ctx.ps[bank][:, :], func=AF.Copy)
            Es.inc(ins, 1)

        ctx.on("scalar", _)
        ctx.psum_set_owner(bank, Es, Es.n)
    ax_tok = Es.n

    for tt in range(4):
        def _(tensor, tt=tt):
            ctx.psum_wait(tensor, 5)
            tensor.wait_ge(Es.h, wf_tok)
            ins = tensor.matmul(ctx.ps[5][:, :], ind_sb[:],
                                wfeat_sb[:, tt * 512:(tt + 1) * 512],
                                start=True, stop=True)
            Et.inc(ins, 1)

        ctx.on("tensor", _)
        wbc_mm = Et.n

        def _(vector, tt=tt, wbc_mm=wbc_mm):
            vector.wait_ge(Et.h, wbc_mm)
            vector.wait_ge(Es.h, ax_tok)
            ins = vector.tensor_tensor(out=sxa_sb[:, tt * 512:(tt + 1) * 512],
                                       in0=ax_sb[:, tt * 512:(tt + 1) * 512],
                                       in1=ctx.ps[5][:, :], op=ALU.mult)
            Ev.inc(ins, 1)
            vector.wait_ge(Ev.h, Ev.n)

        ctx.on("vector", _)
        ctx.psum_set_owner(5, Ev, Ev.n)
    sxa_tok = Ev.n

    def tail_mm(tensor, oc, tt, bank_ap):
        tensor.wait_ge(Ev.h, sxa_tok)
        tensor.wait_ge(dg.h, misc_tok)
        tensor.matmul(bank_ap, B_sb[:, oc * 128:(oc + 1) * 128],
                      sxa_sb[:, tt * 512:(tt + 1) * 512],
                      start=False, stop=False)
        tensor.wait_ge(bb_dep[0].h, bb_dep[1])
        return tensor.matmul(bank_ap, bb_row[:, oc * 128:(oc + 1) * 128],
                             ctx.ones_row_t[:], start=False, stop=True)

    emit_proj(ctx, WDirect(ctx.dram["L0f_base_wT"], D, D), sink_resid(ctx, 0.5),
              tensor_dep=xn_dep, tail_mm=tail_mm)
    return ctx.E["vector"].n


def l0_ffn_host_inputs(inp, pref="L0_ffn"):
    base_w = np.asarray(inp[pref + "_base_w"], dtype=np.float32)
    A = np.asarray(inp[pref + "_lora_A"], dtype=np.float32)
    Bm = np.asarray(inp[pref + "_lora_B"], dtype=np.float32)
    rw = np.asarray(inp[pref + "_router_w"], dtype=np.float32)
    return {
        "L0f_base_wT": np.ascontiguousarray(base_w.T),
        "L0f_base_b": np.asarray(inp[pref + "_base_b"], dtype=np.float32),
        "L0f_A_all": np.ascontiguousarray(A.transpose(1, 0, 2).reshape(D, 128)),
        "L0f_B_all": np.ascontiguousarray(Bm.reshape(128, D)),
        "L0f_rwT": np.ascontiguousarray(rw.T),
        "L0f_rb": np.asarray(inp[pref + "_router_b"], dtype=np.float32),
    }


L0F_INPUT_SPECS = [
    ("L0f_base_wT", [D, D], F32), ("L0f_base_b", [D], F32),
    ("L0f_A_all", [D, 128], F32), ("L0f_B_all", [128, D], F32),
    ("L0f_rwT", [D, 8], F32), ("L0f_rb", [8], F32),
]


def ind_er_const():
    ind = np.zeros((8, 128), dtype=bf16)
    for e_ in range(8):
        ind[e_, e_ * 16:(e_ + 1) * 16] = 1
    return ind



def emit_proj_tokmaj(ctx, wsrc, sink, src=None, obk=512, tensor_dep=None,
                     tail_mm=None, src_tile=None):
    if src is None:
        src = lambda kc: ctx.xn_sb[:, kc * T:(kc + 1) * T]
    Et = ctx.E["tensor"]
    K, N = wsrc.K, wsrc.N
    kc_n = K // 128
    nblocks = ceil_div(N, obk)
    loads = [None] * nblocks

    def emit_block(i):
        bi_, (wsem, wtok), ob0, ncols = loads[i]
        last_tok = None
        for tb in range(T // 128):
            bank = ctx.next_bank()

            def _(tensor):
                tensor.wait_ge(wsem.h, wtok)
                if tensor_dep:
                    for s_, t_ in tensor_dep:
                        tensor.wait_ge(s_.h, t_)
                ctx.psum_wait(tensor, bank)
                for kc_i in range(kc_n):
                    if src_tile is not None:
                        lhsT = src_tile(kc_i, tb * 128, 128)
                    else:
                        lhsT = src(kc_i)[:, tb * 128:(tb + 1) * 128]
                    rhs = ctx.wbuf[bi_][:, kc_i * ncols: (kc_i + 1) * ncols]
                    ins = tensor.matmul(ctx.ps[bank][:, 0:ncols], lhsT, rhs,
                                        start=(kc_i == 0),
                                        stop=(kc_i == kc_n - 1 and tail_mm is None))
                if tail_mm is not None:
                    ins2 = tail_mm(tensor, tb, i, ctx.ps[bank][:, 0:ncols])
                    if ins2 is not None:
                        ins = ins2
                nonlocal_tok[0] = Et.inc(ins, 1)

            nonlocal_tok = [None]
            ctx.on("tensor", _)
            consumer = sink(tb, i, ctx.ps[bank][:, 0:ncols], nonlocal_tok[0])
            ctx.psum_set_owner(bank, *consumer)
            last_tok = nonlocal_tok[0]
        ctx.wbuf_state[bi_] = (Et, last_tok)

    for i in range(nblocks):
        ob0 = i * obk
        ncols = min(obk, N - ob0)
        bi_ = i % 2
        loads[i] = (bi_, wsrc.load(ctx, ob0, ncols, bi_), ob0, ncols)
        if i >= 1:
            emit_block(i - 1)
    emit_block(nblocks - 1)


def emit_attn_dense(ctx, tag, xn_tok, wq, wk, wv, wo, exp_scale,
                    bq_row=None, bk_row=None, bv_row=None, bo_row=None,
                    kv_width=D, kv_group=1, rope_tabs=None,
                    prob_mult=None, skip_tile=None, out_scale=1.0,
                    kv_mod=False):
    nc = ctx.nc
    Et, Es, Ev = ctx.E["tensor"], ctx.E["scalar"], ctx.E["vector"]
    ds = ctx.dma["sync"]
    xn_dep = [(Ev, xn_tok)]
    kv_nch = kv_width // 128
    n_kvh = kv_width // 64
    load_all_v = kv_width <= 256

    sh_k = nc.dram_tensor(f"sh_k_{tag}", [2, kv_width, T], BF16, kind="Internal",
                          addr_space="Shared").ap()
    sh_v = nc.dram_tensor(f"sh_v_{tag}", [2, T, kv_width], BF16, kind="Internal",
                          addr_space="Shared").ap()
    sh_kn = nc.dram_tensor(f"sh_kn_{tag}", [2, 1, 16], F32, kind="Internal",
                           addr_space="Shared").ap()

    q_sb = ctx.tmp("at_q", [128, NCH * T], BF16)
    kst = ctx.tmp("at_kst", [128, 2 * 512], BF16)
    vst = ctx.tmp("at_vst", [128, 2 * 512], BF16)
    nsq = ctx.tmp("at_nsq", [128, 512], F32)
    nrm = ctx.tmp("at_nrm", [16, 16], F32)
    nrm2 = ctx.tmp("at_nrm2", [16, 4 * 512], BF16)
    nrow = ctx.tmp("at_nrow", [1, 48], F32)
    negM = ctx.tmp("at_negM", [128, 16], F32)
    k_sb = ctx.tmp("at_k", [128, S], BF16)
    if load_all_v:
        v_all = ctx.tmp("at_vall", [128, 32 * kv_width], BF16)
    else:
        v_raw = ctx.tmp("at_vraw", [128, 32 * 128], BF16)
    v_aug = ctx.tmp("at_vaug", [128, 32 * 65], BF16)
    pr = ctx.tmp("at_pr", [128, 2 * 512], BF16)
    av_tmp = nsq
    recip = nrm2[0:1, 0:1024].bitcast(F32)
    knrow = ctx.tmp("at_knrow", [1, 16], F32)
    if rope_tabs is not None:
        kraw = ctx.tmp("at_kraw", [128, 512], BF16)
        ksw = ctx.tmp("at_ksw", [128, 512], BF16)
        kc1 = ctx.tmp("at_kc1", [128, 512], BF16)
        ropeC, ropeS, rope_dep = rope_tabs

    ev_kv = ctx.new_event()

    def norm_sink_factory(store_stage, do_rope):
        state = {"stores": [], "count": 0, "rope_ev": 0}

        def sink(oc, tt, psum_ap, mm_tok):
            ring = (oc * 4 + tt) % 2
            rs = ctx.rsem[ring]

            def _(scalar):
                scalar.wait_ge(Et.h, mm_tok)
                if store_stage is None:
                    out = q_sb[:, oc * T + tt * 512: oc * T + (tt + 1) * 512]
                else:
                    if do_rope:
                        out = kraw[:]
                        if state["rope_ev"]:
                            scalar.wait_ge(Ev.h, state["rope_ev"])
                    else:
                        out = store_stage[:, ring * 512:(ring + 1) * 512]
                        if rs.n:
                            scalar.wait_ge(rs.h, rs.n)
                ins = scalar.activation(out=out, in_=psum_ap, func=AF.Copy)
                Es.inc(ins, 1)
                ins = scalar.activation(out=nsq[:], in_=psum_ap, func=AF.Square)
                Es.inc(ins, 1)

            ctx.on("scalar", _)
            sq_tok = Es.n
            cp_tok = sq_tok - 1

            def _(tensor):
                tensor.wait_ge(Es.h, sq_tok)
                ctx.psum_wait(tensor, 4)
                ins = tensor.matmul(ctx.ps[4][0:16, :],
                                    ctx.ind16_sb[:, oc * 16:(oc + 1) * 16],
                                    nsq[:], start=True, stop=True)
                Et.inc(ins, 1)

            ctx.on("tensor", _)
            nm_tok = Et.n

            def _(vector):
                vector.wait_ge(Et.h, nm_tok)
                dst = nrm2[0:16, tt * 512:(tt + 1) * 512]
                if state["count"] < 4:
                    ins = vector.tensor_copy(out=dst, in_=ctx.ps[4][0:16, :])
                else:
                    ins = vector.tensor_tensor(out=dst, in0=dst,
                                               in1=ctx.ps[4][0:16, :], op=ALU.add)
                t = Ev.inc(ins, 1)
                vector.wait_ge(Ev.h, t)

            ctx.on("vector", _)
            state["count"] += 1
            ctx.psum_set_owner(4, Ev, Ev.n)
            consumer_tok = sq_tok
            consumer_sem = Es

            if store_stage is not None and do_rope:
                def _(tensor):
                    tensor.wait_ge(Es.h, cp_tok)
                    ctx.psum_wait(tensor, 3)
                    ins = tensor.matmul(ctx.ps[3][:, :], ctx.permswap_sb[:], kraw[:],
                                        start=True, stop=True)
                    Et.inc(ins, 1)

                ctx.on("tensor", _)
                swmm = Et.n

                def _(scalar):
                    scalar.wait_ge(Et.h, swmm)
                    if state["rope_ev"]:
                        scalar.wait_ge(Ev.h, state["rope_ev"])
                    ins = scalar.activation(out=ksw[:], in_=ctx.ps[3][:, :], func=AF.Copy)
                    Es.inc(ins, 1)

                ctx.on("scalar", _)
                swcp = Es.n
                ctx.psum_set_owner(3, Es, swcp)

                def _(vector):
                    for s_, t_ in rope_dep:
                        vector.wait_ge(s_.h, t_)
                    vector.wait_ge(Es.h, swcp)
                    if rs.n:
                        vector.wait_ge(rs.h, rs.n)
                    tsl = slice(tt * 512, (tt + 1) * 512)
                    ins = vector.tensor_tensor(out=kc1[:], in0=kraw[:],
                                               in1=ropeC[:, tsl], op=ALU.mult)
                    t = Ev.inc(ins, 1)
                    vector.wait_ge(Ev.h, t)
                    ins = vector.tensor_tensor(out=ksw[:], in0=ksw[:],
                                               in1=ropeS[:, tsl], op=ALU.mult)
                    t = Ev.inc(ins, 1)
                    vector.wait_ge(Ev.h, t)
                    ins = vector.tensor_tensor(
                        out=store_stage[:, ring * 512:(ring + 1) * 512],
                        in0=kc1[:], in1=ksw[:], op=ALU.add)
                    t = Ev.inc(ins, 1)
                    vector.wait_ge(Ev.h, t)

                ctx.on("vector", _)
                consumer_tok = Ev.n
                consumer_sem = Ev
                state["rope_ev"] = Ev.n

            if store_stage is not None:
                def emit_store(sync, slot):
                    tgt = sh_k[slot, oc * 128:(oc + 1) * 128, tt * 512:(tt + 1) * 512]
                    sync.dma_start(out=tgt,
                                   in_=store_stage[:, ring * 512:(ring + 1) * 512]).then_inc(rs.h, 16)

                def _(sync):
                    sync.wait_ge(consumer_sem.h, consumer_tok)

                ctx.on("sync", _)
                ctx.by_parity("sync", emit_store)
                rs.n += 16
                state["stores"].append((rs, rs.n))
            return (consumer_sem, consumer_tok)

        return sink, state

    def reduce_norms(dst_col):
        acc_done = Ev.n

        def _(vector):
            vector.wait_ge(Ev.h, acc_done)
            ins = vector.reduce_max(out=nrm[0:16, dst_col:dst_col + 1],
                                    in_=nrm2[0:16, :], axis=AX.X)
            t = Ev.inc(ins, 1)
            vector.wait_ge(Ev.h, t)

        ctx.on("vector", _)
        return Ev.n

    q_sink, _qs = norm_sink_factory(None, False)
    emit_proj(ctx, wq, q_sink, tensor_dep=xn_dep,
              tail_mm=(bias_tail_mm(ctx, bq_row[0], bq_row[1]) if bq_row else None))
    qn_tok = reduce_norms(0)
    q_done = Es.n
    if rope_tabs is not None:
        rope_q_dep = emit_rope_apply_q(ctx, q_sb, NCH, ropeC, ropeS, rope_dep,
                                       [(Es, q_done)],
                                       scratch=(pr[:, 0:512], pr[:, 512:1024]))
        q_done_sem, q_done_tok = rope_q_dep
    else:
        q_done_sem, q_done_tok = Es, q_done

    k_sink, kstate = norm_sink_factory(kst, rope_tabs is not None)
    emit_proj(ctx, wk, k_sink, tensor_dep=xn_dep,
              tail_mm=(bias_tail_mm(ctx, bk_row[0], bk_row[1]) if bk_row else None))
    kn_tok = reduce_norms(1)
    k_store_deps = [(rs, rs.n) for rs in ctx.rsem[0:2]]

    vstate = {"stores": []}

    def v_sink(tb, ob_i, psum_ap, mm_tok):
        ring = (tb + ob_i) % 2
        rs = ctx.rsem[4 + ring]
        ncols = psum_ap.shape[1]

        def _(scalar):
            scalar.wait_ge(Et.h, mm_tok)
            if rs.n:
                scalar.wait_ge(rs.h, rs.n)
            ins = scalar.activation(out=v_aug[:, 1024 + ring * 512: 1024 + ring * 512 + ncols],
                                    in_=psum_ap, func=AF.Copy)
            Es.inc(ins, 1)

        ctx.on("scalar", _)
        cp_tok = Es.n

        def emit_store(sync, slot):
            tgt = sh_v[slot, tb * 128:(tb + 1) * 128,
                       ob_i * 512: ob_i * 512 + ncols]
            sync.dma_start(out=tgt,
                           in_=v_aug[:, 1024 + ring * 512: 1024 + ring * 512 + ncols]).then_inc(rs.h, 16)

        def _(sync):
            sync.wait_ge(Es.h, cp_tok)

        ctx.on("sync", _)
        ctx.by_parity("sync", emit_store)
        rs.n += 16
        vstate["stores"].append((rs, rs.n))
        return (Es, cp_tok)

    def v_tail(tensor, tb, ob_i, bank_ap):
        tensor.wait_ge(bv_row[1][0].h, bv_row[1][1])
        ncols = bank_ap.shape[1]
        return tensor.matmul(bank_ap, ctx.ones_row[:],
                             bv_row[0][0:1, ob_i * 512: ob_i * 512 + ncols],
                             start=False, stop=True)

    emit_proj_tokmaj(ctx, wv, v_sink, tensor_dep=xn_dep,
                     tail_mm=(v_tail if bv_row else None))
    v_store_deps = [(rs, rs.n) for rs in ctx.rsem[4:6]]

    nrm_bf = ctx.tmp("at_nrmbf", [16, 2], BF16)

    def _(scalar):
        scalar.wait_ge(Ev.h, kn_tok)
        ins = scalar.activation(out=nrm_bf[:], in_=nrm[0:16, 0:2], func=AF.Copy)
        Es.inc(ins, 1)

    ctx.on("scalar", _)
    nbf_tok = Es.n

    def _(tensor):
        tensor.wait_ge(Es.h, nbf_tok)
        ctx.psum_wait(tensor, "b0")
        tensor.transpose(ctx.psb[0:1, 0:16], nrm_bf[:, 0:1], ctx.ident[0:16, 0:16])
        ins = tensor.transpose(ctx.psb[0:1, 16:32], nrm_bf[:, 1:2], ctx.ident[0:16, 0:16])
        Et.inc(ins, 1)

    ctx.on("tensor", _)
    nT_mm = Et.n

    def _(scalar):
        scalar.wait_ge(Et.h, nT_mm)
        ins = scalar.activation(out=nrow[0:1, 0:16], in_=ctx.psb[0:1, 0:16], func=AF.Sqrt)
        Es.inc(ins, 1)
        ins = scalar.activation(out=knrow[0:1, 0:16], in_=ctx.psb[0:1, 16:32], func=AF.Copy)
        Es.inc(ins, 1)

    ctx.on("scalar", _)
    ctx.psum_set_owner("b0", Es, Es.n)
    kn_row_tok = Es.n

    def emit_knstore(sync, slot):
        sync.dma_start(out=sh_kn[slot], in_=knrow[0:1, 0:16]).then_inc(ds.h, 16)

    def _(sync):
        sync.wait_ge(Es.h, kn_row_tok)

    ctx.on("sync", _)
    ctx.by_parity("sync", emit_knstore)
    ds.n += 16
    ctx.mbox_flag(ev_kv, dep_waits=[(ds, ds.n)] + k_store_deps + v_store_deps)
    xrdy_tok = ctx.mbox_wait(ev_kv)

    preg = ctx.parity_reg("sync")
    dpk = ctx.new_ldsem()

    def _(sync):
        with sync.If_eq(preg, 0):
            sync.dma_start(out=nrow[0:1, 16:32], in_=sh_kn[1]).then_inc(dpk.h, 16)
        with sync.Else():
            sync.dma_start(out=nrow[0:1, 16:32], in_=sh_kn[0]).then_inc(dpk.h, 16)
        sync.end_ifs()

    ctx.on("sync", _)
    dpk.n += 16
    peer_kn = dpk.n

    def _(vector):
        vector.wait_ge(dpk.h, peer_kn)
        vector.wait_ge(Es.h, kn_row_tok)
        ins = vector.tensor_tensor(out=nrow[0:1, 16:32], in0=knrow[0:1, 0:16],
                                   in1=nrow[0:1, 16:32], op=ALU.max)
        t = Ev.inc(ins, 1)
        vector.wait_ge(Ev.h, t)

    ctx.on("vector", _)
    kmax2_tok = Ev.n

    def _(scalar):
        scalar.wait_ge(Ev.h, kmax2_tok)
        ins = scalar.activation(out=nrow[0:1, 16:32], in_=nrow[0:1, 16:32], func=AF.Sqrt)
        Es.inc(ins, 1)

    ctx.on("scalar", _)
    kmax_tok = Es.n

    def _(vector):
        vector.wait_ge(Es.h, kmax_tok)
        if kv_group == 1:
            ins = vector.tensor_tensor(out=nrow[0:1, 32:48], in0=nrow[0:1, 0:16],
                                       in1=nrow[0:1, 16:32], op=ALU.mult)
            t = Ev.inc(ins, 1)
            vector.wait_ge(Ev.h, t)
        elif kv_mod:
            for h_ in range(16):
                ins = vector.tensor_tensor(
                    out=nrow[0:1, 32 + h_:33 + h_],
                    in0=nrow[0:1, h_:h_ + 1],
                    in1=nrow[0:1, 16 + (h_ % n_kvh):17 + (h_ % n_kvh)],
                    op=ALU.mult)
                t = Ev.inc(ins, 1)
                vector.wait_ge(Ev.h, t)
        else:
            for g in range(n_kvh):
                ins = vector.tensor_scalar(
                    out=nrow[0:1, 32 + g * kv_group: 32 + (g + 1) * kv_group],
                    in0=nrow[0:1, g * kv_group: (g + 1) * kv_group],
                    scalar1=nrow[0:1, 16 + g:17 + g], scalar2=None, op0=ALU.mult)
                t = Ev.inc(ins, 1)
                vector.wait_ge(Ev.h, t)
        ins = vector.tensor_scalar(out=nrow[0:1, 32:48], in0=nrow[0:1, 32:48],
                                   scalar1=-float(exp_scale), scalar2=None,
                                   op0=ALU.mult)
        t = Ev.inc(ins, 1)
        vector.wait_ge(Ev.h, t)

    ctx.on("vector", _)
    M_tok = Ev.n

    def _(tensor):
        tensor.wait_ge(Ev.h, M_tok)
        ctx.psum_wait(tensor, 3)
        ins = tensor.matmul(ctx.ps[3][:, 0:16], ctx.ones_rowf[:], nrow[0:1, 32:48],
                            start=True, stop=True)
        Et.inc(ins, 1)

    ctx.on("tensor", _)
    negM_mm = Et.n

    def _(scalar):
        scalar.wait_ge(Et.h, negM_mm)
        ins = scalar.activation(out=negM[:], in_=ctx.ps[3][:, 0:16], func=AF.Copy)
        Es.inc(ins, 1)

    ctx.on("scalar", _)
    negM_tok = Es.n
    ctx.psum_set_owner(3, Es, negM_tok)

    def _(vector):
        for s_, t_ in k_store_deps + v_store_deps:
            vector.wait_ge(s_.h, t_)
        ins = vector.memset(v_aug[:], 1.0)
        Ev.inc(ins, 1)

    ctx.on("vector", _)
    vaug_ones = Ev.n

    def _(tensor):
        for b_ in (0, 1, 2):
            ctx.psum_wait(tensor, b_)

    ctx.on("tensor", _)
    stage = ctx.xn_sb
    kload_tok = [0]
    vload_tok = [0]
    score_mm_last = [0]
    score_owner = [0, 0]
    score_owner_sem = [Es, Es]
    pr_owner = [0, 0]
    bc_owner = [negM_tok]
    bc_owner_sem = [Es]
    dc = ctx.dma["scalar"]

    if load_all_v:
        def _(sc):
            sc.wait_ge(ctx.xrdy.h, xrdy_tok)
            for sl in range(2):
                for j in range(4):
                    dst = v_all[:, (sl * 16 + j * 4) * kv_width:
                                (sl * 16 + (j + 1) * 4) * kv_width]
                    dst = dst.rearrange("p (b c) -> p b c", c=kv_width)
                    srcv = sh_v[sl, j * 512:(j + 1) * 512, :]
                    srcv = srcv.rearrange("(b p) c -> p b c", p=128)
                    ins = sc.dma_start(out=dst, in_=srcv)
                    ctx.dv.inc(ins, 16)

        ctx.on("scalar", _)
        vload_tok[0] = ctx.dv.n

    for hp in range(8):
        g_kv = (hp * 2) // kv_group
        if not load_all_v:
            def _(sc):
                sc.wait_ge(ctx.xrdy.h, xrdy_tok)
                if score_mm_last[0]:
                    sc.wait_ge(Et.h, score_mm_last[0])
                for sl in range(2):
                    for j in range(4):
                        dst = v_raw[:, (sl * 16 + j * 4) * 128:(sl * 16 + (j + 1) * 4) * 128]
                        dst = dst.rearrange("p (b c) -> p b c", c=128)
                        srcv = sh_v[sl, j * 512:(j + 1) * 512, hp * 128:(hp + 1) * 128]
                        srcv = srcv.rearrange("(b p) c -> p b c", p=128)
                        ins = sc.dma_start(out=dst, in_=srcv)
                        ctx.dv.inc(ins, 16)

            ctx.on("scalar", _)
            vload_tok[0] = ctx.dv.n

        for hh in range(2):
            h = hp * 2 + hh
            kv_h = (h % n_kvh) if kv_mod else (h // kv_group)

            def _(sc):
                sc.wait_ge(ctx.xrdy.h, xrdy_tok)
                if score_mm_last[0]:
                    sc.wait_ge(Et.h, score_mm_last[0])
                for sl in range(2):
                    for half in range(2):
                        ins = sc.dma_start(out=k_sb[half * 64:(half + 1) * 64,
                                                    sl * T:(sl + 1) * T],
                                           in_=sh_k[sl, kv_h * 64:(kv_h + 1) * 64, :])
                        ctx.dk.inc(ins, 16)

            ctx.on("scalar", _)
            kload_tok[0] = ctx.dk.n

            def _(vector):
                vector.wait_ge(ctx.dv.h, vload_tok[0])
                vector.wait_ge(Ev.h, vaug_ones)
                if score_mm_last[0]:
                    vector.wait_ge(Et.h, score_mm_last[0])
                if load_all_v:
                    src_v = v_all[:].rearrange("p (a c) -> p a c", c=kv_width)[
                        :, :, (kv_h % (kv_width // 64)) * 64:(kv_h % (kv_width // 64)) * 64 + 64]
                else:
                    src_v = v_raw[:].rearrange("p (a c) -> p a c", c=128)[
                        :, :, hh * 64:(hh + 1) * 64]
                ins = vector.tensor_copy(
                    out=v_aug[:].rearrange("p (a c) -> p a c", c=65)[:, :, 0:64],
                    in_=src_v)
                Ev.inc(ins, 1)
                vector.wait_ge(Ev.h, Ev.n)

            ctx.on("vector", _)
            vaug_tok = Ev.n

            for qt in range(4):
                qsl = slice((h // 2) * T + qt * 512, (h // 2) * T + (qt + 1) * 512)
                qrows = slice((h % 2) * 64, (h % 2) * 64 + 64)
                kts = [kt for kt in range(32)
                       if skip_tile is None or not skip_tile(h, qt, kt)]
                pend = [None]

                def emit_av(stop_flag):
                    sb_, kt_, i_, osem, otok = pend[0]
                    first_av = (i_ == 0)

                    def _(tensor):
                        tensor.wait_ge(osem.h, otok)
                        if first_av:
                            tensor.wait_ge(Ev.h, vaug_tok)
                            ctx.psum_wait(tensor, 2)
                        ins = tensor.matmul(ctx.ps[2][0:65, :],
                                            v_aug[:, kt_ * 65:(kt_ + 1) * 65],
                                            pr[:, sb_ * 512:(sb_ + 1) * 512],
                                            start=first_av, stop=stop_flag)
                        Et.inc(ins, 1)

                    ctx.on("tensor", _)
                    pr_owner[sb_] = Et.n
                    score_mm_last[0] = Et.n

                for i_kt, kt in enumerate(kts):
                    sb = i_kt % 2

                    def _(tensor, sb=sb, kt=kt, i_kt=i_kt):
                        if i_kt == 0:
                            tensor.wait_ge(ctx.dk.h, kload_tok[0])
                            tensor.wait_ge(q_done_sem.h, q_done_tok)
                        if score_owner[sb]:
                            tensor.wait_ge(score_owner_sem[sb].h, score_owner[sb])
                        b0 = (h % 2) * 64
                        ins = tensor.matmul(ctx.ps[sb][:, :],
                                            k_sb[b0:b0 + 64, kt * 128:(kt + 1) * 128],
                                            q_sb[qrows, qsl],
                                            start=True, stop=True)
                        Et.inc(ins, 1)

                    ctx.on("tensor", _)
                    mm_tok = Et.n
                    score_mm_last[0] = mm_tok

                    def _(scalar, sb=sb, mm_tok=mm_tok):
                        scalar.wait_ge(Et.h, mm_tok)
                        scalar.wait_ge(Es.h, negM_tok)
                        if pr_owner[sb]:
                            scalar.wait_ge(Et.h, pr_owner[sb])
                        ins = scalar.activation(out=pr[:, sb * 512:(sb + 1) * 512],
                                                in_=ctx.ps[sb][:, :], func=AF.Exp,
                                                scale=float(exp_scale),
                                                bias=negM[:, h:h + 1])
                        Es.inc(ins, 1)

                    ctx.on("scalar", _)
                    score_owner[sb] = Es.n
                    score_owner_sem[sb] = Es

                    if prob_mult is not None:
                        pm = prob_mult(h, qt, kt)
                        if pm is not None:
                            pm_ap, pm_dep = pm
                            exp_tok = Es.n

                            def _(vector, sb=sb, pm_ap=pm_ap, pm_dep=pm_dep,
                                  exp_tok=exp_tok):
                                for s_, t_ in pm_dep:
                                    vector.wait_ge(s_.h, t_)
                                vector.wait_ge(Es.h, exp_tok)
                                if pr_owner[sb]:
                                    vector.wait_ge(Et.h, pr_owner[sb])
                                ins = vector.tensor_tensor(
                                    out=pr[:, sb * 512:(sb + 1) * 512],
                                    in0=pr[:, sb * 512:(sb + 1) * 512],
                                    in1=pm_ap, op=ALU.mult)
                                t = Ev.inc(ins, 1)
                                vector.wait_ge(Ev.h, t)

                            ctx.on("vector", _)
                            score_owner[sb] = Ev.n
                            score_owner_sem[sb] = Ev

                    prev = pend[0]
                    pend[0] = (sb, kt, i_kt, score_owner_sem[sb], score_owner[sb])
                    if prev is not None:
                        pend2 = pend[0]
                        pend[0] = prev
                        emit_av(False)
                        pend[0] = pend2
                emit_av(True)
                av_done = Et.n

                def _(scalar, av_done=av_done):
                    scalar.wait_ge(Et.h, av_done)
                    scalar.wait_ge(bc_owner_sem[0].h, bc_owner[0])
                    ins = scalar.activation(out=av_tmp[0:65, 0:512], in_=ctx.ps[2][0:65, :],
                                            func=AF.Copy)
                    Es.inc(ins, 1)

                ctx.on("scalar", _)
                av_cp = Es.n
                ctx.psum_set_owner(2, Es, av_cp)

                def _(vector, av_cp=av_cp):
                    vector.wait_ge(Es.h, av_cp)
                    ins = vector.reciprocal(out=recip, in_=av_tmp[64:65, 0:512])
                    Ev.inc(ins, 1)
                    vector.wait_ge(Ev.h, Ev.n)

                ctx.on("vector", _)
                recip_tok = Ev.n

                def _(tensor, recip_tok=recip_tok):
                    tensor.wait_ge(Ev.h, recip_tok)
                    tensor.wait_ge(bc_owner_sem[0].h, bc_owner[0])
                    ins = tensor.matmul(ctx.ps[3][0:64, :], ctx.ones_rowf[0:1, 0:64],
                                        recip, start=True, stop=True)
                    Et.inc(ins, 1)

                ctx.on("tensor", _)
                bc_mm = Et.n

                def _(vector, bc_mm=bc_mm):
                    vector.wait_ge(Et.h, bc_mm)
                    ins = vector.tensor_tensor(out=stage[qrows, qsl],
                                               in0=av_tmp[0:64, 0:512],
                                               in1=ctx.ps[3][0:64, :], op=ALU.mult)
                    Ev.inc(ins, 1)
                    vector.wait_ge(Ev.h, Ev.n)

                ctx.on("vector", _)
                bc_owner[0] = Ev.n
                bc_owner_sem[0] = Ev

    stage_tok = Ev.n
    ctx.psum_set_owner(3, Ev, Ev.n)

    tail = None
    if bo_row is not None:
        tail = bias_tail_mm(ctx, bo_row[0], bo_row[1])
    emit_proj(ctx, wo, sink_resid(ctx, out_scale),
              src=lambda kc: stage[:, kc * T:(kc + 1) * T],
              tensor_dep=[(Ev, stage_tok)], tail_mm=tail)
    return ctx.E["vector"].n



def emit_rope_tables(ctx, c32, s32, src_dep, C128=None, S128=None):
    Et, Es = ctx.E["tensor"], ctx.E["scalar"]
    if C128 is None:
        C128 = ctx.tmp("rope_C", [128, T], BF16)
        S128 = ctx.tmp("rope_S", [128, T], BF16)
    for tt in range(4):
        def _(tensor, tt=tt):
            for s_, t_ in src_dep:
                tensor.wait_ge(s_.h, t_)
            ctx.psum_wait(tensor, 4)
            ctx.psum_wait(tensor, 5)
            tensor.matmul(ctx.ps[4][:, :], ctx.inddup_sb[:],
                          c32[0:32, tt * 512:(tt + 1) * 512], start=True, stop=True)
            ins = tensor.matmul(ctx.ps[5][:, :], ctx.inddupsgn_sb[:],
                                s32[0:32, tt * 512:(tt + 1) * 512], start=True, stop=True)
            Et.inc(ins, 1)

        ctx.on("tensor", _)
        mm = Et.n

        def _(scalar, tt=tt, mm=mm):
            scalar.wait_ge(Et.h, mm)
            scalar.activation(out=C128[:, tt * 512:(tt + 1) * 512], in_=ctx.ps[4][:, :],
                              func=AF.Copy)
            ins = scalar.activation(out=S128[:, tt * 512:(tt + 1) * 512], in_=ctx.ps[5][:, :],
                                    func=AF.Copy)
            Es.inc(ins, 1)

        ctx.on("scalar", _)
        ctx.psum_set_owner(4, Es, Es.n)
        ctx.psum_set_owner(5, Es, Es.n)
    return C128, S128, [(Es, Es.n)]


def emit_rope_apply_q(ctx, q_sb, nch, C128, S128, tab_dep, q_dep, scratch=None):
    Et, Es, Ev = ctx.E["tensor"], ctx.E["scalar"], ctx.E["vector"]
    if scratch is not None:
        tmp, sw = scratch
    else:
        tmp = ctx.tmp("rope_tmp", [128, 512], BF16)
        sw = ctx.tmp("rope_sw", [128, 512], BF16)
    prev_ev = [0]
    for c in range(nch):
        for tt in range(4):
            sl = slice(c * T + tt * 512, c * T + (tt + 1) * 512)

            def _(tensor, sl=sl):
                for s_, t_ in q_dep:
                    tensor.wait_ge(s_.h, t_)
                if prev_ev[0]:
                    tensor.wait_ge(Ev.h, prev_ev[0])
                ctx.psum_wait(tensor, 3)
                ins = tensor.matmul(ctx.ps[3][:, :], ctx.permswap_sb[:], q_sb[:, sl],
                                    start=True, stop=True)
                Et.inc(ins, 1)

            ctx.on("tensor", _)
            mm = Et.n

            def _(scalar, mm=mm):
                scalar.wait_ge(Et.h, mm)
                if prev_ev[0]:
                    scalar.wait_ge(Ev.h, prev_ev[0])
                ins = scalar.activation(out=sw[:], in_=ctx.ps[3][:, :], func=AF.Copy)
                Es.inc(ins, 1)

            ctx.on("scalar", _)
            sw_tok = Es.n
            ctx.psum_set_owner(3, Es, sw_tok)

            def _(vector, sl=sl, tt=tt, sw_tok=sw_tok):
                for s_, t_ in tab_dep:
                    vector.wait_ge(s_.h, t_)
                vector.wait_ge(Es.h, sw_tok)
                tsl = slice(tt * 512, (tt + 1) * 512)
                ins = vector.tensor_tensor(out=tmp[:], in0=q_sb[:, sl],
                                           in1=C128[:, tsl], op=ALU.mult)
                t1 = Ev.inc(ins, 1)
                vector.wait_ge(Ev.h, t1)
                ins = vector.tensor_tensor(out=sw[:], in0=sw[:], in1=S128[:, tsl],
                                           op=ALU.mult)
                t2 = Ev.inc(ins, 1)
                vector.wait_ge(Ev.h, t2)
                ins = vector.tensor_tensor(out=q_sb[:, sl], in0=tmp[:], in1=sw[:],
                                           op=ALU.add)
                t3 = Ev.inc(ins, 1)
                vector.wait_ge(Ev.h, t3)

            ctx.on("vector", _)
            prev_ev[0] = Ev.n
    return (Ev, Ev.n)


def emit_l1_rope_sources(ctx, scope, c32, s32):
    Et, Es, Ev = ctx.E["tensor"], ctx.E["scalar"], ctx.E["vector"]
    dl = ctx.new_ldsem()
    le = scope.enter_context
    fr = le(ctx.nc.sbuf_tensor(f"l1_fr{ctx.uid()}", [1, 32], F32))
    pos = le(ctx.nc.sbuf_tensor(f"l1_pos{ctx.uid()}", [1, T], F32))
    ang = le(ctx.nc.sbuf_tensor(f"l1_ang{ctx.uid()}", [32, 2 * T], F32))

    def _(g):
        ins = g.dma_start(out=fr[:], in_=ctx.dram["L1a_freqs"].rearrange("(o n) -> o n", o=1))
        dl.inc(ins, 16)
        ins = g.dma_start(out=pos[:], in_=ctx.dram["c_mypos"])
        dl.inc(ins, 16)

    ctx.on("gpsimd", _)
    ld = dl.n
    TWO_PI = 2 * math.pi

    for tt in range(4):
        def _(tensor, tt=tt):
            tensor.wait_ge(dl.h, ld)
            ctx.psum_wait(tensor, 4)
            ins = tensor.matmul(ctx.ps[4][0:32, :], fr[:],
                                pos[0:1, tt * 512:(tt + 1) * 512], start=True, stop=True)
            Et.inc(ins, 1)

        ctx.on("tensor", _)
        mm = Et.n

        def _(vector, tt=tt, mm=mm):
            vector.wait_ge(Et.h, mm)
            tsl = slice(tt * 512, (tt + 1) * 512)
            ins = vector.tensor_copy(out=ang[0:32, tsl], in_=ctx.ps[4][0:32, :])
            t = Ev.inc(ins, 1)
            vector.wait_ge(Ev.h, t)
            ins = vector.tensor_scalar(out=ang[0:32, T + tt * 512: T + (tt + 1) * 512],
                                       in0=ctx.ps[4][0:32, :], scalar1=math.pi / 2,
                                       scalar2=None, op0=ALU.add)
            t = Ev.inc(ins, 1)
            vector.wait_ge(Ev.h, t)

        ctx.on("vector", _)
        ctx.psum_set_owner(4, Ev, Ev.n)
    modtok = Ev.n

    ladscr = scope.enter_context(ctx.nc.sbuf_tensor(f"l1_lad{ctx.uid()}", [32, 512], F32))

    def _(vector):
        vector.wait_ge(Ev.h, modtok)
        for half in range(2):
            for tt in range(4):
                sl = slice(half * T + tt * 512, half * T + (tt + 1) * 512)
                for k in range(9, -1, -1):
                    Ck = TWO_PI * (2 ** k)
                    ins = vector.tensor_scalar(out=ladscr[:], in0=ang[0:32, sl],
                                               scalar1=Ck, scalar2=Ck,
                                               op0=ALU.is_ge, op1=ALU.mult)
                    t = Ev.inc(ins, 1)
                    vector.wait_ge(Ev.h, t)
                    ins = vector.tensor_tensor(out=ang[0:32, sl], in0=ang[0:32, sl],
                                               in1=ladscr[:], op=ALU.subtract)
                    t = Ev.inc(ins, 1)
                    vector.wait_ge(Ev.h, t)

    ctx.on("vector", _)
    modtok = Ev.n

    wrapscr = scope.enter_context(ctx.nc.sbuf_tensor(f"l1_wrap{ctx.uid()}", [32, 512], F32))

    def _(vector):
        vector.wait_ge(Ev.h, modtok)
        for half in range(2):
            for tt in range(4):
                sl = slice(half * T + tt * 512, half * T + (tt + 1) * 512)
                ins = vector.tensor_scalar(out=wrapscr[:], in0=ang[0:32, sl],
                                           scalar1=math.pi, scalar2=-TWO_PI,
                                           op0=ALU.is_gt, op1=ALU.mult)
                t = Ev.inc(ins, 1)
                vector.wait_ge(Ev.h, t)
                ins = vector.tensor_tensor(out=ang[0:32, sl], in0=ang[0:32, sl],
                                           in1=wrapscr[:], op=ALU.add)
                t = Ev.inc(ins, 1)
                vector.wait_ge(Ev.h, t)

    ctx.on("vector", _)
    wrap_tok = Ev.n

    def _(scalar):
        scalar.wait_ge(Ev.h, wrap_tok)
        scalar.activation(out=s32[:], in_=ang[0:32, 0:T], func=AF.Sin)
        ins = scalar.activation(out=c32[:], in_=ang[0:32, T:2 * T], func=AF.Sin)
        Es = ctx.E["scalar"]
        Es.inc(ins, 1)

    ctx.on("scalar", _)
    return [(ctx.E["scalar"], ctx.E["scalar"].n)]



def emit_l1_kb_prep(ctx):
    nc = ctx.nc
    Es, Ev = ctx.E["scalar"], ctx.E["vector"]
    kbext = nc.dram_tensor(f"kbext_{ctx.uid()}", [16, 8448], BF16, kind="Internal").ap()
    scope = ExitStack()
    le = scope.enter_context
    pa = le(nc.sbuf_tensor(f"kb_pa{ctx.uid()}", [16, 4], F32))
    dch = le(nc.sbuf_tensor(f"kb_dch{ctx.uid()}", [16, 2112], F32))
    ech = le(nc.sbuf_tensor(f"kb_ech{ctx.uid()}", [16, 2112], BF16))
    dl = ctx.new_ldsem()
    one_ap, one_dep = ctx.fconst(1.0, 16)

    def _(g):
        ins = g.dma_start(out=pa[:, 0:1], in_=ctx.dram["L1a_bias_p"].rearrange("(p o) -> p o", o=1))
        dl.inc(ins, 16)
        ins = g.dma_start(out=pa[:, 1:2], in_=ctx.dram["L1a_bias_a"].rearrange("(p o) -> p o", o=1))
        dl.inc(ins, 16)

    ctx.on("gpsimd", _)
    pa_ld = dl.n

    def _(vector):
        vector.wait_ge(dl.h, pa_ld)
        ins = vector.tensor_scalar(out=pa[:, 0:2], in0=pa[:, 0:2], scalar1=0.01,
                                   scalar2=None, op0=ALU.max)
        t = Ev.inc(ins, 1)
        vector.wait_ge(Ev.h, t)
        ins = vector.tensor_scalar(out=pa[:, 2:3], in0=pa[:, 0:1], scalar1=-1.0,
                                   scalar2=None, op0=ALU.mult)
        t = Ev.inc(ins, 1)
        vector.wait_ge(Ev.h, t)

    ctx.on("vector", _)
    pa_tok = Ev.n
    st_tok = 0

    for ch in range(4):
        csl = slice(ch * 2112, (ch + 1) * 2112)
        dl2 = dl

        def _(g, ch=ch, st_tok=st_tok):
            if st_tok:
                g.wait_ge(ctx.dma["gpsimd"].h, st_tok)
            src = ctx.dram["c_distR"][0:1, ch * 2112:(ch + 1) * 2112].to_broadcast([16, 2112])
            ins = g.dma_start(out=dch[:], in_=src)
            dl2.inc(ins, 16)

        ctx.on("gpsimd", _)
        d_tok = dl2.n

        def _(vector, d_tok=d_tok):
            vector.wait_ge(dl.h, d_tok)
            vector.wait_ge(Ev.h, pa_tok)
            ins = vector.tensor_scalar(out=dch[:], in0=dch[:], scalar1=pa[:, 1:2],
                                       scalar2=None, op0=ALU.mult)
            t = Ev.inc(ins, 1)
            vector.wait_ge(Ev.h, t)

        ctx.on("vector", _)
        a_tok = Ev.n

        def _(scalar, a_tok=a_tok):
            scalar.wait_ge(Ev.h, a_tok)
            scalar.wait_ge(one_dep[0].h, one_dep[1])
            ins = scalar.activation(out=dch[:], in_=dch[:], func=AF.Ln, bias=one_ap)
            Es.inc(ins, 1)

        ctx.on("scalar", _)
        ln_tok = Es.n

        def _(vector, ln_tok=ln_tok):
            vector.wait_ge(Es.h, ln_tok)
            ins = vector.tensor_scalar(out=dch[:], in0=dch[:], scalar1=pa[:, 2:3],
                                       scalar2=None, op0=ALU.mult)
            t = Ev.inc(ins, 1)
            vector.wait_ge(Ev.h, t)

        ctx.on("vector", _)
        m_tok = Ev.n

        def _(scalar, m_tok=m_tok):
            scalar.wait_ge(Ev.h, m_tok)
            ins = scalar.activation(out=ech[:], in_=dch[:], func=AF.Exp)
            Es.inc(ins, 1)

        ctx.on("scalar", _)
        e_tok = Es.n

        def _(g, ch=ch, e_tok=e_tok):
            g.wait_ge(Es.h, e_tok)
            ins = g.dma_start(out=kbext[:, ch * 2112:(ch + 1) * 2112], in_=ech[:])
            ctx.dma["gpsimd"].inc(ins, 16)

        ctx.on("gpsimd", _)
        st_tok = ctx.dma["gpsimd"].n
    scope.close()
    ctx.barrier()
    return kbext, (ctx.dma["gpsimd"], st_tok)


def make_l1_prob_mult(ctx, kbext, kb_dep):
    from concourse.bass_types import AP as BassAP
    E_sb = ctx.tmp("l1_E", [128, 6144], BF16)
    state = {"h": -1, "tok": 0, "last_read": 0}

    def prob_mult(h, qt, kt):
        Ev = ctx.E["vector"]
        if state["h"] != h:
            def _(sc):
                sc.wait_ge(kb_dep[0].h, kb_dep[1])
                if state["last_read"]:
                    sc.wait_ge(Ev.h, state["last_read"])
                src = BassAP(tensor=kbext.tensor, offset=h * 8448,
                             ap=[[1, 128], [1, 6144]])
                ins = sc.dma_start(out=E_sb[:], in_=src)
                ctx.dE.inc(ins, 16)

            ctx.on("scalar", _)
            state["h"] = h
            state["tok"] = ctx.dE.n
        m0 = 4095 - kt * 128 + qt * 512
        state["last_read"] = Ev.n + 1
        view = BassAP(tensor=E_sb.ap().tensor, offset=6143 - m0,
                      ap=[[6144, 128], [-1, 512]])
        return view, [(ctx.dE, state["tok"])]

    return prob_mult


def emit_l1_attn(ctx, xn_tok):
    C128 = ctx.tmp("rope_C", [128, T], BF16)
    S128 = ctx.tmp("rope_S", [128, T], BF16)
    rs_scope = ExitStack()
    c32 = rs_scope.enter_context(ctx.nc.sbuf_tensor(f"l1_c32{ctx.uid()}", [32, T], BF16))
    s32 = rs_scope.enter_context(ctx.nc.sbuf_tensor(f"l1_s32{ctx.uid()}", [32, T], BF16))
    src_dep = emit_l1_rope_sources(ctx, rs_scope, c32, s32)
    C128, S128, tab_dep = emit_rope_tables(ctx, c32, s32, src_dep, C128, S128)
    rs_scope.close()
    ctx.barrier()
    kbext, kb_dep = emit_l1_kb_prep(ctx)
    pm = make_l1_prob_mult(ctx, kbext, kb_dep)
    qkvT = ctx.dram["L1a_qkvT"]
    return emit_attn_dense(
        ctx, "l1", xn_tok,
        WDirect(qkvT[:, 0:D], D, D),
        WDirect(qkvT[:, D:2 * D], D, D),
        WDirect(qkvT[:, 2 * D:3 * D], D, D),
        WDirect(ctx.dram["L1a_outT"], D, D),
        exp_scale=0.125, rope_tabs=(C128, S128, tab_dep), prob_mult=pm)


L1A_INPUT_SPECS = [
    ("L1a_qkvT", [D, 3 * D], F32), ("L1a_outT", [D, D], F32),
    ("L1a_freqs", [32], F32), ("L1a_bias_p", [16], F32), ("L1a_bias_a", [16], F32),
]


def l1_attn_host_inputs(inp):
    qkv_w = np.asarray(inp["L1_attn_qkv_w"], dtype=np.float32)
    out_w = np.asarray(inp["L1_attn_out_w"], dtype=np.float32)
    return {
        "L1a_qkvT": np.ascontiguousarray(qkv_w.T),
        "L1a_outT": np.ascontiguousarray(out_w.T),
        "L1a_freqs": np.asarray(inp["L1_attn_rope_freqs"], dtype=np.float32),
        "L1a_bias_p": np.ascontiguousarray(np.asarray(inp["L1_attn_bias_p"], dtype=np.float32).reshape(16)),
        "L1a_bias_a": np.ascontiguousarray(np.asarray(inp["L1_attn_bias_a"], dtype=np.float32).reshape(16)),
    }


def percore_consts(core):
    my0 = (core % 2) * T
    out = {
        "c_slot": np.array([[core % 2]], dtype=np.int32),
        "c_mypos": np.arange(my0, my0 + T, dtype=np.float32).reshape(1, T),
        "c_distR": np.abs(np.arange(8448, dtype=np.float32) - 2048.0 - my0).reshape(1, 8448),
    }
    return out



def emit_l1_ffn(ctx, xn_tok, tag):
    nc = ctx.nc
    Et, Es, Ev = ctx.E["tensor"], ctx.E["scalar"], ctx.E["vector"]
    ds = ctx.dma["sync"]
    xn_dep = [(Ev, xn_tok)]
    FH = 4096

    sh_s = nc.dram_tensor(f"sh_s_{tag}", [2, 1, T], F32, kind="Internal",
                          addr_space="Shared").ap()
    wr_sb = ctx.tmp("mf_wr", [128, 8], F32)
    wr_bf = ctx.tmp("mf_wrb", [128, 8], BF16)
    s_row = ctx.tmp("mf_s", [1, T], F32)
    cnt_sb = ctx.tmp("mf_cnt", [128, 32], F32)
    cm_sb = ctx.tmp("mf_cm", [128, 33], F32)
    tb_sb = ctx.tmp("mf_tb", [128, 1], F32)
    r_sb = ctx.tmp("mf_r", [1, 16], F32)
    wm_row = ctx.tmp("mf_wm", [1, T], F32)
    wm_bf = ctx.tmp("mf_wmb", [1, T], BF16)
    wmh_bc = ctx.tmp("mf_wmbc", [128, T], BF16)
    h_sb = ctx.tmp("mf_h", [128, 4 * T], BF16)
    u_sb = ctx.tmp("mf_u", [128, 512], BF16)
    sq_s = ctx.tmp("mf_sq", [128, 512], BF16)
    z_s = ctx.tmp("mf_z", [128, 512], BF16)
    th_s = ctx.tmp("mf_th", [128, 512], BF16)

    wr_dep = emit_load_vec(ctx, ctx.dram["L1f_wr"], D, wr_sb[:])

    def _(scalar):
        scalar.wait_ge(wr_dep[0].h, wr_dep[1])
        ins = scalar.activation(out=wr_bf[:], in_=wr_sb[:], func=AF.Copy)
        Es.inc(ins, 1)

    ctx.on("scalar", _)
    wrb_dep = (Es, Es.n)
    b1_row, b1_dep = emit_bias_row_load(ctx, ctx.dram["L1f_b1"], FH)
    b2_row, b2_dep = emit_bias_row_load(ctx, ctx.dram["L1f_b2"], D)
    ev_s = ctx.new_event()

    for tt in range(4):
        def _(tensor, tt=tt):
            wait_dep_pair(tensor, wrb_dep)
            for s_, t_ in xn_dep:
                tensor.wait_ge(s_.h, t_)
            ctx.psum_wait(tensor, 4)
            for kc in range(NCH):
                ins = tensor.matmul(ctx.ps[4][0:1, :], wr_bf[:, kc:kc + 1],
                                    ctx.xn_sb[:, kc * T + tt * 512: kc * T + (tt + 1) * 512],
                                    start=(kc == 0), stop=(kc == NCH - 1))
            Et.inc(ins, 1)

        ctx.on("tensor", _)
        mm = Et.n

        def _(scalar, tt=tt, mm=mm):
            scalar.wait_ge(Et.h, mm)
            ins = scalar.activation(out=s_row[0:1, tt * 512:(tt + 1) * 512],
                                    in_=ctx.ps[4][0:1, :], func=AF.Copy)
            Es.inc(ins, 1)

        ctx.on("scalar", _)
        ctx.psum_set_owner(4, Es, Es.n)
    s_tok = Es.n

    def emit_sstore(sync, slot):
        sync.dma_start(out=sh_s[slot], in_=s_row[:]).then_inc(ds.h, 16)

    def _(sync):
        sync.wait_ge(Es.h, s_tok)

    ctx.on("sync", _)
    ctx.by_parity("sync", emit_sstore)
    ds.n += 16
    ctx.mbox_flag(ev_s, dep_waits=[(ds, ds.n)])
    xrdy = ctx.mbox_wait(ev_s)
    dcnt = ctx.new_ldsem()

    def _(sync):
        sync.wait_ge(ctx.xrdy.h, xrdy)
        for sl in range(2):
            ins = sync.dma_start(out=cnt_sb[:, sl * 16:(sl + 1) * 16],
                                 in_=sh_s[sl, 0, :].rearrange("(p c) -> p c", p=128))
            dcnt.inc(ins, 16)

    ctx.on("sync", _)
    cnt_tok = dcnt.n

    def _(vector):
        vector.memset(r_sb[0:1, 0:1], -16384.0)
        ins = vector.memset(r_sb[0:1, 1:2], 16384.0)
        Ev.inc(ins, 1)
        vector.wait_ge(Ev.h, Ev.n)

    ctx.on("vector", _)

    for it in range(48):
        def _(vector):
            ins = vector.tensor_tensor(out=r_sb[0:1, 2:3], in0=r_sb[0:1, 0:1],
                                       in1=r_sb[0:1, 1:2], op=ALU.add)
            t = Ev.inc(ins, 1)
            vector.wait_ge(Ev.h, t)
            ins = vector.tensor_scalar(out=r_sb[0:1, 2:3], in0=r_sb[0:1, 2:3],
                                       scalar1=0.5, scalar2=None, op0=ALU.mult)
            t = Ev.inc(ins, 1)
            vector.wait_ge(Ev.h, t)

        ctx.on("vector", _)
        t_tok = Ev.n

        def _(tensor, t_tok=t_tok):
            tensor.wait_ge(Ev.h, t_tok)
            ctx.psum_wait(tensor, 5)
            ins = tensor.matmul(ctx.ps[5][:, 0:1], ctx.ones_rowf[:], r_sb[0:1, 2:3],
                                start=True, stop=True)
            Et.inc(ins, 1)

        ctx.on("tensor", _)
        bc_mm = Et.n

        def _(vector, bc_mm=bc_mm):
            vector.wait_ge(Et.h, bc_mm)
            vector.wait_ge(dcnt.h, cnt_tok)
            ins = vector.tensor_copy(out=tb_sb[:], in_=ctx.ps[5][:, 0:1])
            t = Ev.inc(ins, 1)
            vector.wait_ge(Ev.h, t)
            ins = vector.tensor_scalar(out=cm_sb[:, 0:32], in0=cnt_sb[:],
                                       scalar1=tb_sb[:], scalar2=None, op0=ALU.is_ge,
                                       op1=ALU.add, accum_out=cm_sb[:, 32:33])
            t = Ev.inc(ins, 1)
            vector.wait_ge(Ev.h, t)

        ctx.on("vector", _)
        ctx.psum_set_owner(5, Ev, Ev.n)
        c_tok = Ev.n

        def _(tensor, c_tok=c_tok):
            tensor.wait_ge(Ev.h, c_tok)
            ctx.psum_wait(tensor, 5)
            ins = tensor.matmul(ctx.ps[5][0:1, 0:1], ctx.ones_colf[:], cm_sb[:, 32:33],
                                start=True, stop=True)
            Et.inc(ins, 1)

        ctx.on("tensor", _)
        cnt_mm = Et.n

        def _(vector, cnt_mm=cnt_mm):
            vector.wait_ge(Et.h, cnt_mm)
            ins = vector.tensor_scalar(out=r_sb[0:1, 3:4], in0=ctx.ps[5][0:1, 0:1],
                                       scalar1=2048.0, scalar2=None, op0=ALU.is_ge)
            t = Ev.inc(ins, 1)
            vector.wait_ge(Ev.h, t)
            ins = vector.tensor_scalar(out=r_sb[0:1, 4:5], in0=r_sb[0:1, 3:4],
                                       scalar1=-1.0, scalar2=1.0, op0=ALU.mult,
                                       op1=ALU.add)
            t = Ev.inc(ins, 1)
            vector.wait_ge(Ev.h, t)
            ins = vector.tensor_tensor(out=r_sb[0:1, 5:6], in0=r_sb[0:1, 3:4],
                                       in1=r_sb[0:1, 2:3], op=ALU.mult)
            t = Ev.inc(ins, 1)
            vector.wait_ge(Ev.h, t)
            ins = vector.tensor_tensor(out=r_sb[0:1, 0:1], in0=r_sb[0:1, 0:1],
                                       in1=r_sb[0:1, 4:5], op=ALU.mult)
            t = Ev.inc(ins, 1)
            vector.wait_ge(Ev.h, t)
            ins = vector.tensor_tensor(out=r_sb[0:1, 0:1], in0=r_sb[0:1, 0:1],
                                       in1=r_sb[0:1, 5:6], op=ALU.add)
            t = Ev.inc(ins, 1)
            vector.wait_ge(Ev.h, t)
            ins = vector.tensor_tensor(out=r_sb[0:1, 5:6], in0=r_sb[0:1, 4:5],
                                       in1=r_sb[0:1, 2:3], op=ALU.mult)
            t = Ev.inc(ins, 1)
            vector.wait_ge(Ev.h, t)
            ins = vector.tensor_tensor(out=r_sb[0:1, 1:2], in0=r_sb[0:1, 1:2],
                                       in1=r_sb[0:1, 3:4], op=ALU.mult)
            t = Ev.inc(ins, 1)
            vector.wait_ge(Ev.h, t)
            ins = vector.tensor_tensor(out=r_sb[0:1, 1:2], in0=r_sb[0:1, 1:2],
                                       in1=r_sb[0:1, 5:6], op=ALU.add)
            t = Ev.inc(ins, 1)
            vector.wait_ge(Ev.h, t)

        ctx.on("vector", _)
    thr_tok = Ev.n

    def _(scalar):
        scalar.wait_ge(Es.h, s_tok)
        ins = scalar.activation(out=wm_row[:], in_=s_row[:], func=AF.Sigmoid)
        Es.inc(ins, 1)

    ctx.on("scalar", _)
    sig_tok = Es.n

    def _(vector):
        vector.wait_ge(Es.h, sig_tok)
        vector.wait_ge(Ev.h, thr_tok)
        ins = vector.tensor_scalar(out=s_row[:], in0=s_row[:], scalar1=r_sb[0:1, 0:1],
                                   scalar2=None, op0=ALU.is_ge)
        t = Ev.inc(ins, 1)
        vector.wait_ge(Ev.h, t)
        ins = vector.tensor_tensor(out=wm_row[:], in0=wm_row[:], in1=s_row[:],
                                   op=ALU.mult)
        t = Ev.inc(ins, 1)
        vector.wait_ge(Ev.h, t)

    ctx.on("vector", _)
    wm_tok = Ev.n

    def _(scalar):
        scalar.wait_ge(Ev.h, wm_tok)
        ins = scalar.activation(out=wm_bf[:], in_=wm_row[:], func=AF.Copy)
        Es.inc(ins, 1)

    ctx.on("scalar", _)
    wmbf_tok = Es.n

    for tt in range(4):
        def _(tensor, tt=tt):
            tensor.wait_ge(Ev.h, wm_tok)
            ctx.psum_wait(tensor, 5)
            ins = tensor.matmul(ctx.ps[5][:, :], ctx.ones_rowf[:],
                                wm_row[0:1, tt * 512:(tt + 1) * 512],
                                start=True, stop=True)
            Et.inc(ins, 1)

        ctx.on("tensor", _)
        mm = Et.n

        def _(scalar, tt=tt, mm=mm):
            scalar.wait_ge(Et.h, mm)
            ins = scalar.activation(out=wmh_bc[:, tt * 512:(tt + 1) * 512],
                                    in_=ctx.ps[5][:, :], func=AF.Copy)
            Es.inc(ins, 1)

        ctx.on("scalar", _)
        ctx.psum_set_owner(5, Es, Es.n)
    wmbc_tok = Es.n

    GC = 0.7978845608028654
    GA = 0.044715
    prev_w2_mm = [0]
    gel_prev = [0]

    for ob in range(8):
        def gelu_sink(oc, tt, psum_ap, mm_tok, ob=ob):
            lc = oc % 4

            def _(scalar):
                scalar.wait_ge(Et.h, mm_tok)
                if gel_prev[0]:
                    scalar.wait_ge(Ev.h, gel_prev[0])
                if prev_w2_mm[0]:
                    scalar.wait_ge(Et.h, prev_w2_mm[0])
                ins = scalar.activation(out=u_sb[:], in_=psum_ap, func=AF.Copy)
                Es.inc(ins, 1)

            ctx.on("scalar", _)
            u_tok = Es.n

            def _(vector, u_tok=u_tok):
                vector.wait_ge(Es.h, u_tok)
                ins = vector.tensor_tensor(out=sq_s[:], in0=u_sb[:], in1=u_sb[:],
                                           op=ALU.mult)
                t = Ev.inc(ins, 1)
                vector.wait_ge(Ev.h, t)
                ins = vector.tensor_tensor(out=sq_s[:], in0=sq_s[:], in1=u_sb[:],
                                           op=ALU.mult)
                t = Ev.inc(ins, 1)
                vector.wait_ge(Ev.h, t)
                ins = vector.scalar_tensor_tensor(out=z_s[:], in0=sq_s[:], scalar=GA,
                                                  in1=u_sb[:], op0=ALU.mult,
                                                  op1=ALU.add)
                t = Ev.inc(ins, 1)
                vector.wait_ge(Ev.h, t)

            ctx.on("vector", _)
            z_tok = Ev.n

            def _(scalar, z_tok=z_tok):
                scalar.wait_ge(Ev.h, z_tok)
                ins = scalar.activation(out=th_s[:], in_=z_s[:], func=AF.Tanh,
                                        scale=GC)
                Es.inc(ins, 1)

            ctx.on("scalar", _)
            th_tok = Es.n

            def _(vector, th_tok=th_tok, lc=lc, tt=tt):
                vector.wait_ge(Es.h, th_tok)
                ins = vector.scalar_tensor_tensor(out=th_s[:], in0=th_s[:], scalar=1.0,
                                                  in1=u_sb[:], op0=ALU.add,
                                                  op1=ALU.mult)
                t = Ev.inc(ins, 1)
                vector.wait_ge(Ev.h, t)
                ins = vector.scalar_tensor_tensor(
                    out=h_sb[:, lc * T + tt * 512: lc * T + (tt + 1) * 512],
                    in0=th_s[:], scalar=0.5,
                    in1=wmh_bc[:, tt * 512:(tt + 1) * 512],
                    op0=ALU.mult, op1=ALU.mult)
                t = Ev.inc(ins, 1)
                vector.wait_ge(Ev.h, t)

            ctx.on("vector", _)
            gel_prev[0] = Ev.n
            return (Ev, Ev.n)

        def gelu_sink_wrap(oc, tt, psum_ap, mm_tok):
            def _(scalar):
                scalar.wait_ge(Es.h, wmbc_tok)
            ctx.on("scalar", _)
            return gelu_sink(oc, tt, psum_ap, mm_tok)

        emit_proj(ctx, WDirect(ctx.dram["L1f_W1"][:, ob * 512:(ob + 1) * 512], D, 512),
                  gelu_sink_wrap, tensor_dep=xn_dep,
                  tail_mm=bias_tail_mm_off(ctx, b1_row, b1_dep, ob * 512))
        h_tok = Ev.n

        def w2_tail(tensor, oc, tt, bank_ap, ob=ob):
            if ob != 0:
                return None
            tensor.wait_ge(b2_dep[0].h, b2_dep[1])
            tensor.wait_ge(Es.h, wmbf_tok)
            return tensor.matmul(bank_ap, b2_row[:, oc * 128:(oc + 1) * 128],
                                 wm_bf[0:1, tt * 512:(tt + 1) * 512],
                                 start=False, stop=True)

        emit_proj(ctx, WDirect(ctx.dram["L1f_W2"][ob * 512:(ob + 1) * 512, :], 512, D),
                  sink_resid(ctx, 0.5),
                  src=lambda kc: h_sb[:, kc * T:(kc + 1) * T],
                  tensor_dep=[(Ev, h_tok)],
                  tail_mm=(w2_tail if ob == 0 else None))
        prev_w2_mm[0] = Et.n

    def _(vector):
        for d in range(NCH):
            xs = ctx.x_sb[:, d * T:(d + 1) * T]
            ins = vector.scalar_tensor_tensor(out=xs, in0=ctx.xn_sb[:, d * T:(d + 1) * T],
                                              scalar=0.5, in1=xs, op0=ALU.mult,
                                              op1=ALU.add)
            t = Ev.inc(ins, 1)
            vector.wait_ge(Ev.h, t)

    ctx.on("vector", _)
    return Ev.n


def bias_tail_mm_off(ctx, bias_row, bias_dep, col_off):

    def tail(tensor, oc, tt, bank_ap):
        tensor.wait_ge(bias_dep[0].h, bias_dep[1])
        o128 = bank_ap.shape[0]
        c0 = col_off + oc * 128
        return tensor.matmul(bank_ap, bias_row[:, c0:c0 + o128],
                             ctx.ones_row_t[0:1, 0:bank_ap.shape[1]],
                             start=False, stop=True)

    return tail


def wait_dep_pair(eng, dep):
    eng.wait_ge(dep[0].h, dep[1])



def emit_l3_attn(ctx, xn_tok):
    from concourse.bass_types import AP as BassAP
    Ev = ctx.E["vector"]
    C128 = ctx.tmp("rope_C", [128, T], BF16)
    S128 = ctx.tmp("rope_S", [128, T], BF16)
    cs_scope = ExitStack()
    c32 = cs_scope.enter_context(ctx.nc.sbuf_tensor(f"l3a_c32{ctx.uid()}", [32, T], BF16))
    s32 = cs_scope.enter_context(ctx.nc.sbuf_tensor(f"l3a_s32{ctx.uid()}", [32, T], BF16))
    dl = ctx.new_ldsem()

    def _(g):
        dl.inc(g.dma_start(out=c32[:], in_=ctx.dram["c_l3a_c32"]), 16)
        dl.inc(g.dma_start(out=s32[:], in_=ctx.dram["c_l3a_s32"]), 16)

    ctx.on("gpsimd", _)
    src_dep = [(dl, dl.n)]
    C128, S128, tab_dep = emit_rope_tables(ctx, c32, s32, src_dep, C128, S128)
    cs_scope.close()
    ctx.barrier()

    E_sb = ctx.tmp("l3a_E", [128, 2944], BF16)
    st = {"loaded": False}
    dEl = ctx.new_ldsem()

    def prob_mult(h, qt, kt):
        if kt < 4 * qt:
            return None
        if not st["loaded"]:
            def _(sc):
                src = BassAP(tensor=ctx.dram["c_stepE"].tensor, offset=1537,
                             ap=[[1, 128], [1, 2944]])
                ins = sc.dma_start(out=E_sb[:], in_=src)
                dEl.inc(ins, 16)

            ctx.on("scalar", _)
            st["loaded"] = True
            st["tok"] = dEl.n
        m0 = 4095 - kt * 128 + qt * 512
        view = BassAP(tensor=E_sb.ap().tensor, offset=4606 - m0,
                      ap=[[2944, 128], [-1, 512]])
        return view, [(dEl, st["tok"])]

    def skip_tile(h, qt, kt):
        return kt >= 4 * qt + 20

    return emit_attn_dense(
        ctx, "l3a", xn_tok,
        WDirect(ctx.dram["L3a_wq"], D, D),
        WDirect(ctx.dram["L3a_wk"], D, 256),
        WDirect(ctx.dram["L3a_wv"], D, 256),
        WDirect(ctx.dram["L3a_wo"], D, D),
        exp_scale=0.125, kv_width=256, kv_group=4, kv_mod=True,
        rope_tabs=(C128, S128, tab_dep),
        prob_mult=prob_mult, skip_tile=skip_tile)


L3A_INPUT_SPECS = [
    ("L3a_wq", [D, D], F32), ("L3a_wk", [D, 256], F32),
    ("L3a_wv", [D, 256], F32), ("L3a_wo", [D, D], F32),
]


def l3a_percore_consts(core):
    my0 = (core % 2) * T
    pos = np.arange(my0, my0 + T, dtype=np.float32)
    invf = 1.0 / (10000.0 ** (np.arange(32, dtype=np.float32) / 32.0))
    ang = invf[:, None] * pos[None, :]
    return {
        "c_l3a_c32": np.cos(ang).astype(bf16),
        "c_l3a_s32": np.sin(ang).astype(bf16),
        "c_stepE": (np.arange(8448) <= 2048 + my0).astype(bf16).reshape(1, 8448),
    }



FP4_MIDS = (0.25, 0.75, 1.25, 1.75)
FP4_MIDS2 = (2.5, 3.5)
FP4_MID3 = 5.0


def emit_fp4_quant(ctx, src_ap, out_ap, ncols, scr, dep=None):
    Ev, Es = ctx.E["vector"], ctx.E["scalar"]
    nb = ncols // 16
    vq, acc, acc2, acc3, sgn, scale = (scr["vq"], scr["acc"], scr["acc2"],
                                       scr["acc3"], scr["sgn"], scr["scale"])

    def _(vector):
        if dep:
            for s_, t_ in dep:
                vector.wait_ge(s_.h, t_)
        ins = vector.tensor_reduce(out=scale[0:128, 0:nb].rearrange("p (a o) -> p a o", o=1),
                                   in_=src_ap.rearrange("p (a w) -> p a w", w=16),
                                   axis=AX.X, op=ALU.max, apply_absolute_value=True)
        t = Ev.inc(ins, 1)
        vector.wait_ge(Ev.h, t)
        ins = vector.tensor_scalar(out=scale[0:128, 0:nb], in0=scale[0:128, 0:nb],
                                   scalar1=1.0 / 6.0, scalar2=1e-12,
                                   op0=ALU.mult, op1=ALU.max)
        Ev.inc(ins, 1)

    ctx.on("vector", _)
    v_tok = Ev.n

    def _(scalar):
        scalar.wait_ge(Ev.h, v_tok)
        scalar.activation(out=sgn[0:128, 0:ncols], in_=src_ap, func=AF.Sign)
        ins = scalar.activation(out=vq[0:128, 0:ncols], in_=src_ap, func=AF.Abs)
        Es.inc(ins, 1)

    ctx.on("scalar", _)
    sa_tok = Es.n

    def _(vector):
        vector.wait_ge(Es.h, sa_tok)
        av = vq[0:128, 0:ncols].rearrange("p (a w) -> p a w", w=16)
        thr = scale[0:128, nb:2 * nb]
        thr_bc = thr.rearrange("p (a o) -> p a o", o=1).to_broadcast([128, nb, 16])

        def sw(ins):
            t = Ev.inc(ins, 1)
            vector.wait_ge(Ev.h, t)

        def cmp_to(dst, m):
            sw(vector.tensor_scalar(out=thr, in0=scale[0:128, 0:nb], scalar1=m,
                                    scalar2=None, op0=ALU.mult))
            sw(vector.tensor_tensor(out=dst.rearrange("p (a w) -> p a w", w=16),
                                    in0=av, in1=thr_bc, op=ALU.is_gt))

        a_, a2, a3 = acc[0:128, 0:ncols], acc2[0:128, 0:ncols], acc3[0:128, 0:ncols]
        cmp_to(a_, FP4_MIDS[0])
        for m in FP4_MIDS[1:]:
            cmp_to(a3, m)
            sw(vector.tensor_tensor(out=a_, in0=a_, in1=a3, op=ALU.add))
        cmp_to(a2, FP4_MIDS2[0])
        cmp_to(a3, FP4_MIDS2[1])
        sw(vector.tensor_tensor(out=a2, in0=a2, in1=a3, op=ALU.add))
        cmp_to(a3, FP4_MID3)
        sw(vector.scalar_tensor_tensor(out=a2, in0=a3, scalar=2.0, in1=a2,
                                       op0=ALU.mult, op1=ALU.add))
        sw(vector.scalar_tensor_tensor(out=a_, in0=a_, scalar=0.5, in1=a2,
                                       op0=ALU.mult, op1=ALU.add))
        sw(vector.tensor_tensor(out=a_, in0=a_, in1=sgn[0:128, 0:ncols], op=ALU.mult))
        sbc = scale[0:128, 0:nb].rearrange("p (a o) -> p a o", o=1).to_broadcast(
            [128, nb, 16])
        sw(vector.tensor_tensor(out=out_ap.rearrange("p (a w) -> p a w", w=16),
                                in0=a_.rearrange("p (a w) -> p a w", w=16),
                                in1=sbc, op=ALU.mult))

    ctx.on("vector", _)
    return (Ev, Ev.n)


def l2f_quant_scratch(ctx, ncols):
    nb = ncols // 16
    return {
        "vq": ctx.tmp("q_vq", [128, ncols], F32),
        "acc": ctx.tmp("q_acc", [128, ncols], BF16),
        "acc2": ctx.tmp("q_acc2", [128, ncols], BF16),
        "acc3": ctx.tmp("q_acc3", [128, ncols], BF16),
        "sgn": ctx.tmp("q_sgn", [128, ncols], BF16),
        "scale": ctx.tmp("q_scale", [128, 2 * nb], F32),
    }


def emit_l2f_weight_prep(ctx):
    nc = ctx.nc
    scr = {}
    for nm, shape in (("wg", (2816, D)), ("wu", (2816, D)), ("wd", (D, 2816))):
        scr[nm] = nc.dram_tensor(f"scr_{nm}_{ctx.uid()}", list(shape), BF16,
                                 kind="Internal").ap()
    scope = ExitStack()
    le = scope.enter_context
    wt = le(nc.sbuf_tensor(f"l2p_wt{ctx.uid()}", [128, 2816], F32))
    wq = le(nc.sbuf_tensor(f"l2p_wq{ctx.uid()}", [128, 2816], BF16))
    qscr = {
        "vq": le(nc.sbuf_tensor(f"l2p_vq{ctx.uid()}", [128, 2816], F32)),
        "acc": le(nc.sbuf_tensor(f"l2p_a{ctx.uid()}", [128, 2816], BF16)),
        "acc2": le(nc.sbuf_tensor(f"l2p_a2{ctx.uid()}", [128, 2816], BF16)),
        "acc3": le(nc.sbuf_tensor(f"l2p_a3{ctx.uid()}", [128, 2816], BF16)),
        "sgn": le(nc.sbuf_tensor(f"l2p_sg{ctx.uid()}", [128, 2816], BF16)),
        "scale": le(nc.sbuf_tensor(f"l2p_sc{ctx.uid()}", [128, 2 * 176], F32)),
    }
    dl = ctx.new_ldsem()
    dst_sem = ctx.new_ldsem()
    jobs = []
    for r0 in range(0, 2816, 128):
        jobs.append(("L2f_w_gate", scr["wg"], r0, D))
        jobs.append(("L2f_w_up", scr["wu"], r0, D))
    for r0 in range(0, D, 128):
        jobs.append(("L2f_w_down", scr["wd"], r0, 2816))
    prev_q = [0]
    for src_name, dst, r0, ncols in jobs:
        def _(sync, src_name=src_name, r0=r0, ncols=ncols):
            if prev_q[0]:
                sync.wait_ge(ctx.E["vector"].h, prev_q[0])
            ins = sync.dma_start(out=wt[:, 0:ncols],
                                 in_=ctx.dram[src_name][r0:r0 + 128, :])
            dl.inc(ins, 16)

        ctx.on("sync", _)
        ld_tok = dl.n
        qdep = emit_fp4_quant(ctx, wt[:, 0:ncols], wq[:, 0:ncols], ncols, qscr,
                              dep=[(dl, ld_tok), (dst_sem, dst_sem.n)])
        prev_q[0] = qdep[1]

        def _(sync, dst=dst, r0=r0, ncols=ncols, qdep=qdep):
            sync.wait_ge(qdep[0].h, qdep[1])
            ins = sync.dma_start(out=dst[r0:r0 + 128, :], in_=wq[:, 0:ncols])
            dst_sem.inc(ins, 16)

        ctx.on("sync", _)
    scope.close()
    ctx.barrier()
    return scr["wg"], scr["wu"], scr["wd"], (dst_sem, dst_sem.n)


def emit_l2_ffn(ctx, x_dep, scr_wg, scr_wu, scr_wd, wprep_dep, tag,
                tbs=None, dbg=None):
    nc = ctx.nc
    Et, Es, Ev = ctx.E["tensor"], ctx.E["scalar"], ctx.E["vector"]
    I2 = 2816
    if tbs is None:
        tbs = range(4)
    dbg_sem = [None]

    def dump(tb, name, buf_ap, wait):
        if not dbg or name not in dbg or tb != 0:
            return None
        if dbg_sem[0] is None:
            dbg_sem[0] = ctx.new_ldsem()
        ds_ = dbg_sem[0]

        def _(sync, buf_ap=buf_ap, wait=wait):
            sync.wait_ge(wait[0].h, wait[1])
            ins = sync.dma_start(out=dbg[name], in_=buf_ap)
            ds_.inc(ins, 16)

        ctx.on("sync", _)
        return (ds_, ds_.n)
    xn_t = ctx.tmp("l2_xnt", [128, 4 * 1024], F32)
    xq_t = ctx.tmp("l2_xqt", [128, 4 * 1024], BF16)
    xq_f = ctx.tmp("l2_xqf", [128, 8 * 512], BF16)
    h_t = ctx.tmp("l2_ht", [128, 4 * I2], BF16)
    g_blk = ctx.tmp("l2_g", [128, 4 * 512], F32)
    hqT = ctx.tmp("l2_hqT", [128, 512], BF16)
    lns = ctx.tmp("l2_lns", [128, 8], F32)
    qscr = l2f_quant_scratch(ctx, 1024)
    nb_sizes = [512, 512, 512, 512, 512, 256]
    wsg = WTransposedScratch(scr_wg, D, I2)
    wsu = WTransposedScratch(scr_wu, D, I2)

    prev_phase = list(x_dep)

    for tb in tbs:
        for j in range(4):
            for c in range(NCH):
                bank = 4 + ((j * NCH + c) % 2)

                def _(tensor, j=j, c=c, bank=bank):
                    for s_, t_ in prev_phase:
                        tensor.wait_ge(s_.h, t_)
                    ctx.psum_wait(tensor, bank)
                    ins = tensor.transpose(
                        ctx.ps[bank][:, 0:128],
                        ctx.x_sb[:, c * T + (tb * 4 + j) * 128: c * T + (tb * 4 + j) * 128 + 128],
                        ctx.identf[:])
                    Et.inc(ins, 1)

                ctx.on("tensor", _)
                mm = Et.n

                def _(scalar, j=j, c=c, mm=mm, bank=bank):
                    scalar.wait_ge(Et.h, mm)
                    ins = scalar.activation(out=xn_t[:, j * 1024 + c * 128: j * 1024 + (c + 1) * 128],
                                            in_=ctx.ps[bank][:, 0:128], func=AF.Copy)
                    Es.inc(ins, 1)

                ctx.on("scalar", _)
                ctx.psum_set_owner(bank, Es, Es.n)
        xt_tok = Es.n

        for j in range(4):
            xs = xn_t[:, j * 1024:(j + 1) * 1024]

            def _(vector, j=j, xs=xs):
                if j == 0:
                    vector.wait_ge(Es.h, xt_tok)

                def sw(ins):
                    t = Ev.inc(ins, 1)
                    vector.wait_ge(Ev.h, t)

                sw(vector.tensor_reduce(out=lns[:, 0:1].rearrange("p (a o) -> p a o", o=1),
                                        in_=xs.rearrange("p (a w) -> p a w", a=1),
                                        axis=AX.X, op=ALU.add))
                sw(vector.tensor_scalar(out=lns[:, 2:3], in0=lns[:, 0:1],
                                        scalar1=1.0 / D, scalar2=None, op0=ALU.mult))
                sw(vector.tensor_tensor(out=qscr["vq"][:, 0:1024], in0=xs, in1=xs,
                                        op=ALU.mult))
                sw(vector.tensor_reduce(out=lns[:, 1:2].rearrange("p (a o) -> p a o", o=1),
                                        in_=qscr["vq"][:, 0:1024].rearrange("p (a w) -> p a w", a=1),
                                        axis=AX.X, op=ALU.add))
                sw(vector.tensor_tensor(out=lns[:, 4:5], in0=lns[:, 2:3],
                                        in1=lns[:, 2:3], op=ALU.mult))
                sw(vector.scalar_tensor_tensor(out=lns[:, 3:4], in0=lns[:, 1:2],
                                               scalar=1.0 / D, in1=lns[:, 4:5],
                                               op0=ALU.mult, op1=ALU.subtract))
                sw(vector.tensor_scalar(out=lns[:, 3:4], in0=lns[:, 3:4],
                                        scalar1=LN_EPS, scalar2=None, op0=ALU.add))

            ctx.on("vector", _)
            var_tok = Ev.n

            def _(scalar, var_tok=var_tok):
                scalar.wait_ge(Ev.h, var_tok)
                ins = scalar.activation(out=lns[:, 3:4], in_=lns[:, 3:4], func=AF.Ln)
                t = Es.inc(ins, 1)
                scalar.wait_ge(Es.h, t)
                ins = scalar.activation(out=lns[:, 3:4], in_=lns[:, 3:4],
                                        func=AF.Exp, scale=-0.5)
                Es.inc(ins, 1)

            ctx.on("scalar", _)
            rstd_tok = Es.n

            def _(vector, xs=xs, rstd_tok=rstd_tok):
                vector.wait_ge(Es.h, rstd_tok)
                mu_bc = lns[:, 2:3].rearrange("p (a o) -> p a o", o=1).to_broadcast(
                    [128, 1, 1024])
                rs_bc = lns[:, 3:4].rearrange("p (a o) -> p a o", o=1).to_broadcast(
                    [128, 1, 1024])
                ins = vector.tensor_tensor(out=xs.rearrange("p (a w) -> p a w", a=1),
                                           in0=xs.rearrange("p (a w) -> p a w", a=1),
                                           in1=mu_bc, op=ALU.subtract)
                t = Ev.inc(ins, 1)
                vector.wait_ge(Ev.h, t)
                ins = vector.tensor_tensor(out=xs.rearrange("p (a w) -> p a w", a=1),
                                           in0=xs.rearrange("p (a w) -> p a w", a=1),
                                           in1=rs_bc, op=ALU.mult)
                t = Ev.inc(ins, 1)
                vector.wait_ge(Ev.h, t)

            ctx.on("vector", _)
        ln_tok = Ev.n
        dump(tb, "xnt", xn_t[:, 0:4096], (Ev, ln_tok))

        qdep = (Ev, ln_tok)
        for qq in range(4):
            qdep = emit_fp4_quant(ctx, xn_t[:, qq * 1024:(qq + 1) * 1024],
                                  xq_t[:, qq * 1024:(qq + 1) * 1024], 1024, qscr,
                                  dep=[qdep])
        xq_tok = qdep[1]
        dump(tb, "xq", xq_t[:, 0:4096], (Ev, xq_tok))

        for c in range(NCH):
            for j in range(4):
                def _(tensor, j=j, c=c):
                    tensor.wait_ge(Ev.h, xq_tok)
                    ctx.psum_wait(tensor, "b0")
                    ins = tensor.transpose(
                        ctx.psb[:, 0:128],
                        xq_t[:, j * 1024 + c * 128: j * 1024 + (c + 1) * 128],
                        ctx.ident[:])
                    Et.inc(ins, 1)

                ctx.on("tensor", _)
                mm = Et.n

                def _(scalar, j=j, c=c, mm=mm):
                    scalar.wait_ge(Et.h, mm)
                    ins = scalar.activation(out=xq_f[:, c * 512 + j * 128: c * 512 + (j + 1) * 128],
                                            in_=ctx.psb[:, 0:128], func=AF.Copy)
                    Es.inc(ins, 1)

                ctx.on("scalar", _)
                ctx.psum_set_owner("b0", Es, Es.n)
        xqf_tok = Es.n
        dump(tb, "xqf", xq_f[:, :], (Es, xqf_tok))

        nb0 = 0
        hdep = None
        for nbi, ncols in enumerate(nb_sizes):
            for role in (0, 1):
                wsrc = wsg if role == 0 else wsu
                bi = (nbi * 2 + role) % 2
                ld = wsrc.load(ctx, nb0, ncols, bi)
                for j in range(4):
                    bank = ctx.next_bank()

                    def _(tensor, bi=bi, j=j, ncols=ncols, ld=ld):
                        tensor.wait_ge(ld[0].h, ld[1])
                        tensor.wait_ge(Es.h, xqf_tok)
                        ctx.psum_wait(tensor, bank)
                        for kc in range(NCH):
                            ins = tensor.matmul(
                                ctx.ps[bank][:, 0:ncols],
                                xq_f[:, kc * 512 + j * 128: kc * 512 + (j + 1) * 128],
                                ctx.wbuf[bi][:, kc * ncols:(kc + 1) * ncols],
                                start=(kc == 0), stop=(kc == NCH - 1))
                        Et.inc(ins, 1)

                    ctx.on("tensor", _)
                    mm = Et.n

                    if role == 0:
                        g_guard = hdep

                        def _(scalar, j=j, mm=mm, ncols=ncols, g_guard=g_guard):
                            scalar.wait_ge(Et.h, mm)
                            if g_guard is not None:
                                scalar.wait_ge(g_guard[0].h, g_guard[1])
                            ins = scalar.activation(out=g_blk[:, j * 512: j * 512 + ncols],
                                                    in_=ctx.ps[bank][:, 0:ncols],
                                                    func=AF.Silu)
                            Es.inc(ins, 1)

                        ctx.on("scalar", _)
                        ctx.psum_set_owner(bank, Es, Es.n)
                    else:
                        sg_tok = Es.n

                        def _(vector, j=j, mm=mm, ncols=ncols, sg_tok=sg_tok):
                            vector.wait_ge(Et.h, mm)
                            vector.wait_ge(Es.h, sg_tok)
                            ins = vector.tensor_tensor(
                                out=g_blk[:, j * 512: j * 512 + ncols],
                                in0=g_blk[:, j * 512: j * 512 + ncols],
                                in1=ctx.ps[bank][:, 0:ncols], op=ALU.mult)
                            t = Ev.inc(ins, 1)
                            vector.wait_ge(Ev.h, t)

                        ctx.on("vector", _)
                        ctx.psum_set_owner(bank, Ev, Ev.n)
                ctx.wbuf_state[bi] = (Et, Et.n)
            hdep = (Ev, Ev.n)
            for j in range(4):
                hdep = emit_fp4_quant(ctx, g_blk[:, j * 512: j * 512 + ncols],
                                      h_t[:, j * I2 + nb0: j * I2 + nb0 + ncols],
                                      ncols, qscr, dep=[hdep])
            nb0 += ncols
        hq_tok = hdep[1]
        dump(tb, "hq", h_t[:, :], (Ev, hq_tok))

        wsd_ap = scr_wd
        hq_prev = [0]
        for grp in range(2):
            def _(tensor):
                for b_ in range(4):
                    ctx.psum_wait(tensor, b_)

            ctx.on("tensor", _)
            for kc in range(22):
                bi = kc % 2
                dload = ctx.dwb[bi]
                prevst = ctx.wbuf_state[bi]

                def _(sc, kc=kc, grp=grp, bi=bi, dload=dload, prevst=prevst):
                    if prevst is not None:
                        sc.wait_ge(prevst[0].h, prevst[1])
                    ins = sc.dma_start_transpose(
                        ctx.wbuf[bi][:, 0:512],
                        wsd_ap[grp * 512:(grp + 1) * 512, kc * 128:(kc + 1) * 128])
                    dload.inc(ins, 16)

                ctx.on("scalar", _)
                wtok = dload.n

                for j in range(4):
                    def _(tensor, kc=kc, j=j):
                        tensor.wait_ge(Ev.h, hq_tok)
                        if hq_prev[0]:
                            tensor.wait_ge(Es.h, hq_prev[0])
                        ctx.psum_wait(tensor, "b0")
                        ins = tensor.transpose(
                            ctx.psb[:, 0:128],
                            h_t[:, j * I2 + kc * 128: j * I2 + (kc + 1) * 128],
                            ctx.ident[:])
                        Et.inc(ins, 1)

                    ctx.on("tensor", _)
                    mmT = Et.n

                    def _(scalar, j=j, mmT=mmT):
                        scalar.wait_ge(Et.h, mmT)
                        ins = scalar.activation(out=hqT[:, j * 128:(j + 1) * 128],
                                                in_=ctx.psb[:, 0:128], func=AF.Copy)
                        Es.inc(ins, 1)

                    ctx.on("scalar", _)
                    ctx.psum_set_owner("b0", Es, Es.n)
                hq_prev[0] = Es.n
                hqT_tok = Es.n

                def _(tensor, kc=kc, bi=bi, dload=dload, wtok=wtok, hqT_tok=hqT_tok):
                    tensor.wait_ge(dload.h, wtok)
                    tensor.wait_ge(Es.h, hqT_tok)
                    for ocl in range(4):
                        ins = tensor.matmul(ctx.ps[ocl][:, :],
                                            ctx.wbuf[bi][:, ocl * 128:(ocl + 1) * 128],
                                            hqT[:, 0:512],
                                            start=(kc == 0), stop=(kc == 21))
                        Et.inc(ins, 1)

                ctx.on("tensor", _)
                ctx.wbuf_state[bi] = (Et, Et.n)
            down_mm = Et.n

            for ocl in range(4):
                oc = grp * 4 + ocl

                def _(vector, oc=oc, ocl=ocl, down_mm=down_mm):
                    vector.wait_ge(Et.h, down_mm)
                    xs = ctx.x_sb[:, oc * T + tb * 512: oc * T + (tb + 1) * 512]
                    ins = vector.scalar_tensor_tensor(out=xs, in0=ctx.ps[ocl][:, :],
                                                      scalar=0.5, in1=xs, op0=ALU.mult,
                                                      op1=ALU.add)
                    t = Ev.inc(ins, 1)
                    vector.wait_ge(Ev.h, t)

                ctx.on("vector", _)
                ctx.psum_set_owner(ocl, Ev, Ev.n)
        prev_phase = [(Ev, Ev.n)]
    return Ev.n



L2A_INPUT_SPECS = [
    ("L2a_WqT", [D, D], F32), ("L2a_WkT", [D, D], F32),
    ("L2a_WvT", [D, D], F32), ("L2a_WoT", [D, D], F32),
    ("L2a_bq", [D], F32), ("L2a_bk", [D], F32),
    ("L2a_bv", [D], F32), ("L2a_bo", [D], F32),
    ("L2a_wqs", [128, 3], F32), ("L2a_wvs", [128, 3], F32),
    ("L2a_wk3", [128, 8 * 48], F32),
    ("L2a_bqs", [3], F32), ("L2a_bvs", [3], F32), ("L2a_bks48", [1, 48], F32),
    ("L2a_temp", [1, 1], F32),
]


def l2a_host_inputs(Wq, bq, Wk, bk, Wv, bv, Wo, bo, Wqs, bqs, Wks, bks,
                    Wvs, bvs, temp):
    wk3 = np.zeros((128, 8 * 48), dtype=np.float32)
    for kc in range(8):
        for hh in range(2):
            h = 2 * kc + hh
            for c in range(3):
                wk3[hh * 64:(hh + 1) * 64, kc * 48 + 3 * h + c] = Wks[c]
    dup = lambda w: np.concatenate([w.T, w.T], axis=0).astype(np.float32)
    return {
        "L2a_WqT": np.ascontiguousarray(Wq.T), "L2a_WkT": np.ascontiguousarray(Wk.T),
        "L2a_WvT": np.ascontiguousarray(Wv.T), "L2a_WoT": np.ascontiguousarray(Wo.T),
        "L2a_bq": bq, "L2a_bk": bk, "L2a_bv": bv, "L2a_bo": bo,
        "L2a_wqs": dup(Wqs), "L2a_wvs": dup(Wvs), "L2a_wk3": wk3,
        "L2a_bqs": bqs, "L2a_bvs": bvs,
        "L2a_bks48": np.tile(bks, 16).reshape(1, 48).astype(np.float32),
        "L2a_temp": np.array([[np.float32(temp)]], dtype=np.float32),
    }


def emit_l2_attn(ctx, tag, xn_tok):
    nc = ctx.nc
    Et, Es, Ev = ctx.E["tensor"], ctx.E["scalar"], ctx.E["vector"]
    ds = ctx.dma["sync"]
    xn_dep = [(Ev, xn_tok)]
    dr = ctx.dram

    sh_k = nc.dram_tensor(f"sh_k_{tag}", [2, D, T], BF16, kind="Internal",
                          addr_space="Shared").ap()
    sh_v = nc.dram_tensor(f"sh_v_{tag}", [2, T, D], BF16, kind="Internal",
                          addr_space="Shared").ap()
    sh_v3 = nc.dram_tensor(f"sh_v3_{tag}", [2, 48, T], BF16, kind="Internal",
                           addr_space="Shared").ap()
    sh_kn = nc.dram_tensor(f"sh_kn_{tag}", [2, 1, 16], F32, kind="Internal",
                           addr_space="Shared").ap()

    q_sb = ctx.tmp("l2a_q", [128, NCH * T], BF16)
    v3st = ctx.tmp("l2a_v3st", [3, 512], BF16)
    nsq = ctx.tmp("l2a_nsq", [128, 512], F32)
    nrmS = ctx.tmp("l2a_nrmS", [16, 64], F32)
    nrm = ctx.tmp("l2a_nrm", [16, 2], F32)
    nrm_bf = ctx.tmp("l2a_nrmbf", [16, 2], BF16)
    nrow = ctx.tmp("l2a_nrow", [1, 48], F32)
    knrow = ctx.tmp("l2a_knrow", [1, 16], F32)
    crow = ctx.tmp("l2a_crow", [1, 16], F32)
    crm = ctx.tmp("l2a_crm", [16, 9], F32)
    negM = ctx.tmp("l2a_negM", [128, 16], F32)
    k_sb = ctx.tmp("l2a_k", [128, S], BF16)
    v_aug = ctx.tmp("l2a_vaug", [128, 32 * 65], BF16)
    pr = ctx.tmp("l2a_pr", [128, 2 * 512], BF16)
    aug_pool = ctx.tmp("l2a_augp", [128, 4096], BF16)
    kchunk = aug_pool
    k3h = ctx.tmp("l2a_k3h", [48, 1024], BF16)
    v3h = ctx.tmp("l2a_v3h", [48, 1024], BF16)
    Ra = ctx.tmp("l2a_Ra", [48, 512], BF16)
    Rb = ctx.tmp("l2a_Rb", [48, 512], BF16)
    Rc = ctx.tmp("l2a_Rc", [48, 512], BF16)
    Rd = ctx.tmp("l2a_Rd", [48, 512], BF16)
    cross_all = ctx.tmp("l2a_cra", [48, S], BF16)
    cross_h = aug_pool
    qstp_h = ctx.tmp("l2a_qst", [3, T], BF16)
    wqs_sb = ctx.tmp("l2a_wqs", [128, 3], BF16)
    wvs_sb = ctx.tmp("l2a_wvs", [128, 3], BF16)
    wk3_sb = ctx.tmp("l2a_wk3", [128, 8 * 48], BF16)
    bqs_row = ctx.tmp("l2a_bqs", [1, 3], BF16)
    bvs_row = ctx.tmp("l2a_bvs", [1, 3], BF16)
    bks48_row = ctx.tmp("l2a_bks", [1, 48], BF16)
    temp_sb = ctx.tmp("l2a_tmp", [1, 1], F32)
    tneg = ctx.tmp("l2a_tneg", [1, 1], F32)
    temp_col = ctx.tmp("l2a_tcol", [128, 1], F32)
    mrow = ctx.tmp("l2a_mrow", [1, 8], F32)
    av_tmp = nsq
    rec_sb = ctx.tmp("l2a_rec", [1, 512], F32)[0:1, 0:512]

    ev_kv = ctx.new_event()

    sm = ctx.new_ldsem()

    def _(g):
        for dst, src in ((wqs_sb, dr["L2a_wqs"]), (wvs_sb, dr["L2a_wvs"]),
                         (wk3_sb, dr["L2a_wk3"]), (temp_sb, dr["L2a_temp"])):
            ins = g.dma_start(out=dst[:], in_=src)
            sm.inc(ins, 16)
        for dst, src in ((bqs_row, dr["L2a_bqs"]), (bvs_row, dr["L2a_bvs"])):
            ins = g.dma_start(out=dst[:], in_=src.rearrange("(o n) -> o n", o=1))
            sm.inc(ins, 16)
        ins = g.dma_start(out=bks48_row[:], in_=dr["L2a_bks48"])
        sm.inc(ins, 16)

    ctx.on("gpsimd", _)
    sm_tok = sm.n

    def _(scalar):
        scalar.wait_ge(sm.h, sm_tok)
        ins = scalar.activation(out=tneg[:], in_=temp_sb[:], func=AF.Abs)
        t = Es.inc(ins, 1)
        scalar.wait_ge(Es.h, t)
        ins = scalar.activation(out=tneg[:], in_=tneg[:], func=AF.Copy, scale=-1.0)
        Es.inc(ins, 1)

    ctx.on("scalar", _)
    tneg_tok = Es.n

    def _(tensor):
        tensor.wait_ge(sm.h, sm_tok)
        ctx.psum_wait(tensor, 5)
        ins = tensor.matmul(ctx.ps[5][:, 0:1], ctx.ones_rowf[:], temp_sb[:],
                            start=True, stop=True)
        Et.inc(ins, 1)

    ctx.on("tensor", _)
    tc_mm = Et.n

    def _(scalar):
        scalar.wait_ge(Et.h, tc_mm)
        ins = scalar.activation(out=temp_col[:], in_=ctx.ps[5][:, 0:1], func=AF.Copy)
        Es.inc(ins, 1)

    ctx.on("scalar", _)
    tcol_tok = Es.n
    ctx.psum_set_owner(5, Es, tcol_tok)

    def qk_sink_factory(store, col_base):
        def sink(oc, tt, psum_ap, mm_tok):
            ring = (oc * 4 + tt) % 2
            rs = ctx.rsem[ring]

            def _(scalar):
                scalar.wait_ge(Et.h, mm_tok)
                if store is None:
                    out = q_sb[:, oc * T + tt * 512: oc * T + (tt + 1) * 512]
                else:
                    out = v_aug[:, ring * 512:(ring + 1) * 512]
                    if rs.n:
                        scalar.wait_ge(rs.h, rs.n)
                ins = scalar.activation(out=out, in_=psum_ap, func=AF.Copy)
                Es.inc(ins, 1)
                ins = scalar.activation(out=nsq[:], in_=psum_ap, func=AF.Square)
                Es.inc(ins, 1)

            ctx.on("scalar", _)
            sq_tok = Es.n
            cp_tok = sq_tok - 1

            def _(tensor):
                tensor.wait_ge(Es.h, sq_tok)
                ctx.psum_wait(tensor, 4)
                ins = tensor.matmul(ctx.ps[4][0:16, :],
                                    ctx.ind16_sb[:, oc * 16:(oc + 1) * 16],
                                    nsq[:], start=True, stop=True)
                Et.inc(ins, 1)

            ctx.on("tensor", _)
            nm_tok = Et.n

            def _(vector):
                vector.wait_ge(Et.h, nm_tok)
                col = col_base + oc * 4 + tt
                ins = vector.reduce_max(out=nrmS[0:16, col:col + 1],
                                        in_=ctx.ps[4][0:16, :], axis=AX.X)
                t = Ev.inc(ins, 1)
                vector.wait_ge(Ev.h, t)

            ctx.on("vector", _)
            ctx.psum_set_owner(4, Ev, Ev.n)

            if store is not None:
                def emit_store(sync, slot):
                    tgt = sh_k[slot, oc * 128:(oc + 1) * 128, tt * 512:(tt + 1) * 512]
                    sync.dma_start(out=tgt,
                                   in_=v_aug[:, ring * 512:(ring + 1) * 512]).then_inc(rs.h, 16)

                def _(sync):
                    sync.wait_ge(Es.h, cp_tok)

                ctx.on("sync", _)
                ctx.by_parity("sync", emit_store)
                rs.n += 16
            return (Es, sq_tok)

        return sink

    bq_row = emit_bias_row_load(ctx, dr["L2a_bq"], D)
    bk_row = emit_bias_row_load(ctx, dr["L2a_bk"], D)
    bv_row = emit_bias_row_load(ctx, dr["L2a_bv"], D)
    bo_row = emit_bias_row_load(ctx, dr["L2a_bo"], D)

    emit_proj(ctx, WDirect(dr["L2a_WqT"], D, D), qk_sink_factory(None, 0),
              tensor_dep=xn_dep, tail_mm=bias_tail_mm(ctx, bq_row[0], bq_row[1]))
    q_done = Es.n
    emit_proj(ctx, WDirect(dr["L2a_WkT"], D, D), qk_sink_factory(True, 32),
              tensor_dep=xn_dep, tail_mm=bias_tail_mm(ctx, bk_row[0], bk_row[1]))
    qk_norm_acc = Ev.n
    k_store_deps = [(rs, rs.n) for rs in ctx.rsem[0:2]]

    def v_sink(tb, ob_i, psum_ap, mm_tok):
        ring = (tb + ob_i) % 2
        rs = ctx.rsem[4 + ring]
        ncols = psum_ap.shape[1]

        def _(scalar):
            scalar.wait_ge(Et.h, mm_tok)
            if rs.n:
                scalar.wait_ge(rs.h, rs.n)
            ins = scalar.activation(out=v_aug[:, 1024 + ring * 512: 1024 + ring * 512 + ncols],
                                    in_=psum_ap, func=AF.Copy)
            Es.inc(ins, 1)

        ctx.on("scalar", _)
        cp_tok = Es.n

        def emit_store(sync, slot):
            tgt = sh_v[slot, tb * 128:(tb + 1) * 128, ob_i * 512: ob_i * 512 + ncols]
            sync.dma_start(out=tgt,
                           in_=v_aug[:, 1024 + ring * 512: 1024 + ring * 512 + ncols]).then_inc(rs.h, 16)

        def _(sync):
            sync.wait_ge(Es.h, cp_tok)

        ctx.on("sync", _)
        ctx.by_parity("sync", emit_store)
        rs.n += 16
        return (Es, cp_tok)

    def v_tail(tensor, tb, ob_i, bank_ap):
        tensor.wait_ge(bv_row[1][0].h, bv_row[1][1])
        ncols = bank_ap.shape[1]
        return tensor.matmul(bank_ap, ctx.ones_row[:],
                             bv_row[0][0:1, ob_i * 512: ob_i * 512 + ncols],
                             start=False, stop=True)

    emit_proj_tokmaj(ctx, WDirect(dr["L2a_WvT"], D, D), v_sink,
                     tensor_dep=xn_dep, tail_mm=v_tail)
    v_store_deps = [(rs, rs.n) for rs in ctx.rsem[4:6]]

    v3_rs = ctx.rsem[2]
    v3prev_mm = [0]

    def v3_sink(oc, tt, psum_ap, mm_tok):
        def _(scalar):
            scalar.wait_ge(Et.h, mm_tok)
            if v3prev_mm[0]:
                scalar.wait_ge(Et.h, v3prev_mm[0])
            ins = scalar.activation(out=pr[:, 0:512], in_=psum_ap, func=AF.Copy)
            Es.inc(ins, 1)

        ctx.on("scalar", _)
        cp = Es.n
        for hh in range(2):
            h = oc * 2 + hh
            b0 = hh * 64

            def _(tensor, h=h, b0=b0, cp=cp):
                tensor.wait_ge(Es.h, cp)
                tensor.wait_ge(sm.h, sm_tok)
                ctx.psum_wait(tensor, 5)
                tensor.matmul(ctx.ps[5][0:3, :], wvs_sb[b0:b0 + 64, 0:3],
                              pr[b0:b0 + 64, 0:512], start=True, stop=False)
                ins = tensor.matmul(ctx.ps[5][0:3, :], bvs_row[0:1, 0:3],
                                    ctx.ones_row_t[0:1, 0:512], start=False, stop=True)
                Et.inc(ins, 1)

            ctx.on("tensor", _)
            v3mm = Et.n
            v3prev_mm[0] = v3mm

            def _(scalar, v3mm=v3mm):
                scalar.wait_ge(Et.h, v3mm)
                if v3_rs.n:
                    scalar.wait_ge(v3_rs.h, v3_rs.n)
                ins = scalar.activation(out=v3st[:], in_=ctx.ps[5][0:3, :], func=AF.Copy)
                Es.inc(ins, 1)

            ctx.on("scalar", _)
            stcp = Es.n
            ctx.psum_set_owner(5, Es, stcp)

            def emit_store(sync, slot, h=h, tt=tt):
                sync.dma_start(out=sh_v3[slot, 3 * h:3 * h + 3, tt * 512:(tt + 1) * 512],
                               in_=v3st[:]).then_inc(v3_rs.h, 16)

            def _(sync, stcp=stcp):
                sync.wait_ge(Es.h, stcp)

            ctx.on("sync", _)
            ctx.by_parity("sync", emit_store)
            v3_rs.n += 16
        return (Es, cp)

    emit_proj(ctx, WDirect(dr["L2a_WvT"], D, D), v3_sink, tensor_dep=xn_dep,
              tail_mm=bias_tail_mm(ctx, bv_row[0], bv_row[1]))
    v3_store_deps = [(v3_rs, v3_rs.n)]

    def _(vector):
        vector.wait_ge(Ev.h, qk_norm_acc)
        ins = vector.reduce_max(out=nrm[0:16, 0:1], in_=nrmS[0:16, 0:32], axis=AX.X)
        t = Ev.inc(ins, 1)
        vector.wait_ge(Ev.h, t)
        ins = vector.reduce_max(out=nrm[0:16, 1:2], in_=nrmS[0:16, 32:64], axis=AX.X)
        t = Ev.inc(ins, 1)
        vector.wait_ge(Ev.h, t)

    ctx.on("vector", _)
    nred_tok = Ev.n

    def _(scalar):
        scalar.wait_ge(Ev.h, nred_tok)
        ins = scalar.activation(out=nrm_bf[:], in_=nrm[0:16, 0:2], func=AF.Copy)
        Es.inc(ins, 1)

    ctx.on("scalar", _)
    nbf_tok = Es.n

    def _(tensor):
        tensor.wait_ge(Es.h, nbf_tok)
        ctx.psum_wait(tensor, "b0")
        tensor.transpose(ctx.psb[0:1, 0:16], nrm_bf[:, 0:1], ctx.ident[0:16, 0:16])
        ins = tensor.transpose(ctx.psb[0:1, 16:32], nrm_bf[:, 1:2], ctx.ident[0:16, 0:16])
        Et.inc(ins, 1)

    ctx.on("tensor", _)
    nT_mm = Et.n

    def _(scalar):
        scalar.wait_ge(Et.h, nT_mm)
        ins = scalar.activation(out=nrow[0:1, 0:16], in_=ctx.psb[0:1, 0:16], func=AF.Sqrt)
        Es.inc(ins, 1)
        ins = scalar.activation(out=knrow[0:1, 0:16], in_=ctx.psb[0:1, 16:32], func=AF.Copy)
        Es.inc(ins, 1)

    ctx.on("scalar", _)
    ctx.psum_set_owner("b0", Es, Es.n)
    kn_row_tok = Es.n

    def emit_knstore(sync, slot):
        sync.dma_start(out=sh_kn[slot], in_=knrow[0:1, 0:16]).then_inc(ds.h, 16)

    def _(sync):
        sync.wait_ge(Es.h, kn_row_tok)

    ctx.on("sync", _)
    ctx.by_parity("sync", emit_knstore)
    ds.n += 16
    ctx.mbox_flag(ev_kv, dep_waits=[(ds, ds.n)] + k_store_deps + v_store_deps
                  + v3_store_deps)
    xrdy_tok = ctx.mbox_wait(ev_kv)

    preg = ctx.parity_reg("sync")
    dpk = ctx.new_ldsem()

    def _(sync):
        sync.wait_ge(ctx.xrdy.h, xrdy_tok)
        with sync.If_eq(preg, 0):
            sync.dma_start(out=nrow[0:1, 16:32], in_=sh_kn[1]).then_inc(dpk.h, 16)
        with sync.Else():
            sync.dma_start(out=nrow[0:1, 16:32], in_=sh_kn[0]).then_inc(dpk.h, 16)
        sync.end_ifs()

    ctx.on("sync", _)
    dpk.n += 16
    peer_kn = dpk.n

    def _(vector):
        vector.wait_ge(dpk.h, peer_kn)
        vector.wait_ge(Es.h, kn_row_tok)
        ins = vector.tensor_tensor(out=nrow[0:1, 16:32], in0=knrow[0:1, 0:16],
                                   in1=nrow[0:1, 16:32], op=ALU.max)
        t = Ev.inc(ins, 1)
        vector.wait_ge(Ev.h, t)

    ctx.on("vector", _)
    kmax2_tok = Ev.n

    def _(scalar):
        scalar.wait_ge(Ev.h, kmax2_tok)
        ins = scalar.activation(out=nrow[0:1, 16:32], in_=nrow[0:1, 16:32], func=AF.Sqrt)
        Es.inc(ins, 1)

    ctx.on("scalar", _)
    kmax_tok = Es.n

    def _(vector):
        vector.wait_ge(Es.h, kmax_tok)
        ins = vector.tensor_tensor(out=nrow[0:1, 32:48], in0=nrow[0:1, 0:16],
                                   in1=nrow[0:1, 16:32], op=ALU.mult)
        t = Ev.inc(ins, 1)
        vector.wait_ge(Ev.h, t)

    ctx.on("vector", _)
    arow_tok = Ev.n

    dk, dv = ctx.dk, ctx.dv
    dv3 = ctx.new_ldsem()
    kch_mm = [0, 0]
    cross_parts = []
    for sl in range(2):
        for hf in range(2):
            base = sl * T + hf * 1024
            def _(tensor):
                ctx.psum_wait(tensor, 0)
                ctx.psum_wait(tensor, 1)

            ctx.on("tensor", _)
            for kc in range(NCH):
                kb = kc % 2
                prev_mm = kch_mm[kb]

                def _(sc, sl=sl, hf=hf, kc=kc, kb=kb, prev_mm=prev_mm):
                    sc.wait_ge(ctx.xrdy.h, xrdy_tok)
                    for s_, t_ in k_store_deps:
                        sc.wait_ge(s_.h, t_)
                    if prev_mm:
                        sc.wait_ge(Et.h, prev_mm)
                    ins = sc.dma_start(
                        out=kchunk[:, kb * 1024:(kb + 1) * 1024],
                        in_=sh_k[sl, kc * 128:(kc + 1) * 128, hf * 1024:(hf + 1) * 1024])
                    dk.inc(ins, 16)

                ctx.on("scalar", _)
                ld_tok = dk.n

                def _(tensor, kc=kc, kb=kb, ld_tok=ld_tok):
                    tensor.wait_ge(dk.h, ld_tok)
                    tensor.wait_ge(sm.h, sm_tok)
                    for ti in range(2):
                        ins = tensor.matmul(
                            ctx.ps[ti][0:48, :],
                            wk3_sb[:, kc * 48:(kc + 1) * 48],
                            kchunk[:, kb * 1024 + ti * 512: kb * 1024 + (ti + 1) * 512],
                            start=(kc == 0), stop=False)
                        Et.inc(ins, 1)

                ctx.on("tensor", _)
                kch_mm[kb] = Et.n

            def _(tensor):
                for ti in range(2):
                    ins = tensor.matmul(ctx.ps[ti][0:48, :], bks48_row[0:1, 0:48],
                                        ctx.ones_row_t[0:1, 0:512],
                                        start=False, stop=True)
                    Et.inc(ins, 1)

            ctx.on("tensor", _)
            k3_mm = Et.n

            def _(scalar, k3_mm=k3_mm):
                scalar.wait_ge(Et.h, k3_mm)
                for ti in range(2):
                    ins = scalar.activation(out=k3h[0:48, ti * 512:(ti + 1) * 512],
                                            in_=ctx.ps[ti][0:48, :], func=AF.Copy)
                    Es.inc(ins, 1)

            ctx.on("scalar", _)
            k3cp = Es.n
            ctx.psum_set_owner(0, Es, k3cp)
            ctx.psum_set_owner(1, Es, k3cp)

            def _(sync, sl=sl, hf=hf):
                sync.wait_ge(ctx.xrdy.h, xrdy_tok)
                for s_, t_ in v3_store_deps:
                    sync.wait_ge(s_.h, t_)
                if cross_parts:
                    sync.wait_ge(Ev.h, cross_parts[-1])
                ins = sync.dma_start(out=v3h[:],
                                     in_=sh_v3[sl, :, hf * 1024:(hf + 1) * 1024])
                dv3.inc(ins, 16)

            ctx.on("sync", _)
            v3_tok = dv3.n

            for ti in range(2):
                def _(tensor, ti=ti, k3cp=k3cp, v3_tok=v3_tok):
                    tensor.wait_ge(Es.h, k3cp)
                    tensor.wait_ge(dv3.h, v3_tok)
                    ctx.psum_wait(tensor, 2)
                    ctx.psum_wait(tensor, 3)
                    tensor.matmul(ctx.ps[2][0:48, :], ctx.l2rot1[:],
                                  k3h[0:48, ti * 512:(ti + 1) * 512], start=True, stop=True)
                    ins = tensor.matmul(ctx.ps[3][0:48, :], ctx.l2rot2[:],
                                        k3h[0:48, ti * 512:(ti + 1) * 512], start=True,
                                        stop=True)
                    Et.inc(ins, 1)

                ctx.on("tensor", _)
                mmk = Et.n

                def _(scalar, mmk=mmk):
                    scalar.wait_ge(Et.h, mmk)
                    scalar.activation(out=Ra[:], in_=ctx.ps[2][0:48, :], func=AF.Copy)
                    ins = scalar.activation(out=Rb[:], in_=ctx.ps[3][0:48, :], func=AF.Copy)
                    Es.inc(ins, 1)

                ctx.on("scalar", _)
                kcp = Es.n
                ctx.psum_set_owner(2, Es, kcp)
                ctx.psum_set_owner(3, Es, kcp)

                def _(tensor, ti=ti, v3_tok=v3_tok, kcp=kcp):
                    tensor.wait_ge(dv3.h, v3_tok)
                    tensor.wait_ge(Es.h, kcp)
                    tensor.matmul(ctx.ps[2][0:48, :], ctx.l2rot2[:],
                                  v3h[0:48, ti * 512:(ti + 1) * 512], start=True, stop=True)
                    ins = tensor.matmul(ctx.ps[3][0:48, :], ctx.l2rot1[:],
                                        v3h[0:48, ti * 512:(ti + 1) * 512], start=True,
                                        stop=True)
                    Et.inc(ins, 1)

                ctx.on("tensor", _)
                mmv = Et.n

                def _(scalar, mmv=mmv):
                    scalar.wait_ge(Et.h, mmv)
                    scalar.activation(out=Rc[:], in_=ctx.ps[2][0:48, :], func=AF.Copy)
                    ins = scalar.activation(out=Rd[:], in_=ctx.ps[3][0:48, :], func=AF.Copy)
                    Es.inc(ins, 1)

                ctx.on("scalar", _)
                vcp = Es.n
                ctx.psum_set_owner(2, Es, vcp)
                ctx.psum_set_owner(3, Es, vcp)

                def _(vector, base=base, ti=ti, vcp=vcp):
                    vector.wait_ge(Es.h, vcp)
                    ins = vector.tensor_tensor(out=Ra[:], in0=Ra[:], in1=Rc[:],
                                               op=ALU.mult)
                    t = Ev.inc(ins, 1)
                    vector.wait_ge(Ev.h, t)
                    ins = vector.tensor_tensor(out=Rb[:], in0=Rb[:], in1=Rd[:],
                                               op=ALU.mult)
                    t = Ev.inc(ins, 1)
                    vector.wait_ge(Ev.h, t)
                    ins = vector.tensor_tensor(
                        out=cross_all[0:48, base + ti * 512: base + (ti + 1) * 512],
                        in0=Ra[:], in1=Rb[:], op=ALU.subtract)
                    t = Ev.inc(ins, 1)
                    vector.wait_ge(Ev.h, t)

                ctx.on("vector", _)
                cross_parts.append(Ev.n)
    cross_tok = Ev.n

    for st in range(8):
        def _(scalar, st=st):
            scalar.wait_ge(Ev.h, cross_tok)
            if st:
                scalar.wait_ge(Et.h, Et.n)
            ins = scalar.activation(out=Ra[:], in_=cross_all[0:48, st * 512:(st + 1) * 512],
                                    func=AF.Square)
            Es.inc(ins, 1)

        ctx.on("scalar", _)
        sq_tok = Es.n

        def _(tensor, sq_tok=sq_tok):
            tensor.wait_ge(Es.h, sq_tok)
            ctx.psum_wait(tensor, 4)
            ins = tensor.matmul(ctx.ps[4][0:16, :], ctx.l2tri[:], Ra[:],
                                start=True, stop=True)
            Et.inc(ins, 1)

        ctx.on("tensor", _)
        tm = Et.n

        def _(vector, st=st, tm=tm):
            vector.wait_ge(Et.h, tm)
            ins = vector.reduce_max(out=crm[0:16, st:st + 1], in_=ctx.ps[4][0:16, :],
                                    axis=AX.X)
            t = Ev.inc(ins, 1)
            vector.wait_ge(Ev.h, t)

        ctx.on("vector", _)
        ctx.psum_set_owner(4, Ev, Ev.n)

    def _(vector):
        ins = vector.reduce_max(out=crm[0:16, 8:9], in_=crm[0:16, 0:8], axis=AX.X)
        t = Ev.inc(ins, 1)
        vector.wait_ge(Ev.h, t)

    ctx.on("vector", _)
    crm_red = Ev.n

    def _(scalar):
        scalar.wait_ge(Ev.h, crm_red)
        scalar.wait_ge(Et.h, nT_mm)
        ins = scalar.activation(out=crm[0:16, 8:9], in_=crm[0:16, 8:9], func=AF.Sqrt)
        t = Es.inc(ins, 1)
        scalar.wait_ge(Es.h, t)
        ins = scalar.activation(out=nrm_bf[0:16, 0:1], in_=crm[0:16, 8:9], func=AF.Copy)
        Es.inc(ins, 1)

    ctx.on("scalar", _)
    crbf_tok = Es.n

    def _(tensor):
        tensor.wait_ge(Es.h, crbf_tok)
        ctx.psum_wait(tensor, "b0")
        ins = tensor.transpose(ctx.psb[0:1, 0:16], nrm_bf[:, 0:1], ctx.ident[0:16, 0:16])
        Et.inc(ins, 1)

    ctx.on("tensor", _)
    crT_mm = Et.n

    def _(scalar):
        scalar.wait_ge(Et.h, crT_mm)
        ins = scalar.activation(out=crow[0:1, 0:16], in_=ctx.psb[0:1, 0:16], func=AF.Copy)
        Es.inc(ins, 1)

    ctx.on("scalar", _)
    ctx.psum_set_owner("b0", Es, Es.n)
    crow_tok = Es.n

    def _(vector):
        for s_, t_ in k_store_deps + v_store_deps:
            vector.wait_ge(s_.h, t_)
        ins = vector.memset(v_aug[:], 1.0)
        Ev.inc(ins, 1)

    ctx.on("vector", _)
    vaug_ones = Ev.n

    def _(tensor):
        for b_ in (0, 1, 2, 3):
            ctx.psum_wait(tensor, b_)

    ctx.on("tensor", _)
    stage = ctx.xn_sb
    kload_tok = [0]
    vload_tok = [0]
    score_mm_last = [0]
    score_owner = [0, 0]
    score_owner_sem = [Es, Es]
    pr_owner = [0, 0]
    bc_owner = [0]
    bc_owner_sem = [Es]
    qstp_guard = [(Es, 0)]
    crh_guard = [(Es, 0)]
    mrow_guard = [0]

    for h in range(16):
        b0 = (h % 2) * 64
        qrows = slice(b0, b0 + 64)

        def _(sc, h=h):
            sc.wait_ge(ctx.xrdy.h, xrdy_tok)
            if score_mm_last[0]:
                sc.wait_ge(Et.h, score_mm_last[0])
            for sl in range(2):
                for half in range(2):
                    ins = sc.dma_start(out=k_sb[half * 64:(half + 1) * 64,
                                                sl * T:(sl + 1) * T],
                                       in_=sh_k[sl, h * 64:(h + 1) * 64, :])
                    dk.inc(ins, 16)

        ctx.on("scalar", _)
        kload_tok[0] = dk.n

        def _(sc, h=h):
            sc.wait_ge(ctx.xrdy.h, xrdy_tok)
            if score_mm_last[0]:
                sc.wait_ge(Et.h, score_mm_last[0])
            for sl in range(2):
                for j in range(4):
                    dst = v_aug[:, (sl * 16 + j * 4) * 65:(sl * 16 + (j + 1) * 4) * 65]
                    dst = dst.rearrange("p (b c) -> p b c", c=65)[:, :, 0:64]
                    srcv = sh_v[sl, j * 512:(j + 1) * 512, h * 64:(h + 1) * 64]
                    srcv = srcv.rearrange("(b p) c -> p b c", p=128)
                    ins = sc.dma_start(out=dst, in_=srcv)
                    dv.inc(ins, 16)

        ctx.on("scalar", _)
        vload_tok[0] = dv.n

        for tt in range(4):
            def _(tensor, h=h, tt=tt, b0=b0):
                tensor.wait_ge(sm.h, sm_tok)
                tensor.wait_ge(Es.h, q_done)
                tensor.wait_ge(qstp_guard[0][0].h, qstp_guard[0][1])
                ctx.psum_wait(tensor, 5)
                tensor.matmul(ctx.ps[5][0:3, :], wqs_sb[b0:b0 + 64, 0:3],
                              q_sb[b0:b0 + 64, (h // 2) * T + tt * 512:
                                   (h // 2) * T + (tt + 1) * 512],
                              start=True, stop=False)
                ins = tensor.matmul(ctx.ps[5][0:3, :], bqs_row[0:1, 0:3],
                                    ctx.ones_row_t[0:1, 0:512], start=False, stop=True)
                Et.inc(ins, 1)

            ctx.on("tensor", _)
            qs_mm = Et.n

            def _(scalar, tt=tt, qs_mm=qs_mm):
                scalar.wait_ge(Et.h, qs_mm)
                ins = scalar.activation(out=qstp_h[0:3, tt * 512:(tt + 1) * 512],
                                        in_=ctx.ps[5][0:3, :], func=AF.Copy)
                Es.inc(ins, 1)

            ctx.on("scalar", _)
            ctx.psum_set_owner(5, Es, Es.n)
        qstp_tok = Es.n

        for tt in range(4):
            def _(scalar, tt=tt, h=h):
                if tt == 0:
                    scalar.wait_ge(Es.h, qstp_tok)
                    if h == 0:
                        scalar.wait_ge(Ev.h, cross_tok)
                else:
                    scalar.wait_ge(Et.h, Et.n)
                ins = scalar.activation(out=Rc[0:3, :],
                                        in_=qstp_h[0:3, tt * 512:(tt + 1) * 512],
                                        func=AF.Square)
                Es.inc(ins, 1)

            ctx.on("scalar", _)
            sqt = Es.n

            def _(tensor, sqt=sqt):
                tensor.wait_ge(Es.h, sqt)
                ctx.psum_wait(tensor, 4)
                ins = tensor.matmul(ctx.ps[4][0:1, :], ctx.l2tri[0:3, 0:1],
                                    Rc[0:3, :], start=True, stop=True)
                Et.inc(ins, 1)

            ctx.on("tensor", _)
            qm = Et.n

            def _(vector, tt=tt, qm=qm):
                vector.wait_ge(Et.h, qm)
                ins = vector.reduce_max(out=mrow[0:1, tt:tt + 1], in_=ctx.ps[4][0:1, :],
                                        axis=AX.X)
                t = Ev.inc(ins, 1)
                vector.wait_ge(Ev.h, t)

            ctx.on("vector", _)
            ctx.psum_set_owner(4, Ev, Ev.n)

        def _(vector):
            ins = vector.reduce_max(out=mrow[0:1, 4:5], in_=mrow[0:1, 0:4], axis=AX.X)
            t = Ev.inc(ins, 1)
            vector.wait_ge(Ev.h, t)

        ctx.on("vector", _)
        qsm_red = Ev.n

        def _(scalar, qsm_red=qsm_red):
            scalar.wait_ge(Ev.h, qsm_red)
            ins = scalar.activation(out=mrow[0:1, 4:5], in_=mrow[0:1, 4:5], func=AF.Sqrt)
            Es.inc(ins, 1)

        ctx.on("scalar", _)
        qsm_tok = Es.n

        def _(vector, h=h, qsm_tok=qsm_tok, mg=mrow_guard[0]):
            vector.wait_ge(Es.h, qsm_tok)
            vector.wait_ge(Es.h, crow_tok)
            vector.wait_ge(Ev.h, arow_tok)
            vector.wait_ge(Es.h, tneg_tok)
            if mg:
                vector.wait_ge(Et.h, mg)
            ins = vector.tensor_tensor(out=mrow[0:1, 5:6], in0=mrow[0:1, 4:5],
                                       in1=crow[0:1, h:h + 1], op=ALU.mult)
            t = Ev.inc(ins, 1)
            vector.wait_ge(Ev.h, t)
            ins = vector.tensor_tensor(out=mrow[0:1, 5:6], in0=mrow[0:1, 5:6],
                                       in1=nrow[0:1, 32 + h:33 + h], op=ALU.add)
            t = Ev.inc(ins, 1)
            vector.wait_ge(Ev.h, t)
            ins = vector.tensor_tensor(out=mrow[0:1, 5:6], in0=mrow[0:1, 5:6],
                                       in1=tneg[:], op=ALU.mult)
            t = Ev.inc(ins, 1)
            vector.wait_ge(Ev.h, t)

        ctx.on("vector", _)
        mv_tok = Ev.n

        def _(tensor, mv_tok=mv_tok):
            tensor.wait_ge(Ev.h, mv_tok)
            ctx.psum_wait(tensor, 4)
            ins = tensor.matmul(ctx.ps[4][:, 0:1], ctx.ones_rowf[:], mrow[0:1, 5:6],
                                start=True, stop=True)
            Et.inc(ins, 1)

        ctx.on("tensor", _)
        nm_mm = Et.n
        mrow_guard[0] = nm_mm

        def _(scalar, h=h, nm_mm=nm_mm):
            scalar.wait_ge(Et.h, nm_mm)
            ins = scalar.activation(out=negM[:, h:h + 1], in_=ctx.ps[4][:, 0:1],
                                    func=AF.Copy)
            Es.inc(ins, 1)

        ctx.on("scalar", _)
        negM_tok = Es.n
        ctx.psum_set_owner(4, Es, negM_tok)

        for st in range(8):
            def _(tensor, h=h, st=st):
                tensor.wait_ge(Ev.h, cross_tok)
                tensor.wait_ge(crh_guard[0][0].h, crh_guard[0][1])
                ctx.psum_wait(tensor, 5)
                ins = tensor.matmul(ctx.ps[5][0:3, :], ctx.ident[0:48, 3 * h:3 * h + 3],
                                    cross_all[0:48, st * 512:(st + 1) * 512],
                                    start=True, stop=True)
                Et.inc(ins, 1)

            ctx.on("tensor", _)
            ex_mm = Et.n

            def _(scalar, st=st, ex_mm=ex_mm):
                scalar.wait_ge(Et.h, ex_mm)
                ins = scalar.activation(out=cross_h[0:3, st * 512:(st + 1) * 512],
                                        in_=ctx.ps[5][0:3, :], func=AF.Copy)
                Es.inc(ins, 1)

            ctx.on("scalar", _)
            ctx.psum_set_owner(5, Es, Es.n)
        crh_tok = Es.n

        for qt in range(4):
            qsl = slice((h // 2) * T + qt * 512, (h // 2) * T + (qt + 1) * 512)
            pend = [None]

            def emit_av2(stop_flag):
                sb_, kt_, osem, otok = pend[0]
                first_av = (kt_ == 0)

                def _(tensor):
                    tensor.wait_ge(osem.h, otok)
                    tensor.wait_ge(dv.h, vload_tok[0])
                    tensor.wait_ge(Ev.h, vaug_ones)
                    if first_av:
                        ctx.psum_wait(tensor, 2)
                    ins = tensor.matmul(ctx.ps[2][0:65, :],
                                        v_aug[:, kt_ * 65:(kt_ + 1) * 65],
                                        pr[:, sb_ * 512:(sb_ + 1) * 512],
                                        start=first_av, stop=stop_flag)
                    Et.inc(ins, 1)

                ctx.on("tensor", _)
                pr_owner[sb_] = Et.n
                score_mm_last[0] = Et.n

            for kt in range(32):
                sb = kt % 2
                first_av = (kt == 0)

                def _(tensor, sb=sb, kt=kt, h=h, b0=b0, qsl=qsl, first_av=first_av,
                      crh_tok=crh_tok, qstp_tok=qstp_tok):
                    tensor.wait_ge(dk.h, kload_tok[0])
                    tensor.wait_ge(Es.h, crh_tok)
                    tensor.wait_ge(Es.h, qstp_tok)
                    if score_owner[sb]:
                        tensor.wait_ge(score_owner_sem[sb].h, score_owner[sb])
                    if first_av:
                        tensor.wait_ge(Es.h, q_done)
                    tensor.matmul(ctx.ps[sb][:, :],
                                  k_sb[b0:b0 + 64, kt * 128:(kt + 1) * 128],
                                  q_sb[b0:b0 + 64, qsl], start=True, stop=False)
                    ins = tensor.matmul(ctx.ps[sb][:, :],
                                        cross_h[0:3, kt * 128:(kt + 1) * 128],
                                        qstp_h[0:3, (qsl.start - (h // 2) * T):
                                               (qsl.start - (h // 2) * T) + 512],
                                        start=False, stop=True)
                    Et.inc(ins, 1)

                ctx.on("tensor", _)
                mm_tok = Et.n
                score_mm_last[0] = mm_tok

                def _(scalar, sb=sb, h=h, mm_tok=mm_tok, negM_tok=negM_tok):
                    scalar.wait_ge(Et.h, mm_tok)
                    scalar.wait_ge(Es.h, negM_tok)
                    scalar.wait_ge(Es.h, tcol_tok)
                    if pr_owner[sb]:
                        scalar.wait_ge(Et.h, pr_owner[sb])
                    ins = scalar.activation(out=pr[:, sb * 512:(sb + 1) * 512],
                                            in_=ctx.ps[sb][:, :], func=AF.Exp,
                                            scale=temp_col[:],
                                            bias=negM[:, h:h + 1])
                    Es.inc(ins, 1)

                ctx.on("scalar", _)
                score_owner[sb] = Es.n
                score_owner_sem[sb] = Es

                prev = pend[0]
                pend[0] = (sb, kt, Es, Es.n)
                if prev is not None:
                    cur = pend[0]
                    pend[0] = prev
                    emit_av2(False)
                    pend[0] = cur
            emit_av2(True)
            av_done = Et.n

            def _(scalar, av_done=av_done):
                scalar.wait_ge(Et.h, av_done)
                scalar.wait_ge(bc_owner_sem[0].h, bc_owner[0])
                ins = scalar.activation(out=av_tmp[0:65, 0:512], in_=ctx.ps[2][0:65, :],
                                        func=AF.Copy)
                Es.inc(ins, 1)

            ctx.on("scalar", _)
            av_cp = Es.n
            ctx.psum_set_owner(2, Es, av_cp)

            def _(vector, av_cp=av_cp):
                vector.wait_ge(Es.h, av_cp)
                ins = vector.reciprocal(out=rec_sb, in_=av_tmp[64:65, 0:512])
                Ev.inc(ins, 1)
                vector.wait_ge(Ev.h, Ev.n)

            ctx.on("vector", _)
            recip_tok = Ev.n

            def _(tensor, recip_tok=recip_tok):
                tensor.wait_ge(Ev.h, recip_tok)
                tensor.wait_ge(bc_owner_sem[0].h, bc_owner[0])
                ctx.psum_wait(tensor, 3)
                ins = tensor.matmul(ctx.ps[3][0:64, :], ctx.ones_rowf[0:1, 0:64],
                                    rec_sb, start=True, stop=True)
                Et.inc(ins, 1)

            ctx.on("tensor", _)
            bc_mm = Et.n

            def _(vector, bc_mm=bc_mm, qrows=qrows, qsl=qsl):
                vector.wait_ge(Et.h, bc_mm)
                ins = vector.tensor_tensor(out=stage[qrows, qsl],
                                           in0=av_tmp[0:64, 0:512],
                                           in1=ctx.ps[3][0:64, :], op=ALU.mult)
                Ev.inc(ins, 1)
                vector.wait_ge(Ev.h, Ev.n)

            ctx.on("vector", _)
            bc_owner[0] = Ev.n
            bc_owner_sem[0] = Ev
            ctx.psum_set_owner(3, Ev, Ev.n)
        qstp_guard[0] = (Et, score_mm_last[0])
        crh_guard[0] = (Et, score_mm_last[0])

    stage_tok = Ev.n

    tail = bias_tail_mm(ctx, bo_row[0], bo_row[1])
    emit_proj(ctx, WDirect(dr["L2a_WoT"], D, D), sink_resid(ctx, 1.0),
              src=lambda kc: stage[:, kc * T:(kc + 1) * T],
              tensor_dep=[(Ev, stage_tok)], tail_mm=tail)
    return ctx.E["vector"].n



L3F_HID = 2730
L3F_BLOCKS = [(i * 128, min(128, L3F_HID - i * 128)) for i in range(22)]

L3F_INPUT_SPECS = (
    [(f"L3f_wqkv{l}", [1024, 1536], F32) for l in range(2)]
    + [(f"L3f_wo{l}", [512, 1024], F32) for l in range(2)]
    + [(f"L3f_w1{l}", [1024, 2 * L3F_HID], F32) for l in range(2)]
    + [(f"L3f_b1{l}", [1, 2 * L3F_HID], F32) for l in range(2)]
    + [(f"L3f_w2{l}", [L3F_HID, 1024], F32) for l in range(2)]
    + [(f"L3f_b2{l}", [1, 1024], F32) for l in range(2)]
    + [(f"L3f_qsc{l}", [128, 1], F32) for l in range(2)]
    + [(f"L3f_ksc{l}", [128, 1], F32) for l in range(2)]
    + [(f"L3f_qsr{l}", [1, 64], F32) for l in range(2)]
    + [(f"L3f_ksr{l}", [1, 64], F32) for l in range(2)]
    + [(f"L3f_lg{l}", [128, 8], F32) for l in range(2)]
    + [(f"L3f_lb{l}", [128, 8], F32) for l in range(2)]
    + [("L3f_cq", [128, T], BF16), ("L3f_sq", [128, T], BF16),
       ("L3f_ck", [128, T], BF16), ("L3f_sk", [128, T], BF16),
       ("L3f_band", [128, 8 * 512], BF16), ("L3f_m0sel", [128, 1], F32)]
)


def l3f_host_inputs(wqkv, q_scale, k_scale, wo, ln_g, ln_b, w1, b1, w2, b2):
    d = {}
    for l in range(2):
        d[f"L3f_wqkv{l}"] = np.ascontiguousarray(wqkv[l], dtype=np.float32)
        d[f"L3f_wo{l}"] = np.ascontiguousarray(wo[l], dtype=np.float32)
        d[f"L3f_w1{l}"] = np.ascontiguousarray(w1[l], dtype=np.float32)
        d[f"L3f_b1{l}"] = np.asarray(b1[l], np.float32).reshape(1, 2 * L3F_HID)
        d[f"L3f_w2{l}"] = np.ascontiguousarray(w2[l], dtype=np.float32)
        d[f"L3f_b2{l}"] = np.asarray(b2[l], np.float32).reshape(1, 1024)
        d[f"L3f_qsc{l}"] = np.tile(np.asarray(q_scale[l], np.float32), 2).reshape(128, 1)
        d[f"L3f_ksc{l}"] = np.tile(np.asarray(k_scale[l], np.float32), 2).reshape(128, 1)
        d[f"L3f_qsr{l}"] = np.asarray(q_scale[l], np.float32).reshape(1, 64)
        d[f"L3f_ksr{l}"] = np.asarray(k_scale[l], np.float32).reshape(1, 64)
        d[f"L3f_lg{l}"] = np.ascontiguousarray(
            np.asarray(ln_g[l], np.float32).reshape(8, 128).T)
        d[f"L3f_lb{l}"] = np.ascontiguousarray(
            np.asarray(ln_b[l], np.float32).reshape(8, 128).T)
    return d


def l3f_percore_inputs(parity):
    t = np.arange(T, dtype=np.float64) + parity * T
    half_idx = np.arange(0, 64, 2, dtype=np.float64)
    invf = 1.0 / 10000.0 ** (half_idx / 64.0)
    fr = invf[:, None] * t[None, :]
    base = (half_idx + 0.4 * 64.0) / (1.4 * 64.0)
    power = (t - S // 2) / 256.0
    scale = base[:, None] ** power[None, :]
    cos, sin = np.cos(fr), np.sin(fr)

    def dup4(a32):
        a64 = np.concatenate([a32, a32], axis=0)
        return np.concatenate([a64, a64], axis=0).astype(bf16)

    def dup4s(a32):
        a64 = np.concatenate([-a32, a32], axis=0)
        return np.concatenate([a64, a64], axis=0).astype(bf16)

    band = np.zeros((128, 8 * 512), dtype=np.float32)
    row = np.arange(128)[:, None]
    col = np.arange(512)[None, :]
    for r in range(8):
        delta = 512 - r * 128 + col - row
        band[:, r * 512:(r + 1) * 512] = np.where((delta < 0) | (delta > 512),
                                                  -1e6, 0.0)
    return {
        "L3f_cq": dup4(cos * scale), "L3f_sq": dup4s(sin * scale),
        "L3f_ck": dup4(cos / scale), "L3f_sk": dup4s(sin / scale),
        "L3f_band": band.astype(bf16),
        "L3f_m0sel": np.full((128, 1), -1e13 if parity == 0 else 0.0, np.float32),
    }


def emit_l3_ffn(ctx, tag, xn_tok, dbg=None):
    nc = ctx.nc
    Et, Es, Ev = ctx.E["tensor"], ctx.E["scalar"], ctx.E["vector"]
    ds = ctx.dma["sync"]
    dr = ctx.dram

    x3 = ctx.tmp("l3f_x3", [128, NCH * T], BF16)
    bnd = ctx.tmp("l3f_bnd", [128, 4], BF16)
    zero4 = ctx.tmp("l3f_z4", [128, 4], BF16)

    def _(vector):
        vector.wait_ge(Ev.h, xn_tok)
        vector.memset(zero4[:], 0.0)
        for c_ in range(NCH):
            ins = vector.tensor_copy(out=x3[:, c_ * T:(c_ + 1) * T],
                                     in_=ctx.xn_sb[:, c_ * T:(c_ + 1) * T])
            t_ = Ev.inc(ins, 1)
            vector.wait_ge(Ev.h, t_)

    ctx.on("vector", _)
    x3_tok = (Ev, Ev.n)

    def boundary_exchange(src_buf, src_dep, hsh0_ap, copies_tok):
        sh_b = nc.dram_tensor(f"sh_b_{tag}_{ctx.uid()}", [2, 128, 4], BF16,
                              kind="Internal", addr_space="Shared").ap()
        src_view = src_buf[:].rearrange("p (c t) -> p c t", t=T)[:, 4:8, 2047:2048]
        ev = ctx.mbox_send(
            [(lambda slot: sh_b[slot].rearrange("p (c o) -> p c o", o=1), src_view)],
            dep_waits=[src_dep])
        xt = ctx.mbox_wait(ev)
        dE = ctx.dE

        def _(sync):
            sync.wait_ge(ctx.xrdy.h, xt)
            with sync.If_eq(ctx.parity_reg("sync"), 0):
                sync.dma_start(out=bnd[:], in_=zero4[:]).then_inc(dE.h, 16)
            with sync.Else():
                sync.dma_start(out=bnd[:], in_=sh_b[0]).then_inc(dE.h, 16)
            sync.end_ifs()

        ctx.on("sync", _)
        dE.n += 16
        bt = dE.n

        def _(sync):
            sync.wait_ge(dE.h, bt)
            for s_, t_ in [copies_tok]:
                sync.wait_ge(s_.h, t_)
            dst = hsh0_ap.rearrange("p (c t) -> p c t", t=512)[:, :, 0:1]
            ins = sync.dma_start(out=dst, in_=bnd[:].rearrange("p (c o) -> p c o", o=1),
                                 allow_slow_non_contiguous=True)
            dE.inc(ins, 16)

        ctx.on("sync", _)
        return (dE, dE.n)

    def build_hsh0(src_buf, src_dep, hsh0_ap):
        def _(vector):
            vector.wait_ge(src_dep[0].h, src_dep[1])
            for g in range(4):
                ins = vector.tensor_copy(
                    out=hsh0_ap[:, g * 512 + 1:(g + 1) * 512],
                    in_=src_buf[:, (4 + g) * T:(4 + g) * T + 511])
                t_ = Ev.inc(ins, 1)
                vector.wait_ge(Ev.h, t_)

        ctx.on("vector", _)
        cp_tok = (Ev, Ev.n)
        bdep = boundary_exchange(src_buf, src_dep, hsh0_ap, cp_tok)
        return [cp_tok, bdep]

    def mk_shift_src(src_buf, hsh0_ap):
        def st(kc, col0, w):
            if kc < 4:
                return src_buf[:, kc * T + col0: kc * T + col0 + w]
            if col0 >= 512:
                return src_buf[:, kc * T + col0 - 1: kc * T + col0 - 1 + w]
            return hsh0_ap[:, (kc - 4) * 512 + col0:(kc - 4) * 512 + col0 + w]
        return st

    prev_tok = x3_tok
    for l in range(2):
        ph = ExitStack()
        tp = lambda n_, s_, d_: ph.enter_context(
            nc.sbuf_tensor(n_ + str(ctx.uid()), s_, d_))
        q3 = ctx.xn_sb
        v3 = tp("l3f_v3", [128, 16 * 512], BF16)
        tbuf = tp("l3f_tb", [128, 4 * 512], BF16)
        band = tp("l3f_bd", [128, 8 * 512], BF16)
        A = tp("l3f_A", [128, 3712], BF16)
        kprevh = tbuf[:, 0:512]
        vprevh = tbuf[:, 512:768]
        negMt = tp("l3f_nM", [128, 2], F32)
        nsmall = tp("l3f_ns", [1, 8], F32)
        qsc_sb = tp("l3f_qsc", [128, 2], F32)
        hsh0 = A[:, 0:2048]
        nsq = A[:, 0:1024].bitcast(F32)
        nr2 = A[0:2, 1024:2048].bitcast(F32)
        swp = A[:, 2048:2560]
        pr = A[:, 0:1024]
        v_aug = A[:, 1024:1544]
        av_tmp = A[:, 1544:2568].bitcast(F32)
        rec = A[0:1, 2568:3592].bitcast(F32)

        sm = ctx.new_ldsem()

        def _(g, l=l):
            ins = g.dma_start(out=band[:], in_=dr["L3f_band"])
            sm.inc(ins, 16)
            ins = g.dma_start(out=qsc_sb[:, 0:1], in_=dr[f"L3f_qsc{l}"])
            sm.inc(ins, 16)
            ins = g.dma_start(out=qsc_sb[:, 1:2], in_=dr[f"L3f_ksc{l}"])
            sm.inc(ins, 16)

        ctx.on("gpsimd", _)
        sm_tok = sm.n

        smr = ctx.new_ldsem()
        m0sel_sb = tp("l3f_m0", [128, 1], F32)
        qsr_sb = tp("l3f_qsr", [1, 128], F32)

        def _(g, l=l):
            ins = g.dma_start(out=qsr_sb[0:1, 0:64], in_=dr[f"L3f_qsr{l}"])
            smr.inc(ins, 16)
            ins = g.dma_start(out=qsr_sb[0:1, 64:128], in_=dr[f"L3f_ksr{l}"])
            smr.inc(ins, 16)
            ins = g.dma_start(out=m0sel_sb[:], in_=dr["L3f_m0sel"])
            smr.inc(ins, 16)

        ctx.on("gpsimd", _)
        smr_tok = smr.n

        def _(vector):
            vector.wait_ge(smr.h, smr_tok)
            ins = vector.tensor_reduce(out=nsmall[0:1, 0:1].rearrange("p (a o) -> p a o", o=1),
                                       in_=qsr_sb[0:1, 0:64].rearrange("p (a w) -> p a w", a=1),
                                       axis=AX.X, op=ALU.max, apply_absolute_value=True)
            t_ = Ev.inc(ins, 1)
            vector.wait_ge(Ev.h, t_)
            ins = vector.tensor_reduce(out=nsmall[0:1, 1:2].rearrange("p (a o) -> p a o", o=1),
                                       in_=qsr_sb[0:1, 64:128].rearrange("p (a w) -> p a w", a=1),
                                       axis=AX.X, op=ALU.max, apply_absolute_value=True)
            t_ = Ev.inc(ins, 1)
            vector.wait_ge(Ev.h, t_)
            ins = vector.tensor_tensor(out=nsmall[0:1, 2:3], in0=nsmall[0:1, 0:1],
                                       in1=nsmall[0:1, 1:2], op=ALU.mult)
            t_ = Ev.inc(ins, 1)
            vector.wait_ge(Ev.h, t_)
            ins = vector.tensor_scalar(out=nsmall[0:1, 2:3], in0=nsmall[0:1, 2:3],
                                       scalar1=-8.0, scalar2=None, op0=ALU.mult)
            t_ = Ev.inc(ins, 1)
            vector.wait_ge(Ev.h, t_)

        ctx.on("vector", _)
        mval_tok = Ev.n

        def _(tensor, mval_tok=mval_tok):
            tensor.wait_ge(Ev.h, mval_tok)
            ctx.psum_wait(tensor, 4)
            ins = tensor.matmul(ctx.ps[4][:, 0:1], ctx.ones_rowf[:],
                                nsmall[0:1, 2:3], start=True, stop=True)
            Et.inc(ins, 1)

        ctx.on("tensor", _)
        nmm = Et.n

        def _(scalar, nmm=nmm):
            scalar.wait_ge(Et.h, nmm)
            ins = scalar.activation(out=negMt[:, 0:1], in_=ctx.ps[4][:, 0:1],
                                    func=AF.Copy)
            Es.inc(ins, 1)

        ctx.on("scalar", _)
        negM_cp = Es.n
        ctx.psum_set_owner(4, Es, negM_cp)

        def _(vector, negM_cp=negM_cp):
            vector.wait_ge(Es.h, negM_cp)
            vector.wait_ge(smr.h, smr_tok)
            ins = vector.tensor_tensor(out=negMt[:, 1:2], in0=negMt[:, 0:1],
                                       in1=m0sel_sb[:], op=ALU.add)
            t_ = Ev.inc(ins, 1)
            vector.wait_ge(Ev.h, t_)

        ctx.on("vector", _)
        negM_tok = Ev.n

        sdep = build_hsh0(x3, prev_tok, hsh0)
        shsrc = mk_shift_src(x3, hsh0)

        def qk_sink(oc, tt, psum_ap, mm_tok):
            def _(scalar):
                scalar.wait_ge(Et.h, mm_tok)
                ins = scalar.activation(out=ctx.xn_sb[:, oc * T + tt * 512: oc * T + (tt + 1) * 512],
                                        in_=psum_ap, func=AF.Copy)
                Es.inc(ins, 1)

            ctx.on("scalar", _)
            return (Es, Es.n)

        emit_proj(ctx, WDirect(dr[f"L3f_wqkv{l}"], 1024, 1024), qk_sink,
                  tensor_dep=sdep, src_tile=shsrc)
        qk_done = Es.n

        def v_sink(tb, ob_i, psum_ap, mm_tok):
            def _(scalar):
                scalar.wait_ge(Et.h, mm_tok)
                ins = scalar.activation(out=v3[:, tb * 512:(tb + 1) * 512],
                                        in_=psum_ap, func=AF.Copy)
                Es.inc(ins, 1)

            ctx.on("scalar", _)
            return (Es, Es.n)

        emit_proj_tokmaj(ctx, WDirect(dr[f"L3f_wqkv{l}"], 1024, 512, off=1024),
                         v_sink, tensor_dep=sdep, src_tile=shsrc)
        v_done = Es.n
        v_mm_guard = Et.n

        dtb = ctx.new_ldsem()
        rope_done = [0]
        for tt in range(4):
            def _(sync, tt=tt, l=l):
                sync.wait_ge(Es.h, qk_done)
                if rope_done[0]:
                    sync.wait_ge(Ev.h, rope_done[0])
                for i, nm in enumerate(("L3f_cq", "L3f_sq", "L3f_ck", "L3f_sk")):
                    ins = sync.dma_start(out=tbuf[:, i * 512:(i + 1) * 512],
                                         in_=dr[nm][:, tt * 512:(tt + 1) * 512])
                    dtb.inc(ins, 16)

            ctx.on("sync", _)
            tb_tok = dtb.n

            for qk in range(2):
                ctab = tbuf[:, (qk * 2) * 512:(qk * 2 + 1) * 512]
                stab = tbuf[:, (qk * 2 + 1) * 512:(qk * 2 + 2) * 512]
                for c_ in range(4):
                    cc = c_ + 4 * qk
                    tile = ctx.xn_sb[:, cc * T + tt * 512: cc * T + (tt + 1) * 512]

                    def _(scalar, tile=tile):
                        scalar.wait_ge(Es.h, qk_done)
                        scalar.wait_ge(Et.h, v_mm_guard)
                        ins = scalar.activation(out=nsq[:, 0:512], in_=tile,
                                                func=AF.Square)
                        Es.inc(ins, 1)

                    ctx.on("scalar", _)
                    sq_tok = Es.n

                    def _(tensor, sq_tok=sq_tok):
                        tensor.wait_ge(Es.h, sq_tok)
                        ctx.psum_wait(tensor, 4)
                        ins = tensor.matmul(ctx.ps[4][0:2, :], ctx.l3f_ind2[:],
                                            nsq[:, 0:512], start=True, stop=True)
                        Et.inc(ins, 1)

                    ctx.on("tensor", _)
                    nm_mm = Et.n

                    def _(scalar, nm_mm=nm_mm):
                        scalar.wait_ge(Et.h, nm_mm)
                        ins = scalar.activation(out=nr2[0:2, 0:512],
                                                in_=ctx.ps[4][0:2, :], func=AF.Sqrt)
                        Es.inc(ins, 1)

                    ctx.on("scalar", _)
                    nrm_cp = Es.n
                    ctx.psum_set_owner(4, Es, nrm_cp)

                    def _(vector, nrm_cp=nrm_cp):
                        vector.wait_ge(Es.h, nrm_cp)
                        ins = vector.tensor_scalar(out=nr2[0:2, 0:512], in0=nr2[0:2, 0:512],
                                                   scalar1=1e-12, scalar2=None,
                                                   op0=ALU.max)
                        t_ = Ev.inc(ins, 1)
                        vector.wait_ge(Ev.h, t_)
                        ins = vector.reciprocal(out=nr2[0:2, 0:512], in_=nr2[0:2, 0:512])
                        t_ = Ev.inc(ins, 1)
                        vector.wait_ge(Ev.h, t_)

                    ctx.on("vector", _)
                    rcp_tok = Ev.n

                    def _(tensor, rcp_tok=rcp_tok):
                        tensor.wait_ge(Ev.h, rcp_tok)
                        ctx.psum_wait(tensor, 5)
                        ins = tensor.matmul(ctx.ps[5][:, :], ctx.l3f_ind2T[:],
                                            nr2[0:2, 0:512], start=True, stop=True)
                        Et.inc(ins, 1)

                    ctx.on("tensor", _)
                    bc_mm = Et.n

                    def _(vector, tile=tile, qk=qk, bc_mm=bc_mm):
                        vector.wait_ge(Et.h, bc_mm)
                        vector.wait_ge(sm.h, sm_tok)
                        ins = vector.tensor_tensor(out=tile, in0=tile,
                                                   in1=ctx.ps[5][:, :], op=ALU.mult)
                        t_ = Ev.inc(ins, 1)
                        vector.wait_ge(Ev.h, t_)
                        ins = vector.tensor_scalar(out=tile, in0=tile,
                                                   scalar1=qsc_sb[:, qk:qk + 1],
                                                   scalar2=None, op0=ALU.mult)
                        t_ = Ev.inc(ins, 1)
                        vector.wait_ge(Ev.h, t_)

                    ctx.on("vector", _)
                    scl_tok = Ev.n
                    ctx.psum_set_owner(5, Ev, scl_tok)

                    def _(tensor, tile=tile, scl_tok=scl_tok):
                        tensor.wait_ge(Ev.h, scl_tok)
                        ctx.psum_wait(tensor, 4)
                        ins = tensor.matmul(ctx.ps[4][:, :], ctx.permswap_sb[:],
                                            tile, start=True, stop=True)
                        Et.inc(ins, 1)

                    ctx.on("tensor", _)
                    sw_mm = Et.n

                    def _(vector, tile=tile, sw_mm=sw_mm, tb_tok=tb_tok,
                          ctab=ctab, stab=stab):
                        vector.wait_ge(Et.h, sw_mm)
                        vector.wait_ge(dtb.h, tb_tok)
                        ins = vector.tensor_tensor(out=swp[:, 0:512], in0=tile,
                                                   in1=ctab, op=ALU.mult)
                        t_ = Ev.inc(ins, 1)
                        vector.wait_ge(Ev.h, t_)
                        ins = vector.tensor_tensor(out=tile, in0=ctx.ps[4][:, :],
                                                   in1=stab, op=ALU.mult)
                        t_ = Ev.inc(ins, 1)
                        vector.wait_ge(Ev.h, t_)
                        ins = vector.tensor_tensor(out=tile, in0=tile,
                                                   in1=swp[:, 0:512], op=ALU.add)
                        t_ = Ev.inc(ins, 1)
                        vector.wait_ge(Ev.h, t_)

                    ctx.on("vector", _)
                    ctx.psum_set_owner(4, Ev, Ev.n)
            rope_done[0] = Ev.n
        rope_tok = Ev.n
        if dbg is not None and l == 0:
            dsm = ctx.new_ldsem()

            def _(sync):
                sync.wait_ge(Ev.h, rope_tok)
                ins = sync.dma_start(out=dbg["qk"], in_=ctx.xn_sb[:])
                dsm.inc(ins, 16)
                sync.wait_ge(Es.h, v_done)
                ins = sync.dma_start(out=dbg["v3"], in_=v3[:])
                dsm.inc(ins, 16)
                sync.wait_ge(dsm.h, dsm.n)

            ctx.on("sync", _)

        sh_k3 = nc.dram_tensor(f"sh_l3k_{tag}_{l}", [2, 128, 2048], BF16,
                               kind="Internal", addr_space="Shared").ap()
        sh_v3 = nc.dram_tensor(f"sh_l3v_{tag}_{l}", [2, 128, 2048], BF16,
                               kind="Internal", addr_space="Shared").ap()
        k_w3 = ctx.xn_sb[:].rearrange("p (c t) -> p c t", t=T)[:, 4:8, 1536:2048]
        ev_kv = ctx.mbox_send(
            [(lambda slot: sh_k3[slot].rearrange("p (c t) -> p c t", t=512), k_w3),
             (lambda slot: sh_v3[slot], v3[:, 12 * 512:16 * 512])],
            dep_waits=[(Ev, rope_tok), (Es, v_done)])
        kv_rdy = ctx.mbox_wait(ev_kv)

        def _(tensor):
            for b_ in (0, 1, 2, 3):
                ctx.psum_wait(tensor, b_)

        ctx.on("tensor", _)

        def _(vector):
            vector.wait_ge(Ev.h, rope_tok)
            ins = vector.memset(v_aug, 1.0)
            Ev.inc(ins, 1)

        ctx.on("vector", _)
        vaug_ones = Ev.n

        dkv = ctx.new_ldsem()
        score_owner = [0, 0]
        score_owner_sem = [Es, Es]
        pr_owner = [0, 0]
        bc_owner = [0]
        bc_owner_sem = [Es]
        last_av = [0]
        preg = ctx.parity_reg("sync")

        for h in range(8):
            b0 = (h % 2) * 64
            hc = h // 2

            def _(sync, h=h, b0=b0, hc=hc):
                sync.wait_ge(ctx.xrdy.h, kv_rdy)
                if last_av[0]:
                    sync.wait_ge(Et.h, last_av[0])
                with sync.If_eq(preg, 0):
                    sync.dma_start(out=kprevh[b0:b0 + 64, 0:512],
                                   in_=sh_k3[1][b0:b0 + 64, hc * 512:(hc + 1) * 512]).then_inc(dkv.h, 16)
                    sync.dma_start(
                        out=vprevh.rearrange("p (j f) -> p j f", f=64),
                        in_=sh_v3[1].rearrange("p (j f) -> p j f", f=512)[:, :, h * 64:(h + 1) * 64]).then_inc(dkv.h, 16)
                with sync.Else():
                    sync.dma_start(out=kprevh[b0:b0 + 64, 0:512],
                                   in_=sh_k3[0][b0:b0 + 64, hc * 512:(hc + 1) * 512]).then_inc(dkv.h, 16)
                    sync.dma_start(
                        out=vprevh.rearrange("p (j f) -> p j f", f=64),
                        in_=sh_v3[0].rearrange("p (j f) -> p j f", f=512)[:, :, h * 64:(h + 1) * 64]).then_inc(dkv.h, 16)
                sync.end_ifs()

            ctx.on("sync", _)
            dkv.n += 32
            kv_tok = dkv.n

            for w in range(4):
                def _(vector, h=h, w=w, kv_tok=kv_tok):
                    vector.wait_ge(Ev.h, vaug_ones)
                    vector.wait_ge(Es.h, v_done)
                    if w == 0:
                        vector.wait_ge(dkv.h, kv_tok)
                    if last_av[0]:
                        vector.wait_ge(Et.h, last_av[0])
                    for r in range(8):
                        if w == 0 and r < 4:
                            srcv = vprevh[:, r * 64:(r + 1) * 64]
                        else:
                            g = (w - 1) * 4 + r
                            srcv = v3[:, g * 512 + h * 64: g * 512 + h * 64 + 64]
                        ins = vector.tensor_copy(
                            out=v_aug[:, r * 65: r * 65 + 64], in_=srcv)
                        t_ = Ev.inc(ins, 1)
                        vector.wait_ge(Ev.h, t_)

                ctx.on("vector", _)
                vaug_tok = Ev.n

                pend = [None]

                def emit_av3(stop_flag, vaug_tok=vaug_tok):
                    sb_, r_, osem, otok = pend[0]
                    first = (r_ == 0)

                    def _(tensor):
                        tensor.wait_ge(osem.h, otok)
                        tensor.wait_ge(Ev.h, vaug_tok)
                        if first:
                            ctx.psum_wait(tensor, 2)
                        ins = tensor.matmul(ctx.ps[2][0:65, :],
                                            v_aug[:, r_ * 65:(r_ + 1) * 65],
                                            pr[:, sb_ * 512:(sb_ + 1) * 512],
                                            start=first, stop=stop_flag)
                        Et.inc(ins, 1)

                    ctx.on("tensor", _)
                    pr_owner[sb_] = Et.n
                    last_av[0] = Et.n

                for r in range(8):
                    sb = r % 2

                    def _(tensor, h=h, w=w, r=r, sb=sb, b0=b0, hc=hc):
                        tensor.wait_ge(Ev.h, rope_tok)
                        if w == 0 and r < 4:
                            tensor.wait_ge(dkv.h, kv_tok)
                        if score_owner[sb]:
                            tensor.wait_ge(score_owner_sem[sb].h, score_owner[sb])
                        if w == 0 and r == 0:
                            tensor.wait_ge(sm.h, sm_tok)
                        if w == 0 and r < 4:
                            lhs = kprevh[b0:b0 + 64, r * 128:(r + 1) * 128]
                        else:
                            koff = (w - 1) * 512 + r * 128
                            lhs = ctx.xn_sb[b0:b0 + 64, (4 + hc) * T + koff:
                                            (4 + hc) * T + koff + 128]
                        tensor.matmul(ctx.ps[sb][:, :], lhs,
                                      q3[b0:b0 + 64, hc * T + w * 512: hc * T + (w + 1) * 512],
                                      start=True, stop=False)
                        ins = tensor.matmul(ctx.ps[sb][:, :], ctx.ident[:],
                                            band[:, r * 512:(r + 1) * 512],
                                            start=False, stop=True)
                        Et.inc(ins, 1)

                    ctx.on("tensor", _)
                    mm_tok = Et.n

                    def _(scalar, sb=sb, w=w, r=r, mm_tok=mm_tok):
                        scalar.wait_ge(Et.h, mm_tok)
                        scalar.wait_ge(Ev.h, negM_tok)
                        if pr_owner[sb]:
                            scalar.wait_ge(Et.h, pr_owner[sb])
                        bias_col = negMt[:, 1:2] if (w == 0 and r < 4) else negMt[:, 0:1]
                        ins = scalar.activation(out=pr[:, sb * 512:(sb + 1) * 512],
                                                in_=ctx.ps[sb][:, :], func=AF.Exp,
                                                scale=8.0, bias=bias_col)
                        Es.inc(ins, 1)

                    ctx.on("scalar", _)
                    score_owner[sb] = Es.n
                    score_owner_sem[sb] = Es

                    prev = pend[0]
                    pend[0] = (sb, r, Es, Es.n)
                    if prev is not None:
                        cur = pend[0]
                        pend[0] = prev
                        emit_av3(False)
                        pend[0] = cur
                emit_av3(True)
                av_done = Et.n

                def _(scalar, av_done=av_done):
                    scalar.wait_ge(Et.h, av_done)
                    scalar.wait_ge(bc_owner_sem[0].h, bc_owner[0])
                    ins = scalar.activation(out=av_tmp[0:65, 0:512],
                                            in_=ctx.ps[2][0:65, :], func=AF.Copy)
                    Es.inc(ins, 1)

                ctx.on("scalar", _)
                av_cp = Es.n
                ctx.psum_set_owner(2, Es, av_cp)

                def _(vector, av_cp=av_cp):
                    vector.wait_ge(Es.h, av_cp)
                    ins = vector.reciprocal(out=rec[0:1, 0:512],
                                            in_=av_tmp[64:65, 0:512])
                    Ev.inc(ins, 1)
                    vector.wait_ge(Ev.h, Ev.n)

                ctx.on("vector", _)
                recip_tok = Ev.n

                def _(tensor, recip_tok=recip_tok):
                    tensor.wait_ge(Ev.h, recip_tok)
                    tensor.wait_ge(bc_owner_sem[0].h, bc_owner[0])
                    ctx.psum_wait(tensor, 3)
                    ins = tensor.matmul(ctx.ps[3][0:64, :], ctx.ones_rowf[0:1, 0:64],
                                        rec[0:1, 0:512], start=True, stop=True)
                    Et.inc(ins, 1)

                ctx.on("tensor", _)
                bc_mm = Et.n

                def _(vector, h=h, w=w, b0=b0, hc=hc, bc_mm=bc_mm):
                    vector.wait_ge(Et.h, bc_mm)
                    ins = vector.tensor_tensor(
                        out=q3[b0:b0 + 64, hc * T + w * 512: hc * T + (w + 1) * 512],
                        in0=av_tmp[0:64, 0:512], in1=ctx.ps[3][0:64, :], op=ALU.mult)
                    Ev.inc(ins, 1)
                    vector.wait_ge(Ev.h, Ev.n)

                ctx.on("vector", _)
                bc_owner[0] = Ev.n
                bc_owner_sem[0] = Ev
                ctx.psum_set_owner(3, Ev, Ev.n)
        o_tok = Ev.n

        def x3_sink(oc, tt, psum_ap, mm_tok):
            def _(vector):
                vector.wait_ge(Et.h, mm_tok)
                xs = x3[:, oc * T + tt * 512: oc * T + (tt + 1) * 512]
                ins = vector.tensor_tensor(out=xs, in0=psum_ap, in1=xs, op=ALU.add)
                Ev.inc(ins, 1)
                vector.wait_ge(Ev.h, Ev.n)

            ctx.on("vector", _)
            return (Ev, Ev.n)

        emit_proj(ctx, WDirect(dr[f"L3f_wo{l}"], 512, 1024), x3_sink,
                  src=lambda kc: q3[:, kc * T:(kc + 1) * T],
                  tensor_dep=[(Ev, o_tok)])
        x3a_tok = (Ev, Ev.n)
        if dbg is not None and l == 0:
            dsm2 = ctx.new_ldsem()

            def _(sync):
                sync.wait_ge(Ev.h, o_tok)
                ins = sync.dma_start(out=dbg["o"], in_=ctx.xn_sb[:, 0:4 * T])
                dsm2.inc(ins, 16)
                sync.wait_ge(x3a_tok[0].h, x3a_tok[1])
                ins = sync.dma_start(out=dbg["x3a"], in_=x3[:])
                dsm2.inc(ins, 16)
                sync.wait_ge(dsm2.h, dsm2.n)

            ctx.on("sync", _)
        if dbg is not None and dbg.get("stop_l0a"):
            ph.close()
            ctx.barrier()
            return Ev.n

        ph.close()
        ctx.barrier()

        pf = ExitStack()
        tpf = lambda n_, s_, d_: pf.enter_context(
            nc.sbuf_tensor(n_ + str(ctx.uid()), s_, d_))
        pf1 = ExitStack()
        tpf1 = lambda n_, s_, d_: pf1.enter_context(
            nc.sbuf_tensor(n_ + str(ctx.uid()), s_, d_))
        hsh0f = tpf1("l3f_h0f", [128, 4 * 512], BF16)
        lgb = tpf1("l3f_lgb", [128, 16], F32)
        slg = ctx.new_ldsem()

        def _(g, l=l):
            ins = g.dma_start(out=lgb[:, 0:8], in_=dr[f"L3f_lg{l}"])
            slg.inc(ins, 16)
            ins = g.dma_start(out=lgb[:, 8:16], in_=dr[f"L3f_lb{l}"])
            slg.inc(ins, 16)

        ctx.on("gpsimd", _)
        slg_tok = slg.n

        sdep2 = build_hsh0(x3, x3a_tok, hsh0f)
        sh2 = mk_shift_src(x3, hsh0f)

        def _(eng):
            for s_, t_ in sdep2:
                eng.wait_ge(s_.h, t_)

        for e_ in ("tensor", "scalar", "vector"):
            ctx.on(e_, _)
        ln_tok = emit_layernorm(ctx, src_tile=sh2, src_bf16=True,
                                gain_row=(lgb[:, 0:8], (slg, slg_tok)),
                                beta_row=(lgb[:, 8:16], (slg, slg_tok)))
        pf1.close()

        h3 = tpf("l3f_h3", [128, 22 * 1024], BF16)
        b1row = tpf("l3f_b1", [1, 2 * L3F_HID], BF16)
        b2row = tpf("l3f_b2", [1, 1024], BF16)
        gsb = tpf("l3f_g", [128, 512], BF16)
        sb1 = ctx.new_ldsem()

        def _(g, l=l):
            ins = g.dma_start(out=b1row[:], in_=dr[f"L3f_b1{l}"])
            sb1.inc(ins, 16)
            ins = g.dma_start(out=b2row[:], in_=dr[f"L3f_b2{l}"])
            sb1.inc(ins, 16)

        ctx.on("gpsimd", _)
        sb1_tok = sb1.n

        w1_ap = dr[f"L3f_w1{l}"]
        w2_ap = dr[f"L3f_w2{l}"]
        x3f_tok = None
        for half in range(2):
            tthalf = [half * 2, half * 2 + 1]
            gelu_guard = [0]
            for bi_i, (nb0, wk) in enumerate(L3F_BLOCKS):
                for role in range(2):
                    col0 = nb0 + role * L3F_HID
                    bi = (bi_i * 2 + role) % 2
                    dsem = ctx.dwb[bi]
                    prev = ctx.wbuf_state[bi]

                    def _(g, col0=col0, wk=wk, bi=bi, prev=prev, dsem=dsem):
                        if prev is not None:
                            g.wait_ge(prev[0].h, prev[1])
                        src = w1_ap[:, col0:col0 + wk].rearrange("(c p) n -> p c n", p=128)
                        dst = ctx.wbuf[bi][:, 0:8 * wk].rearrange("p (c n) -> p c n", n=wk)
                        ins = g.dma_start(out=dst, in_=src)
                        dsem.inc(ins, 16)

                    ctx.on("gpsimd", _)
                    ld_tok = dsem.n

                    for ti, tt in enumerate(tthalf):
                        bank = ctx.next_bank()

                        def _(tensor, bi=bi, wk=wk, tt=tt, col0=col0, ld_tok=ld_tok,
                              dsem=dsem):
                            tensor.wait_ge(dsem.h, ld_tok)
                            tensor.wait_ge(Ev.h, ln_tok)
                            tensor.wait_ge(sb1.h, sb1_tok)
                            ctx.psum_wait(tensor, bank)
                            for kc in range(8):
                                tensor.matmul(
                                    ctx.ps[bank][0:wk, :],
                                    ctx.wbuf[bi][:, kc * wk:(kc + 1) * wk],
                                    ctx.xn_sb[:, kc * T + tt * 512: kc * T + (tt + 1) * 512],
                                    start=(kc == 0), stop=False)
                            ins = tensor.matmul(ctx.ps[bank][0:wk, :],
                                                b1row[0:1, col0:col0 + wk],
                                                ctx.ones_row_t[0:1, 0:512],
                                                start=False, stop=True)
                            Et.inc(ins, 1)

                        ctx.on("tensor", _)
                        mm_tok = Et.n
                        ctx.wbuf_state[bi] = (Et, mm_tok)

                        if role == 0:
                            def _(scalar, bi_i=bi_i, ti=ti, wk=wk, mm_tok=mm_tok):
                                scalar.wait_ge(Et.h, mm_tok)
                                ins = scalar.activation(
                                    out=h3[0:wk, bi_i * 1024 + ti * 512: bi_i * 1024 + (ti + 1) * 512],
                                    in_=ctx.ps[bank][0:wk, :], func=AF.Copy)
                                Es.inc(ins, 1)

                            ctx.on("scalar", _)
                            ctx.psum_set_owner(bank, Es, Es.n)
                        else:
                            def _(scalar, wk=wk, mm_tok=mm_tok, gg=gelu_guard[0]):
                                scalar.wait_ge(Et.h, mm_tok)
                                if gg:
                                    scalar.wait_ge(Ev.h, gg)
                                ins = scalar.activation(out=gsb[0:wk, :],
                                                        in_=ctx.ps[bank][0:wk, :],
                                                        func=AF.Gelu)
                                Es.inc(ins, 1)

                            ctx.on("scalar", _)
                            g_tok = Es.n
                            ctx.psum_set_owner(bank, Es, g_tok)

                            def _(vector, bi_i=bi_i, ti=ti, wk=wk, g_tok=g_tok):
                                vector.wait_ge(Es.h, g_tok)
                                hsl = h3[0:wk, bi_i * 1024 + ti * 512: bi_i * 1024 + (ti + 1) * 512]
                                ins = vector.tensor_tensor(out=hsl, in0=hsl,
                                                           in1=gsb[0:wk, :], op=ALU.mult)
                                t_ = Ev.inc(ins, 1)
                                vector.wait_ge(Ev.h, t_)

                            ctx.on("vector", _)
                            gelu_guard[0] = Ev.n
            h3_tok = Ev.n

            for dg in range(4):
                def _(tensor):
                    for b_ in (0, 1, 2, 3):
                        ctx.psum_wait(tensor, b_)

                ctx.on("tensor", _)
                for kci, (nb0, wk) in enumerate(L3F_BLOCKS):
                    bi = kci % 2
                    dsem = ctx.dwb[bi]
                    prev = ctx.wbuf_state[bi]

                    def _(g, nb0=nb0, wk=wk, dg=dg, bi=bi, prev=prev, dsem=dsem):
                        if prev is not None:
                            g.wait_ge(prev[0].h, prev[1])
                        ins = g.dma_start(out=ctx.wbuf[bi][0:wk, 0:256],
                                          in_=w2_ap[nb0:nb0 + wk, dg * 256:(dg + 1) * 256])
                        dsem.inc(ins, 16)

                    ctx.on("gpsimd", _)
                    ld_tok = dsem.n

                    def _(tensor, kci=kci, wk=wk, bi=bi, dg=dg, ld_tok=ld_tok,
                          dsem=dsem, h3_tok=h3_tok):
                        tensor.wait_ge(dsem.h, ld_tok)
                        tensor.wait_ge(Ev.h, h3_tok)
                        if kci == 21:
                            tensor.wait_ge(sb1.h, sb1_tok)
                        ins = None
                        for dd in range(2):
                            for ti in range(2):
                                bank = dd * 2 + ti
                                ins = tensor.matmul(
                                    ctx.ps[bank][:, :],
                                    ctx.wbuf[bi][0:wk, dd * 128:(dd + 1) * 128],
                                    h3[0:wk, kci * 1024 + ti * 512: kci * 1024 + (ti + 1) * 512],
                                    start=(kci == 0), stop=False)
                                if kci == 21:
                                    dcol = dg * 256 + dd * 128
                                    ins = tensor.matmul(ctx.ps[bank][:, :],
                                                        b2row[0:1, dcol:dcol + 128],
                                                        ctx.ones_row_t[0:1, 0:512],
                                                        start=False, stop=True)
                        Et.inc(ins, 1)

                    ctx.on("tensor", _)
                    ctx.wbuf_state[bi] = (Et, Et.n)
                w2_mm = Et.n

                for dd in range(2):
                    for ti in range(2):
                        bank = dd * 2 + ti
                        oc = (dg * 256 + dd * 128) // 128
                        tt = half * 2 + ti

                        def _(vector, bank=bank, oc=oc, tt=tt, w2_mm=w2_mm):
                            vector.wait_ge(Et.h, w2_mm)
                            xs = x3[:, oc * T + tt * 512: oc * T + (tt + 1) * 512]
                            ins = vector.tensor_tensor(out=xs, in0=ctx.ps[bank][:, :],
                                                       in1=xs, op=ALU.add)
                            Ev.inc(ins, 1)
                            vector.wait_ge(Ev.h, Ev.n)

                        ctx.on("vector", _)
                        ctx.psum_set_owner(bank, Ev, Ev.n)
            x3f_tok = (Ev, Ev.n)

        prev_tok = x3f_tok
        pf.close()
        ctx.barrier()

    def _(vector):
        vector.wait_ge(prev_tok[0].h, prev_tok[1])
        for c_ in range(NCH):
            for tt in range(4):
                xs = ctx.x_sb[:, c_ * T + tt * 512: c_ * T + (tt + 1) * 512]
                ins = vector.scalar_tensor_tensor(
                    out=xs, in0=x3[:, c_ * T + tt * 512: c_ * T + (tt + 1) * 512],
                    scalar=0.5, in1=xs, op0=ALU.mult, op1=ALU.add)
                t_ = Ev.inc(ins, 1)
                vector.wait_ge(Ev.h, t_)

    ctx.on("vector", _)
    return Ev.n



L0A_INPUT_SPECS = [
    ("L0a_Wq", [D, D], F32), ("L0a_Wk", [D, D], F32),
    ("L0a_Wv", [D, D], F32), ("L0a_Wo", [D, D], F32), ("L0a_bo", [D], F32),
]
L1F_INPUT_SPECS = [
    ("L1f_wr", [D], F32), ("L1f_W1", [D, 4096], F32), ("L1f_b1", [4096], F32),
    ("L1f_W2", [4096, D], F32), ("L1f_b2", [D], F32),
]
L2F_INPUT_SPECS = [
    ("L2f_w_gate", [2816, D], F32), ("L2f_w_up", [2816, D], F32),
    ("L2f_w_down", [D, 2816], F32),
]
PERCORE_L1A_SPECS = [("c_mypos", [1, T], F32), ("c_distR", [1, 8448], F32)]
PERCORE_L3A_SPECS = [("c_l3a_c32", [32, T], BF16), ("c_l3a_s32", [32, T], BF16),
                     ("c_stepE", [1, 8448], BF16)]

STEP_DRAM = {
    "l0f": L0F_INPUT_SPECS + [("c_ind_er", [8, 128], BF16)],
    "l0a": L0A_INPUT_SPECS,
    "l1f": L1F_INPUT_SPECS,
    "l1a": L1A_INPUT_SPECS + PERCORE_L1A_SPECS,
    "l2f": L2F_INPUT_SPECS,
    "l2a": L2A_INPUT_SPECS,
    "l3f": L3F_INPUT_SPECS,
    "l3a": L3A_INPUT_SPECS + PERCORE_L3A_SPECS,
}

MODULE_STEPS = ["l0f", "l0a", "l0f", "l1f", "l1a", "l1f",
                "l2f", "l2a", "l2f", "l3f", "l3a", "l3f"]
FUSE = True


def emit_module_steps(ctx, kinds):
    l2f_scr = [None]
    last = None
    counts = {}
    for kind in kinds:
        i = counts.get(kind, 0)
        counts[kind] = i + 1
        ctx.sublayer_scope()
        if kind == "l0f":
            last = emit_l0_ffn(ctx, emit_layernorm(ctx))
        elif kind == "l0a":
            bo_row = emit_bias_row_load(ctx, ctx.dram["L0a_bo"], D)
            last = emit_attn_dense(ctx, f"l0a{i}", emit_layernorm(ctx),
                                   WDirect(ctx.dram["L0a_Wq"], D, D),
                                   WDirect(ctx.dram["L0a_Wk"], D, D),
                                   WDirect(ctx.dram["L0a_Wv"], D, D),
                                   WDirect(ctx.dram["L0a_Wo"], D, D),
                                   exp_scale=0.125, bo_row=bo_row)
        elif kind == "l1f":
            last = emit_l1_ffn(ctx, emit_layernorm(ctx), f"l1f{i}")
        elif kind == "l1a":
            last = emit_l1_attn(ctx, emit_layernorm(ctx))
        elif kind == "l2f":
            if l2f_scr[0] is None:
                l2f_scr[0] = emit_l2f_weight_prep(ctx)
            wg, wu, wd, wdep = l2f_scr[0]
            last = emit_l2_ffn(ctx, [], wg, wu, wd, wdep, f"l2f{i}")
        elif kind == "l2a":
            last = emit_l2_attn(ctx, f"l2a{i}", emit_layernorm(ctx))
        elif kind == "l3f":
            last = emit_l3_ffn(ctx, f"l3f{i}", emit_layernorm(ctx))
        elif kind == "l3a":
            last = emit_l3_attn(ctx, emit_layernorm(ctx))
        else:
            raise ValueError(kind)
    return last


def build_program(kinds):
    nc = bass.Bass("TRN2", target_bir_lowering=False, debug=False)
    ctx = Ctx(nc)
    ctx.const_kinds = set(kinds)
    ctx.dram["x_in"] = nc.dram_tensor("x_in", [128, NCH * T], F32,
                                      kind="ExternalInput").ap()
    ctx.dram["x_out"] = nc.dram_tensor("x_out", [128, NCH * T], F32,
                                       kind="ExternalOutput").ap()
    names = set()
    for kind in dict.fromkeys(kinds):
        for name, shape, dt in STEP_DRAM[kind]:
            if name in names:
                continue
            names.add(name)
            ctx.dram[name] = nc.dram_tensor(name, list(shape), dt,
                                            kind="ExternalInput").ap()
    setup_const_inputs(ctx)
    setup_shared_flags(ctx)
    with ctx.stack:
        alloc_core_buffers(ctx)
        ctx.setup_sems()
        ctx.block = ctx.e(nc.Block())
        xtok = emit_setup(ctx, extra_loads=[(ctx.x_sb[:], ctx.dram["x_in"])])
        dsem = ctx.dma["sync"]
        for e_ in ("tensor", "scalar", "vector"):
            def _(eng):
                eng.wait_ge(ctx.dconst.h, xtok)
            ctx.on(e_, _)
        res_tok = emit_module_steps(ctx, kinds)

        def _(sync):
            sync.wait_ge(ctx.E["vector"].h, res_tok)
            ins = sync.dma_start(out=ctx.dram["x_out"], in_=ctx.x_sb[:])
            dsem.inc(ins, 16)
            sync.wait_ge(dsem.h, dsem.n)

        ctx.on("sync", _)
        ctx.close_scopes()
    return nc, ctx, names


def module_host_inputs(inputs):
    f32 = lambda a: np.ascontiguousarray(np.asarray(a, dtype=np.float32))
    hin = dict(l0_ffn_host_inputs(inputs))
    for nm in ("Wq", "Wk", "Wv", "Wo", "bo"):
        hin["L0a_" + nm] = f32(inputs["L0_attn_" + nm])
    hin.update(l1_attn_host_inputs(inputs))
    hin["L1f_wr"] = f32(inputs["L1_ffn_w_router"])
    hin["L1f_W1"] = f32(inputs["L1_ffn_W1"])
    hin["L1f_b1"] = f32(inputs["L1_ffn_b1"])
    hin["L1f_W2"] = f32(inputs["L1_ffn_W2"])
    hin["L1f_b2"] = f32(inputs["L1_ffn_b2"])
    hin["L2f_w_gate"] = f32(inputs["L2_ffn_w_gate"])
    hin["L2f_w_up"] = f32(inputs["L2_ffn_w_up"])
    hin["L2f_w_down"] = f32(inputs["L2_ffn_w_down"])
    hin.update(l2a_host_inputs(
        f32(inputs["L2_attn_Wq"]), f32(inputs["L2_attn_bq"]),
        f32(inputs["L2_attn_Wk"]), f32(inputs["L2_attn_bk"]),
        f32(inputs["L2_attn_Wv"]), f32(inputs["L2_attn_bv"]),
        f32(inputs["L2_attn_Wo"]), f32(inputs["L2_attn_bo"]),
        f32(inputs["L2_attn_Wqs"]), f32(inputs["L2_attn_bqs"]),
        f32(inputs["L2_attn_Wks"]), f32(inputs["L2_attn_bks"]),
        f32(inputs["L2_attn_Wvs"]), f32(inputs["L2_attn_bvs"]),
        np.asarray(inputs["L2_attn_temp"], dtype=np.float32)))
    for nm in ("wq", "wk", "wv", "wo"):
        hin["L3a_" + nm] = f32(inputs["L3_attn_" + nm])
    hin.update(l3f_host_inputs(
        f32(inputs["L3_ffn_wqkv"]), f32(inputs["L3_ffn_q_scale"]),
        f32(inputs["L3_ffn_k_scale"]), f32(inputs["L3_ffn_wo"]),
        f32(inputs["L3_ffn_ln_g"]), f32(inputs["L3_ffn_ln_b"]),
        f32(inputs["L3_ffn_w1"]), f32(inputs["L3_ffn_b1"]),
        f32(inputs["L3_ffn_w2"]), f32(inputs["L3_ffn_b2"])))
    return hin


def module_percore_inputs(core):
    m = {}
    m.update(percore_consts(core))
    m.update(l3a_percore_consts(core))
    m.update(l3f_percore_inputs(core % 2))
    m["c_ind_er"] = ind_er_const()
    return m


def kernel(**inputs):
    x = np.asarray(inputs["L0_attn_x"], dtype=np.float32)
    hin = module_host_inputs(inputs)
    percore = [module_percore_inputs(c) for c in range(8)]
    xs = [to_feature_major(x[c >> 1, (c & 1) * T:(c & 1) * T + T])
          for c in range(8)]
    groups = [MODULE_STEPS] if FUSE else [[k] for k in MODULE_STEPS]
    progs = {}
    for g in groups:
        key = tuple(g)
        if key not in progs:
            progs[key] = build_program(list(g))
    for g in groups:
        nc, ctx, names = progs[tuple(g)]
        in_maps = []
        for c in range(8):
            m = {k: v for k, v in ctx.consts.items()}
            m["c_slot"] = percore[c]["c_slot"]
            for k in names:
                m[k] = hin[k] if k in hin else percore[c][k]
            m["x_in"] = xs[c]
            in_maps.append(m)
        r = run_bass_kernel_spmd(nc, in_maps, core_ids=list(range(8)))
        xs = [np.asarray(r.results[c]["x_out"]) for c in range(8)]
    out = np.zeros((4, S, D), dtype=np.float32)
    for c in range(8):
        h = (c & 1) * T
        out[c >> 1, h:h + T] = from_feature_major(xs[c])
    return out
